# version 05
# baseline (speedup 1.0000x reference)
import numpy as np
import concourse.bass as bass
import concourse.mybir as mybir
from concourse.bass_utils import run_bass_kernel_spmd

F32 = mybir.dt.float32
BF16 = mybir.dt.bfloat16
I32 = mybir.dt.int32
AF = mybir.ActivationFunctionType
OP = mybir.AluOpType

D = 1024
KC = 8
H = 3072
HC = 24
T = 4096
TT = 512
NT = T // TT
P = 128
RING = 6
UNIT = 4096

S127 = 1.0 / 127.0
DOWN_SCALE = 0.5 * S127 * S127
LOG1EM7 = float(np.float32(np.log(np.float32(1e-7))))
DEBUG_TAPS = False
INTERLEAVE = True


class Sched:
    def __init__(self):
        self.ops = {"sync": [], "gpsimd": [], "tensor": [], "vector": [], "scalar": []}
        self.cnt = {}
        self.marks = {}

    def val(self, sem):
        return self.cnt.get(sem, 0)

    def emit(self, eng, fn, waits=(), inc=None, incn=1):
        ws = []
        for s, v in waits:
            if not v or v <= 0:
                continue
            key = (eng, s)
            if self.marks.get(key, 0) >= v:
                continue
            self.marks[key] = v
            ws.append((s, v))
        iv = None
        if inc is not None:
            self.cnt[inc] = self.cnt.get(inc, 0) + incn
            iv = (inc, incn)
        self.ops[eng].append((ws, fn, iv))
        return self.cnt.get(inc, 0) if inc else None


def build_nc():
    nc = bass.Bass("TRN2", target_bir_lowering=False)

    xh = nc.dram_tensor("x", [P, KC, T], F32, kind="ExternalInput")
    wg_i = nc.dram_tensor("wg", [P, HC * 8 * 128], I32, kind="ExternalInput")
    wu_i = nc.dram_tensor("wu", [P, HC * 8 * 128], I32, kind="ExternalInput")
    wd_i = nc.dram_tensor("wd", [P, 8 * HC * 128], I32, kind="ExternalInput")
    wr_i = nc.dram_tensor("wr", [P, KC * 1024], F32, kind="ExternalInput")
    wi_i = nc.dram_tensor("wi", [P, KC * 1024], F32, kind="ExternalInput")
    wo_i = nc.dram_tensor("wo", [P, KC * 1024], F32, kind="ExternalInput")
    rb_i = nc.dram_tensor("rb", [P, KC], F32, kind="ExternalInput")
    ib_i = nc.dram_tensor("ib", [P, KC], F32, kind="ExternalInput")
    la_i = nc.dram_tensor("la", [P, KC], F32, kind="ExternalInput")
    nw_i = nc.dram_tensor("nw", [P, KC], F32, kind="ExternalInput")
    cw_i = nc.dram_tensor("cw", [P, 32], F32, kind="ExternalInput")
    cb_i = nc.dram_tensor("cb", [P, KC], F32, kind="ExternalInput")
    out_h = nc.dram_tensor("out", [P, KC, T], F32, kind="ExternalOutput")

    wg_s = nc.dram_tensor("wg_s", [P, HC * 8 * 128], BF16, kind="Internal")
    wu_s = nc.dram_tensor("wu_s", [P, HC * 8 * 128], BF16, kind="Internal")
    wd_s = nc.dram_tensor("wd_s", [P, 8 * HC * 128], BF16, kind="Internal")
    wr_s = nc.dram_tensor("wr_s", [P, KC * 1024], BF16, kind="Internal")
    wi_s = nc.dram_tensor("wi_s", [P, KC * 1024], BF16, kind="Internal")
    wo_s = nc.dram_tensor("wo_s", [P, KC * 1024], BF16, kind="Internal")

    import contextlib
    ctx = contextlib.ExitStack()
    nc._kernel_ctx = ctx
    sb = lambda n, s, d=F32: ctx.enter_context(nc.sbuf_tensor(n, s, d))

    RES = [sb("RES0", [P, T]), sb("RES1", [P, T])]
    SA = sb("SA", [P, T])
    SB = sb("SB", [P, T])
    SC_ = sb("SC", [P, T])
    SD = sb("SD", [P, T])
    BFS = sb("BFS", [P, 2 * T], BF16)
    HATT = sb("HATT", [P, T], BF16)
    HH = sb("HH", [P, HC * TT], BF16)
    LTMP = sb("LTMP", [P, TT + 8])
    ROWP = LTMP
    TAIL = sb("TAIL", [P, KC * 4])
    WRG = [sb(f"WRG{j}", [P, UNIT], BF16) for j in range(RING)]
    SG = [sb("SG0", [P, TT]), sb("SG1", [P, TT])]
    ONES = sb("ONESF", [P, TT])
    ONESR = ONES
    ONESB = sb("ONESB", [P, 1], BF16)
    CONSTS = sb("CONSTS", [P, 8])
    CW = sb("CWt", [P, 32])
    CB = sb("CBt", [P, KC])
    RB = sb("RBt", [P, KC])
    IB = sb("IBt", [P, KC])
    LAT = sb("LAt", [P, KC])
    NW = sb("NWt", [P, KC])
    L8 = sb("L8", [P, KC])
    L16 = sb("L16", [P, KC])
    SCAR = sb("SCAR", [P, KC])
    CCAR = sb("CCAR", [P, KC])
    RED = sb("REDt", [P, KC])
    PB = [ctx.enter_context(nc.psum_tensor(f"PB{j}", [P, TT], F32)) for j in range(8)]

    cvals = [S127, -1.0, -0.5, 1e-5, 1e-6]
    for j_, v in enumerate(cvals):
        nc.const_aps.register(F32, float(np.float32(v)), CONSTS[:, j_:j_ + 1])
    C127, CM1, CMH, EPS5, EPS6 = [float(np.float32(v)) for v in cvals]

    SEMS = {}
    for n in ["PE", "DV", "AC", "WL", "X", "OUT", "CV"]:
        SEMS[n] = ctx.enter_context(nc.semaphore("sem_" + n))

    sch = Sched()
    ring_state = {"u": 0, "done": {}, "loaded": {}}

    def unit_load(src_ap, ncols):
        u = ring_state["u"]
        ring_state["u"] += 1
        slot = WRG[u % RING]
        waits = [("CV", src_cast_done[src_ap.tensor.name])]
        prev = u - RING
        if prev >= 0:
            waits.append(("PE", ring_state["done"].get(prev, 0)))
        def fn(e, slot=slot, src=src_ap, ncols=ncols):
            return e.dma_start(slot[:, 0:ncols], src)
        v = sch.emit("sync", fn, waits=waits, inc="WL", incn=16)
        ring_state["loaded"][u] = v
        return slot, u

    def unit_used(u):
        ring_state["done"][u] = sch.val("PE")

    src_cast_done = {}
    def cast_dma(dst, src):
        def fn(e, dst=dst, src=src):
            return e.dma_start(dst[:], src[:])
        v = sch.emit("gpsimd", fn, inc="CV", incn=16)
        src_cast_done[dst.name] = v

    cast_dma(wg_s, wg_i)
    cast_dma(wu_s, wu_i)
    cast_dma(wd_s, wd_i)
    cast_dma(wr_s, wr_i)
    cast_dma(wi_s, wi_i)
    cast_dma(wo_s, wo_i)

    for dst, src in [(CW, cw_i), (CB, cb_i), (RB, rb_i), (IB, ib_i),
                     (LAT, la_i), (NW, nw_i)]:
        sch.emit("sync", (lambda e, d=dst, s=src: e.dma_start(d[:], s[:])),
                 inc="X", incn=16)
    x_ready = {}
    for i in range(2):
        x_ready[i] = sch.emit(
            "gpsimd",
            (lambda e, i=i: e.dma_start(RES[i % 2][:].rearrange("p (c t) -> p c t", c=KC),
                                        xh[:, :, i * TT:(i + 1) * TT])),
            inc="X", incn=16)

    def ms(ap, v):
        sch.emit("gpsimd", (lambda e, ap=ap, v=v: e.memset(ap, v)))
    ms(ONES[:], 1.0)
    ms(ONESB[:], 1.0)
    ms(TAIL[:], 0.0)
    ms(SCAR[:], 0.0)
    ms(CCAR[:], 0.0)
    for j_, v in enumerate(cvals):
        ms(CONSTS[:, j_:j_ + 1], float(np.float32(v)))
    setup_done = sch.emit("gpsimd", (lambda e: e.memset(CONSTS[:, 7:8], 0.0)),
                          inc="X", incn=1)

    sch.emit("scalar", (lambda e: e.activation(L8[:], LAT[:], AF.Sigmoid)),
             waits=[("X", setup_done)], inc="AC")
    v_ln = sch.emit("scalar", (lambda e: e.activation(L8[:], L8[:], AF.Ln)), inc="AC")
    sch.emit("vector",
             (lambda e: e.tensor_scalar(L16[:], L8[:], 16.0, -1e30, OP.mult, OP.max)),
             waits=[("AC", v_ln)], inc="DV")
    v_l8 = sch.emit("vector",
                    (lambda e: e.tensor_scalar(L8[:], L8[:], 8.0, -1e30, OP.mult, OP.max)),
                    inc="DV")

    tv = [dict() for _ in range(NT + 1)]
    state = {"scar_v": 0, "ccar_v": 0, "last_silu": 0, "last_rowp": 0}

    dbg = {}
    if DEBUG_TAPS:
        dbg["d_scar"] = nc.dram_tensor("d_scar", [P, KC], F32, kind="ExternalOutput")
        for nm in ["d_l8a", "d_l8b", "d_l8c", "d_l8d"]:
            dbg[nm] = nc.dram_tensor(nm, [P, KC], F32, kind="ExternalOutput")
        dbg["d_r1pre"] = nc.dram_tensor("d_r1pre", [P, T], F32, kind="ExternalOutput")
        dbg["d_xc1"] = nc.dram_tensor("d_xc1", [P, T], F32, kind="ExternalOutput")
        dbg["d_xcb1"] = nc.dram_tensor("d_xcb1", [P, T], BF16, kind="ExternalOutput")
        dbg["d_rstd2"] = nc.dram_tensor("d_rstd2", [1, TT], F32, kind="ExternalOutput")
        dbg["d_q2"] = nc.dram_tensor("d_q2", [1, TT], F32, kind="ExternalOutput")
        for nm, dt_ in [("d_x1", F32), ("d_o", F32), ("d_x2", F32), ("d_hatt", BF16),
                        ("d_logA", F32), ("d_aexp", F32), ("d_cs", F32), ("d_term", F32),
                        ("d_la", F32), ("d_s", F32),
                        ("d_msrow", F32), ("d_rstd", F32)]:
            dbg[nm] = nc.dram_tensor(nm, [P, T] if nm not in ("d_msrow", "d_rstd")
                                     else [1, TT], dt_, kind="ExternalOutput")

    def tap(name, src_ap, dv_wait=None, ac_wait=None):
        if not DEBUG_TAPS:
            return
        w = []
        if dv_wait: w.append(("DV", dv_wait))
        if ac_wait: w.append(("AC", ac_wait))
        v = sch.emit("gpsimd", (lambda e, s=src_ap, d=dbg[name]: e.dma_start(d[:], s)),
                     waits=w, inc="OUT", incn=16)
        sch.emit("vector", (lambda e: e.engine_nop()), waits=[("OUT", v)], inc="DV")
        state["dbg_out"] = v

    def ln_stats(i, res, pb_m, pb_v, bfs1_war_pe, bfs2_war_pe, res_dv=0,
                 extra_pe_waits=()):
        r = {}
        r["xb"] = sch.emit("vector", (lambda e: e.tensor_copy(BFS[:, 0:T], res[:])),
                           waits=[("PE", bfs1_war_pe), ("X", x_ready.get(i, 0)),
                                  ("X", setup_done), ("DV", res_dv)], inc="DV")
        r["xsq"] = sch.emit("scalar",
                            (lambda e: e.activation(BFS[:, T:2 * T], res[:], AF.Square)),
                            waits=[("PE", bfs2_war_pe), ("X", x_ready.get(i, 0)),
                                   ("X", setup_done), ("DV", res_dv)], inc="AC")

        def mm_stats(e, pb, off):
            last = None
            for kc in range(KC):
                last = e.matmul(pb[0:1, :], ONESB[:],
                                BFS[:, off + kc * TT: off + (kc + 1) * TT],
                                start=(kc == 0), stop=(kc == KC - 1))
            return last
        r["sm"] = sch.emit("tensor", (lambda e: mm_stats(e, pb_m, 0)),
                           waits=[("DV", r["xb"]), ("X", setup_done)] + list(extra_pe_waits),
                           inc="PE")
        r["sv"] = sch.emit("tensor", (lambda e: mm_stats(e, pb_v, T)),
                           waits=[("AC", r["xsq"])], inc="PE")
        sch.emit("vector",
                 (lambda e: e.tensor_scalar_mul(SG[0][0:1, :], pb_m[0:1, :], 1.0 / D)),
                 waits=[("PE", r["sm"]), ("AC", state["last_silu"]),
                        ("AC", state["last_rowp"])], inc="DV")
        sch.emit("vector",
                 (lambda e: e.tensor_mul(ROWP[0:1, 0:TT], SG[0][0:1, :], SG[0][0:1, :])),
                 inc="DV")
        r["var"] = sch.emit("vector",
                            (lambda e: e.scalar_tensor_tensor(
                                SG[1][0:1, :], pb_v[0:1, :], 1.0 / D,
                                ROWP[0:1, 0:TT], OP.mult, OP.subtract)),
                            waits=[("PE", r["sv"])], inc="DV")
        sch.emit("scalar",
                 (lambda e: e.activation(SG[1][0:1, :], SG[1][0:1, :], AF.Ln, bias=EPS5)),
                 waits=[("DV", r["var"])], inc="AC")
        r["p"] = sch.emit("scalar",
                          (lambda e: e.activation(ROWP[0:1, 0:TT], SG[1][0:1, :],
                                                  AF.Exp, scale=CMH)),
                          inc="AC")
        state["last_rowp"] = r["p"]
        r["q"] = sch.emit("vector",
                          (lambda e: e.scalar_tensor_tensor(SG[0][0:1, :], SG[0][0:1, :],
                                                            -1.0, ROWP[0:1, 0:TT],
                                                            OP.mult, OP.mult)),
                          waits=[("AC", r["p"])], inc="DV")
        r["rows_dv"] = sch.val("DV")
        return r

    def bcast_rows(rr, pb_p, pb_q, war_ac=0, war_dv=0):
        v_bp = sch.emit("tensor",
                        (lambda e: e.matmul(pb_p[:], ONESR[0:1, 0:P], ROWP[0:1, 0:TT],
                                            start=True, stop=True)),
                        waits=[("AC", rr["p"]), ("AC", war_ac), ("DV", war_dv)], inc="PE")
        v_bq = sch.emit("tensor",
                        (lambda e: e.matmul(pb_q[:], ONESR[0:1, 0:P], SG[0][0:1, :],
                                            start=True, stop=True)),
                        waits=[("DV", rr["q"])], inc="PE")
        return v_bp, v_bq

    def normalize_bf16(res, v_bp, v_bq, pb_p, pb_q, bfs1_war_pe):
        last = 0
        for c in range(KC):
            sch.emit("vector",
                     (lambda e, c=c: e.tensor_mul(LTMP[:, 0:TT],
                                                  res[:, c * TT:(c + 1) * TT], pb_p[:])),
                     waits=[("PE", v_bp), ("PE", v_bq), ("PE", bfs1_war_pe)], inc="DV")
            last = sch.emit("vector",
                            (lambda e, c=c: e.tensor_add(BFS[:, c * TT:(c + 1) * TT],
                                                         LTMP[:, 0:TT], pb_q[:])),
                            inc="DV")
        return last

    def mid_make(i):
        g = {}
        def g_la_ixc():
            if DEBUG_TAPS and i == 1:
                tap("d_l8c", L8[:], ac_wait=tv[i]["sig_r"])
                tap("d_r1pre", SC_[:], ac_wait=tv[i]["sig_r"])
            for c in range(KC):
                sch.emit("vector",
                         (lambda e, c=c: e.tensor_scalar(
                             SA[:, c * TT:(c + 1) * TT], SC_[:, c * TT:(c + 1) * TT],
                             L8[:, c:c + 1], LOG1EM7, OP.mult, OP.max)),
                         waits=[("AC", tv[i]["sig_r"]), ("DV", v_l8)], inc="DV")
            for c in range(KC):
                tv[i]["ixc"] = sch.emit(
                    "vector",
                    (lambda e, c=c: e.tensor_mul(
                        SD[:, c * TT:(c + 1) * TT], SD[:, c * TT:(c + 1) * TT],
                        SB[:, c * TT:(c + 1) * TT])),
                    waits=[("AC", tv[i]["sig_i"])], inc="DV")
        def a_a2():
            for c in range(KC):
                tv[i]["a2"] = sch.emit(
                    "scalar",
                    (lambda e, c=c: e.activation(
                        SB[:, c * TT:(c + 1) * TT], SC_[:, c * TT:(c + 1) * TT],
                        AF.Exp, scale=L16[:, c:c + 1])),
                    waits=[("DV", tv[i]["ixc"])], inc="AC")
        def g_v():
            sch.emit("vector",
                     (lambda e: e.tensor_scalar(SC_[:], SB[:], 1.0, -1.0,
                                                OP.subtract, OP.mult)),
                     waits=[("AC", tv[i]["a2"])], inc="DV")
            tv[i]["vmax"] = sch.emit(
                "vector", (lambda e: e.tensor_scalar_max(SC_[:], SC_[:], 1e-6)), inc="DV")
        def a_bamp():
            tv[i]["bamp"] = sch.emit(
                "scalar", (lambda e: e.activation(SB[:], SC_[:], AF.Sqrt)),
                waits=[("DV", tv[i]["vmax"])], inc="AC")
        def g_bt():
            tv[i]["bt"] = sch.emit(
                "vector", (lambda e: e.tensor_mul(SD[:], SD[:], SB[:])),
                waits=[("AC", tv[i]["bamp"])], inc="DV")
        def g_ss(c0, c1):
            def f():
                for c in range(c0, c1):
                    sch.emit("vector",
                             (lambda e, c=c: e.tensor_tensor_scan(
                                 SB[:, c * TT:(c + 1) * TT], ONES[:],
                                 SA[:, c * TT:(c + 1) * TT], SCAR[:, c:c + 1],
                                 OP.mult, OP.add)),
                             waits=[("DV", state["scar_v"]), ("DV", tv[i]["bt"])],
                             inc="DV")
            return f
        def g_sred():
            for c in range(KC):
                sch.emit("vector",
                         (lambda e, c=c: e.tensor_reduce(
                             RED[:, c:c + 1], SA[:, c * TT:(c + 1) * TT],
                             mybir.AxisListType.X, OP.add)),
                         inc="DV")
            sch.emit("vector", (lambda e: e.engine_nop()), inc="DV")
            sch.emit("vector", (lambda e: e.engine_nop()), inc="DV")
            state["scar_v"] = sch.emit(
                "vector", (lambda e: e.tensor_add(SCAR[:], SCAR[:], RED[:])), inc="DV")
        def g_logA():
            tv[i]["logA"] = sch.emit(
                "vector", (lambda e: e.tensor_scalar_max(SB[:], SB[:], -80.0)), inc="DV")
            if DEBUG_TAPS and i == 1:
                tap("d_logA", SB[:], dv_wait=tv[i]["logA"])
        def a_en():
            tv[i]["en"] = sch.emit(
                "scalar", (lambda e: e.activation(SA[:], SB[:], AF.Exp, scale=CM1)),
                waits=[("DV", tv[i]["logA"])], inc="AC")
        def g_term():
            tv[i]["term"] = sch.emit(
                "vector", (lambda e: e.tensor_mul(SD[:], SD[:], SA[:])),
                waits=[("AC", tv[i]["en"])], inc="DV")
            if DEBUG_TAPS and i == 1:
                tap("d_term", SD[:], dv_wait=tv[i]["term"])
        def a_aexp():
            tv[i]["aexp"] = sch.emit(
                "scalar", (lambda e: e.activation(SC_[:], SB[:], AF.Exp)),
                waits=[("DV", tv[i]["term"])], inc="AC")
        def g_cs(c0, c1):
            def f():
                for c in range(c0, c1):
                    sch.emit("vector",
                             (lambda e, c=c: e.tensor_tensor_scan(
                                 SA[:, c * TT:(c + 1) * TT], ONES[:],
                                 SD[:, c * TT:(c + 1) * TT], CCAR[:, c:c + 1],
                                 OP.mult, OP.add)),
                             waits=[("DV", state["ccar_v"]), ("DV", tv[i]["term"])],
                             inc="DV")
            return f
        def g_cred():
            for c in range(KC):
                sch.emit("vector",
                         (lambda e, c=c: e.tensor_reduce(
                             RED[:, c:c + 1], SD[:, c * TT:(c + 1) * TT],
                             mybir.AxisListType.X, OP.add)),
                         inc="DV")
            sch.emit("vector", (lambda e: e.engine_nop()), inc="DV")
            sch.emit("vector", (lambda e: e.engine_nop()), inc="DV")
            state["ccar_v"] = sch.emit(
                "vector", (lambda e: e.tensor_add(CCAR[:], CCAR[:], RED[:])), inc="DV")
        def g_hatt():
            if DEBUG_TAPS and i == 1:
                tap("d_aexp", SC_[:], ac_wait=tv[i]["aexp"])
                tap("d_cs", SA[:], dv_wait=state["ccar_v"])
            tv[i]["hatt"] = sch.emit(
                "vector", (lambda e: e.tensor_mul(HATT[:], SC_[:], SA[:])),
                waits=[("AC", tv[i]["aexp"]),
                       ("PE", tv[i - 1].get("outw", 0) if i > 0 else 0)],
                inc="DV")
        g["dve"] = {2: [g_la_ixc], 4: [g_v], 6: [g_bt], 8: [g_ss(0, 4)],
                    10: [g_ss(4, 8)], 12: [g_sred, g_logA], 14: [g_term],
                    16: [g_cs(0, 4)], 18: [g_cs(4, 8)], 20: [g_cred], 21: [g_hatt]}
        g["act"] = {2: [a_a2], 4: [a_bamp], 13: [a_en], 17: [a_aexp]}
        g["all"] = [g_la_ixc, a_a2, g_v, a_bamp, g_bt, g_ss(0, 8), g_sred, g_logA,
                    a_en, g_term, a_aexp, g_cs(0, 8), g_cred, g_hatt]
        return g

    def ffn(res, lnb_ready_dv, pg_pair, pu_pair, pd_pair,
            gate_war_ac=0, up_war_ac=0, up_war_dv=0, mid=None):
        silu_done = {}
        h_done = {}
        gslot = uslot = None
        gu = uu = None
        for hc in range(HC):
            if mid is not None and INTERLEAVE:
                for fn_ in mid["dve"].get(hc, []):
                    fn_()
                for fn_ in mid["act"].get(hc, []):
                    fn_()
            if hc % 4 == 0:
                gslot, gu = unit_load(wg_s[:, hc * 1024:(hc + 4) * 1024], UNIT)
                uslot, uu = unit_load(wu_s[:, hc * 1024:(hc + 4) * 1024], UNIT)
            pg = pg_pair[hc % 2]
            pu = pu_pair[hc % 2]
            base = (hc % 4) * 1024
            def mm_gate(e, pg=pg, gslot=gslot, base=base):
                last = None
                for kc in range(KC):
                    last = e.matmul(pg[:], gslot[:, base + kc * 128: base + kc * 128 + 128],
                                    BFS[:, kc * TT:(kc + 1) * TT],
                                    start=(kc == 0), stop=(kc == KC - 1))
                return last
            gv = sch.emit("tensor", mm_gate,
                          waits=[("WL", ring_state["loaded"][gu]), ("DV", lnb_ready_dv),
                                 ("AC", silu_done.get(hc - 2, gate_war_ac))], inc="PE")
            unit_used(gu)
            def mm_up(e, pu=pu, uslot=uslot, base=base):
                last = None
                for kc in range(KC):
                    last = e.matmul(pu[:], uslot[:, base + kc * 128: base + kc * 128 + 128],
                                    BFS[:, kc * TT:(kc + 1) * TT],
                                    start=(kc == 0), stop=(kc == KC - 1))
                return last
            uv = sch.emit("tensor", mm_up,
                          waits=[("WL", ring_state["loaded"][uu]),
                                 ("DV", h_done.get(hc - 2, up_war_dv)),
                                 ("AC", up_war_ac if hc < 2 else 0)], inc="PE")
            unit_used(uu)
            sg = SG[hc % 2]
            sv = sch.emit("scalar",
                          (lambda e, sg=sg, pg=pg: e.activation(sg[:], pg[:], AF.Silu,
                                                                scale=C127)),
                          waits=[("PE", gv), ("DV", h_done.get(hc - 2, 0))], inc="AC")
            silu_done[hc] = sv
            state["last_silu"] = sv
            h_done[hc] = sch.emit(
                "vector",
                (lambda e, sg=sg, pu=pu, hc=hc:
                 e.tensor_mul(HH[:, hc * TT:(hc + 1) * TT], sg[:], pu[:])),
                waits=[("AC", sv), ("PE", uv)], inc="DV")
        gateup_pe = sch.val("PE")
        silu_last = silu_done[HC - 1]
        if mid is not None and not INTERLEAVE:
            for fn_ in mid["all"]:
                fn_()
        stt_done = {}
        for md in range(KC):
            dslot, du = unit_load(wd_s[:, md * HC * 128:(md + 1) * HC * 128], HC * 128)
            pd = pd_pair[md % 2]
            def mm_down(e, pd=pd, dslot=dslot):
                last = None
                for hc in range(HC):
                    last = e.matmul(pd[:], dslot[:, hc * 128: hc * 128 + 128],
                                    HH[:, hc * TT:(hc + 1) * TT],
                                    start=(hc == 0), stop=(hc == HC - 1))
                return last
            dv_ = sch.emit("tensor", mm_down,
                           waits=[("WL", ring_state["loaded"][du]), ("DV", h_done[HC - 1]),
                                  ("DV", stt_done.get(md - 2, 0))], inc="PE")
            unit_used(du)
            stt_done[md] = sch.emit(
                "vector",
                (lambda e, pd=pd, md=md: e.scalar_tensor_tensor(
                    res[:, md * TT:(md + 1) * TT], pd[:], DOWN_SCALE,
                    res[:, md * TT:(md + 1) * TT], OP.mult, OP.add)),
                waits=[("PE", dv_)], inc="DV")
        return gateup_pe, silu_last, sch.val("DV")

    def proj8(wsrc, pp_pair, dst, bias, rhs_dv, war_ac, war_dv, sig_store):
        units = []
        last_pe = 0
        for mc in range(KC):
            if mc % 4 == 0:
                units.append(unit_load(wsrc[:, mc * 1024:(mc + 4) * 1024], UNIT))
            slot, u = units[-1]
            pr = pp_pair[mc % 2]
            base = (mc % 4) * 1024
            def mm(e, pr=pr, slot=slot, base=base):
                last = None
                for kc in range(KC):
                    last = e.matmul(pr[:], slot[:, base + kc * 128: base + kc * 128 + 128],
                                    BFS[:, kc * TT:(kc + 1) * TT],
                                    start=(kc == 0), stop=(kc == KC - 1))
                return last
            rv = sch.emit("tensor", mm,
                          waits=[("WL", ring_state["loaded"][u]), ("DV", rhs_dv),
                                 ("AC", sig_store.get(mc - 2, war_ac)),
                                 ("DV", war_dv if mc < 2 else 0)], inc="PE")
            last_pe = rv
            unit_used(u)
            sig_store[mc] = sch.emit(
                "scalar",
                (lambda e, pr=pr, mc=mc: e.activation(
                    dst[:, mc * TT:(mc + 1) * TT], pr[:], AF.Sigmoid,
                    bias=bias[:, mc:mc + 1])),
                waits=[("PE", rv)], inc="AC")
        return last_pe

    def attn_tail(j, res_j, conv_war_dv):
        for c in range(KC):
            sch.emit("vector",
                     (lambda e, c=c: e.tensor_mul(SD[:, 0:TT],
                                                  res_j[:, c * TT:(c + 1) * TT], PB[6][:])),
                     waits=[("PE", tv[j]["bc2p"]), ("PE", tv[j]["bc2q"]),
                            ("DV", conv_war_dv)], inc="DV")
            sch.emit("vector",
                     (lambda e: e.tensor_add(LTMP[:, 3:3 + TT], SD[:, 0:TT], PB[7][:])),
                     inc="DV")
            sch.emit("vector",
                     (lambda e, c=c: e.tensor_copy(LTMP[:, 0:3], TAIL[:, c * 4:c * 4 + 3])),
                     inc="DV")
            sch.emit("vector",
                     (lambda e, c=c: e.tensor_copy(TAIL[:, c * 4:c * 4 + 3],
                                                   LTMP[:, TT:TT + 3])),
                     inc="DV")
            sch.emit("vector",
                     (lambda e, c=c: e.tensor_scalar(SB[:, c * TT:(c + 1) * TT],
                                                     LTMP[:, 0:TT], CW[:, c:c + 1],
                                                     CB[:, c:c + 1], OP.mult, OP.add)),
                     inc="DV")
            for k in range(1, 4):
                sch.emit("vector",
                         (lambda e, c=c, k=k: e.scalar_tensor_tensor(
                             SB[:, c * TT:(c + 1) * TT], LTMP[:, k:k + TT],
                             CW[:, k * 8 + c:k * 8 + c + 1],
                             SB[:, c * TT:(c + 1) * TT], OP.mult, OP.add)),
                         inc="DV")
        tv[j]["conv_dv"] = sch.val("DV")
        if DEBUG_TAPS and j == 1:
            tap("d_xc1", SB[:], dv_wait=tv[j]["conv_dv"])
        tv[j]["xcb"] = sch.emit("vector", (lambda e: e.tensor_copy(BFS[:, 0:T], SB[:])),
                                waits=[("PE", tv[j]["ln2_sv"])], inc="DV")
        if DEBUG_TAPS and j == 1:
            tap("d_xcb1", BFS[:, 0:T], dv_wait=tv[j]["xcb"])
        sig_r = {}
        sig_i = {}
        proj8(wr_s, (PB[0], PB[1]), SC_, RB, tv[j]["xcb"],
              tv[j].get("ri_war_ac", 0), tv[j].get("ri_war_dv", 0), sig_r)
        tv[j]["ri_pe"] = proj8(wi_s, (PB[2], PB[3]), SD, IB, tv[j]["xcb"],
                               tv[j].get("ri_war_ac", 0), tv[j].get("ri_war_dv2", 0), sig_i)
        tv[j]["sig_r"] = sig_r[KC - 1]
        tv[j]["sig_i"] = sig_i[KC - 1]

    if DEBUG_TAPS:
        tap("d_l8a", L8[:], dv_wait=v_l8)
    rr = ln_stats(0, RES[0], PB[6], PB[7], 0, 0)
    bp, bq = bcast_rows(rr, PB[0], PB[1])
    lnb = normalize_bf16(RES[0], bp, bq, PB[0], PB[1], 0)
    gateup_pe, silu_last, stt_dv = ffn(RES[0], lnb, (PB[0], PB[1]), (PB[2], PB[3]),
                                       (PB[4], PB[5]))
    tv[0]["gateup_pe"] = gateup_pe
    tv[0]["silu_last"] = silu_last
    tv[0]["stt_dv"] = stt_dv
    rr2 = ln_stats(0, RES[0], PB[6], PB[7], gateup_pe, gateup_pe, res_dv=stt_dv)
    tv[0]["ln2_sv"] = rr2["sv"]
    tv[0]["bc2p"], tv[0]["bc2q"] = bcast_rows(rr2, PB[6], PB[7], war_dv=rr2["rows_dv"])
    tv[0]["ri_war_ac"] = silu_last
    tv[0]["ri_war_dv"] = lnb
    tv[0]["ri_war_dv2"] = stt_dv
    attn_tail(0, RES[0], 0)
    if DEBUG_TAPS:
        tap("d_l8b", L8[:], ac_wait=tv[0]["sig_i"])

    for i in range(NT):
        j = i + 1
        mid = mid_make(i)
        rr2 = None
        if j < NT:
            res_j = RES[j % 2]
            rr = ln_stats(j, res_j, PB[6], PB[7],
                          tv[i]["ri_pe"], tv[i]["ln2_sv"],
                          extra_pe_waits=[("DV", tv[i].get("conv_dv", 0))])
            bp, bq = bcast_rows(rr, PB[0], PB[1], war_ac=tv[i]["sig_i"])
            lnb = normalize_bf16(res_j, bp, bq, PB[0], PB[1], tv[i]["ri_pe"])
            tv[j]["lnb"] = lnb
            gateup_pe, silu_last, stt_dv = ffn(
                res_j, lnb, (PB[0], PB[1]), (PB[2], PB[3]), (PB[4], PB[5]),
                gate_war_ac=0, up_war_ac=tv[i]["sig_i"],
                up_war_dv=tv[i].get("stt2_dv", tv[i].get("stt_dv", 0)), mid=mid)
            tv[j]["gateup_pe"] = gateup_pe
            tv[j]["silu_last"] = silu_last
            tv[j]["stt_dv"] = stt_dv
            if DEBUG_TAPS and j == 1:
                tap("d_x1", RES[1][:], dv_wait=stt_dv)
            rr2 = ln_stats(j, res_j, PB[6], PB[7], gateup_pe, gateup_pe,
                           res_dv=stt_dv)
            tv[j]["ln2_sv"] = rr2["sv"]
        else:
            for fn_ in mid["all"]:
                fn_()

        o_copy = {}
        units = []
        pb45_war = tv[j]["stt_dv"] if j < NT else tv[i - 1]["stt2_dv"]
        for md in range(KC):
            if md % 4 == 0:
                units.append(unit_load(wo_s[:, md * 1024:(md + 4) * 1024], UNIT))
            slot, u = units[-1]
            po = PB[4 + md % 2]
            base = (md % 4) * 1024
            def mm_o(e, po=po, slot=slot, base=base):
                last = None
                for kc in range(KC):
                    last = e.matmul(po[:], slot[:, base + kc * 128: base + kc * 128 + 128],
                                    HATT[:, kc * TT:(kc + 1) * TT],
                                    start=(kc == 0), stop=(kc == KC - 1))
                return last
            ov = sch.emit("tensor", mm_o,
                          waits=[("WL", ring_state["loaded"][u]), ("DV", tv[i]["hatt"]),
                                 ("DV", pb45_war), ("DV", o_copy.get(md - 2, 0))],
                          inc="PE")
            unit_used(u)
            o_copy[md] = sch.emit(
                "vector",
                (lambda e, po=po, md=md: e.tensor_copy(SB[:, md * TT:(md + 1) * TT], po[:])),
                waits=[("PE", ov)], inc="DV")
        tv[i]["outw"] = sch.val("PE")
        if DEBUG_TAPS and i == 1:
            tap("d_hatt", HATT[:], dv_wait=tv[i]["hatt"])
            tap("d_o", SB[:], dv_wait=o_copy[KC - 1])
        if j < NT:
            if DEBUG_TAPS and j == 1:
                tap("d_rstd2", ROWP[0:1, 0:TT], ac_wait=rr2["p"])
                tap("d_q2", SG[0][0:1, :], dv_wait=rr2["q"])
            tv[j]["bc2p"], tv[j]["bc2q"] = bcast_rows(rr2, PB[6], PB[7],
                                                      war_dv=rr2["rows_dv"])
        osq_last = 0
        bfs2_war = tv[j]["ln2_sv"] if j < NT else tv[i]["ln2_sv"]
        for md in range(KC):
            osq_last = sch.emit(
                "scalar",
                (lambda e, md=md: e.activation(
                    BFS[:, T + md * TT:T + (md + 1) * TT],
                    SB[:, md * TT:(md + 1) * TT], AF.Square)),
                waits=[("DV", o_copy[md]), ("PE", bfs2_war)], inc="AC")
        def mm_rms(e):
            last = None
            for md in range(KC):
                last = e.matmul(PB[4][0:1, :], ONESB[:],
                                BFS[:, T + md * TT:T + (md + 1) * TT],
                                start=(md == 0), stop=(md == KC - 1))
            return last
        v_rs = sch.emit("tensor", mm_rms,
                        waits=[("AC", osq_last), ("DV", o_copy[KC - 1])], inc="PE")
        v_ms = sch.emit("vector",
                        (lambda e: e.tensor_scalar_mul(SG[0][0:1, :], PB[4][0:1, :],
                                                       1.0 / D)),
                        waits=[("PE", v_rs), ("AC", state["last_silu"]),
                               ("AC", state["last_rowp"])], inc="DV")
        sch.emit("scalar",
                 (lambda e: e.activation(SG[0][0:1, :], SG[0][0:1, :], AF.Ln, bias=EPS6)),
                 waits=[("DV", v_ms)], inc="AC")
        v_rp = sch.emit("scalar",
                        (lambda e: e.activation(ROWP[0:1, 0:TT], SG[0][0:1, :],
                                                AF.Exp, scale=CMH)),
                        inc="AC")
        state["last_rowp"] = v_rp
        v_rb = sch.emit("tensor",
                        (lambda e: e.matmul(PB[5][:], ONESR[0:1, 0:P], ROWP[0:1, 0:TT],
                                            start=True, stop=True)),
                        waits=[("AC", v_rp)], inc="PE")
        res_i = RES[i % 2]
        apply_last = 0
        for md in range(KC):
            sch.emit("vector",
                     (lambda e, md=md: e.tensor_mul(SB[:, md * TT:(md + 1) * TT],
                                                    SB[:, md * TT:(md + 1) * TT], PB[5][:])),
                     waits=[("PE", v_rb)], inc="DV")
            sch.emit("vector",
                     (lambda e, md=md: e.tensor_scalar_mul(SB[:, md * TT:(md + 1) * TT],
                                                           SB[:, md * TT:(md + 1) * TT],
                                                           NW[:, md:md + 1])),
                     inc="DV")
            apply_last = sch.emit(
                "vector",
                (lambda e, md=md: e.tensor_add(res_i[:, md * TT:(md + 1) * TT],
                                               res_i[:, md * TT:(md + 1) * TT],
                                               SB[:, md * TT:(md + 1) * TT])),
                inc="DV")
        if DEBUG_TAPS and i == 1:
            tap("d_msrow", SG[0][0:1, :], dv_wait=v_ms)
            tap("d_rstd", ROWP[0:1, 0:TT], ac_wait=v_rp)
            tap("d_x2", res_i[:], dv_wait=apply_last)
        bfs1_war = tv[j]["gateup_pe"] if j < NT else tv[i]["ri_pe"]
        rr3 = ln_stats(i, res_i, PB[2], PB[3], bfs1_war, v_rs, res_dv=apply_last)
        bp3, bq3 = bcast_rows(rr3, PB[4], PB[5], war_dv=apply_last)
        lnb3 = normalize_bf16(res_i, bp3, bq3, PB[4], PB[5], bfs1_war)
        g2_war_ac = tv[j]["silu_last"] if j < NT else tv[i]["sig_i"]
        gateup2_pe, silu2_last, stt2_dv = ffn(
            res_i, lnb3, (PB[0], PB[1]), (PB[2], PB[3]), (PB[4], PB[5]),
            gate_war_ac=g2_war_ac, up_war_ac=0, up_war_dv=rr3["rows_dv"], mid=None)
        tv[i]["gateup2_pe"] = gateup2_pe
        tv[i]["stt2_dv"] = stt2_dv
        if j < NT:
            tv[j]["ri_war_ac"] = silu2_last
            tv[j]["ri_war_dv"] = lnb3
            tv[j]["ri_war_dv2"] = stt2_dv
        sch.emit("gpsimd",
                 (lambda e, i=i, res_i=res_i: e.dma_start(
                     out_h[:, :, i * TT:(i + 1) * TT],
                     res_i[:].rearrange("p (c t) -> p c t", c=KC))),
                 waits=[("DV", stt2_dv)], inc="OUT", incn=16)
        if i + 2 < NT:
            x_ready[i + 2] = sch.emit(
                "gpsimd",
                (lambda e, i=i: e.dma_start(
                    RES[i % 2][:].rearrange("p (c t) -> p c t", c=KC),
                    xh[:, :, (i + 2) * TT:(i + 3) * TT])),
                waits=[("OUT", 16 * (i + 1))], inc="X", incn=16)
        if j < NT:
            attn_tail(j, RES[j % 2], stt2_dv)

    if False:
        d_x1 = nc.dram_tensor("d_x1", [P, T], F32, kind="ExternalOutput")
        d_hatt = nc.dram_tensor("d_hatt", [P, T], BF16, kind="ExternalOutput")
        d_r1 = nc.dram_tensor("d_r1", [P, T], F32, kind="ExternalOutput")
        d_i1 = nc.dram_tensor("d_i1", [P, T], F32, kind="ExternalOutput")
        d_xcb1 = nc.dram_tensor("d_xcb1", [P, T], BF16, kind="ExternalOutput")
        sch.emit("gpsimd", (lambda e: e.dma_start(d_x1[:], RES[1][:])),
                 waits=[("DV", tv[1]["stt_dv"])], inc="OUT", incn=0 if False else 16)
        sch.emit("gpsimd", (lambda e: e.dma_start(d_hatt[:], HATT[:])),
                 waits=[("DV", tv[0]["hatt"])], inc="OUT", incn=16)
        sch.emit("gpsimd", (lambda e: e.dma_start(d_r1[:], SC_[:])),
                 waits=[("AC", tv[1]["sig_r"])], inc="OUT", incn=16)
        sch.emit("gpsimd", (lambda e: e.dma_start(d_i1[:], SD[:])),
                 waits=[("AC", tv[1]["sig_i"])], inc="OUT", incn=16)
        sch.emit("gpsimd", (lambda e: e.dma_start(d_xcb1[:], BFS[:, 0:T])),
                 waits=[("DV", tv[1]["xcb"])], inc="OUT", incn=16)
        sch.ops["gpsimd"].append(([("OUT", 16 * NT + 80)], None, None))
    else:
        sch.ops["gpsimd"].append(([("OUT", sch.val("OUT"))], None, None))

    with nc.Block() as blk:
        def runner(lst):
            def body(e):
                for ws, fn, iv in lst:
                    for s, v in ws:
                        e.wait_ge(SEMS[s], v)
                    if fn is None:
                        continue
                    inst = fn(e)
                    if iv is not None:
                        inst.then_inc(SEMS[iv[0]], iv[1])
            return body
        blk.sync(runner(sch.ops["sync"]))
        blk.gpsimd(runner(sch.ops["gpsimd"]))
        blk.tensor(runner(sch.ops["tensor"]))
        blk.vector(runner(sch.ops["vector"]))
        blk.scalar(runner(sch.ops["scalar"]))
    return nc


def _host_layouts(inputs):
    wg = np.asarray(inputs["L0_ffn_w_gate"])
    wu = np.asarray(inputs["L0_ffn_w_up"])
    wd = np.asarray(inputs["L0_ffn_w_down"])
    wr = np.asarray(inputs["L0_attn_Wr_w"], dtype=np.float32)
    wi = np.asarray(inputs["L0_attn_Wi_w"], dtype=np.float32)
    wo = np.asarray(inputs["L0_attn_out_w"], dtype=np.float32)

    def hcmaj(w):
        return np.ascontiguousarray(
            w.reshape(HC, 128, KC, 128).transpose(3, 0, 2, 1).reshape(128, -1))

    def mdmaj(w):
        return np.ascontiguousarray(
            w.reshape(KC, 128, HC, 128).transpose(3, 0, 2, 1).reshape(128, -1))

    def mcmaj(w):
        return np.ascontiguousarray(
            w.reshape(KC, 128, KC, 128).transpose(3, 0, 2, 1).reshape(128, -1))

    def vec(v):
        return np.ascontiguousarray(np.asarray(v, dtype=np.float32).reshape(KC, 128).T)

    cwf = np.asarray(inputs["L0_attn_conv_weight"], dtype=np.float32)[:, 0, :]
    cw = np.ascontiguousarray(
        cwf.reshape(KC, 128, 4).transpose(1, 2, 0).reshape(128, 32))

    common = {
        "wg": hcmaj(wg), "wu": hcmaj(wu), "wd": mdmaj(wd),
        "wr": mcmaj(wr), "wi": mcmaj(wi), "wo": mcmaj(wo),
        "rb": vec(inputs["L0_attn_Wr_b"]), "ib": vec(inputs["L0_attn_Wi_b"]),
        "la": vec(inputs["L0_attn_log_a"]), "nw": vec(inputs["L0_attn_norm_w"]),
        "cw": cw, "cb": vec(inputs["L0_attn_conv_bias"]),
    }
    x = np.asarray(inputs["L0_attn_x"], dtype=np.float32)
    maps = []
    for b in range(8):
        xb = np.ascontiguousarray(
            x[b].T.reshape(KC, 128, T).transpose(1, 0, 2))
        m = dict(common)
        m["x"] = xb
        maps.append(m)
    return maps


def kernel(**inputs):
    in_maps = _host_layouts(inputs)
    nc = build_nc()
    res = run_bass_kernel_spmd(nc, in_maps, core_ids=list(range(8)))
    outs = []
    for b in range(8):
        o = np.asarray(res.results[b]["out"])
        outs.append(o.transpose(1, 0, 2).reshape(D, T).T)
    return np.stack(outs).astype(np.float32)


# version 06
# speedup vs baseline: 1.0588x; 1.0588x over previous
import numpy as np
import concourse.bass as bass
import concourse.mybir as mybir
from concourse.bass_utils import run_bass_kernel_spmd

F32 = mybir.dt.float32
BF16 = mybir.dt.bfloat16
I32 = mybir.dt.int32
AF = mybir.ActivationFunctionType
OP = mybir.AluOpType

D = 1024
KC = 8
H = 3072
HC = 24
T = 4096
TT = 512
NT = T // TT
P = 128
RING = 6
UNIT = 4096

S127 = 1.0 / 127.0
DOWN_SCALE = 0.5 * S127 * S127
LOG1EM7 = float(np.float32(np.log(np.float32(1e-7))))
DEBUG_TAPS = False
INTERLEAVE = True


class Sched:
    def __init__(self):
        self.ops = {"sync": [], "gpsimd": [], "tensor": [], "vector": [], "scalar": []}
        self.cnt = {}
        self.marks = {}

    def val(self, sem):
        return self.cnt.get(sem, 0)

    def emit(self, eng, fn, waits=(), inc=None, incn=1):
        ws = []
        for s, v in waits:
            if not v or v <= 0:
                continue
            key = (eng, s)
            if self.marks.get(key, 0) >= v:
                continue
            self.marks[key] = v
            ws.append((s, v))
        iv = None
        if inc is not None:
            self.cnt[inc] = self.cnt.get(inc, 0) + incn
            iv = (inc, incn)
        self.ops[eng].append((ws, fn, iv))
        return self.cnt.get(inc, 0) if inc else None


def build_nc():
    nc = bass.Bass("TRN2", target_bir_lowering=False)

    xh = nc.dram_tensor("x", [P, KC, T], F32, kind="ExternalInput")
    wg_i = nc.dram_tensor("wg", [P, HC * 8 * 128], I32, kind="ExternalInput")
    wu_i = nc.dram_tensor("wu", [P, HC * 8 * 128], I32, kind="ExternalInput")
    wd_i = nc.dram_tensor("wd", [P, 8 * HC * 128], I32, kind="ExternalInput")
    wr_i = nc.dram_tensor("wr", [P, KC * 1024], F32, kind="ExternalInput")
    wi_i = nc.dram_tensor("wi", [P, KC * 1024], F32, kind="ExternalInput")
    wo_i = nc.dram_tensor("wo", [P, KC * 1024], F32, kind="ExternalInput")
    rb_i = nc.dram_tensor("rb", [P, KC], F32, kind="ExternalInput")
    ib_i = nc.dram_tensor("ib", [P, KC], F32, kind="ExternalInput")
    la_i = nc.dram_tensor("la", [P, KC], F32, kind="ExternalInput")
    nw_i = nc.dram_tensor("nw", [P, KC], F32, kind="ExternalInput")
    cw_i = nc.dram_tensor("cw", [P, 32], F32, kind="ExternalInput")
    cb_i = nc.dram_tensor("cb", [P, KC], F32, kind="ExternalInput")
    out_h = nc.dram_tensor("out", [P, KC, T], F32, kind="ExternalOutput")

    wg_s = nc.dram_tensor("wg_s", [P, HC * 8 * 128], BF16, kind="Internal")
    wu_s = nc.dram_tensor("wu_s", [P, HC * 8 * 128], BF16, kind="Internal")
    wd_s = nc.dram_tensor("wd_s", [P, 8 * HC * 128], BF16, kind="Internal")
    wr_s = nc.dram_tensor("wr_s", [P, KC * 1024], BF16, kind="Internal")
    wi_s = nc.dram_tensor("wi_s", [P, KC * 1024], BF16, kind="Internal")
    wo_s = nc.dram_tensor("wo_s", [P, KC * 1024], BF16, kind="Internal")

    import contextlib
    ctx = contextlib.ExitStack()
    nc._kernel_ctx = ctx
    sb = lambda n, s, d=F32: ctx.enter_context(nc.sbuf_tensor(n, s, d))

    RES = [sb("RES0", [P, T]), sb("RES1", [P, T])]
    SA = sb("SA", [P, T])
    SB = sb("SB", [P, T])
    SC_ = sb("SC", [P, T])
    SD = sb("SD", [P, T])
    BFS = sb("BFS", [P, 2 * T], BF16)
    HATT = sb("HATT", [P, T], BF16)
    HH = sb("HH", [P, HC * TT], BF16)
    LTMP = sb("LTMP", [P, TT + 8])
    ROWP = LTMP
    TAIL = sb("TAIL", [P, KC * 4])
    WRG = [sb(f"WRG{j}", [P, UNIT], BF16) for j in range(RING)]
    SG = [sb("SG0", [P, TT]), sb("SG1", [P, TT])]
    ONES = sb("ONESF", [P, TT])
    ONESR = ONES
    ONESB = sb("ONESB", [P, 1], BF16)
    CONSTS = sb("CONSTS", [P, 8])
    CW = sb("CWt", [P, 32])
    CB = sb("CBt", [P, KC])
    RB = sb("RBt", [P, KC])
    IB = sb("IBt", [P, KC])
    LAT = sb("LAt", [P, KC])
    NW = sb("NWt", [P, KC])
    L8 = sb("L8", [P, KC])
    L16 = sb("L16", [P, KC])
    SCAR = sb("SCAR", [P, KC])
    CCAR = sb("CCAR", [P, KC])
    RED = sb("REDt", [P, KC])
    PB = [ctx.enter_context(nc.psum_tensor(f"PB{j}", [P, TT], F32)) for j in range(8)]

    cvals = [S127, -1.0, -0.5, 1e-5, 1e-6]
    for j_, v in enumerate(cvals):
        nc.const_aps.register(F32, float(np.float32(v)), CONSTS[:, j_:j_ + 1])
    C127, CM1, CMH, EPS5, EPS6 = [float(np.float32(v)) for v in cvals]

    SEMS = {}
    for n in ["PE", "DV", "AC", "WL", "X", "OUT", "CV"]:
        SEMS[n] = ctx.enter_context(nc.semaphore("sem_" + n))

    sch = Sched()
    ring_state = {"u": 0, "done": {}, "loaded": {}}

    def unit_load(src_ap, ncols):
        u = ring_state["u"]
        ring_state["u"] += 1
        slot = WRG[u % RING]
        waits = [("CV", src_cast_done[src_ap.tensor.name])]
        prev = u - RING
        if prev >= 0:
            waits.append(("PE", ring_state["done"].get(prev, 0)))
        def fn(e, slot=slot, src=src_ap, ncols=ncols):
            return e.dma_start(slot[:, 0:ncols], src)
        v = sch.emit("sync", fn, waits=waits, inc="WL", incn=16)
        ring_state["loaded"][u] = v
        return slot, u

    def unit_used(u):
        ring_state["done"][u] = sch.val("PE")

    src_cast_done = {}
    def cast_dma(dst, src):
        def fn(e, dst=dst, src=src):
            return e.dma_start(dst[:], src[:])
        v = sch.emit("gpsimd", fn, inc="CV", incn=16)
        src_cast_done[dst.name] = v

    cast_dma(wg_s, wg_i)
    cast_dma(wu_s, wu_i)
    cast_dma(wd_s, wd_i)
    cast_dma(wr_s, wr_i)
    cast_dma(wi_s, wi_i)
    cast_dma(wo_s, wo_i)

    for dst, src in [(CW, cw_i), (CB, cb_i), (RB, rb_i), (IB, ib_i),
                     (LAT, la_i), (NW, nw_i)]:
        sch.emit("sync", (lambda e, d=dst, s=src: e.dma_start(d[:], s[:])),
                 inc="X", incn=16)
    x_ready = {}
    for i in range(2):
        x_ready[i] = sch.emit(
            "gpsimd",
            (lambda e, i=i: e.dma_start(RES[i % 2][:].rearrange("p (c t) -> p c t", c=KC),
                                        xh[:, :, i * TT:(i + 1) * TT])),
            inc="X", incn=16)

    def ms(ap, v):
        sch.emit("gpsimd", (lambda e, ap=ap, v=v: e.memset(ap, v)))
    ms(ONES[:], 1.0)
    ms(ONESB[:], 1.0)
    ms(TAIL[:], 0.0)
    ms(SCAR[:], 0.0)
    ms(CCAR[:], 0.0)
    for j_, v in enumerate(cvals):
        ms(CONSTS[:, j_:j_ + 1], float(np.float32(v)))
    setup_done = sch.emit("gpsimd", (lambda e: e.memset(CONSTS[:, 7:8], 0.0)),
                          inc="X", incn=1)

    sch.emit("scalar", (lambda e: e.activation(L8[:], LAT[:], AF.Sigmoid)),
             waits=[("X", setup_done)], inc="AC")
    v_ln = sch.emit("scalar", (lambda e: e.activation(L8[:], L8[:], AF.Ln)), inc="AC")
    sch.emit("vector",
             (lambda e: e.tensor_scalar(L16[:], L8[:], 16.0, -1e30, OP.mult, OP.max)),
             waits=[("AC", v_ln)], inc="DV")
    v_l8 = sch.emit("vector",
                    (lambda e: e.tensor_scalar(L8[:], L8[:], 8.0, -1e30, OP.mult, OP.max)),
                    inc="DV")

    tv = [dict() for _ in range(NT + 1)]
    state = {"scar_v": 0, "ccar_v": 0, "last_silu": 0, "last_rowp": 0}

    dbg = {}
    if DEBUG_TAPS:
        dbg["d_scar"] = nc.dram_tensor("d_scar", [P, KC], F32, kind="ExternalOutput")
        for nm in ["d_l8a", "d_l8b", "d_l8c", "d_l8d"]:
            dbg[nm] = nc.dram_tensor(nm, [P, KC], F32, kind="ExternalOutput")
        dbg["d_r1pre"] = nc.dram_tensor("d_r1pre", [P, T], F32, kind="ExternalOutput")
        dbg["d_xc1"] = nc.dram_tensor("d_xc1", [P, T], F32, kind="ExternalOutput")
        dbg["d_xcb1"] = nc.dram_tensor("d_xcb1", [P, T], BF16, kind="ExternalOutput")
        dbg["d_rstd2"] = nc.dram_tensor("d_rstd2", [1, TT], F32, kind="ExternalOutput")
        dbg["d_q2"] = nc.dram_tensor("d_q2", [1, TT], F32, kind="ExternalOutput")
        for nm, dt_ in [("d_x1", F32), ("d_o", F32), ("d_x2", F32), ("d_hatt", BF16),
                        ("d_logA", F32), ("d_aexp", F32), ("d_cs", F32), ("d_term", F32),
                        ("d_la", F32), ("d_s", F32),
                        ("d_msrow", F32), ("d_rstd", F32)]:
            dbg[nm] = nc.dram_tensor(nm, [P, T] if nm not in ("d_msrow", "d_rstd")
                                     else [1, TT], dt_, kind="ExternalOutput")

    def tap(name, src_ap, dv_wait=None, ac_wait=None):
        if not DEBUG_TAPS:
            return
        w = []
        if dv_wait: w.append(("DV", dv_wait))
        if ac_wait: w.append(("AC", ac_wait))
        v = sch.emit("gpsimd", (lambda e, s=src_ap, d=dbg[name]: e.dma_start(d[:], s)),
                     waits=w, inc="OUT", incn=16)
        sch.emit("vector", (lambda e: e.engine_nop()), waits=[("OUT", v)], inc="DV")
        state["dbg_out"] = v

    def ln_stats(i, res, pb_m, pb_v, bfs1_war_pe, bfs2_war_pe, res_dv=0,
                 extra_pe_waits=()):
        r = {}
        r["xb"] = sch.emit("vector", (lambda e: e.tensor_copy(BFS[:, 0:T], res[:])),
                           waits=[("PE", bfs1_war_pe), ("X", x_ready.get(i, 0)),
                                  ("X", setup_done), ("DV", res_dv)], inc="DV")
        r["xsq"] = sch.emit("scalar",
                            (lambda e: e.activation(BFS[:, T:2 * T], res[:], AF.Square)),
                            waits=[("PE", bfs2_war_pe), ("X", x_ready.get(i, 0)),
                                   ("X", setup_done), ("DV", res_dv)], inc="AC")

        def mm_stats(e, pb, off):
            last = None
            for kc in range(KC):
                last = e.matmul(pb[0:1, :], ONESB[:],
                                BFS[:, off + kc * TT: off + (kc + 1) * TT],
                                start=(kc == 0), stop=(kc == KC - 1))
            return last
        r["sm"] = sch.emit("tensor", (lambda e: mm_stats(e, pb_m, 0)),
                           waits=[("DV", r["xb"]), ("X", setup_done)] + list(extra_pe_waits),
                           inc="PE")
        r["sv"] = sch.emit("tensor", (lambda e: mm_stats(e, pb_v, T)),
                           waits=[("AC", r["xsq"])], inc="PE")
        sch.emit("vector",
                 (lambda e: e.tensor_scalar_mul(SG[0][0:1, :], pb_m[0:1, :], 1.0 / D)),
                 waits=[("PE", r["sm"]), ("AC", state["last_silu"]),
                        ("AC", state["last_rowp"])], inc="DV")
        sch.emit("vector",
                 (lambda e: e.tensor_mul(ROWP[0:1, 0:TT], SG[0][0:1, :], SG[0][0:1, :])),
                 inc="DV")
        r["var"] = sch.emit("vector",
                            (lambda e: e.scalar_tensor_tensor(
                                SG[1][0:1, :], pb_v[0:1, :], 1.0 / D,
                                ROWP[0:1, 0:TT], OP.mult, OP.subtract)),
                            waits=[("PE", r["sv"])], inc="DV")
        sch.emit("scalar",
                 (lambda e: e.activation(SG[1][0:1, :], SG[1][0:1, :], AF.Ln, bias=EPS5)),
                 waits=[("DV", r["var"])], inc="AC")
        r["p"] = sch.emit("scalar",
                          (lambda e: e.activation(ROWP[0:1, 0:TT], SG[1][0:1, :],
                                                  AF.Exp, scale=CMH)),
                          inc="AC")
        state["last_rowp"] = r["p"]
        r["q"] = sch.emit("vector",
                          (lambda e: e.scalar_tensor_tensor(SG[0][0:1, :], SG[0][0:1, :],
                                                            -1.0, ROWP[0:1, 0:TT],
                                                            OP.mult, OP.mult)),
                          waits=[("AC", r["p"])], inc="DV")
        r["rows_dv"] = sch.val("DV")
        return r

    def bcast_rows(rr, pb_p, pb_q, war_ac=0, war_dv=0):
        v_bp = sch.emit("tensor",
                        (lambda e: e.matmul(pb_p[:], ONESR[0:1, 0:P], ROWP[0:1, 0:TT],
                                            start=True, stop=True)),
                        waits=[("AC", rr["p"]), ("AC", war_ac), ("DV", war_dv)], inc="PE")
        v_bq = sch.emit("tensor",
                        (lambda e: e.matmul(pb_q[:], ONESR[0:1, 0:P], SG[0][0:1, :],
                                            start=True, stop=True)),
                        waits=[("DV", rr["q"])], inc="PE")
        return v_bp, v_bq

    def normalize_bf16(res, v_bp, v_bq, pb_p, pb_q, bfs1_war_pe):
        last = 0
        for c in range(KC):
            sch.emit("vector",
                     (lambda e, c=c: e.tensor_mul(LTMP[:, 0:TT],
                                                  res[:, c * TT:(c + 1) * TT], pb_p[:])),
                     waits=[("PE", v_bp), ("PE", v_bq), ("PE", bfs1_war_pe)], inc="DV")
            last = sch.emit("vector",
                            (lambda e, c=c: e.tensor_add(BFS[:, c * TT:(c + 1) * TT],
                                                         LTMP[:, 0:TT], pb_q[:])),
                            inc="DV")
        return last

    def mid_make(i):
        g = {}
        def g_la_ixc():
            if DEBUG_TAPS and i == 1:
                tap("d_l8c", L8[:], ac_wait=tv[i]["sig_r"])
                tap("d_r1pre", SC_[:], ac_wait=tv[i]["sig_r"])
            for c in range(KC):
                sch.emit("vector",
                         (lambda e, c=c: e.tensor_scalar(
                             SA[:, c * TT:(c + 1) * TT], SC_[:, c * TT:(c + 1) * TT],
                             L8[:, c:c + 1], LOG1EM7, OP.mult, OP.max)),
                         waits=[("AC", tv[i]["sig_r"]), ("DV", v_l8)], inc="DV")
            for c in range(KC):
                tv[i]["ixc"] = sch.emit(
                    "vector",
                    (lambda e, c=c: e.tensor_mul(
                        SD[:, c * TT:(c + 1) * TT], SD[:, c * TT:(c + 1) * TT],
                        SB[:, c * TT:(c + 1) * TT])),
                    waits=[("AC", tv[i]["sig_i"])], inc="DV")
        def a_a2():
            for c in range(KC):
                tv[i]["a2"] = sch.emit(
                    "scalar",
                    (lambda e, c=c: e.activation(
                        SB[:, c * TT:(c + 1) * TT], SC_[:, c * TT:(c + 1) * TT],
                        AF.Exp, scale=L16[:, c:c + 1])),
                    waits=[("DV", tv[i]["ixc"])], inc="AC")
        def g_v():
            sch.emit("vector",
                     (lambda e: e.tensor_scalar(SC_[:], SB[:], 1.0, -1.0,
                                                OP.subtract, OP.mult)),
                     waits=[("AC", tv[i]["a2"])], inc="DV")
            tv[i]["vmax"] = sch.emit(
                "vector", (lambda e: e.tensor_scalar_max(SC_[:], SC_[:], 1e-6)), inc="DV")
        def a_bamp():
            tv[i]["bamp"] = sch.emit(
                "scalar", (lambda e: e.activation(SB[:], SC_[:], AF.Sqrt)),
                waits=[("DV", tv[i]["vmax"])], inc="AC")
        def g_bt():
            tv[i]["bt"] = sch.emit(
                "vector", (lambda e: e.tensor_mul(SD[:], SD[:], SB[:])),
                waits=[("AC", tv[i]["bamp"])], inc="DV")
        def g_ss(c0, c1):
            def f():
                for c in range(c0, c1):
                    sch.emit("vector",
                             (lambda e, c=c: e.tensor_tensor_scan(
                                 SB[:, c * TT:(c + 1) * TT], ONES[:],
                                 SA[:, c * TT:(c + 1) * TT], SCAR[:, c:c + 1],
                                 OP.mult, OP.add)),
                             waits=[("DV", state["scar_v"]), ("DV", tv[i]["bt"])],
                             inc="DV")
            return f
        def g_sred():
            for c in range(KC):
                sch.emit("vector",
                         (lambda e, c=c: e.tensor_reduce(
                             RED[:, c:c + 1], SA[:, c * TT:(c + 1) * TT],
                             mybir.AxisListType.X, OP.add)),
                         inc="DV")
            sch.emit("vector", (lambda e: e.engine_nop()), inc="DV")
            sch.emit("vector", (lambda e: e.engine_nop()), inc="DV")
            state["scar_v"] = sch.emit(
                "vector", (lambda e: e.tensor_add(SCAR[:], SCAR[:], RED[:])), inc="DV")
        def g_logA():
            tv[i]["logA"] = sch.emit(
                "vector", (lambda e: e.tensor_scalar_max(SB[:], SB[:], -80.0)), inc="DV")
            if DEBUG_TAPS and i == 1:
                tap("d_logA", SB[:], dv_wait=tv[i]["logA"])
        def a_en():
            tv[i]["en"] = sch.emit(
                "scalar", (lambda e: e.activation(SA[:], SB[:], AF.Exp, scale=CM1)),
                waits=[("DV", tv[i]["logA"])], inc="AC")
        def g_term():
            tv[i]["term"] = sch.emit(
                "vector", (lambda e: e.tensor_mul(SD[:], SD[:], SA[:])),
                waits=[("AC", tv[i]["en"])], inc="DV")
            if DEBUG_TAPS and i == 1:
                tap("d_term", SD[:], dv_wait=tv[i]["term"])
        def a_aexp():
            tv[i]["aexp"] = sch.emit(
                "scalar", (lambda e: e.activation(SC_[:], SB[:], AF.Exp)),
                waits=[("DV", tv[i]["term"])], inc="AC")
        def g_cs(c0, c1):
            def f():
                for c in range(c0, c1):
                    sch.emit("vector",
                             (lambda e, c=c: e.tensor_tensor_scan(
                                 SA[:, c * TT:(c + 1) * TT], ONES[:],
                                 SD[:, c * TT:(c + 1) * TT], CCAR[:, c:c + 1],
                                 OP.mult, OP.add)),
                             waits=[("DV", state["ccar_v"]), ("DV", tv[i]["term"])],
                             inc="DV")
            return f
        def g_cred():
            for c in range(KC):
                sch.emit("vector",
                         (lambda e, c=c: e.tensor_reduce(
                             RED[:, c:c + 1], SD[:, c * TT:(c + 1) * TT],
                             mybir.AxisListType.X, OP.add)),
                         inc="DV")
            sch.emit("vector", (lambda e: e.engine_nop()), inc="DV")
            sch.emit("vector", (lambda e: e.engine_nop()), inc="DV")
            state["ccar_v"] = sch.emit(
                "vector", (lambda e: e.tensor_add(CCAR[:], CCAR[:], RED[:])), inc="DV")
        def g_hatt():
            if DEBUG_TAPS and i == 1:
                tap("d_aexp", SC_[:], ac_wait=tv[i]["aexp"])
                tap("d_cs", SA[:], dv_wait=state["ccar_v"])
            tv[i]["hatt"] = sch.emit(
                "vector", (lambda e: e.tensor_mul(HATT[:], SC_[:], SA[:])),
                waits=[("AC", tv[i]["aexp"]),
                       ("PE", tv[i - 1].get("outw", 0) if i > 0 else 0)],
                inc="DV")
        g["dve"] = {2: [g_la_ixc], 4: [g_v], 6: [g_bt], 8: [g_ss(0, 4)],
                    10: [g_ss(4, 8)], 12: [g_sred, g_logA], 14: [g_term],
                    16: [g_cs(0, 4)], 18: [g_cs(4, 8)], 20: [g_cred], 21: [g_hatt]}
        g["act"] = {2: [a_a2], 4: [a_bamp], 13: [a_en], 17: [a_aexp]}
        g["all"] = [g_la_ixc, a_a2, g_v, a_bamp, g_bt, g_ss(0, 8), g_sred, g_logA,
                    a_en, g_term, a_aexp, g_cs(0, 8), g_cred, g_hatt]
        return g

    def ffn(res, lnb_ready_dv, pg_pair, pu_pair, pd_pair,
            gate_war_ac=0, up_war_ac=0, up_war_dv=0, mid=None):
        silu_done = {}
        h_done = {}
        gslot = uslot = None
        gu = uu = None
        for hc in range(HC):
            if mid is not None and INTERLEAVE:
                for fn_ in mid["dve"].get(hc, []):
                    fn_()
                for fn_ in mid["act"].get(hc, []):
                    fn_()
            if hc % 4 == 0:
                gslot, gu = unit_load(wg_s[:, hc * 1024:(hc + 4) * 1024], UNIT)
                uslot, uu = unit_load(wu_s[:, hc * 1024:(hc + 4) * 1024], UNIT)
            pg = pg_pair[hc % 2]
            pu = pu_pair[hc % 2]
            base = (hc % 4) * 1024
            def mm_gate(e, pg=pg, gslot=gslot, base=base):
                last = None
                for kc in range(KC):
                    last = e.matmul(pg[:], gslot[:, base + kc * 128: base + kc * 128 + 128],
                                    BFS[:, kc * TT:(kc + 1) * TT],
                                    start=(kc == 0), stop=(kc == KC - 1))
                return last
            gv = sch.emit("tensor", mm_gate,
                          waits=[("WL", ring_state["loaded"][gu]), ("DV", lnb_ready_dv),
                                 ("AC", silu_done.get(hc - 2, gate_war_ac))], inc="PE")
            unit_used(gu)
            def mm_up(e, pu=pu, uslot=uslot, base=base):
                last = None
                for kc in range(KC):
                    last = e.matmul(pu[:], uslot[:, base + kc * 128: base + kc * 128 + 128],
                                    BFS[:, kc * TT:(kc + 1) * TT],
                                    start=(kc == 0), stop=(kc == KC - 1))
                return last
            uv = sch.emit("tensor", mm_up,
                          waits=[("WL", ring_state["loaded"][uu]),
                                 ("DV", h_done.get(hc - 2, up_war_dv)),
                                 ("AC", up_war_ac if hc < 2 else 0)], inc="PE")
            unit_used(uu)
            sg = SG[hc % 2]
            sv = sch.emit("scalar",
                          (lambda e, sg=sg, pg=pg: e.activation(sg[:], pg[:], AF.Silu,
                                                                scale=C127)),
                          waits=[("PE", gv), ("DV", h_done.get(hc - 2, 0))], inc="AC")
            silu_done[hc] = sv
            state["last_silu"] = sv
            h_done[hc] = sch.emit(
                "vector",
                (lambda e, sg=sg, pu=pu, hc=hc:
                 e.tensor_mul(HH[:, hc * TT:(hc + 1) * TT], sg[:], pu[:])),
                waits=[("AC", sv), ("PE", uv)], inc="DV")
        gateup_pe = sch.val("PE")
        silu_last = silu_done[HC - 1]
        if mid is not None and not INTERLEAVE:
            for fn_ in mid["all"]:
                fn_()
        stt_done = {}
        for md in range(KC):
            dslot, du = unit_load(wd_s[:, md * HC * 128:(md + 1) * HC * 128], HC * 128)
            pd = pd_pair[md % 2]
            def mm_down(e, pd=pd, dslot=dslot):
                last = None
                for hc in range(HC):
                    last = e.matmul(pd[:], dslot[:, hc * 128: hc * 128 + 128],
                                    HH[:, hc * TT:(hc + 1) * TT],
                                    start=(hc == 0), stop=(hc == HC - 1))
                return last
            dv_ = sch.emit("tensor", mm_down,
                           waits=[("WL", ring_state["loaded"][du]), ("DV", h_done[HC - 1]),
                                  ("DV", stt_done.get(md - 2, 0))], inc="PE")
            unit_used(du)
            stt_done[md] = sch.emit(
                "vector",
                (lambda e, pd=pd, md=md: e.scalar_tensor_tensor(
                    res[:, md * TT:(md + 1) * TT], pd[:], DOWN_SCALE,
                    res[:, md * TT:(md + 1) * TT], OP.mult, OP.add)),
                waits=[("PE", dv_)], inc="DV")
        return gateup_pe, silu_last, sch.val("DV")

    def proj8(wsrc, pp_pair, dst, bias, rhs_dv, war_ac, war_dv, sig_store):
        units = []
        last_pe = 0
        for mc in range(KC):
            if mc % 4 == 0:
                units.append(unit_load(wsrc[:, mc * 1024:(mc + 4) * 1024], UNIT))
            slot, u = units[-1]
            pr = pp_pair[mc % 2]
            base = (mc % 4) * 1024
            def mm(e, pr=pr, slot=slot, base=base):
                last = None
                for kc in range(KC):
                    last = e.matmul(pr[:], slot[:, base + kc * 128: base + kc * 128 + 128],
                                    BFS[:, kc * TT:(kc + 1) * TT],
                                    start=(kc == 0), stop=(kc == KC - 1))
                return last
            rv = sch.emit("tensor", mm,
                          waits=[("WL", ring_state["loaded"][u]), ("DV", rhs_dv),
                                 ("AC", sig_store.get(mc - 2, war_ac)),
                                 ("DV", war_dv if mc < 2 else 0)], inc="PE")
            last_pe = rv
            unit_used(u)
            sig_store[mc] = sch.emit(
                "scalar",
                (lambda e, pr=pr, mc=mc: e.activation(
                    dst[:, mc * TT:(mc + 1) * TT], pr[:], AF.Sigmoid,
                    bias=bias[:, mc:mc + 1])),
                waits=[("PE", rv)], inc="AC")
        return last_pe

    def attn_tail(j, res_j, conv_war_dv):
        for c in range(KC):
            sch.emit("vector",
                     (lambda e, c=c: e.tensor_mul(SD[:, 0:TT],
                                                  res_j[:, c * TT:(c + 1) * TT], PB[6][:])),
                     waits=[("PE", tv[j]["bc2p"]), ("PE", tv[j]["bc2q"]),
                            ("DV", conv_war_dv)], inc="DV")
            sch.emit("vector",
                     (lambda e: e.tensor_add(LTMP[:, 3:3 + TT], SD[:, 0:TT], PB[7][:])),
                     inc="DV")
            sch.emit("vector",
                     (lambda e, c=c: e.tensor_copy(LTMP[:, 0:3], TAIL[:, c * 4:c * 4 + 3])),
                     inc="DV")
            sch.emit("vector",
                     (lambda e, c=c: e.tensor_copy(TAIL[:, c * 4:c * 4 + 3],
                                                   LTMP[:, TT:TT + 3])),
                     inc="DV")
            sch.emit("vector",
                     (lambda e, c=c: e.tensor_scalar(SB[:, c * TT:(c + 1) * TT],
                                                     LTMP[:, 0:TT], CW[:, c:c + 1],
                                                     CB[:, c:c + 1], OP.mult, OP.add)),
                     inc="DV")
            for k in range(1, 4):
                sch.emit("vector",
                         (lambda e, c=c, k=k: e.scalar_tensor_tensor(
                             SB[:, c * TT:(c + 1) * TT], LTMP[:, k:k + TT],
                             CW[:, k * 8 + c:k * 8 + c + 1],
                             SB[:, c * TT:(c + 1) * TT], OP.mult, OP.add)),
                         inc="DV")
        tv[j]["conv_dv"] = sch.val("DV")
        if DEBUG_TAPS and j == 1:
            tap("d_xc1", SB[:], dv_wait=tv[j]["conv_dv"])
        tv[j]["xcb"] = sch.emit("vector", (lambda e: e.tensor_copy(BFS[:, 0:T], SB[:])),
                                waits=[("PE", tv[j]["ln2_sv"])], inc="DV")
        if DEBUG_TAPS and j == 1:
            tap("d_xcb1", BFS[:, 0:T], dv_wait=tv[j]["xcb"])
        sig_r = {}
        sig_i = {}
        proj8(wr_s, (PB[0], PB[1]), SC_, RB, tv[j]["xcb"],
              tv[j].get("ri_war_ac", 0), tv[j].get("ri_war_dv", 0), sig_r)
        tv[j]["ri_pe"] = proj8(wi_s, (PB[2], PB[3]), SD, IB, tv[j]["xcb"],
                               tv[j].get("ri_war_ac", 0), tv[j].get("ri_war_dv2", 0), sig_i)
        tv[j]["sig_r"] = sig_r[KC - 1]
        tv[j]["sig_i"] = sig_i[KC - 1]

    if DEBUG_TAPS:
        tap("d_l8a", L8[:], dv_wait=v_l8)
    rr = ln_stats(0, RES[0], PB[6], PB[7], 0, 0)
    bp, bq = bcast_rows(rr, PB[0], PB[1])
    lnb = normalize_bf16(RES[0], bp, bq, PB[0], PB[1], 0)
    gateup_pe, silu_last, stt_dv = ffn(RES[0], lnb, (PB[0], PB[1]), (PB[2], PB[3]),
                                       (PB[4], PB[5]))
    tv[0]["gateup_pe"] = gateup_pe
    tv[0]["silu_last"] = silu_last
    tv[0]["stt_dv"] = stt_dv
    rr2 = ln_stats(0, RES[0], PB[6], PB[7], gateup_pe, gateup_pe, res_dv=stt_dv)
    tv[0]["ln2_sv"] = rr2["sv"]
    tv[0]["bc2p"], tv[0]["bc2q"] = bcast_rows(rr2, PB[6], PB[7], war_dv=rr2["rows_dv"])
    tv[0]["ri_war_ac"] = silu_last
    tv[0]["ri_war_dv"] = lnb
    tv[0]["ri_war_dv2"] = stt_dv
    attn_tail(0, RES[0], 0)
    if DEBUG_TAPS:
        tap("d_l8b", L8[:], ac_wait=tv[0]["sig_i"])

    for i in range(NT):
        j = i + 1
        mid = mid_make(i)
        rr2 = None
        if j < NT:
            res_j = RES[j % 2]
            rr = ln_stats(j, res_j, PB[6], PB[7],
                          tv[i - 1]["gateup2_pe"] if i > 0 else tv[i]["ri_pe"],
                          tv[i]["ln2_sv"],
                          extra_pe_waits=[("DV", tv[i].get("conv_dv", 0))])
            bp, bq = bcast_rows(rr, PB[0], PB[1], war_ac=tv[i]["sig_i"])
            lnb = normalize_bf16(res_j, bp, bq, PB[0], PB[1],
                                 tv[i - 1]["gateup2_pe"] if i > 0 else tv[i]["ri_pe"])
            tv[j]["lnb"] = lnb
            gateup_pe, silu_last, stt_dv = ffn(
                res_j, lnb, (PB[0], PB[1]), (PB[2], PB[3]), (PB[4], PB[5]),
                gate_war_ac=0, up_war_ac=tv[i]["sig_i"],
                up_war_dv=tv[i].get("stt2_dv", tv[i].get("stt_dv", 0)), mid=mid)
            tv[j]["gateup_pe"] = gateup_pe
            tv[j]["silu_last"] = silu_last
            tv[j]["stt_dv"] = stt_dv
            if DEBUG_TAPS and j == 1:
                tap("d_x1", RES[1][:], dv_wait=stt_dv)
            rr2 = ln_stats(j, res_j, PB[6], PB[7], gateup_pe, gateup_pe,
                           res_dv=stt_dv)
            tv[j]["ln2_sv"] = rr2["sv"]
        else:
            for fn_ in mid["all"]:
                fn_()

        o_copy = {}
        units = []
        pb45_war = tv[j]["stt_dv"] if j < NT else tv[i - 1]["stt2_dv"]
        for md in range(KC):
            if md % 4 == 0:
                units.append(unit_load(wo_s[:, md * 1024:(md + 4) * 1024], UNIT))
            slot, u = units[-1]
            po = PB[4 + md % 2]
            base = (md % 4) * 1024
            def mm_o(e, po=po, slot=slot, base=base):
                last = None
                for kc in range(KC):
                    last = e.matmul(po[:], slot[:, base + kc * 128: base + kc * 128 + 128],
                                    HATT[:, kc * TT:(kc + 1) * TT],
                                    start=(kc == 0), stop=(kc == KC - 1))
                return last
            ov = sch.emit("tensor", mm_o,
                          waits=[("WL", ring_state["loaded"][u]), ("DV", tv[i]["hatt"]),
                                 ("DV", pb45_war), ("DV", o_copy.get(md - 2, 0))],
                          inc="PE")
            unit_used(u)
            o_copy[md] = sch.emit(
                "vector",
                (lambda e, po=po, md=md: e.tensor_copy(SA[:, md * TT:(md + 1) * TT], po[:])),
                waits=[("PE", ov)], inc="DV")
        tv[i]["outw"] = sch.val("PE")
        if DEBUG_TAPS and i == 1:
            tap("d_hatt", HATT[:], dv_wait=tv[i]["hatt"])
            tap("d_o", SB[:], dv_wait=o_copy[KC - 1])
        if j < NT:
            if DEBUG_TAPS and j == 1:
                tap("d_rstd2", ROWP[0:1, 0:TT], ac_wait=rr2["p"])
                tap("d_q2", SG[0][0:1, :], dv_wait=rr2["q"])
            tv[j]["bc2p"], tv[j]["bc2q"] = bcast_rows(rr2, PB[6], PB[7],
                                                      war_dv=rr2["rows_dv"])
        osq_last = 0
        bfs2_war = tv[j]["ln2_sv"] if j < NT else tv[i]["ln2_sv"]
        for md in range(KC):
            osq_last = sch.emit(
                "scalar",
                (lambda e, md=md: e.activation(
                    BFS[:, T + md * TT:T + (md + 1) * TT],
                    SA[:, md * TT:(md + 1) * TT], AF.Square)),
                waits=[("DV", o_copy[md]), ("PE", bfs2_war)], inc="AC")
        def mm_rms(e):
            last = None
            for md in range(KC):
                last = e.matmul(PB[4][0:1, :], ONESB[:],
                                BFS[:, T + md * TT:T + (md + 1) * TT],
                                start=(md == 0), stop=(md == KC - 1))
            return last
        v_rs = sch.emit("tensor", mm_rms,
                        waits=[("AC", osq_last), ("DV", o_copy[KC - 1])], inc="PE")
        v_ms = sch.emit("vector",
                        (lambda e: e.tensor_scalar_mul(SG[0][0:1, :], PB[4][0:1, :],
                                                       1.0 / D)),
                        waits=[("PE", v_rs), ("AC", state["last_silu"]),
                               ("AC", state["last_rowp"])], inc="DV")
        sch.emit("scalar",
                 (lambda e: e.activation(SG[0][0:1, :], SG[0][0:1, :], AF.Ln, bias=EPS6)),
                 waits=[("DV", v_ms)], inc="AC")
        v_rp = sch.emit("scalar",
                        (lambda e: e.activation(ROWP[0:1, 0:TT], SG[0][0:1, :],
                                                AF.Exp, scale=CMH)),
                        inc="AC")
        state["last_rowp"] = v_rp
        v_rb = sch.emit("tensor",
                        (lambda e: e.matmul(PB[5][:], ONESR[0:1, 0:P], ROWP[0:1, 0:TT],
                                            start=True, stop=True)),
                        waits=[("AC", v_rp), ("DV", o_copy[KC - 1])], inc="PE")
        if j < NT:
            tv[j]["ri_war_ac"] = tv[j]["silu_last"]
            tv[j]["ri_war_dv"] = tv[j]["lnb"]
            tv[j]["ri_war_dv2"] = tv[j]["stt_dv"]
            attn_tail(j, RES[j % 2], o_copy[KC - 1])
        res_i = RES[i % 2]
        apply_last = 0
        for md in range(KC):
            sch.emit("vector",
                     (lambda e, md=md: e.scalar_tensor_tensor(
                         SA[:, md * TT:(md + 1) * TT], SA[:, md * TT:(md + 1) * TT],
                         NW[:, md:md + 1], PB[5][:], OP.mult, OP.mult)),
                     waits=[("PE", v_rb)], inc="DV")
            apply_last = sch.emit(
                "vector",
                (lambda e, md=md: e.tensor_add(res_i[:, md * TT:(md + 1) * TT],
                                               res_i[:, md * TT:(md + 1) * TT],
                                               SA[:, md * TT:(md + 1) * TT])),
                inc="DV")
        if DEBUG_TAPS and i == 1:
            tap("d_msrow", SG[0][0:1, :], dv_wait=v_ms)
            tap("d_rstd", ROWP[0:1, 0:TT], ac_wait=v_rp)
            tap("d_x2", res_i[:], dv_wait=apply_last)
        bfs1_war = tv[j]["ri_pe"] if j < NT else tv[i]["ri_pe"]
        rr3 = ln_stats(i, res_i, PB[2], PB[3], bfs1_war, v_rs, res_dv=apply_last)
        bp3, bq3 = bcast_rows(rr3, PB[4], PB[5], war_dv=apply_last)
        lnb3 = normalize_bf16(res_i, bp3, bq3, PB[4], PB[5], bfs1_war)
        g2_war_ac = tv[j]["sig_i"] if j < NT else tv[i]["sig_i"]
        gateup2_pe, silu2_last, stt2_dv = ffn(
            res_i, lnb3, (PB[0], PB[1]), (PB[2], PB[3]), (PB[4], PB[5]),
            gate_war_ac=g2_war_ac, up_war_ac=g2_war_ac, up_war_dv=rr3["rows_dv"],
            mid=None)
        tv[i]["gateup2_pe"] = gateup2_pe
        tv[i]["stt2_dv"] = stt2_dv
        if j < NT:
            tv[j]["ri_war_ac"] = silu2_last
            tv[j]["ri_war_dv"] = lnb3
            tv[j]["ri_war_dv2"] = stt2_dv
        sch.emit("gpsimd",
                 (lambda e, i=i, res_i=res_i: e.dma_start(
                     out_h[:, :, i * TT:(i + 1) * TT],
                     res_i[:].rearrange("p (c t) -> p c t", c=KC))),
                 waits=[("DV", stt2_dv)], inc="OUT", incn=16)
        if i + 2 < NT:
            x_ready[i + 2] = sch.emit(
                "gpsimd",
                (lambda e, i=i: e.dma_start(
                    RES[i % 2][:].rearrange("p (c t) -> p c t", c=KC),
                    xh[:, :, (i + 2) * TT:(i + 3) * TT])),
                waits=[("OUT", 16 * (i + 1))], inc="X", incn=16)

    if False:
        d_x1 = nc.dram_tensor("d_x1", [P, T], F32, kind="ExternalOutput")
        d_hatt = nc.dram_tensor("d_hatt", [P, T], BF16, kind="ExternalOutput")
        d_r1 = nc.dram_tensor("d_r1", [P, T], F32, kind="ExternalOutput")
        d_i1 = nc.dram_tensor("d_i1", [P, T], F32, kind="ExternalOutput")
        d_xcb1 = nc.dram_tensor("d_xcb1", [P, T], BF16, kind="ExternalOutput")
        sch.emit("gpsimd", (lambda e: e.dma_start(d_x1[:], RES[1][:])),
                 waits=[("DV", tv[1]["stt_dv"])], inc="OUT", incn=0 if False else 16)
        sch.emit("gpsimd", (lambda e: e.dma_start(d_hatt[:], HATT[:])),
                 waits=[("DV", tv[0]["hatt"])], inc="OUT", incn=16)
        sch.emit("gpsimd", (lambda e: e.dma_start(d_r1[:], SC_[:])),
                 waits=[("AC", tv[1]["sig_r"])], inc="OUT", incn=16)
        sch.emit("gpsimd", (lambda e: e.dma_start(d_i1[:], SD[:])),
                 waits=[("AC", tv[1]["sig_i"])], inc="OUT", incn=16)
        sch.emit("gpsimd", (lambda e: e.dma_start(d_xcb1[:], BFS[:, 0:T])),
                 waits=[("DV", tv[1]["xcb"])], inc="OUT", incn=16)
        sch.ops["gpsimd"].append(([("OUT", 16 * NT + 80)], None, None))
    else:
        sch.ops["gpsimd"].append(([("OUT", sch.val("OUT"))], None, None))

    with nc.Block() as blk:
        def runner(lst):
            def body(e):
                for ws, fn, iv in lst:
                    for s, v in ws:
                        e.wait_ge(SEMS[s], v)
                    if fn is None:
                        continue
                    inst = fn(e)
                    if iv is not None:
                        inst.then_inc(SEMS[iv[0]], iv[1])
            return body
        blk.sync(runner(sch.ops["sync"]))
        blk.gpsimd(runner(sch.ops["gpsimd"]))
        blk.tensor(runner(sch.ops["tensor"]))
        blk.vector(runner(sch.ops["vector"]))
        blk.scalar(runner(sch.ops["scalar"]))
    return nc


def _host_layouts(inputs):
    wg = np.asarray(inputs["L0_ffn_w_gate"])
    wu = np.asarray(inputs["L0_ffn_w_up"])
    wd = np.asarray(inputs["L0_ffn_w_down"])
    wr = np.asarray(inputs["L0_attn_Wr_w"], dtype=np.float32)
    wi = np.asarray(inputs["L0_attn_Wi_w"], dtype=np.float32)
    wo = np.asarray(inputs["L0_attn_out_w"], dtype=np.float32)

    def hcmaj(w):
        return np.ascontiguousarray(
            w.reshape(HC, 128, KC, 128).transpose(3, 0, 2, 1).reshape(128, -1))

    def mdmaj(w):
        return np.ascontiguousarray(
            w.reshape(KC, 128, HC, 128).transpose(3, 0, 2, 1).reshape(128, -1))

    def mcmaj(w):
        return np.ascontiguousarray(
            w.reshape(KC, 128, KC, 128).transpose(3, 0, 2, 1).reshape(128, -1))

    def vec(v):
        return np.ascontiguousarray(np.asarray(v, dtype=np.float32).reshape(KC, 128).T)

    cwf = np.asarray(inputs["L0_attn_conv_weight"], dtype=np.float32)[:, 0, :]
    cw = np.ascontiguousarray(
        cwf.reshape(KC, 128, 4).transpose(1, 2, 0).reshape(128, 32))

    common = {
        "wg": hcmaj(wg), "wu": hcmaj(wu), "wd": mdmaj(wd),
        "wr": mcmaj(wr), "wi": mcmaj(wi), "wo": mcmaj(wo),
        "rb": vec(inputs["L0_attn_Wr_b"]), "ib": vec(inputs["L0_attn_Wi_b"]),
        "la": vec(inputs["L0_attn_log_a"]), "nw": vec(inputs["L0_attn_norm_w"]),
        "cw": cw, "cb": vec(inputs["L0_attn_conv_bias"]),
    }
    x = np.asarray(inputs["L0_attn_x"], dtype=np.float32)
    maps = []
    for b in range(8):
        xb = np.ascontiguousarray(
            x[b].T.reshape(KC, 128, T).transpose(1, 0, 2))
        m = dict(common)
        m["x"] = xb
        maps.append(m)
    return maps


def kernel(**inputs):
    in_maps = _host_layouts(inputs)
    nc = build_nc()
    res = run_bass_kernel_spmd(nc, in_maps, core_ids=list(range(8)))
    outs = []
    for b in range(8):
        o = np.asarray(res.results[b]["out"])
        outs.append(o.transpose(1, 0, 2).reshape(D, T).T)
    return np.stack(outs).astype(np.float32)


# version 07
# speedup vs baseline: 1.1032x; 1.0420x over previous
import numpy as np
import concourse.bass as bass
import concourse.mybir as mybir
from concourse.bass_utils import run_bass_kernel_spmd

F32 = mybir.dt.float32
BF16 = mybir.dt.bfloat16
I32 = mybir.dt.int32
AF = mybir.ActivationFunctionType
OP = mybir.AluOpType

D = 1024
KC = 8
H = 3072
HC = 24
T = 4096
TT = 512
NT = T // TT
P = 128
RING = 6
UNIT = 4096

S127 = 1.0 / 127.0
DOWN_SCALE = 0.5 * S127 * S127
LOG1EM7 = float(np.float32(np.log(np.float32(1e-7))))
DEBUG_TAPS = False
INTERLEAVE = True


class Sched:
    def __init__(self):
        self.ops = {"sync": [], "gpsimd": [], "tensor": [], "vector": [], "scalar": []}
        self.cnt = {}
        self.marks = {}

    def val(self, sem):
        return self.cnt.get(sem, 0)

    def emit(self, eng, fn, waits=(), inc=None, incn=1):
        ws = []
        for s, v in waits:
            if not v or v <= 0:
                continue
            key = (eng, s)
            if self.marks.get(key, 0) >= v:
                continue
            self.marks[key] = v
            ws.append((s, v))
        iv = None
        if inc is not None:
            self.cnt[inc] = self.cnt.get(inc, 0) + incn
            iv = (inc, incn)
        self.ops[eng].append((ws, fn, iv))
        return self.cnt.get(inc, 0) if inc else None


def build_nc():
    nc = bass.Bass("TRN2", target_bir_lowering=False)

    xh = nc.dram_tensor("x", [P, KC, T], F32, kind="ExternalInput")
    wg_i = nc.dram_tensor("wg", [P, HC * 8 * 128], I32, kind="ExternalInput")
    wu_i = nc.dram_tensor("wu", [P, HC * 8 * 128], I32, kind="ExternalInput")
    wd_i = nc.dram_tensor("wd", [P, 8 * HC * 128], I32, kind="ExternalInput")
    wr_i = nc.dram_tensor("wr", [P, KC * 1024], F32, kind="ExternalInput")
    wi_i = nc.dram_tensor("wi", [P, KC * 1024], F32, kind="ExternalInput")
    wo_i = nc.dram_tensor("wo", [P, KC * 1024], F32, kind="ExternalInput")
    rb_i = nc.dram_tensor("rb", [P, KC], F32, kind="ExternalInput")
    ib_i = nc.dram_tensor("ib", [P, KC], F32, kind="ExternalInput")
    la_i = nc.dram_tensor("la", [P, KC], F32, kind="ExternalInput")
    nw_i = nc.dram_tensor("nw", [P, KC], F32, kind="ExternalInput")
    cw_i = nc.dram_tensor("cw", [P, 32], F32, kind="ExternalInput")
    cb_i = nc.dram_tensor("cb", [P, KC], F32, kind="ExternalInput")
    out_h = nc.dram_tensor("out", [P, KC, T], F32, kind="ExternalOutput")

    wg_s = nc.dram_tensor("wg_s", [P, HC * 8 * 128], BF16, kind="Internal")
    wu_s = nc.dram_tensor("wu_s", [P, HC * 8 * 128], BF16, kind="Internal")
    wd_s = nc.dram_tensor("wd_s", [P, 8 * HC * 128], BF16, kind="Internal")
    wr_s = nc.dram_tensor("wr_s", [P, KC * 1024], BF16, kind="Internal")
    wi_s = nc.dram_tensor("wi_s", [P, KC * 1024], BF16, kind="Internal")
    wo_s = nc.dram_tensor("wo_s", [P, KC * 1024], BF16, kind="Internal")

    import contextlib
    ctx = contextlib.ExitStack()
    nc._kernel_ctx = ctx
    sb = lambda n, s, d=F32: ctx.enter_context(nc.sbuf_tensor(n, s, d))

    RES = [sb("RES0", [P, T]), sb("RES1", [P, T])]
    SA = sb("SA", [P, T])
    SB = sb("SB", [P, T])
    SC_ = sb("SC", [P, T])
    SD = sb("SD", [P, T])
    BFS = sb("BFS", [P, 2 * T], BF16)
    HATT = sb("HATT", [P, T], BF16)
    HH = sb("HH", [P, HC * TT], BF16)
    LTMP = sb("LTMP", [P, TT + 8])
    ROWP = LTMP
    TAIL = sb("TAIL", [P, KC * 4])
    WRG = [sb(f"WRG{j}", [P, UNIT], BF16) for j in range(RING)]
    SG = [sb("SG0", [P, TT]), sb("SG1", [P, TT])]
    ONES = sb("ONESF", [P, TT])
    ONESR = ONES
    ONESB = sb("ONESB", [P, P], BF16)
    CONSTS = sb("CONSTS", [P, 8])
    CW = sb("CWt", [P, 32])
    CB = sb("CBt", [P, KC])
    RB = sb("RBt", [P, KC])
    IB = sb("IBt", [P, KC])
    LAT = sb("LAt", [P, KC])
    NW = sb("NWt", [P, KC])
    L8 = sb("L8", [P, KC])
    L16 = sb("L16", [P, KC])
    SCAR = sb("SCAR", [P, KC])
    CCAR = sb("CCAR", [P, KC])
    RED = sb("REDt", [P, KC])
    PB = [ctx.enter_context(nc.psum_tensor(f"PB{j}", [P, TT], F32)) for j in range(8)]

    cvals = [S127, -1.0, -0.5, 1e-5, 1e-6]
    for j_, v in enumerate(cvals):
        nc.const_aps.register(F32, float(np.float32(v)), CONSTS[:, j_:j_ + 1])
    C127, CM1, CMH, EPS5, EPS6 = [float(np.float32(v)) for v in cvals]

    SEMS = {}
    for n in ["PE", "DV", "AC", "WL", "X", "OUT", "CV"]:
        SEMS[n] = ctx.enter_context(nc.semaphore("sem_" + n))

    sch = Sched()
    ring_state = {"u": 0, "done": {}, "loaded": {}}

    def unit_load(src_ap, ncols):
        u = ring_state["u"]
        ring_state["u"] += 1
        slot = WRG[u % RING]
        waits = [("CV", src_cast_done[src_ap.tensor.name])]
        prev = u - RING
        if prev >= 0:
            waits.append(("PE", ring_state["done"].get(prev, 0)))
        def fn(e, slot=slot, src=src_ap, ncols=ncols):
            return e.dma_start(slot[:, 0:ncols], src)
        v = sch.emit("sync", fn, waits=waits, inc="WL", incn=16)
        ring_state["loaded"][u] = v
        return slot, u

    def unit_used(u):
        ring_state["done"][u] = sch.val("PE")

    src_cast_done = {}
    def cast_dma(dst, src):
        def fn(e, dst=dst, src=src):
            return e.dma_start(dst[:], src[:])
        v = sch.emit("gpsimd", fn, inc="CV", incn=16)
        src_cast_done[dst.name] = v

    cast_dma(wg_s, wg_i)
    cast_dma(wu_s, wu_i)
    cast_dma(wd_s, wd_i)
    cast_dma(wr_s, wr_i)
    cast_dma(wi_s, wi_i)
    cast_dma(wo_s, wo_i)

    for dst, src in [(CW, cw_i), (CB, cb_i), (RB, rb_i), (IB, ib_i),
                     (LAT, la_i), (NW, nw_i)]:
        sch.emit("sync", (lambda e, d=dst, s=src: e.dma_start(d[:], s[:])),
                 inc="X", incn=16)
    x_ready = {}
    for i in range(2):
        x_ready[i] = sch.emit(
            "gpsimd",
            (lambda e, i=i: e.dma_start(RES[i % 2][:].rearrange("p (c t) -> p c t", c=KC),
                                        xh[:, :, i * TT:(i + 1) * TT])),
            inc="X", incn=16)

    def ms(ap, v):
        sch.emit("gpsimd", (lambda e, ap=ap, v=v: e.memset(ap, v)))
    ms(ONES[:], 1.0)
    ms(ONESB[:], 1.0)
    ms(TAIL[:], 0.0)
    ms(SCAR[:], 0.0)
    ms(CCAR[:], 0.0)
    for j_, v in enumerate(cvals):
        ms(CONSTS[:, j_:j_ + 1], float(np.float32(v)))
    setup_done = sch.emit("gpsimd", (lambda e: e.memset(CONSTS[:, 7:8], 0.0)),
                          inc="X", incn=1)

    sch.emit("scalar", (lambda e: e.activation(L8[:], LAT[:], AF.Sigmoid)),
             waits=[("X", setup_done)], inc="AC")
    v_ln = sch.emit("scalar", (lambda e: e.activation(L8[:], L8[:], AF.Ln)), inc="AC")
    sch.emit("vector",
             (lambda e: e.tensor_scalar(L16[:], L8[:], 16.0, -1e30, OP.mult, OP.max)),
             waits=[("AC", v_ln)], inc="DV")
    v_l8 = sch.emit("vector",
                    (lambda e: e.tensor_scalar(L8[:], L8[:], 8.0, -1e30, OP.mult, OP.max)),
                    inc="DV")

    tv = [dict() for _ in range(NT + 1)]
    state = {"scar_v": 0, "ccar_v": 0, "last_silu": 0, "last_rowp": 0}

    dbg = {}
    if DEBUG_TAPS:
        dbg["d_scar"] = nc.dram_tensor("d_scar", [P, KC], F32, kind="ExternalOutput")
        for nm in ["d_l8a", "d_l8b", "d_l8c", "d_l8d"]:
            dbg[nm] = nc.dram_tensor(nm, [P, KC], F32, kind="ExternalOutput")
        dbg["d_r1pre"] = nc.dram_tensor("d_r1pre", [P, T], F32, kind="ExternalOutput")
        dbg["d_xc1"] = nc.dram_tensor("d_xc1", [P, T], F32, kind="ExternalOutput")
        dbg["d_xcb1"] = nc.dram_tensor("d_xcb1", [P, T], BF16, kind="ExternalOutput")
        dbg["d_rstd2"] = nc.dram_tensor("d_rstd2", [1, TT], F32, kind="ExternalOutput")
        dbg["d_q2"] = nc.dram_tensor("d_q2", [1, TT], F32, kind="ExternalOutput")
        for nm, dt_ in [("d_x1", F32), ("d_o", F32), ("d_x2", F32), ("d_hatt", BF16),
                        ("d_logA", F32), ("d_aexp", F32), ("d_cs", F32), ("d_term", F32),
                        ("d_la", F32), ("d_s", F32),
                        ("d_msrow", F32), ("d_rstd", F32)]:
            dbg[nm] = nc.dram_tensor(nm, [P, T] if nm not in ("d_msrow", "d_rstd")
                                     else [1, TT], dt_, kind="ExternalOutput")

    def tap(name, src_ap, dv_wait=None, ac_wait=None):
        if not DEBUG_TAPS:
            return
        w = []
        if dv_wait: w.append(("DV", dv_wait))
        if ac_wait: w.append(("AC", ac_wait))
        v = sch.emit("gpsimd", (lambda e, s=src_ap, d=dbg[name]: e.dma_start(d[:], s)),
                     waits=w, inc="OUT", incn=16)
        sch.emit("vector", (lambda e: e.engine_nop()), waits=[("OUT", v)], inc="DV")
        state["dbg_out"] = v

    def ln_stats(i, res, pb_m, pb_v, bfs1_war_pe, bfs2_war_pe, res_dv=0,
                 extra_pe_waits=()):
        r = {}
        r["xb"] = sch.emit("vector", (lambda e: e.tensor_copy(BFS[:, 0:T], res[:])),
                           waits=[("PE", bfs1_war_pe), ("X", x_ready.get(i, 0)),
                                  ("X", setup_done), ("DV", res_dv)], inc="DV")
        r["xsq"] = sch.emit("scalar",
                            (lambda e: e.activation(BFS[:, T:2 * T], res[:], AF.Square)),
                            waits=[("PE", bfs2_war_pe), ("X", x_ready.get(i, 0)),
                                   ("X", setup_done), ("DV", res_dv)], inc="AC")

        def mm_stats(e, pb, off):
            last = None
            for kc in range(KC):
                last = e.matmul(pb[0:1, :], ONESB[:, 0:1],
                                BFS[:, off + kc * TT: off + (kc + 1) * TT],
                                start=(kc == 0), stop=(kc == KC - 1))
            return last
        r["sm"] = sch.emit("tensor", (lambda e: mm_stats(e, pb_m, 0)),
                           waits=[("DV", r["xb"]), ("X", setup_done)] + list(extra_pe_waits),
                           inc="PE")
        r["sv"] = sch.emit("tensor", (lambda e: mm_stats(e, pb_v, T)),
                           waits=[("AC", r["xsq"])], inc="PE")
        sch.emit("vector",
                 (lambda e: e.tensor_scalar_mul(SG[0][0:1, :], pb_m[0:1, :], 1.0 / D)),
                 waits=[("PE", r["sm"]), ("AC", state["last_silu"]),
                        ("AC", state["last_rowp"])], inc="DV")
        sch.emit("vector",
                 (lambda e: e.tensor_mul(ROWP[0:1, 0:TT], SG[0][0:1, :], SG[0][0:1, :])),
                 inc="DV")
        r["var"] = sch.emit("vector",
                            (lambda e: e.scalar_tensor_tensor(
                                SG[1][0:1, :], pb_v[0:1, :], 1.0 / D,
                                ROWP[0:1, 0:TT], OP.mult, OP.subtract)),
                            waits=[("PE", r["sv"])], inc="DV")
        sch.emit("scalar",
                 (lambda e: e.activation(SG[1][0:1, :], SG[1][0:1, :], AF.Ln, bias=EPS5)),
                 waits=[("DV", r["var"])], inc="AC")
        r["p"] = sch.emit("scalar",
                          (lambda e: e.activation(ROWP[0:1, 0:TT], SG[1][0:1, :],
                                                  AF.Exp, scale=CMH)),
                          inc="AC")
        state["last_rowp"] = r["p"]
        r["q"] = sch.emit("vector",
                          (lambda e: e.scalar_tensor_tensor(SG[0][0:1, :], SG[0][0:1, :],
                                                            -1.0, ROWP[0:1, 0:TT],
                                                            OP.mult, OP.mult)),
                          waits=[("AC", r["p"])], inc="DV")
        r["rows_dv"] = sch.val("DV")
        return r

    def bcast_rows(rr, pb_p, pb_q, war_ac=0, war_dv=0):
        v_bp = sch.emit("tensor",
                        (lambda e: e.matmul(pb_p[:], ONESR[0:1, 0:P], ROWP[0:1, 0:TT],
                                            start=True, stop=True)),
                        waits=[("AC", rr["p"]), ("AC", war_ac), ("DV", war_dv)], inc="PE")
        v_bq = sch.emit("tensor",
                        (lambda e: e.matmul(pb_q[:], ONESR[0:1, 0:P], SG[0][0:1, :],
                                            start=True, stop=True)),
                        waits=[("DV", rr["q"])], inc="PE")
        return v_bp, v_bq

    def ln_stats_bcast(i, res, pb_m, pb_v, bfs1_war_pe, bfs2_war_pe, res_dv=0,
                       eps=None, extra_pe_waits=()):
        r = {}
        r["xb"] = sch.emit("vector", (lambda e: e.tensor_copy(BFS[:, 0:T], res[:])),
                           waits=[("PE", bfs1_war_pe), ("X", x_ready.get(i, 0)),
                                  ("X", setup_done), ("DV", res_dv)], inc="DV")
        r["xsq"] = sch.emit("scalar",
                            (lambda e: e.activation(BFS[:, T:2 * T], res[:], AF.Square)),
                            waits=[("PE", bfs2_war_pe), ("X", x_ready.get(i, 0)),
                                   ("X", setup_done), ("DV", res_dv)], inc="AC")
        def mm_stats(e, pb, off):
            last = None
            for kc in range(KC):
                last = e.matmul(pb[:], ONESB[:, 0:P],
                                BFS[:, off + kc * TT: off + (kc + 1) * TT],
                                start=(kc == 0), stop=(kc == KC - 1))
            return last
        r["sm"] = sch.emit("tensor", (lambda e: mm_stats(e, pb_m, 0)),
                           waits=[("DV", r["xb"]), ("X", setup_done)] + list(extra_pe_waits),
                           inc="PE")
        r["sv"] = sch.emit("tensor", (lambda e: mm_stats(e, pb_v, T)),
                           waits=[("AC", r["xsq"])], inc="PE")
        sch.emit("vector",
                 (lambda e: e.tensor_scalar_mul(SG[1][:], pb_m[:], 1.0 / D)),
                 waits=[("PE", r["sm"]), ("AC", state["last_silu"]),
                        ("AC", state["last_rowp"])], inc="DV")
        sch.emit("vector", (lambda e: e.tensor_mul(SG[0][:], SG[1][:], SG[1][:])),
                 inc="DV")
        r["var"] = sch.emit("vector",
                            (lambda e: e.scalar_tensor_tensor(
                                SG[0][:], pb_v[:], 1.0 / D, SG[0][:],
                                OP.mult, OP.subtract)),
                            waits=[("PE", r["sv"])], inc="DV")
        sch.emit("scalar",
                 (lambda e: e.activation(SG[0][:], SG[0][:], AF.Ln, bias=eps)),
                 waits=[("DV", r["var"])], inc="AC")
        r["p"] = sch.emit("scalar",
                          (lambda e: e.activation(SG[0][:], SG[0][:], AF.Exp, scale=CMH)),
                          inc="AC")
        state["last_rowp"] = r["p"]
        r["q"] = sch.emit("vector",
                          (lambda e: e.scalar_tensor_tensor(SG[1][:], SG[1][:], -1.0,
                                                            SG[0][:], OP.mult, OP.mult)),
                          waits=[("AC", r["p"])], inc="DV")
        return r

    def normalize_bf16(res, v_bp, v_bq, pb_p, pb_q, bfs1_war_pe, q_dv=0, p_ac=0):
        last = 0
        pp = pb_p[:, 0:TT] if pb_p in (SG[0], SG[1]) else pb_p[:]
        qq = pb_q[:, 0:TT] if pb_q in (SG[0], SG[1]) else pb_q[:]
        for c in range(KC):
            sch.emit("vector",
                     (lambda e, c=c: e.tensor_mul(LTMP[:, 0:TT],
                                                  res[:, c * TT:(c + 1) * TT], pp)),
                     waits=[("PE", v_bp), ("PE", v_bq), ("PE", bfs1_war_pe),
                            ("DV", q_dv), ("AC", p_ac)], inc="DV")
            last = sch.emit("vector",
                            (lambda e, c=c: e.tensor_add(BFS[:, c * TT:(c + 1) * TT],
                                                         LTMP[:, 0:TT], qq)),
                            inc="DV")
        return last

    def mid_make(i):
        g = {}
        def g_la_ixc():
            if DEBUG_TAPS and i == 1:
                tap("d_l8c", L8[:], ac_wait=tv[i]["sig_r"])
                tap("d_r1pre", SC_[:], ac_wait=tv[i]["sig_r"])
            for c in range(KC):
                sch.emit("vector",
                         (lambda e, c=c: e.tensor_scalar(
                             SA[:, c * TT:(c + 1) * TT], SC_[:, c * TT:(c + 1) * TT],
                             L8[:, c:c + 1], LOG1EM7, OP.mult, OP.max)),
                         waits=[("AC", tv[i]["sig_r"]), ("DV", v_l8)], inc="DV")
            for c in range(KC):
                tv[i]["ixc"] = sch.emit(
                    "vector",
                    (lambda e, c=c: e.tensor_mul(
                        SD[:, c * TT:(c + 1) * TT], SD[:, c * TT:(c + 1) * TT],
                        SB[:, c * TT:(c + 1) * TT])),
                    waits=[("AC", tv[i]["sig_i"])], inc="DV")
        def a_a2():
            for c in range(KC):
                tv[i]["a2"] = sch.emit(
                    "scalar",
                    (lambda e, c=c: e.activation(
                        SB[:, c * TT:(c + 1) * TT], SC_[:, c * TT:(c + 1) * TT],
                        AF.Exp, scale=L16[:, c:c + 1])),
                    waits=[("DV", tv[i]["ixc"])], inc="AC")
        def g_v():
            sch.emit("vector",
                     (lambda e: e.tensor_scalar(SC_[:], SB[:], 1.0, -1.0,
                                                OP.subtract, OP.mult)),
                     waits=[("AC", tv[i]["a2"])], inc="DV")
            tv[i]["vmax"] = sch.emit(
                "vector", (lambda e: e.tensor_scalar_max(SC_[:], SC_[:], 1e-6)), inc="DV")
        def a_bamp():
            tv[i]["bamp"] = sch.emit(
                "scalar", (lambda e: e.activation(SB[:], SC_[:], AF.Sqrt)),
                waits=[("DV", tv[i]["vmax"])], inc="AC")
        def g_bt():
            tv[i]["bt"] = sch.emit(
                "vector", (lambda e: e.tensor_mul(SD[:], SD[:], SB[:])),
                waits=[("AC", tv[i]["bamp"])], inc="DV")
        def g_ss(c0, c1):
            def f():
                for c in range(c0, c1):
                    sch.emit("vector",
                             (lambda e, c=c: e.tensor_tensor_scan(
                                 SB[:, c * TT:(c + 1) * TT], ONES[:],
                                 SA[:, c * TT:(c + 1) * TT], SCAR[:, c:c + 1],
                                 OP.mult, OP.add)),
                             waits=[("DV", state["scar_v"]), ("DV", tv[i]["bt"])],
                             inc="DV")
            return f
        def g_sred():
            for c in range(KC):
                sch.emit("vector",
                         (lambda e, c=c: e.tensor_reduce(
                             RED[:, c:c + 1], SA[:, c * TT:(c + 1) * TT],
                             mybir.AxisListType.X, OP.add)),
                         inc="DV")
            sch.emit("vector", (lambda e: e.engine_nop()), inc="DV")
            sch.emit("vector", (lambda e: e.engine_nop()), inc="DV")
            state["scar_v"] = sch.emit(
                "vector", (lambda e: e.tensor_add(SCAR[:], SCAR[:], RED[:])), inc="DV")
        def g_logA():
            tv[i]["logA"] = sch.emit(
                "vector", (lambda e: e.tensor_scalar_max(SB[:], SB[:], -80.0)), inc="DV")
            if DEBUG_TAPS and i == 1:
                tap("d_logA", SB[:], dv_wait=tv[i]["logA"])
        def a_en():
            tv[i]["en"] = sch.emit(
                "scalar", (lambda e: e.activation(SA[:], SB[:], AF.Exp, scale=CM1)),
                waits=[("DV", tv[i]["logA"])], inc="AC")
        def g_term():
            tv[i]["term"] = sch.emit(
                "vector", (lambda e: e.tensor_mul(SD[:], SD[:], SA[:])),
                waits=[("AC", tv[i]["en"])], inc="DV")
            if DEBUG_TAPS and i == 1:
                tap("d_term", SD[:], dv_wait=tv[i]["term"])
        def a_aexp():
            tv[i]["aexp"] = sch.emit(
                "scalar", (lambda e: e.activation(SC_[:], SB[:], AF.Exp)),
                waits=[("DV", tv[i]["term"])], inc="AC")
        def g_cs(c0, c1):
            def f():
                for c in range(c0, c1):
                    sch.emit("vector",
                             (lambda e, c=c: e.tensor_tensor_scan(
                                 SA[:, c * TT:(c + 1) * TT], ONES[:],
                                 SD[:, c * TT:(c + 1) * TT], CCAR[:, c:c + 1],
                                 OP.mult, OP.add)),
                             waits=[("DV", state["ccar_v"]), ("DV", tv[i]["term"])],
                             inc="DV")
            return f
        def g_cred():
            for c in range(KC):
                sch.emit("vector",
                         (lambda e, c=c: e.tensor_reduce(
                             RED[:, c:c + 1], SD[:, c * TT:(c + 1) * TT],
                             mybir.AxisListType.X, OP.add)),
                         inc="DV")
            sch.emit("vector", (lambda e: e.engine_nop()), inc="DV")
            sch.emit("vector", (lambda e: e.engine_nop()), inc="DV")
            state["ccar_v"] = sch.emit(
                "vector", (lambda e: e.tensor_add(CCAR[:], CCAR[:], RED[:])), inc="DV")
        def g_hatt():
            if DEBUG_TAPS and i == 1:
                tap("d_aexp", SC_[:], ac_wait=tv[i]["aexp"])
                tap("d_cs", SA[:], dv_wait=state["ccar_v"])
            tv[i]["hatt"] = sch.emit(
                "vector", (lambda e: e.tensor_mul(HATT[:], SC_[:], SA[:])),
                waits=[("AC", tv[i]["aexp"]),
                       ("PE", tv[i - 1].get("outw", 0) if i > 0 else 0)],
                inc="DV")
        g["dve"] = {2: [g_la_ixc], 4: [g_v], 6: [g_bt], 8: [g_ss(0, 4)],
                    10: [g_ss(4, 8)], 12: [g_sred, g_logA], 14: [g_term],
                    16: [g_cs(0, 4)], 18: [g_cs(4, 8)], 20: [g_cred], 21: [g_hatt]}
        g["act"] = {2: [a_a2], 4: [a_bamp], 13: [a_en], 17: [a_aexp]}
        g["all"] = [g_la_ixc, a_a2, g_v, a_bamp, g_bt, g_ss(0, 8), g_sred, g_logA,
                    a_en, g_term, a_aexp, g_cs(0, 8), g_cred, g_hatt]
        return g

    def ffn(res, lnb_ready_dv, pg_pair, pu_pair, pd_pair,
            gate_war_ac=0, up_war_ac=0, up_war_dv=0, mid=None):
        silu_done = {}
        h_done = {}
        gslot = uslot = None
        gu = uu = None
        for hc in range(HC):
            if mid is not None and INTERLEAVE:
                for fn_ in mid["dve"].get(hc, []):
                    fn_()
                for fn_ in mid["act"].get(hc, []):
                    fn_()
            if hc % 4 == 0:
                gslot, gu = unit_load(wg_s[:, hc * 1024:(hc + 4) * 1024], UNIT)
                uslot, uu = unit_load(wu_s[:, hc * 1024:(hc + 4) * 1024], UNIT)
            pg = pg_pair[hc % 2]
            pu = pu_pair[hc % 2]
            base = (hc % 4) * 1024
            def mm_gate(e, pg=pg, gslot=gslot, base=base):
                last = None
                for kc in range(KC):
                    last = e.matmul(pg[:], gslot[:, base + kc * 128: base + kc * 128 + 128],
                                    BFS[:, kc * TT:(kc + 1) * TT],
                                    start=(kc == 0), stop=(kc == KC - 1))
                return last
            gv = sch.emit("tensor", mm_gate,
                          waits=[("WL", ring_state["loaded"][gu]), ("DV", lnb_ready_dv),
                                 ("AC", silu_done.get(hc - 2, gate_war_ac))], inc="PE")
            unit_used(gu)
            def mm_up(e, pu=pu, uslot=uslot, base=base):
                last = None
                for kc in range(KC):
                    last = e.matmul(pu[:], uslot[:, base + kc * 128: base + kc * 128 + 128],
                                    BFS[:, kc * TT:(kc + 1) * TT],
                                    start=(kc == 0), stop=(kc == KC - 1))
                return last
            uv = sch.emit("tensor", mm_up,
                          waits=[("WL", ring_state["loaded"][uu]),
                                 ("DV", h_done.get(hc - 2, up_war_dv)),
                                 ("AC", up_war_ac if hc < 2 else 0)], inc="PE")
            unit_used(uu)
            sg = SG[hc % 2]
            sv = sch.emit("scalar",
                          (lambda e, sg=sg, pg=pg: e.activation(sg[:], pg[:], AF.Silu,
                                                                scale=C127)),
                          waits=[("PE", gv), ("DV", h_done.get(hc - 2, 0))], inc="AC")
            silu_done[hc] = sv
            state["last_silu"] = sv
            h_done[hc] = sch.emit(
                "vector",
                (lambda e, sg=sg, pu=pu, hc=hc:
                 e.tensor_mul(HH[:, hc * TT:(hc + 1) * TT], sg[:], pu[:])),
                waits=[("AC", sv), ("PE", uv)], inc="DV")
        gateup_pe = sch.val("PE")
        silu_last = silu_done[HC - 1]
        if mid is not None and not INTERLEAVE:
            for fn_ in mid["all"]:
                fn_()
        stt_done = {}
        for md in range(KC):
            dslot, du = unit_load(wd_s[:, md * HC * 128:(md + 1) * HC * 128], HC * 128)
            pd = pd_pair[md % 2]
            def mm_down(e, pd=pd, dslot=dslot):
                last = None
                for hc in range(HC):
                    last = e.matmul(pd[:], dslot[:, hc * 128: hc * 128 + 128],
                                    HH[:, hc * TT:(hc + 1) * TT],
                                    start=(hc == 0), stop=(hc == HC - 1))
                return last
            dv_ = sch.emit("tensor", mm_down,
                           waits=[("WL", ring_state["loaded"][du]), ("DV", h_done[HC - 1]),
                                  ("DV", stt_done.get(md - 2, 0))], inc="PE")
            unit_used(du)
            stt_done[md] = sch.emit(
                "vector",
                (lambda e, pd=pd, md=md: e.scalar_tensor_tensor(
                    res[:, md * TT:(md + 1) * TT], pd[:], DOWN_SCALE,
                    res[:, md * TT:(md + 1) * TT], OP.mult, OP.add)),
                waits=[("PE", dv_)], inc="DV")
        return gateup_pe, silu_last, sch.val("DV")

    def proj8(wsrc, pp_pair, dst, bias, rhs_dv, war_ac, war_dv, sig_store):
        units = []
        last_pe = 0
        for mc in range(KC):
            if mc % 4 == 0:
                units.append(unit_load(wsrc[:, mc * 1024:(mc + 4) * 1024], UNIT))
            slot, u = units[-1]
            pr = pp_pair[mc % 2]
            base = (mc % 4) * 1024
            def mm(e, pr=pr, slot=slot, base=base):
                last = None
                for kc in range(KC):
                    last = e.matmul(pr[:], slot[:, base + kc * 128: base + kc * 128 + 128],
                                    BFS[:, kc * TT:(kc + 1) * TT],
                                    start=(kc == 0), stop=(kc == KC - 1))
                return last
            rv = sch.emit("tensor", mm,
                          waits=[("WL", ring_state["loaded"][u]), ("DV", rhs_dv),
                                 ("AC", sig_store.get(mc - 2, war_ac)),
                                 ("DV", war_dv if mc < 2 else 0)], inc="PE")
            last_pe = rv
            unit_used(u)
            sig_store[mc] = sch.emit(
                "scalar",
                (lambda e, pr=pr, mc=mc: e.activation(
                    dst[:, mc * TT:(mc + 1) * TT], pr[:], AF.Sigmoid,
                    bias=bias[:, mc:mc + 1])),
                waits=[("PE", rv)], inc="AC")
        return last_pe

    def attn_tail(j, res_j, conv_war_dv):
        for c in range(KC):
            sch.emit("vector",
                     (lambda e, c=c: e.tensor_mul(SD[:, 0:TT],
                                                  res_j[:, c * TT:(c + 1) * TT], PB[6][:])),
                     waits=[("PE", tv[j]["bc2p"]), ("PE", tv[j]["bc2q"]),
                            ("DV", conv_war_dv)], inc="DV")
            sch.emit("vector",
                     (lambda e: e.tensor_add(LTMP[:, 3:3 + TT], SD[:, 0:TT], PB[7][:])),
                     inc="DV")
            sch.emit("vector",
                     (lambda e, c=c: e.tensor_copy(LTMP[:, 0:3], TAIL[:, c * 4:c * 4 + 3])),
                     inc="DV")
            sch.emit("vector",
                     (lambda e, c=c: e.tensor_copy(TAIL[:, c * 4:c * 4 + 3],
                                                   LTMP[:, TT:TT + 3])),
                     inc="DV")
            sch.emit("vector",
                     (lambda e, c=c: e.tensor_scalar(SB[:, c * TT:(c + 1) * TT],
                                                     LTMP[:, 0:TT], CW[:, c:c + 1],
                                                     CB[:, c:c + 1], OP.mult, OP.add)),
                     inc="DV")
            for k in range(1, 4):
                sch.emit("vector",
                         (lambda e, c=c, k=k: e.scalar_tensor_tensor(
                             SB[:, c * TT:(c + 1) * TT], LTMP[:, k:k + TT],
                             CW[:, k * 8 + c:k * 8 + c + 1],
                             SB[:, c * TT:(c + 1) * TT], OP.mult, OP.add)),
                         inc="DV")
        tv[j]["conv_dv"] = sch.val("DV")
        if DEBUG_TAPS and j == 1:
            tap("d_xc1", SB[:], dv_wait=tv[j]["conv_dv"])
        tv[j]["xcb"] = sch.emit("vector", (lambda e: e.tensor_copy(BFS[:, 0:T], SB[:])),
                                waits=[("PE", tv[j]["ln2_sv"])], inc="DV")
        if DEBUG_TAPS and j == 1:
            tap("d_xcb1", BFS[:, 0:T], dv_wait=tv[j]["xcb"])
        sig_r = {}
        sig_i = {}
        proj8(wr_s, (PB[0], PB[1]), SC_, RB, tv[j]["xcb"],
              tv[j].get("ri_war_ac", 0), tv[j].get("ri_war_dv", 0), sig_r)
        tv[j]["ri_pe"] = proj8(wi_s, (PB[2], PB[3]), SD, IB, tv[j]["xcb"],
                               tv[j].get("ri_war_ac", 0), tv[j].get("ri_war_dv2", 0), sig_i)
        tv[j]["sig_r"] = sig_r[KC - 1]
        tv[j]["sig_i"] = sig_i[KC - 1]

    if DEBUG_TAPS:
        tap("d_l8a", L8[:], dv_wait=v_l8)
    rr = ln_stats_bcast(0, RES[0], PB[6], PB[7], 0, 0, eps=EPS5)
    lnb = normalize_bf16(RES[0], 0, 0, SG[0], SG[1], 0, q_dv=rr["q"], p_ac=rr["p"])
    gateup_pe, silu_last, stt_dv = ffn(RES[0], lnb, (PB[0], PB[1]), (PB[2], PB[3]),
                                       (PB[4], PB[5]))
    tv[0]["gateup_pe"] = gateup_pe
    tv[0]["silu_last"] = silu_last
    tv[0]["stt_dv"] = stt_dv
    rr2 = ln_stats(0, RES[0], PB[6], PB[7], gateup_pe, gateup_pe, res_dv=stt_dv)
    tv[0]["ln2_sv"] = rr2["sv"]
    tv[0]["bc2p"], tv[0]["bc2q"] = bcast_rows(rr2, PB[6], PB[7], war_dv=rr2["rows_dv"])
    tv[0]["ri_war_ac"] = silu_last
    tv[0]["ri_war_dv"] = lnb
    tv[0]["ri_war_dv2"] = stt_dv
    attn_tail(0, RES[0], 0)
    if DEBUG_TAPS:
        tap("d_l8b", L8[:], ac_wait=tv[0]["sig_i"])

    for i in range(NT):
        j = i + 1
        mid = mid_make(i)
        rr2 = None
        if j < NT:
            res_j = RES[j % 2]
            rr = ln_stats_bcast(j, res_j, PB[6], PB[7],
                                tv[i]["ri_pe"], tv[i]["ln2_sv"], eps=EPS5,
                                extra_pe_waits=[("DV", tv[i].get("conv_dv", 0)),
                                                ("AC", tv[i]["sig_i"])])
            lnb = normalize_bf16(res_j, 0, 0, SG[0], SG[1], tv[i]["ri_pe"],
                                 q_dv=rr["q"], p_ac=rr["p"])
            tv[j]["lnb"] = lnb
            gateup_pe, silu_last, stt_dv = ffn(
                res_j, lnb, (PB[0], PB[1]), (PB[2], PB[3]), (PB[4], PB[5]),
                gate_war_ac=state["last_silu"], up_war_ac=tv[i]["sig_i"],
                up_war_dv=tv[i].get("stt2_dv", tv[i].get("stt_dv", 0)), mid=mid)
            tv[j]["gateup_pe"] = gateup_pe
            tv[j]["silu_last"] = silu_last
            tv[j]["stt_dv"] = stt_dv
            if DEBUG_TAPS and j == 1:
                tap("d_x1", RES[1][:], dv_wait=stt_dv)
            rr2 = ln_stats(j, res_j, PB[6], PB[7], gateup_pe, gateup_pe,
                           res_dv=stt_dv)
            tv[j]["ln2_sv"] = rr2["sv"]
        else:
            for fn_ in mid["all"]:
                fn_()

        o_copy = {}
        units = []
        pb45_war = tv[j]["stt_dv"] if j < NT else tv[i - 1]["stt2_dv"]
        for md in range(KC):
            if md % 4 == 0:
                units.append(unit_load(wo_s[:, md * 1024:(md + 4) * 1024], UNIT))
            slot, u = units[-1]
            po = PB[4 + md % 2]
            base = (md % 4) * 1024
            def mm_o(e, po=po, slot=slot, base=base):
                last = None
                for kc in range(KC):
                    last = e.matmul(po[:], slot[:, base + kc * 128: base + kc * 128 + 128],
                                    HATT[:, kc * TT:(kc + 1) * TT],
                                    start=(kc == 0), stop=(kc == KC - 1))
                return last
            ov = sch.emit("tensor", mm_o,
                          waits=[("WL", ring_state["loaded"][u]), ("DV", tv[i]["hatt"]),
                                 ("DV", pb45_war), ("DV", o_copy.get(md - 2, 0))],
                          inc="PE")
            unit_used(u)
            o_copy[md] = sch.emit(
                "vector",
                (lambda e, po=po, md=md: e.tensor_copy(SB[:, md * TT:(md + 1) * TT], po[:])),
                waits=[("PE", ov)], inc="DV")
        tv[i]["outw"] = sch.val("PE")
        if DEBUG_TAPS and i == 1:
            tap("d_hatt", HATT[:], dv_wait=tv[i]["hatt"])
            tap("d_o", SB[:], dv_wait=o_copy[KC - 1])
        if j < NT:
            if DEBUG_TAPS and j == 1:
                tap("d_rstd2", ROWP[0:1, 0:TT], ac_wait=rr2["p"])
                tap("d_q2", SG[0][0:1, :], dv_wait=rr2["q"])
            tv[j]["bc2p"], tv[j]["bc2q"] = bcast_rows(rr2, PB[6], PB[7],
                                                      war_dv=rr2["rows_dv"])
        osq_last = 0
        bfs2_war = tv[j]["ln2_sv"] if j < NT else tv[i]["ln2_sv"]
        for md in range(KC):
            osq_last = sch.emit(
                "scalar",
                (lambda e, md=md: e.activation(
                    BFS[:, T + md * TT:T + (md + 1) * TT],
                    SB[:, md * TT:(md + 1) * TT], AF.Square)),
                waits=[("DV", o_copy[md]), ("PE", bfs2_war)], inc="AC")
        def mm_rms(e):
            last = None
            for md in range(KC):
                last = e.matmul(PB[4][:], ONESB[:, 0:P],
                                BFS[:, T + md * TT:T + (md + 1) * TT],
                                start=(md == 0), stop=(md == KC - 1))
            return last
        v_rs = sch.emit("tensor", mm_rms,
                        waits=[("AC", osq_last), ("DV", o_copy[KC - 1])], inc="PE")
        v_ms = sch.emit("vector",
                        (lambda e: e.tensor_scalar_mul(SG[0][:], PB[4][:], 1.0 / D)),
                        waits=[("PE", v_rs), ("AC", state["last_silu"]),
                               ("AC", state["last_rowp"])], inc="DV")
        sch.emit("scalar",
                 (lambda e: e.activation(SG[0][:], SG[0][:], AF.Ln, bias=EPS6)),
                 waits=[("DV", v_ms)], inc="AC")
        v_rp = sch.emit("scalar",
                        (lambda e: e.activation(SG[0][:], SG[0][:], AF.Exp, scale=CMH)),
                        inc="AC")
        state["last_rowp"] = v_rp
        res_i = RES[i % 2]
        apply_last = 0
        for md in range(KC):
            sch.emit("vector",
                     (lambda e, md=md: e.scalar_tensor_tensor(
                         SB[:, md * TT:(md + 1) * TT], SB[:, md * TT:(md + 1) * TT],
                         NW[:, md:md + 1], SG[0][:, 0:TT], OP.mult, OP.mult)),
                     waits=[("AC", v_rp)], inc="DV")
            apply_last = sch.emit(
                "vector",
                (lambda e, md=md: e.tensor_add(res_i[:, md * TT:(md + 1) * TT],
                                               res_i[:, md * TT:(md + 1) * TT],
                                               SB[:, md * TT:(md + 1) * TT])),
                inc="DV")
        if DEBUG_TAPS and i == 1:
            tap("d_msrow", SG[0][0:1, :], dv_wait=v_ms)
            tap("d_rstd", ROWP[0:1, 0:TT], ac_wait=v_rp)
            tap("d_x2", res_i[:], dv_wait=apply_last)
        bfs1_war = tv[j]["gateup_pe"] if j < NT else tv[i]["ri_pe"]
        rr3 = ln_stats_bcast(i, res_i, PB[2], PB[3], bfs1_war, v_rs,
                             res_dv=apply_last, eps=EPS5)
        lnb3 = normalize_bf16(res_i, 0, 0, SG[0], SG[1], bfs1_war,
                              q_dv=rr3["q"], p_ac=rr3["p"])
        g2_war_ac = tv[j]["silu_last"] if j < NT else tv[i]["sig_i"]
        gateup2_pe, silu2_last, stt2_dv = ffn(
            res_i, lnb3, (PB[0], PB[1]), (PB[2], PB[3]), (PB[4], PB[5]),
            gate_war_ac=g2_war_ac, up_war_ac=0, up_war_dv=rr3["var"], mid=None)
        tv[i]["gateup2_pe"] = gateup2_pe
        tv[i]["stt2_dv"] = stt2_dv
        if j < NT:
            tv[j]["ri_war_ac"] = silu2_last
            tv[j]["ri_war_dv"] = lnb3
            tv[j]["ri_war_dv2"] = stt2_dv
        sch.emit("gpsimd",
                 (lambda e, i=i, res_i=res_i: e.dma_start(
                     out_h[:, :, i * TT:(i + 1) * TT],
                     res_i[:].rearrange("p (c t) -> p c t", c=KC))),
                 waits=[("DV", stt2_dv)], inc="OUT", incn=16)
        if i + 2 < NT:
            x_ready[i + 2] = sch.emit(
                "gpsimd",
                (lambda e, i=i: e.dma_start(
                    RES[i % 2][:].rearrange("p (c t) -> p c t", c=KC),
                    xh[:, :, (i + 2) * TT:(i + 3) * TT])),
                waits=[("OUT", 16 * (i + 1))], inc="X", incn=16)
        if j < NT:
            attn_tail(j, RES[j % 2], stt2_dv)

    if False:
        d_x1 = nc.dram_tensor("d_x1", [P, T], F32, kind="ExternalOutput")
        d_hatt = nc.dram_tensor("d_hatt", [P, T], BF16, kind="ExternalOutput")
        d_r1 = nc.dram_tensor("d_r1", [P, T], F32, kind="ExternalOutput")
        d_i1 = nc.dram_tensor("d_i1", [P, T], F32, kind="ExternalOutput")
        d_xcb1 = nc.dram_tensor("d_xcb1", [P, T], BF16, kind="ExternalOutput")
        sch.emit("gpsimd", (lambda e: e.dma_start(d_x1[:], RES[1][:])),
                 waits=[("DV", tv[1]["stt_dv"])], inc="OUT", incn=0 if False else 16)
        sch.emit("gpsimd", (lambda e: e.dma_start(d_hatt[:], HATT[:])),
                 waits=[("DV", tv[0]["hatt"])], inc="OUT", incn=16)
        sch.emit("gpsimd", (lambda e: e.dma_start(d_r1[:], SC_[:])),
                 waits=[("AC", tv[1]["sig_r"])], inc="OUT", incn=16)
        sch.emit("gpsimd", (lambda e: e.dma_start(d_i1[:], SD[:])),
                 waits=[("AC", tv[1]["sig_i"])], inc="OUT", incn=16)
        sch.emit("gpsimd", (lambda e: e.dma_start(d_xcb1[:], BFS[:, 0:T])),
                 waits=[("DV", tv[1]["xcb"])], inc="OUT", incn=16)
        sch.ops["gpsimd"].append(([("OUT", 16 * NT + 80)], None, None))
    else:
        sch.ops["gpsimd"].append(([("OUT", sch.val("OUT"))], None, None))

    with nc.Block() as blk:
        def runner(lst):
            def body(e):
                for ws, fn, iv in lst:
                    for s, v in ws:
                        e.wait_ge(SEMS[s], v)
                    if fn is None:
                        continue
                    inst = fn(e)
                    if iv is not None:
                        inst.then_inc(SEMS[iv[0]], iv[1])
            return body
        blk.sync(runner(sch.ops["sync"]))
        blk.gpsimd(runner(sch.ops["gpsimd"]))
        blk.tensor(runner(sch.ops["tensor"]))
        blk.vector(runner(sch.ops["vector"]))
        blk.scalar(runner(sch.ops["scalar"]))
    return nc


def _host_layouts(inputs):
    wg = np.asarray(inputs["L0_ffn_w_gate"])
    wu = np.asarray(inputs["L0_ffn_w_up"])
    wd = np.asarray(inputs["L0_ffn_w_down"])
    wr = np.asarray(inputs["L0_attn_Wr_w"], dtype=np.float32)
    wi = np.asarray(inputs["L0_attn_Wi_w"], dtype=np.float32)
    wo = np.asarray(inputs["L0_attn_out_w"], dtype=np.float32)

    def hcmaj(w):
        return np.ascontiguousarray(
            w.reshape(HC, 128, KC, 128).transpose(3, 0, 2, 1).reshape(128, -1))

    def mdmaj(w):
        return np.ascontiguousarray(
            w.reshape(KC, 128, HC, 128).transpose(3, 0, 2, 1).reshape(128, -1))

    def mcmaj(w):
        return np.ascontiguousarray(
            w.reshape(KC, 128, KC, 128).transpose(3, 0, 2, 1).reshape(128, -1))

    def vec(v):
        return np.ascontiguousarray(np.asarray(v, dtype=np.float32).reshape(KC, 128).T)

    cwf = np.asarray(inputs["L0_attn_conv_weight"], dtype=np.float32)[:, 0, :]
    cw = np.ascontiguousarray(
        cwf.reshape(KC, 128, 4).transpose(1, 2, 0).reshape(128, 32))

    common = {
        "wg": hcmaj(wg), "wu": hcmaj(wu), "wd": mdmaj(wd),
        "wr": mcmaj(wr), "wi": mcmaj(wi), "wo": mcmaj(wo),
        "rb": vec(inputs["L0_attn_Wr_b"]), "ib": vec(inputs["L0_attn_Wi_b"]),
        "la": vec(inputs["L0_attn_log_a"]), "nw": vec(inputs["L0_attn_norm_w"]),
        "cw": cw, "cb": vec(inputs["L0_attn_conv_bias"]),
    }
    x = np.asarray(inputs["L0_attn_x"], dtype=np.float32)
    maps = []
    for b in range(8):
        xb = np.ascontiguousarray(
            x[b].T.reshape(KC, 128, T).transpose(1, 0, 2))
        m = dict(common)
        m["x"] = xb
        maps.append(m)
    return maps


def kernel(**inputs):
    in_maps = _host_layouts(inputs)
    nc = build_nc()
    res = run_bass_kernel_spmd(nc, in_maps, core_ids=list(range(8)))
    outs = []
    for b in range(8):
        o = np.asarray(res.results[b]["out"])
        outs.append(o.transpose(1, 0, 2).reshape(D, T).T)
    return np.stack(outs).astype(np.float32)


# version 08
# speedup vs baseline: 1.1036x; 1.0003x over previous
import numpy as np
import concourse.bass as bass
import concourse.mybir as mybir
from concourse.bass_utils import run_bass_kernel_spmd

F32 = mybir.dt.float32
BF16 = mybir.dt.bfloat16
I32 = mybir.dt.int32
AF = mybir.ActivationFunctionType
OP = mybir.AluOpType

D = 1024
KC = 8
H = 3072
HC = 24
T = 4096
TT = 512
NT = T // TT
P = 128
RING = 6
UNIT = 4096

S127 = 1.0 / 127.0
DOWN_SCALE = 0.5 * S127 * S127
LOG1EM7 = float(np.float32(np.log(np.float32(1e-7))))
DEBUG_TAPS = False
INTERLEAVE = True


class Sched:
    def __init__(self):
        self.ops = {"sync": [], "gpsimd": [], "tensor": [], "vector": [], "scalar": []}
        self.cnt = {}
        self.marks = {}

    def val(self, sem):
        return self.cnt.get(sem, 0)

    def emit(self, eng, fn, waits=(), inc=None, incn=1):
        ws = []
        for s, v in waits:
            if not v or v <= 0:
                continue
            key = (eng, s)
            if self.marks.get(key, 0) >= v:
                continue
            self.marks[key] = v
            ws.append((s, v))
        iv = None
        if inc is not None:
            self.cnt[inc] = self.cnt.get(inc, 0) + incn
            iv = (inc, incn)
        self.ops[eng].append((ws, fn, iv))
        return self.cnt.get(inc, 0) if inc else None


def build_nc():
    nc = bass.Bass("TRN2", target_bir_lowering=False)

    xh = nc.dram_tensor("x", [P, KC, T], F32, kind="ExternalInput")
    wg_i = nc.dram_tensor("wg", [P, HC * 8 * 128], I32, kind="ExternalInput")
    wu_i = nc.dram_tensor("wu", [P, HC * 8 * 128], I32, kind="ExternalInput")
    wd_i = nc.dram_tensor("wd", [P, 8 * HC * 128], I32, kind="ExternalInput")
    wr_i = nc.dram_tensor("wr", [P, KC * 1024], F32, kind="ExternalInput")
    wi_i = nc.dram_tensor("wi", [P, KC * 1024], F32, kind="ExternalInput")
    wo_i = nc.dram_tensor("wo", [P, KC * 1024], F32, kind="ExternalInput")
    rb_i = nc.dram_tensor("rb", [P, KC], F32, kind="ExternalInput")
    ib_i = nc.dram_tensor("ib", [P, KC], F32, kind="ExternalInput")
    la_i = nc.dram_tensor("la", [P, KC], F32, kind="ExternalInput")
    nw_i = nc.dram_tensor("nw", [P, KC], F32, kind="ExternalInput")
    cw_i = nc.dram_tensor("cw", [P, 32], F32, kind="ExternalInput")
    cb_i = nc.dram_tensor("cb", [P, KC], F32, kind="ExternalInput")
    out_h = nc.dram_tensor("out", [P, KC, T], F32, kind="ExternalOutput")

    wg_s = nc.dram_tensor("wg_s", [P, HC * 8 * 128], BF16, kind="Internal")
    wu_s = nc.dram_tensor("wu_s", [P, HC * 8 * 128], BF16, kind="Internal")
    wd_s = nc.dram_tensor("wd_s", [P, 8 * HC * 128], BF16, kind="Internal")
    wr_s = nc.dram_tensor("wr_s", [P, KC * 1024], BF16, kind="Internal")
    wi_s = nc.dram_tensor("wi_s", [P, KC * 1024], BF16, kind="Internal")
    wo_s = nc.dram_tensor("wo_s", [P, KC * 1024], BF16, kind="Internal")

    import contextlib
    ctx = contextlib.ExitStack()
    nc._kernel_ctx = ctx
    sb = lambda n, s, d=F32: ctx.enter_context(nc.sbuf_tensor(n, s, d))

    RES = [sb("RES0", [P, T]), sb("RES1", [P, T])]
    SA = sb("SA", [P, T])
    SB = sb("SB", [P, T])
    SC_ = sb("SC", [P, T])
    SD = sb("SD", [P, T])
    BFS = sb("BFS", [P, 2 * T], BF16)
    HATT = sb("HATT", [P, T], BF16)
    HH = sb("HH", [P, HC * TT], BF16)
    LTMP = sb("LTMP", [P, TT + 8])
    ROWP = LTMP
    TAIL = sb("TAIL", [P, KC * 4])
    WRG = [sb(f"WRG{j}", [P, UNIT], BF16) for j in range(RING)]
    SG = [sb("SG0", [P, TT]), sb("SG1", [P, TT])]
    ONES = sb("ONESF", [P, TT])
    ONESR = ONES
    ONESB = sb("ONESB", [P, P], BF16)
    CONSTS = sb("CONSTS", [P, 8])
    CW = sb("CWt", [P, 32])
    CB = sb("CBt", [P, KC])
    RB = sb("RBt", [P, KC])
    IB = sb("IBt", [P, KC])
    LAT = sb("LAt", [P, KC])
    NW = sb("NWt", [P, KC])
    L8 = sb("L8", [P, KC])
    L16 = sb("L16", [P, KC])
    SCAR = sb("SCAR", [P, KC])
    CCAR = sb("CCAR", [P, KC])
    RED = sb("REDt", [P, KC])
    PB = [ctx.enter_context(nc.psum_tensor(f"PB{j}", [P, TT], F32)) for j in range(8)]

    cvals = [S127, -1.0, -0.5, 1e-5, 1e-6]
    for j_, v in enumerate(cvals):
        nc.const_aps.register(F32, float(np.float32(v)), CONSTS[:, j_:j_ + 1])
    C127, CM1, CMH, EPS5, EPS6 = [float(np.float32(v)) for v in cvals]

    SEMS = {}
    for n in ["PE", "DV", "AC", "WL", "X", "OUT", "CV"]:
        SEMS[n] = ctx.enter_context(nc.semaphore("sem_" + n))

    sch = Sched()
    ring_state = {"u": 0, "done": {}, "loaded": {}}

    def unit_load(src_ap, ncols):
        u = ring_state["u"]
        ring_state["u"] += 1
        slot = WRG[u % RING]
        waits = [("CV", src_cast_done[src_ap.tensor.name])]
        prev = u - RING
        if prev >= 0:
            waits.append(("PE", ring_state["done"].get(prev, 0)))
        def fn(e, slot=slot, src=src_ap, ncols=ncols):
            return e.dma_start(slot[:, 0:ncols], src)
        v = sch.emit("sync", fn, waits=waits, inc="WL", incn=16)
        ring_state["loaded"][u] = v
        return slot, u

    def unit_used(u):
        ring_state["done"][u] = sch.val("PE")

    src_cast_done = {}
    def cast_dma(dst, src):
        def fn(e, dst=dst, src=src):
            return e.dma_start(dst[:], src[:])
        v = sch.emit("gpsimd", fn, inc="CV", incn=16)
        src_cast_done[dst.name] = v

    cast_dma(wg_s, wg_i)
    cast_dma(wu_s, wu_i)
    cast_dma(wd_s, wd_i)
    cast_dma(wr_s, wr_i)
    cast_dma(wi_s, wi_i)
    cast_dma(wo_s, wo_i)

    for dst, src in [(CW, cw_i), (CB, cb_i), (RB, rb_i), (IB, ib_i),
                     (LAT, la_i), (NW, nw_i)]:
        sch.emit("sync", (lambda e, d=dst, s=src: e.dma_start(d[:], s[:])),
                 inc="X", incn=16)
    x_ready = {}
    for i in range(2):
        x_ready[i] = sch.emit(
            "gpsimd",
            (lambda e, i=i: e.dma_start(RES[i % 2][:].rearrange("p (c t) -> p c t", c=KC),
                                        xh[:, :, i * TT:(i + 1) * TT])),
            inc="X", incn=16)

    def ms(ap, v):
        sch.emit("gpsimd", (lambda e, ap=ap, v=v: e.memset(ap, v)))
    ms(ONES[:], 1.0)
    ms(ONESB[:], 1.0)
    ms(TAIL[:], 0.0)
    ms(SCAR[:], 0.0)
    ms(CCAR[:], 0.0)
    for j_, v in enumerate(cvals):
        ms(CONSTS[:, j_:j_ + 1], float(np.float32(v)))
    setup_done = sch.emit("gpsimd", (lambda e: e.memset(CONSTS[:, 7:8], 0.0)),
                          inc="X", incn=1)

    sch.emit("scalar", (lambda e: e.activation(L8[:], LAT[:], AF.Sigmoid)),
             waits=[("X", setup_done)], inc="AC")
    v_ln = sch.emit("scalar", (lambda e: e.activation(L8[:], L8[:], AF.Ln)), inc="AC")
    sch.emit("vector",
             (lambda e: e.tensor_scalar(L16[:], L8[:], 16.0, -1e30, OP.mult, OP.max)),
             waits=[("AC", v_ln)], inc="DV")
    v_l8 = sch.emit("vector",
                    (lambda e: e.tensor_scalar(L8[:], L8[:], 8.0, -1e30, OP.mult, OP.max)),
                    inc="DV")

    tv = [dict() for _ in range(NT + 1)]
    state = {"scar_v": 0, "ccar_v": 0, "last_silu": 0, "last_rowp": 0}

    dbg = {}
    if DEBUG_TAPS:
        dbg["d_scar"] = nc.dram_tensor("d_scar", [P, KC], F32, kind="ExternalOutput")
        for nm in ["d_l8a", "d_l8b", "d_l8c", "d_l8d"]:
            dbg[nm] = nc.dram_tensor(nm, [P, KC], F32, kind="ExternalOutput")
        dbg["d_r1pre"] = nc.dram_tensor("d_r1pre", [P, T], F32, kind="ExternalOutput")
        dbg["d_xc1"] = nc.dram_tensor("d_xc1", [P, T], F32, kind="ExternalOutput")
        dbg["d_xcb1"] = nc.dram_tensor("d_xcb1", [P, T], BF16, kind="ExternalOutput")
        dbg["d_rstd2"] = nc.dram_tensor("d_rstd2", [1, TT], F32, kind="ExternalOutput")
        dbg["d_q2"] = nc.dram_tensor("d_q2", [1, TT], F32, kind="ExternalOutput")
        for nm, dt_ in [("d_x1", F32), ("d_o", F32), ("d_x2", F32), ("d_hatt", BF16),
                        ("d_logA", F32), ("d_aexp", F32), ("d_cs", F32), ("d_term", F32),
                        ("d_la", F32), ("d_s", F32),
                        ("d_msrow", F32), ("d_rstd", F32)]:
            dbg[nm] = nc.dram_tensor(nm, [P, T] if nm not in ("d_msrow", "d_rstd")
                                     else [1, TT], dt_, kind="ExternalOutput")

    def tap(name, src_ap, dv_wait=None, ac_wait=None):
        if not DEBUG_TAPS:
            return
        w = []
        if dv_wait: w.append(("DV", dv_wait))
        if ac_wait: w.append(("AC", ac_wait))
        v = sch.emit("gpsimd", (lambda e, s=src_ap, d=dbg[name]: e.dma_start(d[:], s)),
                     waits=w, inc="OUT", incn=16)
        sch.emit("vector", (lambda e: e.engine_nop()), waits=[("OUT", v)], inc="DV")
        state["dbg_out"] = v

    def ln_stats(i, res, pb_m, pb_v, bfs1_war_pe, bfs2_war_pe, res_dv=0,
                 extra_pe_waits=()):
        r = {}
        r["xb"] = sch.emit("vector", (lambda e: e.tensor_copy(BFS[:, 0:T], res[:])),
                           waits=[("PE", bfs1_war_pe), ("X", x_ready.get(i, 0)),
                                  ("X", setup_done), ("DV", res_dv)], inc="DV")
        r["xsq"] = sch.emit("scalar",
                            (lambda e: e.activation(BFS[:, T:2 * T], res[:], AF.Square)),
                            waits=[("PE", bfs2_war_pe), ("X", x_ready.get(i, 0)),
                                   ("X", setup_done), ("DV", res_dv)], inc="AC")

        def mm_stats(e, pb, off):
            last = None
            for kc in range(KC):
                last = e.matmul(pb[0:1, :], ONESB[:, 0:1],
                                BFS[:, off + kc * TT: off + (kc + 1) * TT],
                                start=(kc == 0), stop=(kc == KC - 1))
            return last
        r["sm"] = sch.emit("tensor", (lambda e: mm_stats(e, pb_m, 0)),
                           waits=[("DV", r["xb"]), ("X", setup_done)] + list(extra_pe_waits),
                           inc="PE")
        r["sv"] = sch.emit("tensor", (lambda e: mm_stats(e, pb_v, T)),
                           waits=[("AC", r["xsq"])], inc="PE")
        sch.emit("vector",
                 (lambda e: e.tensor_scalar_mul(SG[0][0:1, :], pb_m[0:1, :], 1.0 / D)),
                 waits=[("PE", r["sm"]), ("AC", state["last_silu"]),
                        ("AC", state["last_rowp"])], inc="DV")
        sch.emit("vector",
                 (lambda e: e.tensor_mul(ROWP[0:1, 0:TT], SG[0][0:1, :], SG[0][0:1, :])),
                 inc="DV")
        r["var"] = sch.emit("vector",
                            (lambda e: e.scalar_tensor_tensor(
                                SG[1][0:1, :], pb_v[0:1, :], 1.0 / D,
                                ROWP[0:1, 0:TT], OP.mult, OP.subtract)),
                            waits=[("PE", r["sv"])], inc="DV")
        sch.emit("scalar",
                 (lambda e: e.activation(SG[1][0:1, :], SG[1][0:1, :], AF.Ln, bias=EPS5)),
                 waits=[("DV", r["var"])], inc="AC")
        r["p"] = sch.emit("scalar",
                          (lambda e: e.activation(ROWP[0:1, 0:TT], SG[1][0:1, :],
                                                  AF.Exp, scale=CMH)),
                          inc="AC")
        state["last_rowp"] = r["p"]
        r["q"] = sch.emit("vector",
                          (lambda e: e.scalar_tensor_tensor(SG[0][0:1, :], SG[0][0:1, :],
                                                            -1.0, ROWP[0:1, 0:TT],
                                                            OP.mult, OP.mult)),
                          waits=[("AC", r["p"])], inc="DV")
        r["rows_dv"] = sch.val("DV")
        return r

    def bcast_rows(rr, pb_p, pb_q, war_ac=0, war_dv=0):
        v_bp = sch.emit("tensor",
                        (lambda e: e.matmul(pb_p[:], ONESR[0:1, 0:P], ROWP[0:1, 0:TT],
                                            start=True, stop=True)),
                        waits=[("AC", rr["p"]), ("AC", war_ac), ("DV", war_dv)], inc="PE")
        v_bq = sch.emit("tensor",
                        (lambda e: e.matmul(pb_q[:], ONESR[0:1, 0:P], SG[0][0:1, :],
                                            start=True, stop=True)),
                        waits=[("DV", rr["q"])], inc="PE")
        return v_bp, v_bq

    def ln_stats_bcast(i, res, pb_m, pb_v, bfs1_war_pe, bfs2_war_pe, res_dv=0,
                       eps=None, extra_pe_waits=()):
        r = {}
        r["xb"] = sch.emit("vector", (lambda e: e.tensor_copy(BFS[:, 0:T], res[:])),
                           waits=[("PE", bfs1_war_pe), ("X", x_ready.get(i, 0)),
                                  ("X", setup_done), ("DV", res_dv)], inc="DV")
        r["xsq"] = sch.emit("scalar",
                            (lambda e: e.activation(BFS[:, T:2 * T], res[:], AF.Square)),
                            waits=[("PE", bfs2_war_pe), ("X", x_ready.get(i, 0)),
                                   ("X", setup_done), ("DV", res_dv)], inc="AC")
        def mm_stats(e, pb, off):
            last = None
            for kc in range(KC):
                last = e.matmul(pb[:], ONESB[:, 0:P],
                                BFS[:, off + kc * TT: off + (kc + 1) * TT],
                                start=(kc == 0), stop=(kc == KC - 1))
            return last
        r["sm"] = sch.emit("tensor", (lambda e: mm_stats(e, pb_m, 0)),
                           waits=[("DV", r["xb"]), ("X", setup_done)] + list(extra_pe_waits),
                           inc="PE")
        r["sv"] = sch.emit("tensor", (lambda e: mm_stats(e, pb_v, T)),
                           waits=[("AC", r["xsq"])], inc="PE")
        sch.emit("vector",
                 (lambda e: e.tensor_scalar_mul(SG[1][:], pb_m[:], 1.0 / D)),
                 waits=[("PE", r["sm"]), ("AC", state["last_silu"]),
                        ("AC", state["last_rowp"])], inc="DV")
        sch.emit("vector", (lambda e: e.tensor_mul(SG[0][:], SG[1][:], SG[1][:])),
                 inc="DV")
        r["var"] = sch.emit("vector",
                            (lambda e: e.scalar_tensor_tensor(
                                SG[0][:], pb_v[:], 1.0 / D, SG[0][:],
                                OP.mult, OP.subtract)),
                            waits=[("PE", r["sv"])], inc="DV")
        sch.emit("scalar",
                 (lambda e: e.activation(SG[0][:], SG[0][:], AF.Ln, bias=eps)),
                 waits=[("DV", r["var"])], inc="AC")
        r["p"] = sch.emit("scalar",
                          (lambda e: e.activation(SG[0][:], SG[0][:], AF.Exp, scale=CMH)),
                          inc="AC")
        state["last_rowp"] = r["p"]
        r["q"] = sch.emit("vector",
                          (lambda e: e.scalar_tensor_tensor(SG[1][:], SG[1][:], -1.0,
                                                            SG[0][:], OP.mult, OP.mult)),
                          waits=[("AC", r["p"])], inc="DV")
        return r

    def ln2_stats_psum(j, res, bfs_war_pe, res_dv):
        r = {}
        r["xb"] = sch.emit("vector", (lambda e: e.tensor_copy(BFS[:, 0:T], res[:])),
                           waits=[("PE", bfs_war_pe), ("DV", res_dv)], inc="DV")
        r["xsq"] = sch.emit("scalar",
                            (lambda e: e.activation(BFS[:, T:2 * T], res[:], AF.Square)),
                            waits=[("PE", bfs_war_pe), ("DV", res_dv)], inc="AC")
        def mm_stats(e, pb, off):
            last = None
            for kc in range(KC):
                last = e.matmul(pb[:], ONESB[:, 0:P],
                                BFS[:, off + kc * TT: off + (kc + 1) * TT],
                                start=(kc == 0), stop=(kc == KC - 1))
            return last
        r["sm"] = sch.emit("tensor", (lambda e: mm_stats(e, PB[6], 0)),
                           waits=[("DV", r["xb"])], inc="PE")
        r["sv"] = sch.emit("tensor", (lambda e: mm_stats(e, PB[7], T)),
                           waits=[("AC", r["xsq"])], inc="PE")
        sch.emit("vector",
                 (lambda e: e.tensor_scalar_mul(SG[1][:], PB[6][:], 1.0 / D)),
                 waits=[("PE", r["sm"]), ("AC", state["last_silu"]),
                        ("AC", state["last_rowp"])], inc="DV")
        sch.emit("vector", (lambda e: e.tensor_mul(SG[0][:], SG[1][:], SG[1][:])),
                 inc="DV")
        r["var"] = sch.emit("vector",
                            (lambda e: e.scalar_tensor_tensor(
                                SG[0][:], PB[7][:], 1.0 / D, SG[0][:],
                                OP.mult, OP.subtract)),
                            waits=[("PE", r["sv"])], inc="DV")
        sch.emit("scalar",
                 (lambda e: e.activation(SG[0][:], SG[0][:], AF.Ln, bias=EPS5)),
                 waits=[("DV", r["var"])], inc="AC")
        r["bc2p"] = sch.emit("scalar",
                             (lambda e: e.activation(PB[6][:], SG[0][:], AF.Exp,
                                                     scale=CMH)),
                             inc="AC")
        state["last_rowp"] = r["bc2p"]
        r["bc2q"] = sch.emit("vector",
                             (lambda e: e.scalar_tensor_tensor(PB[7][:], SG[1][:], -1.0,
                                                               PB[6][:], OP.mult,
                                                               OP.mult)),
                             waits=[("AC", r["bc2p"])], inc="DV")
        tv[j]["bc2p"] = r["bc2p"]
        tv[j]["bc2q"] = r["bc2q"]
        return r

    def normalize_bf16(res, v_bp, v_bq, pb_p, pb_q, bfs1_war_pe, q_dv=0, p_ac=0):
        last = 0
        pp = pb_p[:, 0:TT] if pb_p in (SG[0], SG[1]) else pb_p[:]
        qq = pb_q[:, 0:TT] if pb_q in (SG[0], SG[1]) else pb_q[:]
        for c in range(KC):
            sch.emit("vector",
                     (lambda e, c=c: e.tensor_mul(LTMP[:, 0:TT],
                                                  res[:, c * TT:(c + 1) * TT], pp)),
                     waits=[("PE", v_bp), ("PE", v_bq), ("PE", bfs1_war_pe),
                            ("DV", q_dv), ("AC", p_ac)], inc="DV")
            last = sch.emit("vector",
                            (lambda e, c=c: e.tensor_add(BFS[:, c * TT:(c + 1) * TT],
                                                         LTMP[:, 0:TT], qq)),
                            inc="DV")
        return last

    def mid_make(i):
        g = {}
        def g_la_ixc():
            if DEBUG_TAPS and i == 1:
                tap("d_l8c", L8[:], ac_wait=tv[i]["sig_r"])
                tap("d_r1pre", SC_[:], ac_wait=tv[i]["sig_r"])
            for c in range(KC):
                sch.emit("vector",
                         (lambda e, c=c: e.tensor_scalar(
                             SA[:, c * TT:(c + 1) * TT], SC_[:, c * TT:(c + 1) * TT],
                             L8[:, c:c + 1], LOG1EM7, OP.mult, OP.max)),
                         waits=[("AC", tv[i]["sig_r"]), ("DV", v_l8)], inc="DV")
            for c in range(KC):
                tv[i]["ixc"] = sch.emit(
                    "vector",
                    (lambda e, c=c: e.tensor_mul(
                        SD[:, c * TT:(c + 1) * TT], SD[:, c * TT:(c + 1) * TT],
                        SB[:, c * TT:(c + 1) * TT])),
                    waits=[("AC", tv[i]["sig_i"])], inc="DV")
        def a_a2():
            for c in range(KC):
                tv[i]["a2"] = sch.emit(
                    "scalar",
                    (lambda e, c=c: e.activation(
                        SB[:, c * TT:(c + 1) * TT], SC_[:, c * TT:(c + 1) * TT],
                        AF.Exp, scale=L16[:, c:c + 1])),
                    waits=[("DV", tv[i]["ixc"])], inc="AC")
        def g_v():
            sch.emit("vector",
                     (lambda e: e.tensor_scalar(SC_[:], SB[:], 1.0, -1.0,
                                                OP.subtract, OP.mult)),
                     waits=[("AC", tv[i]["a2"])], inc="DV")
            tv[i]["vmax"] = sch.emit(
                "vector", (lambda e: e.tensor_scalar_max(SC_[:], SC_[:], 1e-6)), inc="DV")
        def a_bamp():
            tv[i]["bamp"] = sch.emit(
                "scalar", (lambda e: e.activation(SB[:], SC_[:], AF.Sqrt)),
                waits=[("DV", tv[i]["vmax"])], inc="AC")
        def g_bt():
            tv[i]["bt"] = sch.emit(
                "vector", (lambda e: e.tensor_mul(SD[:], SD[:], SB[:])),
                waits=[("AC", tv[i]["bamp"])], inc="DV")
        def g_ss(c0, c1):
            def f():
                for c in range(c0, c1):
                    sch.emit("vector",
                             (lambda e, c=c: e.tensor_tensor_scan(
                                 SB[:, c * TT:(c + 1) * TT], ONES[:],
                                 SA[:, c * TT:(c + 1) * TT], SCAR[:, c:c + 1],
                                 OP.mult, OP.add)),
                             waits=[("DV", state["scar_v"]), ("DV", tv[i]["bt"])],
                             inc="DV")
            return f
        def g_sred():
            for c in range(KC):
                sch.emit("vector",
                         (lambda e, c=c: e.tensor_reduce(
                             RED[:, c:c + 1], SA[:, c * TT:(c + 1) * TT],
                             mybir.AxisListType.X, OP.add)),
                         inc="DV")
            sch.emit("vector", (lambda e: e.engine_nop()), inc="DV")
            sch.emit("vector", (lambda e: e.engine_nop()), inc="DV")
            state["scar_v"] = sch.emit(
                "vector", (lambda e: e.tensor_add(SCAR[:], SCAR[:], RED[:])), inc="DV")
        def g_logA():
            tv[i]["logA"] = sch.emit(
                "vector", (lambda e: e.tensor_scalar_max(SB[:], SB[:], -80.0)), inc="DV")
            if DEBUG_TAPS and i == 1:
                tap("d_logA", SB[:], dv_wait=tv[i]["logA"])
        def a_en():
            tv[i]["en"] = sch.emit(
                "scalar", (lambda e: e.activation(SA[:], SB[:], AF.Exp, scale=CM1)),
                waits=[("DV", tv[i]["logA"])], inc="AC")
        def g_term():
            tv[i]["term"] = sch.emit(
                "vector", (lambda e: e.tensor_mul(SD[:], SD[:], SA[:])),
                waits=[("AC", tv[i]["en"])], inc="DV")
            if DEBUG_TAPS and i == 1:
                tap("d_term", SD[:], dv_wait=tv[i]["term"])
        def a_aexp():
            tv[i]["aexp"] = sch.emit(
                "scalar", (lambda e: e.activation(SC_[:], SB[:], AF.Exp)),
                waits=[("DV", tv[i]["term"])], inc="AC")
        def g_cs(c0, c1):
            def f():
                for c in range(c0, c1):
                    sch.emit("vector",
                             (lambda e, c=c: e.tensor_tensor_scan(
                                 SA[:, c * TT:(c + 1) * TT], ONES[:],
                                 SD[:, c * TT:(c + 1) * TT], CCAR[:, c:c + 1],
                                 OP.mult, OP.add)),
                             waits=[("DV", state["ccar_v"]), ("DV", tv[i]["term"])],
                             inc="DV")
            return f
        def g_cred():
            for c in range(KC):
                sch.emit("vector",
                         (lambda e, c=c: e.tensor_reduce(
                             RED[:, c:c + 1], SD[:, c * TT:(c + 1) * TT],
                             mybir.AxisListType.X, OP.add)),
                         inc="DV")
            sch.emit("vector", (lambda e: e.engine_nop()), inc="DV")
            sch.emit("vector", (lambda e: e.engine_nop()), inc="DV")
            state["ccar_v"] = sch.emit(
                "vector", (lambda e: e.tensor_add(CCAR[:], CCAR[:], RED[:])), inc="DV")
        def g_hatt():
            if DEBUG_TAPS and i == 1:
                tap("d_aexp", SC_[:], ac_wait=tv[i]["aexp"])
                tap("d_cs", SA[:], dv_wait=state["ccar_v"])
            tv[i]["hatt"] = sch.emit(
                "vector", (lambda e: e.tensor_mul(HATT[:], SC_[:], SA[:])),
                waits=[("AC", tv[i]["aexp"]),
                       ("PE", tv[i - 1].get("outw", 0) if i > 0 else 0)],
                inc="DV")
        g["dve"] = {2: [g_la_ixc], 4: [g_v], 6: [g_bt], 8: [g_ss(0, 4)],
                    10: [g_ss(4, 8)], 12: [g_sred, g_logA], 14: [g_term],
                    16: [g_cs(0, 4)], 18: [g_cs(4, 8)], 20: [g_cred], 21: [g_hatt]}
        g["act"] = {2: [a_a2], 4: [a_bamp], 13: [a_en], 17: [a_aexp]}
        g["all"] = [g_la_ixc, a_a2, g_v, a_bamp, g_bt, g_ss(0, 8), g_sred, g_logA,
                    a_en, g_term, a_aexp, g_cs(0, 8), g_cred, g_hatt]
        return g

    def ffn(res, lnb_ready_dv, pg_pair, pu_pair, pd_pair,
            gate_war_ac=0, up_war_ac=0, up_war_dv=0, mid=None):
        silu_done = {}
        h_done = {}
        gslot = uslot = None
        gu = uu = None
        for hc in range(HC):
            if mid is not None and INTERLEAVE:
                for fn_ in mid["dve"].get(hc, []):
                    fn_()
                for fn_ in mid["act"].get(hc, []):
                    fn_()
            if hc % 4 == 0:
                gslot, gu = unit_load(wg_s[:, hc * 1024:(hc + 4) * 1024], UNIT)
                uslot, uu = unit_load(wu_s[:, hc * 1024:(hc + 4) * 1024], UNIT)
            pg = pg_pair[hc % 2]
            pu = pu_pair[hc % 2]
            base = (hc % 4) * 1024
            def mm_gate(e, pg=pg, gslot=gslot, base=base):
                last = None
                for kc in range(KC):
                    last = e.matmul(pg[:], gslot[:, base + kc * 128: base + kc * 128 + 128],
                                    BFS[:, kc * TT:(kc + 1) * TT],
                                    start=(kc == 0), stop=(kc == KC - 1))
                return last
            gv = sch.emit("tensor", mm_gate,
                          waits=[("WL", ring_state["loaded"][gu]), ("DV", lnb_ready_dv),
                                 ("AC", silu_done.get(hc - 2, gate_war_ac))], inc="PE")
            unit_used(gu)
            def mm_up(e, pu=pu, uslot=uslot, base=base):
                last = None
                for kc in range(KC):
                    last = e.matmul(pu[:], uslot[:, base + kc * 128: base + kc * 128 + 128],
                                    BFS[:, kc * TT:(kc + 1) * TT],
                                    start=(kc == 0), stop=(kc == KC - 1))
                return last
            uv = sch.emit("tensor", mm_up,
                          waits=[("WL", ring_state["loaded"][uu]),
                                 ("DV", h_done.get(hc - 2, up_war_dv)),
                                 ("AC", up_war_ac if hc < 2 else 0)], inc="PE")
            unit_used(uu)
            sg = SG[hc % 2]
            sv = sch.emit("scalar",
                          (lambda e, sg=sg, pg=pg: e.activation(sg[:], pg[:], AF.Silu,
                                                                scale=C127)),
                          waits=[("PE", gv), ("DV", h_done.get(hc - 2, 0))], inc="AC")
            silu_done[hc] = sv
            state["last_silu"] = sv
            h_done[hc] = sch.emit(
                "vector",
                (lambda e, sg=sg, pu=pu, hc=hc:
                 e.tensor_mul(HH[:, hc * TT:(hc + 1) * TT], sg[:], pu[:])),
                waits=[("AC", sv), ("PE", uv)], inc="DV")
        gateup_pe = sch.val("PE")
        silu_last = silu_done[HC - 1]
        if mid is not None and not INTERLEAVE:
            for fn_ in mid["all"]:
                fn_()
        stt_done = {}
        for md in range(KC):
            dslot, du = unit_load(wd_s[:, md * HC * 128:(md + 1) * HC * 128], HC * 128)
            pd = pd_pair[md % 2]
            def mm_down(e, pd=pd, dslot=dslot):
                last = None
                for hc in range(HC):
                    last = e.matmul(pd[:], dslot[:, hc * 128: hc * 128 + 128],
                                    HH[:, hc * TT:(hc + 1) * TT],
                                    start=(hc == 0), stop=(hc == HC - 1))
                return last
            dv_ = sch.emit("tensor", mm_down,
                           waits=[("WL", ring_state["loaded"][du]), ("DV", h_done[HC - 1]),
                                  ("DV", stt_done.get(md - 2, 0))], inc="PE")
            unit_used(du)
            stt_done[md] = sch.emit(
                "vector",
                (lambda e, pd=pd, md=md: e.scalar_tensor_tensor(
                    res[:, md * TT:(md + 1) * TT], pd[:], DOWN_SCALE,
                    res[:, md * TT:(md + 1) * TT], OP.mult, OP.add)),
                waits=[("PE", dv_)], inc="DV")
        return gateup_pe, silu_last, sch.val("DV")

    def proj8(wsrc, pp_pair, dst, bias, rhs_dv, war_ac, war_dv, sig_store):
        units = []
        last_pe = 0
        for mc in range(KC):
            if mc % 4 == 0:
                units.append(unit_load(wsrc[:, mc * 1024:(mc + 4) * 1024], UNIT))
            slot, u = units[-1]
            pr = pp_pair[mc % 2]
            base = (mc % 4) * 1024
            def mm(e, pr=pr, slot=slot, base=base):
                last = None
                for kc in range(KC):
                    last = e.matmul(pr[:], slot[:, base + kc * 128: base + kc * 128 + 128],
                                    HATT[:, kc * TT:(kc + 1) * TT],
                                    start=(kc == 0), stop=(kc == KC - 1))
                return last
            rv = sch.emit("tensor", mm,
                          waits=[("WL", ring_state["loaded"][u]), ("DV", rhs_dv),
                                 ("AC", sig_store.get(mc - 2, war_ac)),
                                 ("DV", war_dv if mc < 2 else 0)], inc="PE")
            last_pe = rv
            unit_used(u)
            sig_store[mc] = sch.emit(
                "scalar",
                (lambda e, pr=pr, mc=mc: e.activation(
                    dst[:, mc * TT:(mc + 1) * TT], pr[:], AF.Sigmoid,
                    bias=bias[:, mc:mc + 1])),
                waits=[("PE", rv)], inc="AC")
        return last_pe

    def attn_tail(j, res_j, conv_war_dv):
        for c in range(KC):
            sch.emit("vector",
                     (lambda e, c=c: e.tensor_mul(SD[:, 0:TT],
                                                  res_j[:, c * TT:(c + 1) * TT], PB[6][:])),
                     waits=[("AC", tv[j]["bc2p"]), ("DV", tv[j]["bc2q"]),
                            ("DV", conv_war_dv)], inc="DV")
            sch.emit("vector",
                     (lambda e: e.tensor_add(LTMP[:, 3:3 + TT], SD[:, 0:TT], PB[7][:])),
                     inc="DV")
            sch.emit("vector",
                     (lambda e, c=c: e.tensor_copy(LTMP[:, 0:3], TAIL[:, c * 4:c * 4 + 3])),
                     inc="DV")
            sch.emit("vector",
                     (lambda e, c=c: e.tensor_copy(TAIL[:, c * 4:c * 4 + 3],
                                                   LTMP[:, TT:TT + 3])),
                     inc="DV")
            sch.emit("vector",
                     (lambda e, c=c: e.tensor_scalar(SB[:, c * TT:(c + 1) * TT],
                                                     LTMP[:, 0:TT], CW[:, c:c + 1],
                                                     CB[:, c:c + 1], OP.mult, OP.add)),
                     inc="DV")
            for k in range(1, 4):
                sch.emit("vector",
                         (lambda e, c=c, k=k: e.scalar_tensor_tensor(
                             SB[:, c * TT:(c + 1) * TT], LTMP[:, k:k + TT],
                             CW[:, k * 8 + c:k * 8 + c + 1],
                             SB[:, c * TT:(c + 1) * TT], OP.mult, OP.add)),
                         inc="DV")
        tv[j]["conv_dv"] = sch.val("DV")
        if DEBUG_TAPS and j == 1:
            tap("d_xc1", SB[:], dv_wait=tv[j]["conv_dv"])
        tv[j]["xcb"] = sch.emit("vector", (lambda e: e.tensor_copy(HATT[:], SB[:])),
                                waits=[("PE", tv[j]["hatt_war"])], inc="DV")
        if DEBUG_TAPS and j == 1:
            tap("d_xcb1", BFS[:, 0:T], dv_wait=tv[j]["xcb"])
        sig_r = {}
        sig_i = {}
        proj8(wr_s, (PB[0], PB[1]), SC_, RB, tv[j]["xcb"],
              tv[j].get("ri_war_ac", 0), tv[j].get("ri_war_dv", 0), sig_r)
        tv[j]["ri_pe"] = proj8(wi_s, (PB[2], PB[3]), SD, IB, tv[j]["xcb"],
                               tv[j].get("ri_war_ac", 0), tv[j].get("ri_war_dv2", 0), sig_i)
        tv[j]["sig_r"] = sig_r[KC - 1]
        tv[j]["sig_i"] = sig_i[KC - 1]

    if DEBUG_TAPS:
        tap("d_l8a", L8[:], dv_wait=v_l8)
    rr = ln_stats_bcast(0, RES[0], PB[6], PB[7], 0, 0, eps=EPS5)
    lnb = normalize_bf16(RES[0], 0, 0, SG[0], SG[1], 0, q_dv=rr["q"], p_ac=rr["p"])
    gateup_pe, silu_last, stt_dv = ffn(RES[0], lnb, (PB[0], PB[1]), (PB[2], PB[3]),
                                       (PB[4], PB[5]))
    tv[0]["gateup_pe"] = gateup_pe
    tv[0]["silu_last"] = silu_last
    tv[0]["stt_dv"] = stt_dv
    rr2 = ln2_stats_psum(0, RES[0], gateup_pe, stt_dv)
    tv[0]["ln2_sv"] = rr2["sv"]
    tv[0]["bc2p"], tv[0]["bc2q"] = rr2["bc2p"], rr2["bc2q"]
    tv[0]["ri_war_ac"] = silu_last
    tv[0]["ri_war_dv"] = lnb
    tv[0]["ri_war_dv2"] = stt_dv
    tv[0]["hatt_war"] = 0
    attn_tail(0, RES[0], 0)
    if DEBUG_TAPS:
        tap("d_l8b", L8[:], ac_wait=tv[0]["sig_i"])

    for i in range(NT):
        j = i + 1
        mid = mid_make(i)
        rr2 = None
        if j < NT:
            res_j = RES[j % 2]
            bfs1w = tv[i - 1]["gateup2_pe"] if i > 0 else tv[i]["ln2_sv"]
            rr = ln_stats_bcast(j, res_j, PB[6], PB[7],
                                bfs1w, tv[i]["ln2_sv"], eps=EPS5,
                                extra_pe_waits=[("DV", tv[i].get("conv_dv", 0)),
                                                ("AC", tv[i]["sig_i"])])
            lnb = normalize_bf16(res_j, 0, 0, SG[0], SG[1], bfs1w,
                                 q_dv=rr["q"], p_ac=rr["p"])
            tv[j]["lnb"] = lnb
            gateup_pe, silu_last, stt_dv = ffn(
                res_j, lnb, (PB[0], PB[1]), (PB[2], PB[3]), (PB[4], PB[5]),
                gate_war_ac=state["last_silu"], up_war_ac=tv[i]["sig_i"],
                up_war_dv=tv[i].get("stt2_dv", tv[i].get("stt_dv", 0)), mid=mid)
            tv[j]["gateup_pe"] = gateup_pe
            tv[j]["silu_last"] = silu_last
            tv[j]["stt_dv"] = stt_dv
            if DEBUG_TAPS and j == 1:
                tap("d_x1", RES[1][:], dv_wait=stt_dv)
            rr2 = ln2_stats_psum(j, res_j, gateup_pe, stt_dv)
            tv[j]["ln2_sv"] = rr2["sv"]
        else:
            for fn_ in mid["all"]:
                fn_()

        o_copy = {}
        units = []
        pb45_war = tv[j]["stt_dv"] if j < NT else tv[i - 1]["stt2_dv"]
        for md in range(KC):
            if md % 4 == 0:
                units.append(unit_load(wo_s[:, md * 1024:(md + 4) * 1024], UNIT))
            slot, u = units[-1]
            po = PB[4 + md % 2]
            base = (md % 4) * 1024
            def mm_o(e, po=po, slot=slot, base=base):
                last = None
                for kc in range(KC):
                    last = e.matmul(po[:], slot[:, base + kc * 128: base + kc * 128 + 128],
                                    HATT[:, kc * TT:(kc + 1) * TT],
                                    start=(kc == 0), stop=(kc == KC - 1))
                return last
            ov = sch.emit("tensor", mm_o,
                          waits=[("WL", ring_state["loaded"][u]), ("DV", tv[i]["hatt"]),
                                 ("DV", pb45_war), ("DV", o_copy.get(md - 2, 0))],
                          inc="PE")
            unit_used(u)
            o_copy[md] = sch.emit(
                "vector",
                (lambda e, po=po, md=md: e.tensor_copy(SB[:, md * TT:(md + 1) * TT], po[:])),
                waits=[("PE", ov)], inc="DV")
        tv[i]["outw"] = sch.val("PE")
        if DEBUG_TAPS and i == 1:
            tap("d_hatt", HATT[:], dv_wait=tv[i]["hatt"])
            tap("d_o", SB[:], dv_wait=o_copy[KC - 1])
        osq_last = 0
        bfs2_war = tv[j]["ln2_sv"] if j < NT else tv[i]["ln2_sv"]
        for md in range(KC):
            osq_last = sch.emit(
                "scalar",
                (lambda e, md=md: e.activation(
                    BFS[:, T + md * TT:T + (md + 1) * TT],
                    SB[:, md * TT:(md + 1) * TT], AF.Square)),
                waits=[("DV", o_copy[md]), ("PE", bfs2_war)], inc="AC")
        def mm_rms(e):
            last = None
            for md in range(KC):
                last = e.matmul(PB[4][:], ONESB[:, 0:P],
                                BFS[:, T + md * TT:T + (md + 1) * TT],
                                start=(md == 0), stop=(md == KC - 1))
            return last
        v_rs = sch.emit("tensor", mm_rms,
                        waits=[("AC", osq_last), ("DV", o_copy[KC - 1])], inc="PE")
        v_ms = sch.emit("vector",
                        (lambda e: e.tensor_scalar_mul(SG[0][:], PB[4][:], 1.0 / D)),
                        waits=[("PE", v_rs), ("AC", state["last_silu"]),
                               ("AC", state["last_rowp"])], inc="DV")
        sch.emit("scalar",
                 (lambda e: e.activation(SG[0][:], SG[0][:], AF.Ln, bias=EPS6)),
                 waits=[("DV", v_ms)], inc="AC")
        v_rp = sch.emit("scalar",
                        (lambda e: e.activation(SG[0][:], SG[0][:], AF.Exp, scale=CMH)),
                        inc="AC")
        state["last_rowp"] = v_rp
        res_i = RES[i % 2]
        apply_last = 0
        for md in range(KC):
            sch.emit("vector",
                     (lambda e, md=md: e.scalar_tensor_tensor(
                         SB[:, md * TT:(md + 1) * TT], SB[:, md * TT:(md + 1) * TT],
                         NW[:, md:md + 1], SG[0][:, 0:TT], OP.mult, OP.mult)),
                     waits=[("AC", v_rp)], inc="DV")
            apply_last = sch.emit(
                "vector",
                (lambda e, md=md: e.tensor_add(res_i[:, md * TT:(md + 1) * TT],
                                               res_i[:, md * TT:(md + 1) * TT],
                                               SB[:, md * TT:(md + 1) * TT])),
                inc="DV")
        if DEBUG_TAPS and i == 1:
            tap("d_msrow", SG[0][0:1, :], dv_wait=v_ms)
            tap("d_rstd", ROWP[0:1, 0:TT], ac_wait=v_rp)
            tap("d_x2", res_i[:], dv_wait=apply_last)
        bfs1_war = tv[j]["gateup_pe"] if j < NT else tv[i]["ri_pe"]
        rr3 = ln_stats_bcast(i, res_i, PB[2], PB[3], bfs1_war, v_rs,
                             res_dv=apply_last, eps=EPS5)
        lnb3 = normalize_bf16(res_i, 0, 0, SG[0], SG[1], bfs1_war,
                              q_dv=rr3["q"], p_ac=rr3["p"])
        g2_war_ac = tv[j]["silu_last"] if j < NT else tv[i]["sig_i"]
        gateup2_pe, silu2_last, stt2_dv = ffn(
            res_i, lnb3, (PB[0], PB[1]), (PB[2], PB[3]), (PB[4], PB[5]),
            gate_war_ac=g2_war_ac, up_war_ac=0, up_war_dv=rr3["var"], mid=None)
        tv[i]["gateup2_pe"] = gateup2_pe
        tv[i]["stt2_dv"] = stt2_dv
        if j < NT:
            tv[j]["ri_war_ac"] = silu2_last
            tv[j]["ri_war_dv"] = lnb3
            tv[j]["ri_war_dv2"] = stt2_dv
            tv[j]["hatt_war"] = tv[i]["outw"]
        sch.emit("gpsimd",
                 (lambda e, i=i, res_i=res_i: e.dma_start(
                     out_h[:, :, i * TT:(i + 1) * TT],
                     res_i[:].rearrange("p (c t) -> p c t", c=KC))),
                 waits=[("DV", stt2_dv)], inc="OUT", incn=16)
        if i + 2 < NT:
            x_ready[i + 2] = sch.emit(
                "gpsimd",
                (lambda e, i=i: e.dma_start(
                    RES[i % 2][:].rearrange("p (c t) -> p c t", c=KC),
                    xh[:, :, (i + 2) * TT:(i + 3) * TT])),
                waits=[("OUT", 16 * (i + 1))], inc="X", incn=16)
        if j < NT:
            attn_tail(j, RES[j % 2], stt2_dv)

    if False:
        d_x1 = nc.dram_tensor("d_x1", [P, T], F32, kind="ExternalOutput")
        d_hatt = nc.dram_tensor("d_hatt", [P, T], BF16, kind="ExternalOutput")
        d_r1 = nc.dram_tensor("d_r1", [P, T], F32, kind="ExternalOutput")
        d_i1 = nc.dram_tensor("d_i1", [P, T], F32, kind="ExternalOutput")
        d_xcb1 = nc.dram_tensor("d_xcb1", [P, T], BF16, kind="ExternalOutput")
        sch.emit("gpsimd", (lambda e: e.dma_start(d_x1[:], RES[1][:])),
                 waits=[("DV", tv[1]["stt_dv"])], inc="OUT", incn=0 if False else 16)
        sch.emit("gpsimd", (lambda e: e.dma_start(d_hatt[:], HATT[:])),
                 waits=[("DV", tv[0]["hatt"])], inc="OUT", incn=16)
        sch.emit("gpsimd", (lambda e: e.dma_start(d_r1[:], SC_[:])),
                 waits=[("AC", tv[1]["sig_r"])], inc="OUT", incn=16)
        sch.emit("gpsimd", (lambda e: e.dma_start(d_i1[:], SD[:])),
                 waits=[("AC", tv[1]["sig_i"])], inc="OUT", incn=16)
        sch.emit("gpsimd", (lambda e: e.dma_start(d_xcb1[:], BFS[:, 0:T])),
                 waits=[("DV", tv[1]["xcb"])], inc="OUT", incn=16)
        sch.ops["gpsimd"].append(([("OUT", 16 * NT + 80)], None, None))
    else:
        sch.ops["gpsimd"].append(([("OUT", sch.val("OUT"))], None, None))

    with nc.Block() as blk:
        def runner(lst):
            def body(e):
                for ws, fn, iv in lst:
                    for s, v in ws:
                        e.wait_ge(SEMS[s], v)
                    if fn is None:
                        continue
                    inst = fn(e)
                    if iv is not None:
                        inst.then_inc(SEMS[iv[0]], iv[1])
            return body
        blk.sync(runner(sch.ops["sync"]))
        blk.gpsimd(runner(sch.ops["gpsimd"]))
        blk.tensor(runner(sch.ops["tensor"]))
        blk.vector(runner(sch.ops["vector"]))
        blk.scalar(runner(sch.ops["scalar"]))
    return nc


def _host_layouts(inputs):
    wg = np.asarray(inputs["L0_ffn_w_gate"])
    wu = np.asarray(inputs["L0_ffn_w_up"])
    wd = np.asarray(inputs["L0_ffn_w_down"])
    wr = np.asarray(inputs["L0_attn_Wr_w"], dtype=np.float32)
    wi = np.asarray(inputs["L0_attn_Wi_w"], dtype=np.float32)
    wo = np.asarray(inputs["L0_attn_out_w"], dtype=np.float32)

    def hcmaj(w):
        return np.ascontiguousarray(
            w.reshape(HC, 128, KC, 128).transpose(3, 0, 2, 1).reshape(128, -1))

    def mdmaj(w):
        return np.ascontiguousarray(
            w.reshape(KC, 128, HC, 128).transpose(3, 0, 2, 1).reshape(128, -1))

    def mcmaj(w):
        return np.ascontiguousarray(
            w.reshape(KC, 128, KC, 128).transpose(3, 0, 2, 1).reshape(128, -1))

    def vec(v):
        return np.ascontiguousarray(np.asarray(v, dtype=np.float32).reshape(KC, 128).T)

    cwf = np.asarray(inputs["L0_attn_conv_weight"], dtype=np.float32)[:, 0, :]
    cw = np.ascontiguousarray(
        cwf.reshape(KC, 128, 4).transpose(1, 2, 0).reshape(128, 32))

    common = {
        "wg": hcmaj(wg), "wu": hcmaj(wu), "wd": mdmaj(wd),
        "wr": mcmaj(wr), "wi": mcmaj(wi), "wo": mcmaj(wo),
        "rb": vec(inputs["L0_attn_Wr_b"]), "ib": vec(inputs["L0_attn_Wi_b"]),
        "la": vec(inputs["L0_attn_log_a"]), "nw": vec(inputs["L0_attn_norm_w"]),
        "cw": cw, "cb": vec(inputs["L0_attn_conv_bias"]),
    }
    x = np.asarray(inputs["L0_attn_x"], dtype=np.float32)
    maps = []
    for b in range(8):
        xb = np.ascontiguousarray(
            x[b].T.reshape(KC, 128, T).transpose(1, 0, 2))
        m = dict(common)
        m["x"] = xb
        maps.append(m)
    return maps


def kernel(**inputs):
    in_maps = _host_layouts(inputs)
    nc = build_nc()
    res = run_bass_kernel_spmd(nc, in_maps, core_ids=list(range(8)))
    outs = []
    for b in range(8):
        o = np.asarray(res.results[b]["out"])
        outs.append(o.transpose(1, 0, 2).reshape(D, T).T)
    return np.stack(outs).astype(np.float32)


# version 09
# speedup vs baseline: 1.1656x; 1.0561x over previous
import numpy as np
import concourse.bass as bass
import concourse.mybir as mybir
from concourse.bass_utils import run_bass_kernel_spmd

F32 = mybir.dt.float32
BF16 = mybir.dt.bfloat16
I32 = mybir.dt.int32
AF = mybir.ActivationFunctionType
OP = mybir.AluOpType

D = 1024
KC = 8
H = 3072
HC = 24
T = 4096
TT = 512
NT = T // TT
P = 128
RING = 6
UNIT = 4096

S127 = 1.0 / 127.0
DOWN_SCALE = 0.5 * S127 * S127
LOG1EM7 = float(np.float32(np.log(np.float32(1e-7))))
DEBUG_TAPS = False
INTERLEAVE = True


class Sched:
    def __init__(self):
        self.ops = {"sync": [], "gpsimd": [], "tensor": [], "vector": [], "scalar": []}
        self.cnt = {}
        self.marks = {}

    def val(self, sem):
        return self.cnt.get(sem, 0)

    def emit(self, eng, fn, waits=(), inc=None, incn=1):
        ws = []
        for s, v in waits:
            if not v or v <= 0:
                continue
            key = (eng, s)
            if self.marks.get(key, 0) >= v:
                continue
            self.marks[key] = v
            ws.append((s, v))
        iv = None
        if inc is not None:
            self.cnt[inc] = self.cnt.get(inc, 0) + incn
            iv = (inc, incn)
        self.ops[eng].append((ws, fn, iv))
        return self.cnt.get(inc, 0) if inc else None


def build_nc():
    nc = bass.Bass("TRN2", target_bir_lowering=False)

    xh = nc.dram_tensor("x", [P, KC, T], F32, kind="ExternalInput")
    wg_i = nc.dram_tensor("wg", [P, HC * 8 * 128], I32, kind="ExternalInput")
    wu_i = nc.dram_tensor("wu", [P, HC * 8 * 128], I32, kind="ExternalInput")
    wd_i = nc.dram_tensor("wd", [P, 8 * HC * 128], I32, kind="ExternalInput")
    wr_i = nc.dram_tensor("wr", [P, KC * 1024], F32, kind="ExternalInput")
    wi_i = nc.dram_tensor("wi", [P, KC * 1024], F32, kind="ExternalInput")
    wo_i = nc.dram_tensor("wo", [P, KC * 1024], F32, kind="ExternalInput")
    rb_i = nc.dram_tensor("rb", [P, KC], F32, kind="ExternalInput")
    ib_i = nc.dram_tensor("ib", [P, KC], F32, kind="ExternalInput")
    la_i = nc.dram_tensor("la", [P, KC], F32, kind="ExternalInput")
    nw_i = nc.dram_tensor("nw", [P, KC], F32, kind="ExternalInput")
    cw_i = nc.dram_tensor("cw", [P, 32], F32, kind="ExternalInput")
    cb_i = nc.dram_tensor("cb", [P, KC], F32, kind="ExternalInput")
    out_h = nc.dram_tensor("out", [P, KC, T], F32, kind="ExternalOutput")

    wg_s = nc.dram_tensor("wg_s", [P, HC * 8 * 128], BF16, kind="Internal")
    wu_s = nc.dram_tensor("wu_s", [P, HC * 8 * 128], BF16, kind="Internal")
    wd_s = nc.dram_tensor("wd_s", [P, 8 * HC * 128], BF16, kind="Internal")
    wr_s = nc.dram_tensor("wr_s", [P, KC * 1024], BF16, kind="Internal")
    wi_s = nc.dram_tensor("wi_s", [P, KC * 1024], BF16, kind="Internal")
    wo_s = nc.dram_tensor("wo_s", [P, KC * 1024], BF16, kind="Internal")

    import contextlib
    ctx = contextlib.ExitStack()
    nc._kernel_ctx = ctx
    sb = lambda n, s, d=F32: ctx.enter_context(nc.sbuf_tensor(n, s, d))

    RES = [sb("RES0", [P, T]), sb("RES1", [P, T])]
    SA = sb("SA", [P, T])
    SB = sb("SB", [P, T])
    SC_ = sb("SC", [P, T])
    SD = sb("SD", [P, T])
    BFS = sb("BFS", [P, 2 * T], BF16)
    HATT = sb("HATT", [P, T], BF16)
    HH = sb("HH", [P, HC * TT], BF16)
    LTMP = sb("LTMP", [P, TT + 8])
    ROWP = LTMP
    TAIL = sb("TAIL", [P, KC * 4])
    WRG = [sb(f"WRG{j}", [P, UNIT], BF16) for j in range(RING)]
    SG = [sb("SG0", [P, TT]), sb("SG1", [P, TT])]
    ONES = sb("ONESF", [P, TT])
    ONESR = ONES
    ONESB = sb("ONESB", [P, P], BF16)
    CONSTS = sb("CONSTS", [P, 8])
    CW = sb("CWt", [P, 32])
    CB = sb("CBt", [P, KC])
    RB = sb("RBt", [P, KC])
    IB = sb("IBt", [P, KC])
    LAT = sb("LAt", [P, KC])
    NW = sb("NWt", [P, KC])
    L8 = sb("L8", [P, KC])
    L16 = sb("L16", [P, KC])
    SCAR = sb("SCAR", [P, KC])
    CCAR = sb("CCAR", [P, KC])
    RED = sb("REDt", [P, KC])
    PB = [ctx.enter_context(nc.psum_tensor(f"PB{j}", [P, TT], F32)) for j in range(8)]

    cvals = [S127, -1.0, -0.5, 1e-5, 1e-6]
    for j_, v in enumerate(cvals):
        nc.const_aps.register(F32, float(np.float32(v)), CONSTS[:, j_:j_ + 1])
    C127, CM1, CMH, EPS5, EPS6 = [float(np.float32(v)) for v in cvals]

    SEMS = {}
    for n in ["PE", "DV", "AC", "WL", "X", "OUT", "CV"]:
        SEMS[n] = ctx.enter_context(nc.semaphore("sem_" + n))

    sch = Sched()
    ring_state = {"u": 0, "done": {}, "loaded": {}}

    def unit_load(src_ap, ncols):
        u = ring_state["u"]
        ring_state["u"] += 1
        slot = WRG[u % RING]
        waits = [("CV", src_cast_done[src_ap.tensor.name])]
        prev = u - RING
        if prev >= 0:
            waits.append(("PE", ring_state["done"].get(prev, 0)))
        def fn(e, slot=slot, src=src_ap, ncols=ncols):
            return e.dma_start(slot[:, 0:ncols], src)
        v = sch.emit("sync", fn, waits=waits, inc="WL", incn=16)
        ring_state["loaded"][u] = v
        return slot, u

    def unit_used(u):
        ring_state["done"][u] = sch.val("PE")

    src_cast_done = {}
    def cast_dma(dst, src):
        def fn(e, dst=dst, src=src):
            return e.dma_start(dst[:], src[:])
        v = sch.emit("gpsimd", fn, inc="CV", incn=16)
        src_cast_done[dst.name] = v

    cast_dma(wg_s, wg_i)
    cast_dma(wu_s, wu_i)
    cast_dma(wd_s, wd_i)
    cast_dma(wr_s, wr_i)
    cast_dma(wi_s, wi_i)
    cast_dma(wo_s, wo_i)

    for dst, src in [(CW, cw_i), (CB, cb_i), (RB, rb_i), (IB, ib_i),
                     (LAT, la_i), (NW, nw_i)]:
        sch.emit("sync", (lambda e, d=dst, s=src: e.dma_start(d[:], s[:])),
                 inc="X", incn=16)
    x_ready = {}
    for i in range(2):
        x_ready[i] = sch.emit(
            "gpsimd",
            (lambda e, i=i: e.dma_start(RES[i % 2][:].rearrange("p (c t) -> p c t", c=KC),
                                        xh[:, :, i * TT:(i + 1) * TT])),
            inc="X", incn=16)

    def ms(ap, v):
        sch.emit("gpsimd", (lambda e, ap=ap, v=v: e.memset(ap, v)))
    ms(ONES[:], 1.0)
    ms(ONESB[:], 1.0)
    ms(TAIL[:], 0.0)
    ms(SCAR[:], 0.0)
    ms(CCAR[:], 0.0)
    for j_, v in enumerate(cvals):
        ms(CONSTS[:, j_:j_ + 1], float(np.float32(v)))
    setup_done = sch.emit("gpsimd", (lambda e: e.memset(CONSTS[:, 7:8], 0.0)),
                          inc="X", incn=1)

    sch.emit("scalar", (lambda e: e.activation(L8[:], LAT[:], AF.Sigmoid)),
             waits=[("X", setup_done)], inc="AC")
    v_ln = sch.emit("scalar", (lambda e: e.activation(L8[:], L8[:], AF.Ln)), inc="AC")
    sch.emit("vector",
             (lambda e: e.tensor_scalar(L16[:], L8[:], 16.0, -1e30, OP.mult, OP.max)),
             waits=[("AC", v_ln)], inc="DV")
    v_l8 = sch.emit("vector",
                    (lambda e: e.tensor_scalar(L8[:], L8[:], 8.0, -1e30, OP.mult, OP.max)),
                    inc="DV")

    tv = [dict() for _ in range(NT + 1)]
    state = {"scar_v": 0, "ccar_v": 0, "last_silu": 0, "last_rowp": 0}

    dbg = {}
    if DEBUG_TAPS:
        dbg["d_scar"] = nc.dram_tensor("d_scar", [P, KC], F32, kind="ExternalOutput")
        for nm in ["d_l8a", "d_l8b", "d_l8c", "d_l8d"]:
            dbg[nm] = nc.dram_tensor(nm, [P, KC], F32, kind="ExternalOutput")
        dbg["d_r1pre"] = nc.dram_tensor("d_r1pre", [P, T], F32, kind="ExternalOutput")
        dbg["d_xc1"] = nc.dram_tensor("d_xc1", [P, T], F32, kind="ExternalOutput")
        dbg["d_xcb1"] = nc.dram_tensor("d_xcb1", [P, T], BF16, kind="ExternalOutput")
        dbg["d_rstd2"] = nc.dram_tensor("d_rstd2", [1, TT], F32, kind="ExternalOutput")
        dbg["d_q2"] = nc.dram_tensor("d_q2", [1, TT], F32, kind="ExternalOutput")
        for nm, dt_ in [("d_x1", F32), ("d_o", F32), ("d_x2", F32), ("d_hatt", BF16),
                        ("d_logA", F32), ("d_aexp", F32), ("d_cs", F32), ("d_term", F32),
                        ("d_la", F32), ("d_s", F32),
                        ("d_msrow", F32), ("d_rstd", F32)]:
            dbg[nm] = nc.dram_tensor(nm, [P, T] if nm not in ("d_msrow", "d_rstd")
                                     else [1, TT], dt_, kind="ExternalOutput")

    def tap(name, src_ap, dv_wait=None, ac_wait=None):
        if not DEBUG_TAPS:
            return
        w = []
        if dv_wait: w.append(("DV", dv_wait))
        if ac_wait: w.append(("AC", ac_wait))
        v = sch.emit("gpsimd", (lambda e, s=src_ap, d=dbg[name]: e.dma_start(d[:], s)),
                     waits=w, inc="OUT", incn=16)
        sch.emit("vector", (lambda e: e.engine_nop()), waits=[("OUT", v)], inc="DV")
        state["dbg_out"] = v

    def ln_stats(i, res, pb_m, pb_v, bfs1_war_pe, bfs2_war_pe, res_dv=0,
                 extra_pe_waits=()):
        r = {}
        r["xb"] = sch.emit("vector", (lambda e: e.tensor_copy(BFS[:, 0:T], res[:])),
                           waits=[("PE", bfs1_war_pe), ("X", x_ready.get(i, 0)),
                                  ("X", setup_done), ("DV", res_dv)], inc="DV")
        r["xsq"] = sch.emit("scalar",
                            (lambda e: e.activation(BFS[:, T:2 * T], res[:], AF.Square)),
                            waits=[("PE", bfs2_war_pe), ("X", x_ready.get(i, 0)),
                                   ("X", setup_done), ("DV", res_dv)], inc="AC")

        def mm_stats(e, pb, off):
            last = None
            for kc in range(KC):
                last = e.matmul(pb[0:1, :], ONESB[:, 0:1],
                                BFS[:, off + kc * TT: off + (kc + 1) * TT],
                                start=(kc == 0), stop=(kc == KC - 1))
            return last
        r["sm"] = sch.emit("tensor", (lambda e: mm_stats(e, pb_m, 0)),
                           waits=[("DV", r["xb"]), ("X", setup_done)] + list(extra_pe_waits),
                           inc="PE")
        r["sv"] = sch.emit("tensor", (lambda e: mm_stats(e, pb_v, T)),
                           waits=[("AC", r["xsq"])], inc="PE")
        sch.emit("vector",
                 (lambda e: e.tensor_scalar_mul(SG[0][0:1, :], pb_m[0:1, :], 1.0 / D)),
                 waits=[("PE", r["sm"]), ("AC", state["last_silu"]),
                        ("AC", state["last_rowp"])], inc="DV")
        sch.emit("vector",
                 (lambda e: e.tensor_mul(ROWP[0:1, 0:TT], SG[0][0:1, :], SG[0][0:1, :])),
                 inc="DV")
        r["var"] = sch.emit("vector",
                            (lambda e: e.scalar_tensor_tensor(
                                SG[1][0:1, :], pb_v[0:1, :], 1.0 / D,
                                ROWP[0:1, 0:TT], OP.mult, OP.subtract)),
                            waits=[("PE", r["sv"])], inc="DV")
        sch.emit("scalar",
                 (lambda e: e.activation(SG[1][0:1, :], SG[1][0:1, :], AF.Ln, bias=EPS5)),
                 waits=[("DV", r["var"])], inc="AC")
        r["p"] = sch.emit("scalar",
                          (lambda e: e.activation(ROWP[0:1, 0:TT], SG[1][0:1, :],
                                                  AF.Exp, scale=CMH)),
                          inc="AC")
        state["last_rowp"] = r["p"]
        r["q"] = sch.emit("vector",
                          (lambda e: e.scalar_tensor_tensor(SG[0][0:1, :], SG[0][0:1, :],
                                                            -1.0, ROWP[0:1, 0:TT],
                                                            OP.mult, OP.mult)),
                          waits=[("AC", r["p"])], inc="DV")
        r["rows_dv"] = sch.val("DV")
        return r

    def bcast_rows(rr, pb_p, pb_q, war_ac=0, war_dv=0):
        v_bp = sch.emit("tensor",
                        (lambda e: e.matmul(pb_p[:], ONESR[0:1, 0:P], ROWP[0:1, 0:TT],
                                            start=True, stop=True)),
                        waits=[("AC", rr["p"]), ("AC", war_ac), ("DV", war_dv)], inc="PE")
        v_bq = sch.emit("tensor",
                        (lambda e: e.matmul(pb_q[:], ONESR[0:1, 0:P], SG[0][0:1, :],
                                            start=True, stop=True)),
                        waits=[("DV", rr["q"])], inc="PE")
        return v_bp, v_bq

    def ln_stats_bcast(i, res, pb_m, pb_v, bfs1_war_pe, bfs2_war_pe, res_dv=0,
                       eps=None, extra_pe_waits=()):
        r = {}
        r["xb"] = sch.emit("vector", (lambda e: e.tensor_copy(BFS[:, 0:T], res[:])),
                           waits=[("PE", bfs1_war_pe), ("X", x_ready.get(i, 0)),
                                  ("X", setup_done), ("DV", res_dv)], inc="DV")
        r["xsq"] = sch.emit("scalar",
                            (lambda e: e.activation(BFS[:, T:2 * T], res[:], AF.Square)),
                            waits=[("PE", bfs2_war_pe), ("X", x_ready.get(i, 0)),
                                   ("X", setup_done), ("DV", res_dv)], inc="AC")
        def mm_stats(e, pb, off):
            last = None
            for kc in range(KC):
                last = e.matmul(pb[:], ONESB[:, 0:P],
                                BFS[:, off + kc * TT: off + (kc + 1) * TT],
                                start=(kc == 0), stop=(kc == KC - 1))
            return last
        r["sm"] = sch.emit("tensor", (lambda e: mm_stats(e, pb_m, 0)),
                           waits=[("DV", r["xb"]), ("X", setup_done)] + list(extra_pe_waits),
                           inc="PE")
        r["sv"] = sch.emit("tensor", (lambda e: mm_stats(e, pb_v, T)),
                           waits=[("AC", r["xsq"])], inc="PE")
        sch.emit("vector",
                 (lambda e: e.tensor_scalar_mul(SG[1][:], pb_m[:], 1.0 / D)),
                 waits=[("PE", r["sm"]), ("AC", state["last_silu"]),
                        ("AC", state["last_rowp"])], inc="DV")
        sch.emit("vector", (lambda e: e.tensor_mul(SG[0][:], SG[1][:], SG[1][:])),
                 inc="DV")
        r["var"] = sch.emit("vector",
                            (lambda e: e.scalar_tensor_tensor(
                                SG[0][:], pb_v[:], 1.0 / D, SG[0][:],
                                OP.mult, OP.subtract)),
                            waits=[("PE", r["sv"])], inc="DV")
        sch.emit("scalar",
                 (lambda e: e.activation(SG[0][:], SG[0][:], AF.Ln, bias=eps)),
                 waits=[("DV", r["var"])], inc="AC")
        r["p"] = sch.emit("scalar",
                          (lambda e: e.activation(SG[0][:], SG[0][:], AF.Exp, scale=CMH)),
                          inc="AC")
        state["last_rowp"] = r["p"]
        r["q"] = sch.emit("vector",
                          (lambda e: e.scalar_tensor_tensor(SG[1][:], SG[1][:], -1.0,
                                                            SG[0][:], OP.mult, OP.mult)),
                          waits=[("AC", r["p"])], inc="DV")
        return r

    def ln2_stats_psum(j, res, bfs_war_pe, res_dv):
        r = {}
        r["xb"] = sch.emit("vector", (lambda e: e.tensor_copy(BFS[:, 0:T], res[:])),
                           waits=[("PE", bfs_war_pe), ("DV", res_dv)], inc="DV")
        r["xsq"] = sch.emit("scalar",
                            (lambda e: e.activation(BFS[:, T:2 * T], res[:], AF.Square)),
                            waits=[("PE", bfs_war_pe), ("DV", res_dv)], inc="AC")
        def mm_stats(e, pb, off):
            last = None
            for kc in range(KC):
                last = e.matmul(pb[:], ONESB[:, 0:P],
                                BFS[:, off + kc * TT: off + (kc + 1) * TT],
                                start=(kc == 0), stop=(kc == KC - 1))
            return last
        r["sm"] = sch.emit("tensor", (lambda e: mm_stats(e, PB[6], 0)),
                           waits=[("DV", r["xb"])], inc="PE")
        r["sv"] = sch.emit("tensor", (lambda e: mm_stats(e, PB[7], T)),
                           waits=[("AC", r["xsq"])], inc="PE")
        sch.emit("vector",
                 (lambda e: e.tensor_scalar_mul(SG[1][:], PB[6][:], 1.0 / D)),
                 waits=[("PE", r["sm"]), ("AC", state["last_silu"]),
                        ("AC", state["last_rowp"])], inc="DV")
        sch.emit("vector", (lambda e: e.tensor_mul(SG[0][:], SG[1][:], SG[1][:])),
                 inc="DV")
        r["var"] = sch.emit("vector",
                            (lambda e: e.scalar_tensor_tensor(
                                SG[0][:], PB[7][:], 1.0 / D, SG[0][:],
                                OP.mult, OP.subtract)),
                            waits=[("PE", r["sv"])], inc="DV")
        sch.emit("scalar",
                 (lambda e: e.activation(SG[0][:], SG[0][:], AF.Ln, bias=EPS5)),
                 waits=[("DV", r["var"])], inc="AC")
        r["bc2p"] = sch.emit("scalar",
                             (lambda e: e.activation(PB[6][:], SG[0][:], AF.Exp,
                                                     scale=CMH)),
                             inc="AC")
        state["last_rowp"] = r["bc2p"]
        r["bc2q"] = sch.emit("vector",
                             (lambda e: e.scalar_tensor_tensor(PB[7][:], SG[1][:], -1.0,
                                                               PB[6][:], OP.mult,
                                                               OP.mult)),
                             waits=[("AC", r["bc2p"])], inc="DV")
        tv[j]["bc2p"] = r["bc2p"]
        tv[j]["bc2q"] = r["bc2q"]
        return r

    def normalize_bf16(res, v_bp, v_bq, pb_p, pb_q, bfs1_war_pe, q_dv=0, p_ac=0):
        last = 0
        pp = pb_p[:, 0:TT] if pb_p in (SG[0], SG[1]) else pb_p[:]
        qq = pb_q[:, 0:TT] if pb_q in (SG[0], SG[1]) else pb_q[:]
        for c in range(KC):
            sch.emit("vector",
                     (lambda e, c=c: e.tensor_mul(LTMP[:, 0:TT],
                                                  res[:, c * TT:(c + 1) * TT], pp)),
                     waits=[("PE", v_bp), ("PE", v_bq), ("PE", bfs1_war_pe),
                            ("DV", q_dv), ("AC", p_ac)], inc="DV")
            last = sch.emit("vector",
                            (lambda e, c=c: e.tensor_add(BFS[:, c * TT:(c + 1) * TT],
                                                         LTMP[:, 0:TT], qq)),
                            inc="DV")
        return last

    def mid_make(i):
        g = {}
        def g_la_ixc():
            if DEBUG_TAPS and i == 1:
                tap("d_l8c", L8[:], ac_wait=tv[i]["sig_r"])
                tap("d_r1pre", SC_[:], ac_wait=tv[i]["sig_r"])
            for c in range(KC):
                sch.emit("vector",
                         (lambda e, c=c: e.tensor_scalar(
                             SA[:, c * TT:(c + 1) * TT], SC_[:, c * TT:(c + 1) * TT],
                             L8[:, c:c + 1], LOG1EM7, OP.mult, OP.max)),
                         waits=[("AC", tv[i]["sig_r"]), ("DV", v_l8)], inc="DV")
            for c in range(KC):
                tv[i]["ixc"] = sch.emit(
                    "vector",
                    (lambda e, c=c: e.tensor_mul(
                        SD[:, c * TT:(c + 1) * TT], SD[:, c * TT:(c + 1) * TT],
                        SB[:, c * TT:(c + 1) * TT])),
                    waits=[("AC", tv[i]["sig_i"])], inc="DV")
        def a_a2():
            for c in range(KC):
                tv[i]["a2"] = sch.emit(
                    "scalar",
                    (lambda e, c=c: e.activation(
                        SB[:, c * TT:(c + 1) * TT], SC_[:, c * TT:(c + 1) * TT],
                        AF.Exp, scale=L16[:, c:c + 1])),
                    waits=[("DV", tv[i]["ixc"])], inc="AC")
        def g_v():
            sch.emit("vector",
                     (lambda e: e.tensor_scalar(SC_[:], SB[:], 1.0, -1.0,
                                                OP.subtract, OP.mult)),
                     waits=[("AC", tv[i]["a2"])], inc="DV")
            tv[i]["vmax"] = sch.emit(
                "vector", (lambda e: e.tensor_scalar_max(SC_[:], SC_[:], 1e-6)), inc="DV")
        def a_bamp():
            tv[i]["bamp"] = sch.emit(
                "scalar", (lambda e: e.activation(SB[:], SC_[:], AF.Sqrt)),
                waits=[("DV", tv[i]["vmax"])], inc="AC")
        def g_bt():
            tv[i]["bt"] = sch.emit(
                "vector", (lambda e: e.tensor_mul(SD[:], SD[:], SB[:])),
                waits=[("AC", tv[i]["bamp"])], inc="DV")
        def g_ss(c0, c1):
            def f():
                for c in range(c0, c1):
                    sch.emit("vector",
                             (lambda e, c=c: e.tensor_tensor_scan(
                                 SB[:, c * TT:(c + 1) * TT], ONES[:],
                                 SA[:, c * TT:(c + 1) * TT], SCAR[:, c:c + 1],
                                 OP.mult, OP.add)),
                             waits=[("DV", state["scar_v"]), ("DV", tv[i]["bt"])],
                             inc="DV")
            return f
        def g_sred():
            for c in range(KC):
                sch.emit("vector",
                         (lambda e, c=c: e.tensor_reduce(
                             RED[:, c:c + 1], SA[:, c * TT:(c + 1) * TT],
                             mybir.AxisListType.X, OP.add)),
                         inc="DV")
            sch.emit("vector", (lambda e: e.engine_nop()), inc="DV")
            sch.emit("vector", (lambda e: e.engine_nop()), inc="DV")
            state["scar_v"] = sch.emit(
                "vector", (lambda e: e.tensor_add(SCAR[:], SCAR[:], RED[:])), inc="DV")
        def g_logA():
            tv[i]["logA"] = sch.emit(
                "vector", (lambda e: e.tensor_scalar_max(SB[:], SB[:], -80.0)), inc="DV")
            if DEBUG_TAPS and i == 1:
                tap("d_logA", SB[:], dv_wait=tv[i]["logA"])
        def a_en():
            tv[i]["en"] = sch.emit(
                "scalar", (lambda e: e.activation(SA[:], SB[:], AF.Exp, scale=CM1)),
                waits=[("DV", tv[i]["logA"])], inc="AC")
        def g_term():
            tv[i]["term"] = sch.emit(
                "vector", (lambda e: e.tensor_mul(SD[:], SD[:], SA[:])),
                waits=[("AC", tv[i]["en"])], inc="DV")
            if DEBUG_TAPS and i == 1:
                tap("d_term", SD[:], dv_wait=tv[i]["term"])
        def a_aexp():
            tv[i]["aexp"] = sch.emit(
                "scalar", (lambda e: e.activation(SC_[:], SB[:], AF.Exp)),
                waits=[("DV", tv[i]["term"])], inc="AC")
        def g_cs(c0, c1):
            def f():
                for c in range(c0, c1):
                    sch.emit("vector",
                             (lambda e, c=c: e.tensor_tensor_scan(
                                 SA[:, c * TT:(c + 1) * TT], ONES[:],
                                 SD[:, c * TT:(c + 1) * TT], CCAR[:, c:c + 1],
                                 OP.mult, OP.add)),
                             waits=[("DV", state["ccar_v"]), ("DV", tv[i]["term"])],
                             inc="DV")
            return f
        def g_cred():
            for c in range(KC):
                sch.emit("vector",
                         (lambda e, c=c: e.tensor_reduce(
                             RED[:, c:c + 1], SD[:, c * TT:(c + 1) * TT],
                             mybir.AxisListType.X, OP.add)),
                         inc="DV")
            sch.emit("vector", (lambda e: e.engine_nop()), inc="DV")
            sch.emit("vector", (lambda e: e.engine_nop()), inc="DV")
            state["ccar_v"] = sch.emit(
                "vector", (lambda e: e.tensor_add(CCAR[:], CCAR[:], RED[:])), inc="DV")
        def g_hatt():
            if DEBUG_TAPS and i == 1:
                tap("d_aexp", SC_[:], ac_wait=tv[i]["aexp"])
                tap("d_cs", SA[:], dv_wait=state["ccar_v"])
            tv[i]["hatt"] = sch.emit(
                "vector", (lambda e: e.tensor_mul(HATT[:], SC_[:], SA[:])),
                waits=[("AC", tv[i]["aexp"]),
                       ("PE", tv[i - 1].get("outw", 0) if i > 0 else 0)],
                inc="DV")
        g["dve"] = {2: [g_la_ixc], 4: [g_v], 6: [g_bt], 8: [g_ss(0, 4)],
                    10: [g_ss(4, 8)], 12: [g_sred, g_logA], 14: [g_term],
                    16: [g_cs(0, 4)], 18: [g_cs(4, 8)], 20: [g_cred], 21: [g_hatt]}
        g["act"] = {2: [a_a2], 4: [a_bamp], 13: [a_en], 17: [a_aexp]}
        g["all"] = [g_la_ixc, a_a2, g_v, a_bamp, g_bt, g_ss(0, 8), g_sred, g_logA,
                    a_en, g_term, a_aexp, g_cs(0, 8), g_cred, g_hatt]
        return g

    def ffn(res, lnb_ready_dv, pg_pair, pu_pair, pd_pair,
            gate_war_ac=0, up_war_ac=0, up_war_dv=0, mid=None):
        silu_done = {}
        h_done = {}
        gslot = uslot = None
        gu = uu = None
        for hc in range(HC):
            if mid is not None and INTERLEAVE:
                for fn_ in mid["dve"].get(hc, []):
                    fn_()
                for fn_ in mid["act"].get(hc, []):
                    fn_()
            if hc % 4 == 0:
                gslot, gu = unit_load(wg_s[:, hc * 1024:(hc + 4) * 1024], UNIT)
                uslot, uu = unit_load(wu_s[:, hc * 1024:(hc + 4) * 1024], UNIT)
            pg = pg_pair[hc % 2]
            pu = pu_pair[hc % 2]
            base = (hc % 4) * 1024
            def mm_gate(e, pg=pg, gslot=gslot, base=base):
                last = None
                for kc in range(KC):
                    last = e.matmul(pg[:], gslot[:, base + kc * 128: base + kc * 128 + 128],
                                    BFS[:, kc * TT:(kc + 1) * TT],
                                    start=(kc == 0), stop=(kc == KC - 1))
                return last
            gv = sch.emit("tensor", mm_gate,
                          waits=[("WL", ring_state["loaded"][gu]), ("DV", lnb_ready_dv),
                                 ("AC", silu_done.get(hc - 2, gate_war_ac))], inc="PE")
            unit_used(gu)
            def mm_up(e, pu=pu, uslot=uslot, base=base):
                last = None
                for kc in range(KC):
                    last = e.matmul(pu[:], uslot[:, base + kc * 128: base + kc * 128 + 128],
                                    BFS[:, kc * TT:(kc + 1) * TT],
                                    start=(kc == 0), stop=(kc == KC - 1))
                return last
            uv = sch.emit("tensor", mm_up,
                          waits=[("WL", ring_state["loaded"][uu]),
                                 ("DV", h_done.get(hc - 2, up_war_dv)),
                                 ("AC", up_war_ac if hc < 2 else 0)], inc="PE")
            unit_used(uu)
            sg = SG[hc % 2]
            sv = sch.emit("scalar",
                          (lambda e, sg=sg, pg=pg: e.activation(sg[:], pg[:], AF.Silu,
                                                                scale=C127)),
                          waits=[("PE", gv), ("DV", h_done.get(hc - 2, 0))], inc="AC")
            silu_done[hc] = sv
            state["last_silu"] = sv
            h_done[hc] = sch.emit(
                "vector",
                (lambda e, sg=sg, pu=pu, hc=hc:
                 e.tensor_mul(HH[:, hc * TT:(hc + 1) * TT], sg[:], pu[:])),
                waits=[("AC", sv), ("PE", uv)], inc="DV")
        gateup_pe = sch.val("PE")
        silu_last = silu_done[HC - 1]
        if mid is not None and not INTERLEAVE:
            for fn_ in mid["all"]:
                fn_()
        stt_done = {}
        for md in range(KC):
            dslot, du = unit_load(wd_s[:, md * HC * 128:(md + 1) * HC * 128], HC * 128)
            pd = pd_pair[md % 2]
            def mm_down(e, pd=pd, dslot=dslot):
                last = None
                for hc in range(HC):
                    last = e.matmul(pd[:], dslot[:, hc * 128: hc * 128 + 128],
                                    HH[:, hc * TT:(hc + 1) * TT],
                                    start=(hc == 0), stop=(hc == HC - 1))
                return last
            dv_ = sch.emit("tensor", mm_down,
                           waits=[("WL", ring_state["loaded"][du]), ("DV", h_done[HC - 1]),
                                  ("DV", stt_done.get(md - 2, 0))], inc="PE")
            unit_used(du)
            stt_done[md] = sch.emit(
                "vector",
                (lambda e, pd=pd, md=md: e.scalar_tensor_tensor(
                    res[:, md * TT:(md + 1) * TT], pd[:], DOWN_SCALE,
                    res[:, md * TT:(md + 1) * TT], OP.mult, OP.add)),
                waits=[("PE", dv_)], inc="DV")
        return gateup_pe, silu_last, sch.val("DV")

    def proj8(wsrc, pp_pair, dst, bias, rhs_dv, war_ac, war_dv, sig_store):
        units = []
        last_pe = 0
        for mc in range(KC):
            if mc % 4 == 0:
                units.append(unit_load(wsrc[:, mc * 1024:(mc + 4) * 1024], UNIT))
            slot, u = units[-1]
            pr = pp_pair[mc % 2]
            base = (mc % 4) * 1024
            def mm(e, pr=pr, slot=slot, base=base):
                last = None
                for kc in range(KC):
                    last = e.matmul(pr[:], slot[:, base + kc * 128: base + kc * 128 + 128],
                                    HATT[:, kc * TT:(kc + 1) * TT],
                                    start=(kc == 0), stop=(kc == KC - 1))
                return last
            rv = sch.emit("tensor", mm,
                          waits=[("WL", ring_state["loaded"][u]), ("DV", rhs_dv),
                                 ("AC", sig_store.get(mc - 2, war_ac)),
                                 ("DV", war_dv if mc < 2 else 0)], inc="PE")
            last_pe = rv
            unit_used(u)
            sig_store[mc] = sch.emit(
                "scalar",
                (lambda e, pr=pr, mc=mc: e.activation(
                    dst[:, mc * TT:(mc + 1) * TT], pr[:], AF.Sigmoid,
                    bias=bias[:, mc:mc + 1])),
                waits=[("PE", rv)], inc="AC")
        return last_pe

    def make_conv_mid(j, res_j, conv_war_dv):
        def mk(c):
            def f():
                sch.emit("vector",
                         (lambda e: e.tensor_mul(SD[:, 0:TT],
                                                 res_j[:, c * TT:(c + 1) * TT], PB[6][:])),
                         waits=[("AC", tv[j]["bc2p"]), ("DV", tv[j]["bc2q"]),
                                ("DV", conv_war_dv)], inc="DV")
                sch.emit("vector",
                         (lambda e: e.tensor_add(LTMP[:, 3:3 + TT], SD[:, 0:TT],
                                                 PB[7][:])), inc="DV")
                sch.emit("vector",
                         (lambda e: e.tensor_copy(LTMP[:, 0:3],
                                                  TAIL[:, c * 4:c * 4 + 3])), inc="DV")
                sch.emit("vector",
                         (lambda e: e.tensor_copy(TAIL[:, c * 4:c * 4 + 3],
                                                  LTMP[:, TT:TT + 3])), inc="DV")
                sch.emit("vector",
                         (lambda e: e.tensor_scalar(SB[:, c * TT:(c + 1) * TT],
                                                    LTMP[:, 0:TT], CW[:, c:c + 1],
                                                    CB[:, c:c + 1], OP.mult, OP.add)),
                         inc="DV")
                for k in range(1, 4):
                    sch.emit("vector",
                             (lambda e, k=k: e.scalar_tensor_tensor(
                                 SB[:, c * TT:(c + 1) * TT], LTMP[:, k:k + TT],
                                 CW[:, k * 8 + c:k * 8 + c + 1],
                                 SB[:, c * TT:(c + 1) * TT], OP.mult, OP.add)),
                             inc="DV")
                tv[j]["conv_dv"] = sch.val("DV")
            return f
        return {"dve": {3 + 2 * c: [mk(c)] for c in range(KC)}, "act": {}}

    def attn_proj(j):
        tv[j]["xcb"] = sch.emit("vector", (lambda e: e.tensor_copy(HATT[:], SB[:])),
                                waits=[("PE", tv[j]["hatt_war"])], inc="DV")
        sig_r = {}
        sig_i = {}
        proj8(wr_s, (PB[0], PB[1]), SC_, RB, tv[j]["xcb"],
              tv[j].get("ri_war_ac", 0), tv[j].get("ri_war_dv", 0), sig_r)
        tv[j]["ri_pe"] = proj8(wi_s, (PB[2], PB[3]), SD, IB, tv[j]["xcb"],
                               tv[j].get("ri_war_ac", 0), tv[j].get("ri_war_dv2", 0), sig_i)
        tv[j]["sig_r"] = sig_r[KC - 1]
        tv[j]["sig_i"] = sig_i[KC - 1]

    def attn_tail(j, res_j, conv_war_dv):
        for c in range(KC):
            sch.emit("vector",
                     (lambda e, c=c: e.tensor_mul(SD[:, 0:TT],
                                                  res_j[:, c * TT:(c + 1) * TT], PB[6][:])),
                     waits=[("AC", tv[j]["bc2p"]), ("DV", tv[j]["bc2q"]),
                            ("DV", conv_war_dv)], inc="DV")
            sch.emit("vector",
                     (lambda e: e.tensor_add(LTMP[:, 3:3 + TT], SD[:, 0:TT], PB[7][:])),
                     inc="DV")
            sch.emit("vector",
                     (lambda e, c=c: e.tensor_copy(LTMP[:, 0:3], TAIL[:, c * 4:c * 4 + 3])),
                     inc="DV")
            sch.emit("vector",
                     (lambda e, c=c: e.tensor_copy(TAIL[:, c * 4:c * 4 + 3],
                                                   LTMP[:, TT:TT + 3])),
                     inc="DV")
            sch.emit("vector",
                     (lambda e, c=c: e.tensor_scalar(SB[:, c * TT:(c + 1) * TT],
                                                     LTMP[:, 0:TT], CW[:, c:c + 1],
                                                     CB[:, c:c + 1], OP.mult, OP.add)),
                     inc="DV")
            for k in range(1, 4):
                sch.emit("vector",
                         (lambda e, c=c, k=k: e.scalar_tensor_tensor(
                             SB[:, c * TT:(c + 1) * TT], LTMP[:, k:k + TT],
                             CW[:, k * 8 + c:k * 8 + c + 1],
                             SB[:, c * TT:(c + 1) * TT], OP.mult, OP.add)),
                         inc="DV")
        tv[j]["conv_dv"] = sch.val("DV")
        if DEBUG_TAPS and j == 1:
            tap("d_xc1", SB[:], dv_wait=tv[j]["conv_dv"])
        tv[j]["xcb"] = sch.emit("vector", (lambda e: e.tensor_copy(HATT[:], SB[:])),
                                waits=[("PE", tv[j]["hatt_war"])], inc="DV")
        if DEBUG_TAPS and j == 1:
            tap("d_xcb1", BFS[:, 0:T], dv_wait=tv[j]["xcb"])
        sig_r = {}
        sig_i = {}
        proj8(wr_s, (PB[0], PB[1]), SC_, RB, tv[j]["xcb"],
              tv[j].get("ri_war_ac", 0), tv[j].get("ri_war_dv", 0), sig_r)
        tv[j]["ri_pe"] = proj8(wi_s, (PB[2], PB[3]), SD, IB, tv[j]["xcb"],
                               tv[j].get("ri_war_ac", 0), tv[j].get("ri_war_dv2", 0), sig_i)
        tv[j]["sig_r"] = sig_r[KC - 1]
        tv[j]["sig_i"] = sig_i[KC - 1]

    if DEBUG_TAPS:
        tap("d_l8a", L8[:], dv_wait=v_l8)
    rr = ln_stats_bcast(0, RES[0], PB[6], PB[7], 0, 0, eps=EPS5)
    lnb = normalize_bf16(RES[0], 0, 0, SG[0], SG[1], 0, q_dv=rr["q"], p_ac=rr["p"])
    gateup_pe, silu_last, stt_dv = ffn(RES[0], lnb, (PB[0], PB[1]), (PB[2], PB[3]),
                                       (PB[4], PB[5]))
    tv[0]["gateup_pe"] = gateup_pe
    tv[0]["silu_last"] = silu_last
    tv[0]["stt_dv"] = stt_dv
    rr2 = ln2_stats_psum(0, RES[0], gateup_pe, stt_dv)
    tv[0]["ln2_sv"] = rr2["sv"]
    tv[0]["bc2p"], tv[0]["bc2q"] = rr2["bc2p"], rr2["bc2q"]
    tv[0]["ri_war_ac"] = silu_last
    tv[0]["ri_war_dv"] = lnb
    tv[0]["ri_war_dv2"] = stt_dv
    tv[0]["hatt_war"] = 0
    attn_tail(0, RES[0], 0)
    if DEBUG_TAPS:
        tap("d_l8b", L8[:], ac_wait=tv[0]["sig_i"])

    for i in range(NT):
        j = i + 1
        mid = mid_make(i)
        rr2 = None
        if j < NT:
            res_j = RES[j % 2]
            bfs1w = tv[i - 1]["gateup2_pe"] if i > 0 else tv[i]["ln2_sv"]
            rr = ln_stats_bcast(j, res_j, PB[6], PB[7],
                                bfs1w, tv[i]["ln2_sv"], eps=EPS5,
                                extra_pe_waits=[("DV", tv[i].get("conv_dv", 0)),
                                                ("AC", tv[i]["sig_i"])])
            lnb = normalize_bf16(res_j, 0, 0, SG[0], SG[1], bfs1w,
                                 q_dv=rr["q"], p_ac=rr["p"])
            tv[j]["lnb"] = lnb
            gateup_pe, silu_last, stt_dv = ffn(
                res_j, lnb, (PB[0], PB[1]), (PB[2], PB[3]), (PB[4], PB[5]),
                gate_war_ac=state["last_silu"], up_war_ac=tv[i]["sig_i"],
                up_war_dv=tv[i].get("stt2_dv", tv[i].get("stt_dv", 0)), mid=mid)
            tv[j]["gateup_pe"] = gateup_pe
            tv[j]["silu_last"] = silu_last
            tv[j]["stt_dv"] = stt_dv
            if DEBUG_TAPS and j == 1:
                tap("d_x1", RES[1][:], dv_wait=stt_dv)
            rr2 = ln2_stats_psum(j, res_j, gateup_pe, stt_dv)
            tv[j]["ln2_sv"] = rr2["sv"]
        else:
            for fn_ in mid["all"]:
                fn_()

        o_copy = {}
        units = []
        pb45_war = tv[j]["stt_dv"] if j < NT else tv[i - 1]["stt2_dv"]
        for md in range(KC):
            if md % 4 == 0:
                units.append(unit_load(wo_s[:, md * 1024:(md + 4) * 1024], UNIT))
            slot, u = units[-1]
            po = PB[4 + md % 2]
            base = (md % 4) * 1024
            def mm_o(e, po=po, slot=slot, base=base):
                last = None
                for kc in range(KC):
                    last = e.matmul(po[:], slot[:, base + kc * 128: base + kc * 128 + 128],
                                    HATT[:, kc * TT:(kc + 1) * TT],
                                    start=(kc == 0), stop=(kc == KC - 1))
                return last
            ov = sch.emit("tensor", mm_o,
                          waits=[("WL", ring_state["loaded"][u]), ("DV", tv[i]["hatt"]),
                                 ("DV", pb45_war), ("DV", o_copy.get(md - 2, 0))],
                          inc="PE")
            unit_used(u)
            o_copy[md] = sch.emit(
                "vector",
                (lambda e, po=po, md=md: e.tensor_copy(SB[:, md * TT:(md + 1) * TT], po[:])),
                waits=[("PE", ov)], inc="DV")
        tv[i]["outw"] = sch.val("PE")
        if DEBUG_TAPS and i == 1:
            tap("d_hatt", HATT[:], dv_wait=tv[i]["hatt"])
            tap("d_o", SB[:], dv_wait=o_copy[KC - 1])
        osq_last = 0
        bfs2_war = tv[j]["ln2_sv"] if j < NT else tv[i]["ln2_sv"]
        for md in range(KC):
            osq_last = sch.emit(
                "scalar",
                (lambda e, md=md: e.activation(
                    BFS[:, T + md * TT:T + (md + 1) * TT],
                    SB[:, md * TT:(md + 1) * TT], AF.Square)),
                waits=[("DV", o_copy[md]), ("PE", bfs2_war)], inc="AC")
        def mm_rms(e):
            last = None
            for md in range(KC):
                last = e.matmul(PB[4][:], ONESB[:, 0:P],
                                BFS[:, T + md * TT:T + (md + 1) * TT],
                                start=(md == 0), stop=(md == KC - 1))
            return last
        v_rs = sch.emit("tensor", mm_rms,
                        waits=[("AC", osq_last), ("DV", o_copy[KC - 1])], inc="PE")
        v_ms = sch.emit("vector",
                        (lambda e: e.tensor_scalar_mul(SG[0][:], PB[4][:], 1.0 / D)),
                        waits=[("PE", v_rs), ("AC", state["last_silu"]),
                               ("AC", state["last_rowp"])], inc="DV")
        sch.emit("scalar",
                 (lambda e: e.activation(SG[0][:], SG[0][:], AF.Ln, bias=EPS6)),
                 waits=[("DV", v_ms)], inc="AC")
        v_rp = sch.emit("scalar",
                        (lambda e: e.activation(SG[0][:], SG[0][:], AF.Exp, scale=CMH)),
                        inc="AC")
        state["last_rowp"] = v_rp
        res_i = RES[i % 2]
        apply_last = 0
        for md in range(KC):
            sch.emit("vector",
                     (lambda e, md=md: e.scalar_tensor_tensor(
                         SB[:, md * TT:(md + 1) * TT], SB[:, md * TT:(md + 1) * TT],
                         NW[:, md:md + 1], SG[0][:, 0:TT], OP.mult, OP.mult)),
                     waits=[("AC", v_rp)], inc="DV")
            apply_last = sch.emit(
                "vector",
                (lambda e, md=md: e.tensor_add(res_i[:, md * TT:(md + 1) * TT],
                                               res_i[:, md * TT:(md + 1) * TT],
                                               SB[:, md * TT:(md + 1) * TT])),
                inc="DV")
        if DEBUG_TAPS and i == 1:
            tap("d_msrow", SG[0][0:1, :], dv_wait=v_ms)
            tap("d_rstd", ROWP[0:1, 0:TT], ac_wait=v_rp)
            tap("d_x2", res_i[:], dv_wait=apply_last)
        bfs1_war = tv[j]["gateup_pe"] if j < NT else tv[i]["ri_pe"]
        rr3 = ln_stats_bcast(i, res_i, PB[2], PB[3], bfs1_war, v_rs,
                             res_dv=apply_last, eps=EPS5)
        lnb3 = normalize_bf16(res_i, 0, 0, SG[0], SG[1], bfs1_war,
                              q_dv=rr3["q"], p_ac=rr3["p"])
        g2_war_ac = tv[j]["silu_last"] if j < NT else tv[i]["sig_i"]
        conv_mid = make_conv_mid(j, RES[j % 2], apply_last) if j < NT else None
        gateup2_pe, silu2_last, stt2_dv = ffn(
            res_i, lnb3, (PB[0], PB[1]), (PB[2], PB[3]), (PB[4], PB[5]),
            gate_war_ac=g2_war_ac, up_war_ac=0, up_war_dv=rr3["var"], mid=conv_mid)
        tv[i]["gateup2_pe"] = gateup2_pe
        tv[i]["stt2_dv"] = stt2_dv
        if j < NT:
            tv[j]["ri_war_ac"] = silu2_last
            tv[j]["ri_war_dv"] = lnb3
            tv[j]["ri_war_dv2"] = stt2_dv
        sch.emit("gpsimd",
                 (lambda e, i=i, res_i=res_i: e.dma_start(
                     out_h[:, :, i * TT:(i + 1) * TT],
                     res_i[:].rearrange("p (c t) -> p c t", c=KC))),
                 waits=[("DV", stt2_dv)], inc="OUT", incn=16)
        if i + 2 < NT:
            x_ready[i + 2] = sch.emit(
                "gpsimd",
                (lambda e, i=i: e.dma_start(
                    RES[i % 2][:].rearrange("p (c t) -> p c t", c=KC),
                    xh[:, :, (i + 2) * TT:(i + 3) * TT])),
                waits=[("OUT", 16 * (i + 1))], inc="X", incn=16)
        if j < NT:
            tv[j]["hatt_war"] = tv[i]["outw"]
            attn_proj(j)

    if False:
        d_x1 = nc.dram_tensor("d_x1", [P, T], F32, kind="ExternalOutput")
        d_hatt = nc.dram_tensor("d_hatt", [P, T], BF16, kind="ExternalOutput")
        d_r1 = nc.dram_tensor("d_r1", [P, T], F32, kind="ExternalOutput")
        d_i1 = nc.dram_tensor("d_i1", [P, T], F32, kind="ExternalOutput")
        d_xcb1 = nc.dram_tensor("d_xcb1", [P, T], BF16, kind="ExternalOutput")
        sch.emit("gpsimd", (lambda e: e.dma_start(d_x1[:], RES[1][:])),
                 waits=[("DV", tv[1]["stt_dv"])], inc="OUT", incn=0 if False else 16)
        sch.emit("gpsimd", (lambda e: e.dma_start(d_hatt[:], HATT[:])),
                 waits=[("DV", tv[0]["hatt"])], inc="OUT", incn=16)
        sch.emit("gpsimd", (lambda e: e.dma_start(d_r1[:], SC_[:])),
                 waits=[("AC", tv[1]["sig_r"])], inc="OUT", incn=16)
        sch.emit("gpsimd", (lambda e: e.dma_start(d_i1[:], SD[:])),
                 waits=[("AC", tv[1]["sig_i"])], inc="OUT", incn=16)
        sch.emit("gpsimd", (lambda e: e.dma_start(d_xcb1[:], BFS[:, 0:T])),
                 waits=[("DV", tv[1]["xcb"])], inc="OUT", incn=16)
        sch.ops["gpsimd"].append(([("OUT", 16 * NT + 80)], None, None))
    else:
        sch.ops["gpsimd"].append(([("OUT", sch.val("OUT"))], None, None))

    with nc.Block() as blk:
        def runner(lst):
            def body(e):
                for ws, fn, iv in lst:
                    for s, v in ws:
                        e.wait_ge(SEMS[s], v)
                    if fn is None:
                        continue
                    inst = fn(e)
                    if iv is not None:
                        inst.then_inc(SEMS[iv[0]], iv[1])
            return body
        blk.sync(runner(sch.ops["sync"]))
        blk.gpsimd(runner(sch.ops["gpsimd"]))
        blk.tensor(runner(sch.ops["tensor"]))
        blk.vector(runner(sch.ops["vector"]))
        blk.scalar(runner(sch.ops["scalar"]))
    return nc


def _host_layouts(inputs):
    wg = np.asarray(inputs["L0_ffn_w_gate"])
    wu = np.asarray(inputs["L0_ffn_w_up"])
    wd = np.asarray(inputs["L0_ffn_w_down"])
    wr = np.asarray(inputs["L0_attn_Wr_w"], dtype=np.float32)
    wi = np.asarray(inputs["L0_attn_Wi_w"], dtype=np.float32)
    wo = np.asarray(inputs["L0_attn_out_w"], dtype=np.float32)

    def hcmaj(w):
        return np.ascontiguousarray(
            w.reshape(HC, 128, KC, 128).transpose(3, 0, 2, 1).reshape(128, -1))

    def mdmaj(w):
        return np.ascontiguousarray(
            w.reshape(KC, 128, HC, 128).transpose(3, 0, 2, 1).reshape(128, -1))

    def mcmaj(w):
        return np.ascontiguousarray(
            w.reshape(KC, 128, KC, 128).transpose(3, 0, 2, 1).reshape(128, -1))

    def vec(v):
        return np.ascontiguousarray(np.asarray(v, dtype=np.float32).reshape(KC, 128).T)

    cwf = np.asarray(inputs["L0_attn_conv_weight"], dtype=np.float32)[:, 0, :]
    cw = np.ascontiguousarray(
        cwf.reshape(KC, 128, 4).transpose(1, 2, 0).reshape(128, 32))

    common = {
        "wg": hcmaj(wg), "wu": hcmaj(wu), "wd": mdmaj(wd),
        "wr": mcmaj(wr), "wi": mcmaj(wi), "wo": mcmaj(wo),
        "rb": vec(inputs["L0_attn_Wr_b"]), "ib": vec(inputs["L0_attn_Wi_b"]),
        "la": vec(inputs["L0_attn_log_a"]), "nw": vec(inputs["L0_attn_norm_w"]),
        "cw": cw, "cb": vec(inputs["L0_attn_conv_bias"]),
    }
    x = np.asarray(inputs["L0_attn_x"], dtype=np.float32)
    maps = []
    for b in range(8):
        xb = np.ascontiguousarray(
            x[b].T.reshape(KC, 128, T).transpose(1, 0, 2))
        m = dict(common)
        m["x"] = xb
        maps.append(m)
    return maps


def kernel(**inputs):
    in_maps = _host_layouts(inputs)
    nc = build_nc()
    res = run_bass_kernel_spmd(nc, in_maps, core_ids=list(range(8)))
    outs = []
    for b in range(8):
        o = np.asarray(res.results[b]["out"])
        outs.append(o.transpose(1, 0, 2).reshape(D, T).T)
    return np.stack(outs).astype(np.float32)


# version 10
# speedup vs baseline: 1.1908x; 1.0217x over previous
import numpy as np
import concourse.bass as bass
import concourse.mybir as mybir
from concourse.bass_utils import run_bass_kernel_spmd

F32 = mybir.dt.float32
BF16 = mybir.dt.bfloat16
I32 = mybir.dt.int32
AF = mybir.ActivationFunctionType
OP = mybir.AluOpType

D = 1024
KC = 8
H = 3072
HC = 24
T = 4096
TT = 512
NT = T // TT
P = 128
RING = 6
UNIT = 4096

S127 = 1.0 / 127.0
DOWN_SCALE = 0.5 * S127 * S127
LOG1EM7 = float(np.float32(np.log(np.float32(1e-7))))
DEBUG_TAPS = False
INTERLEAVE = True


class Sched:
    def __init__(self):
        self.ops = {"sync": [], "gpsimd": [], "tensor": [], "vector": [], "scalar": []}
        self.cnt = {}
        self.marks = {}

    def val(self, sem):
        return self.cnt.get(sem, 0)

    def emit(self, eng, fn, waits=(), inc=None, incn=1):
        ws = []
        for s, v in waits:
            if not v or v <= 0:
                continue
            key = (eng, s)
            if self.marks.get(key, 0) >= v:
                continue
            self.marks[key] = v
            ws.append((s, v))
        iv = None
        if inc is not None:
            self.cnt[inc] = self.cnt.get(inc, 0) + incn
            iv = (inc, incn)
        self.ops[eng].append((ws, fn, iv))
        return self.cnt.get(inc, 0) if inc else None


def build_nc():
    nc = bass.Bass("TRN2", target_bir_lowering=False)

    xh = nc.dram_tensor("x", [P, KC, T], F32, kind="ExternalInput")
    wg_i = nc.dram_tensor("wg", [P, HC * 8 * 128], I32, kind="ExternalInput")
    wu_i = nc.dram_tensor("wu", [P, HC * 8 * 128], I32, kind="ExternalInput")
    wd_i = nc.dram_tensor("wd", [P, 8 * HC * 128], I32, kind="ExternalInput")
    wr_i = nc.dram_tensor("wr", [P, KC * 1024], F32, kind="ExternalInput")
    wi_i = nc.dram_tensor("wi", [P, KC * 1024], F32, kind="ExternalInput")
    wo_i = nc.dram_tensor("wo", [P, KC * 1024], F32, kind="ExternalInput")
    rb_i = nc.dram_tensor("rb", [P, KC], F32, kind="ExternalInput")
    ib_i = nc.dram_tensor("ib", [P, KC], F32, kind="ExternalInput")
    la_i = nc.dram_tensor("la", [P, KC], F32, kind="ExternalInput")
    nw_i = nc.dram_tensor("nw", [P, KC], F32, kind="ExternalInput")
    cw_i = nc.dram_tensor("cw", [P, 32], F32, kind="ExternalInput")
    cb_i = nc.dram_tensor("cb", [P, KC], F32, kind="ExternalInput")
    out_h = nc.dram_tensor("out", [P, KC, T], F32, kind="ExternalOutput")

    wg_s = nc.dram_tensor("wg_s", [P, HC * 8 * 128], BF16, kind="Internal")
    wu_s = nc.dram_tensor("wu_s", [P, HC * 8 * 128], BF16, kind="Internal")
    wd_s = nc.dram_tensor("wd_s", [P, 8 * HC * 128], BF16, kind="Internal")
    wr_s = nc.dram_tensor("wr_s", [P, KC * 1024], BF16, kind="Internal")
    wi_s = nc.dram_tensor("wi_s", [P, KC * 1024], BF16, kind="Internal")
    wo_s = nc.dram_tensor("wo_s", [P, KC * 1024], BF16, kind="Internal")

    import contextlib
    ctx = contextlib.ExitStack()
    nc._kernel_ctx = ctx
    sb = lambda n, s, d=F32: ctx.enter_context(nc.sbuf_tensor(n, s, d))

    RES = [sb("RES0", [P, T]), sb("RES1", [P, T])]
    SA = sb("SA", [P, T])
    SB = sb("SB", [P, T])
    SC_ = sb("SC", [P, T])
    SD = sb("SD", [P, T])
    BFS = sb("BFS", [P, 2 * T], BF16)
    HATT = sb("HATT", [P, T], BF16)
    HH = sb("HH", [P, HC * TT], BF16)
    LTMP = sb("LTMP", [P, TT + 8])
    ROWP = LTMP
    TAIL = sb("TAIL", [P, KC * 4])
    WRG = [sb(f"WRG{j}", [P, UNIT], BF16) for j in range(RING)]
    SG = [sb("SG0", [P, TT]), sb("SG1", [P, TT])]
    ONES = sb("ONESF", [P, TT])
    ONESR = ONES
    ONESB = sb("ONESB", [P, P], BF16)
    CONSTS = sb("CONSTS", [P, 8])
    CW = sb("CWt", [P, 32])
    CB = sb("CBt", [P, KC])
    RB = sb("RBt", [P, KC])
    IB = sb("IBt", [P, KC])
    LAT = sb("LAt", [P, KC])
    NW = sb("NWt", [P, KC])
    L8 = sb("L8", [P, KC])
    L16 = sb("L16", [P, KC])
    SCAR = sb("SCAR", [P, KC])
    CCAR = sb("CCAR", [P, KC])
    RED = sb("REDt", [P, KC])
    PB = [ctx.enter_context(nc.psum_tensor(f"PB{j}", [P, TT], F32)) for j in range(8)]

    cvals = [S127, -1.0, -0.5, 1e-5, 1e-6]
    for j_, v in enumerate(cvals):
        nc.const_aps.register(F32, float(np.float32(v)), CONSTS[:, j_:j_ + 1])
    C127, CM1, CMH, EPS5, EPS6 = [float(np.float32(v)) for v in cvals]

    SEMS = {}
    for n in ["PE", "DV", "AC", "WL", "X", "OUT", "CV"]:
        SEMS[n] = ctx.enter_context(nc.semaphore("sem_" + n))

    sch = Sched()
    ring_state = {"u": 0, "done": {}, "loaded": {}}

    def unit_load(src_ap, ncols):
        u = ring_state["u"]
        ring_state["u"] += 1
        slot = WRG[u % RING]
        waits = [("CV", src_cast_done[src_ap.tensor.name])]
        prev = u - RING
        if prev >= 0:
            waits.append(("PE", ring_state["done"].get(prev, 0)))
        def fn(e, slot=slot, src=src_ap, ncols=ncols):
            return e.dma_start(slot[:, 0:ncols], src)
        v = sch.emit("sync", fn, waits=waits, inc="WL", incn=16)
        ring_state["loaded"][u] = v
        return slot, u

    def unit_used(u):
        ring_state["done"][u] = sch.val("PE")

    src_cast_done = {}
    def cast_dma(dst, src):
        def fn(e, dst=dst, src=src):
            return e.dma_start(dst[:], src[:])
        v = sch.emit("gpsimd", fn, inc="CV", incn=16)
        src_cast_done[dst.name] = v

    cast_dma(wg_s, wg_i)
    cast_dma(wu_s, wu_i)
    cast_dma(wd_s, wd_i)
    cast_dma(wr_s, wr_i)
    cast_dma(wi_s, wi_i)
    cast_dma(wo_s, wo_i)

    for dst, src in [(CW, cw_i), (CB, cb_i), (RB, rb_i), (IB, ib_i),
                     (LAT, la_i), (NW, nw_i)]:
        sch.emit("sync", (lambda e, d=dst, s=src: e.dma_start(d[:], s[:])),
                 inc="X", incn=16)
    x_ready = {}
    for i in range(2):
        x_ready[i] = sch.emit(
            "gpsimd",
            (lambda e, i=i: e.dma_start(RES[i % 2][:].rearrange("p (c t) -> p c t", c=KC),
                                        xh[:, :, i * TT:(i + 1) * TT])),
            inc="X", incn=16)

    def ms(ap, v):
        sch.emit("gpsimd", (lambda e, ap=ap, v=v: e.memset(ap, v)))
    ms(ONES[:], 1.0)
    ms(ONESB[:], 1.0)
    ms(TAIL[:], 0.0)
    ms(SCAR[:], 0.0)
    ms(CCAR[:], 0.0)
    for j_, v in enumerate(cvals):
        ms(CONSTS[:, j_:j_ + 1], float(np.float32(v)))
    setup_done = sch.emit("gpsimd", (lambda e: e.memset(CONSTS[:, 7:8], 0.0)),
                          inc="X", incn=1)

    sch.emit("scalar", (lambda e: e.activation(L8[:], LAT[:], AF.Sigmoid)),
             waits=[("X", setup_done)], inc="AC")
    v_ln = sch.emit("scalar", (lambda e: e.activation(L8[:], L8[:], AF.Ln)), inc="AC")
    sch.emit("vector",
             (lambda e: e.tensor_scalar(L16[:], L8[:], 16.0, -1e30, OP.mult, OP.max)),
             waits=[("AC", v_ln)], inc="DV")
    v_l8 = sch.emit("vector",
                    (lambda e: e.tensor_scalar(L8[:], L8[:], 8.0, -1e30, OP.mult, OP.max)),
                    inc="DV")

    tv = [dict() for _ in range(NT + 1)]
    state = {"scar_v": 0, "ccar_v": 0, "last_silu": 0, "last_rowp": 0}

    dbg = {}
    if DEBUG_TAPS:
        dbg["d_scar"] = nc.dram_tensor("d_scar", [P, KC], F32, kind="ExternalOutput")
        for nm in ["d_l8a", "d_l8b", "d_l8c", "d_l8d"]:
            dbg[nm] = nc.dram_tensor(nm, [P, KC], F32, kind="ExternalOutput")
        dbg["d_r1pre"] = nc.dram_tensor("d_r1pre", [P, T], F32, kind="ExternalOutput")
        dbg["d_xc1"] = nc.dram_tensor("d_xc1", [P, T], F32, kind="ExternalOutput")
        dbg["d_xcb1"] = nc.dram_tensor("d_xcb1", [P, T], BF16, kind="ExternalOutput")
        dbg["d_rstd2"] = nc.dram_tensor("d_rstd2", [1, TT], F32, kind="ExternalOutput")
        dbg["d_q2"] = nc.dram_tensor("d_q2", [1, TT], F32, kind="ExternalOutput")
        for nm, dt_ in [("d_x1", F32), ("d_o", F32), ("d_x2", F32), ("d_hatt", BF16),
                        ("d_logA", F32), ("d_aexp", F32), ("d_cs", F32), ("d_term", F32),
                        ("d_la", F32), ("d_s", F32),
                        ("d_msrow", F32), ("d_rstd", F32)]:
            dbg[nm] = nc.dram_tensor(nm, [P, T] if nm not in ("d_msrow", "d_rstd")
                                     else [1, TT], dt_, kind="ExternalOutput")

    def tap(name, src_ap, dv_wait=None, ac_wait=None):
        if not DEBUG_TAPS:
            return
        w = []
        if dv_wait: w.append(("DV", dv_wait))
        if ac_wait: w.append(("AC", ac_wait))
        v = sch.emit("gpsimd", (lambda e, s=src_ap, d=dbg[name]: e.dma_start(d[:], s)),
                     waits=w, inc="OUT", incn=16)
        sch.emit("vector", (lambda e: e.engine_nop()), waits=[("OUT", v)], inc="DV")
        state["dbg_out"] = v

    def ln_stats(i, res, pb_m, pb_v, bfs1_war_pe, bfs2_war_pe, res_dv=0,
                 extra_pe_waits=()):
        r = {}
        r["xb"] = sch.emit("vector", (lambda e: e.tensor_copy(BFS[:, 0:T], res[:])),
                           waits=[("PE", bfs1_war_pe), ("X", x_ready.get(i, 0)),
                                  ("X", setup_done), ("DV", res_dv)], inc="DV")
        r["xsq"] = sch.emit("scalar",
                            (lambda e: e.activation(BFS[:, T:2 * T], res[:], AF.Square)),
                            waits=[("PE", bfs2_war_pe), ("X", x_ready.get(i, 0)),
                                   ("X", setup_done), ("DV", res_dv)], inc="AC")

        def mm_stats(e, pb, off):
            last = None
            for kc in range(KC):
                last = e.matmul(pb[0:1, :], ONESB[:, 0:1],
                                BFS[:, off + kc * TT: off + (kc + 1) * TT],
                                start=(kc == 0), stop=(kc == KC - 1))
            return last
        r["sm"] = sch.emit("tensor", (lambda e: mm_stats(e, pb_m, 0)),
                           waits=[("DV", r["xb"]), ("X", setup_done)] + list(extra_pe_waits),
                           inc="PE")
        r["sv"] = sch.emit("tensor", (lambda e: mm_stats(e, pb_v, T)),
                           waits=[("AC", r["xsq"])], inc="PE")
        sch.emit("vector",
                 (lambda e: e.tensor_scalar_mul(SG[0][0:1, :], pb_m[0:1, :], 1.0 / D)),
                 waits=[("PE", r["sm"]), ("AC", state["last_silu"]),
                        ("AC", state["last_rowp"])], inc="DV")
        sch.emit("vector",
                 (lambda e: e.tensor_mul(ROWP[0:1, 0:TT], SG[0][0:1, :], SG[0][0:1, :])),
                 inc="DV")
        r["var"] = sch.emit("vector",
                            (lambda e: e.scalar_tensor_tensor(
                                SG[1][0:1, :], pb_v[0:1, :], 1.0 / D,
                                ROWP[0:1, 0:TT], OP.mult, OP.subtract)),
                            waits=[("PE", r["sv"])], inc="DV")
        sch.emit("scalar",
                 (lambda e: e.activation(SG[1][0:1, :], SG[1][0:1, :], AF.Ln, bias=EPS5)),
                 waits=[("DV", r["var"])], inc="AC")
        r["p"] = sch.emit("scalar",
                          (lambda e: e.activation(ROWP[0:1, 0:TT], SG[1][0:1, :],
                                                  AF.Exp, scale=CMH)),
                          inc="AC")
        state["last_rowp"] = r["p"]
        r["q"] = sch.emit("vector",
                          (lambda e: e.scalar_tensor_tensor(SG[0][0:1, :], SG[0][0:1, :],
                                                            -1.0, ROWP[0:1, 0:TT],
                                                            OP.mult, OP.mult)),
                          waits=[("AC", r["p"])], inc="DV")
        r["rows_dv"] = sch.val("DV")
        return r

    def bcast_rows(rr, pb_p, pb_q, war_ac=0, war_dv=0):
        v_bp = sch.emit("tensor",
                        (lambda e: e.matmul(pb_p[:], ONESR[0:1, 0:P], ROWP[0:1, 0:TT],
                                            start=True, stop=True)),
                        waits=[("AC", rr["p"]), ("AC", war_ac), ("DV", war_dv)], inc="PE")
        v_bq = sch.emit("tensor",
                        (lambda e: e.matmul(pb_q[:], ONESR[0:1, 0:P], SG[0][0:1, :],
                                            start=True, stop=True)),
                        waits=[("DV", rr["q"])], inc="PE")
        return v_bp, v_bq

    def ln_stats_bcast(i, res, pb_m, pb_v, bfs1_war_pe, bfs2_war_pe, res_dv=0,
                       eps=None, extra_pe_waits=()):
        r = {}
        r["xb"] = sch.emit("vector", (lambda e: e.tensor_copy(BFS[:, 0:T], res[:])),
                           waits=[("PE", bfs1_war_pe), ("X", x_ready.get(i, 0)),
                                  ("X", setup_done), ("DV", res_dv)], inc="DV")
        r["xsq"] = sch.emit("scalar",
                            (lambda e: e.activation(BFS[:, T:2 * T], res[:], AF.Square)),
                            waits=[("PE", bfs2_war_pe), ("X", x_ready.get(i, 0)),
                                   ("X", setup_done), ("DV", res_dv)], inc="AC")
        def mm_stats(e, pb, off):
            last = None
            for kc in range(KC):
                last = e.matmul(pb[:], ONESB[:, 0:P],
                                BFS[:, off + kc * TT: off + (kc + 1) * TT],
                                start=(kc == 0), stop=(kc == KC - 1))
            return last
        r["sm"] = sch.emit("tensor", (lambda e: mm_stats(e, pb_m, 0)),
                           waits=[("DV", r["xb"]), ("X", setup_done)] + list(extra_pe_waits),
                           inc="PE")
        r["sv"] = sch.emit("tensor", (lambda e: mm_stats(e, pb_v, T)),
                           waits=[("AC", r["xsq"])], inc="PE")
        sch.emit("vector",
                 (lambda e: e.tensor_scalar_mul(SG[1][:], pb_m[:], 1.0 / D)),
                 waits=[("PE", r["sm"]), ("AC", state["last_silu"]),
                        ("AC", state["last_rowp"])], inc="DV")
        sch.emit("vector", (lambda e: e.tensor_mul(SG[0][:], SG[1][:], SG[1][:])),
                 inc="DV")
        r["var"] = sch.emit("vector",
                            (lambda e: e.scalar_tensor_tensor(
                                SG[0][:], pb_v[:], 1.0 / D, SG[0][:],
                                OP.mult, OP.subtract)),
                            waits=[("PE", r["sv"])], inc="DV")
        sch.emit("scalar",
                 (lambda e: e.activation(SG[0][:], SG[0][:], AF.Ln, bias=eps)),
                 waits=[("DV", r["var"])], inc="AC")
        r["p"] = sch.emit("scalar",
                          (lambda e: e.activation(SG[0][:], SG[0][:], AF.Exp, scale=CMH)),
                          inc="AC")
        state["last_rowp"] = r["p"]
        r["q"] = sch.emit("vector",
                          (lambda e: e.scalar_tensor_tensor(SG[1][:], SG[1][:], -1.0,
                                                            SG[0][:], OP.mult, OP.mult)),
                          waits=[("AC", r["p"])], inc="DV")
        return r

    def ln2_stats_psum(j, res, bfs_war_pe, res_dv):
        r = {}
        r["xb"] = sch.emit("vector", (lambda e: e.tensor_copy(BFS[:, 0:T], res[:])),
                           waits=[("PE", bfs_war_pe), ("DV", res_dv)], inc="DV")
        r["xsq"] = sch.emit("scalar",
                            (lambda e: e.activation(BFS[:, T:2 * T], res[:], AF.Square)),
                            waits=[("PE", bfs_war_pe), ("DV", res_dv)], inc="AC")
        def mm_stats(e, pb, off):
            last = None
            for kc in range(KC):
                last = e.matmul(pb[:], ONESB[:, 0:P],
                                BFS[:, off + kc * TT: off + (kc + 1) * TT],
                                start=(kc == 0), stop=(kc == KC - 1))
            return last
        r["sm"] = sch.emit("tensor", (lambda e: mm_stats(e, PB[6], 0)),
                           waits=[("DV", r["xb"])], inc="PE")
        r["sv"] = sch.emit("tensor", (lambda e: mm_stats(e, PB[7], T)),
                           waits=[("AC", r["xsq"])], inc="PE")
        sch.emit("vector",
                 (lambda e: e.tensor_scalar_mul(SG[1][:], PB[6][:], 1.0 / D)),
                 waits=[("PE", r["sm"]), ("AC", state["last_silu"]),
                        ("AC", state["last_rowp"])], inc="DV")
        sch.emit("vector", (lambda e: e.tensor_mul(SG[0][:], SG[1][:], SG[1][:])),
                 inc="DV")
        r["var"] = sch.emit("vector",
                            (lambda e: e.scalar_tensor_tensor(
                                SG[0][:], PB[7][:], 1.0 / D, SG[0][:],
                                OP.mult, OP.subtract)),
                            waits=[("PE", r["sv"])], inc="DV")
        sch.emit("scalar",
                 (lambda e: e.activation(SG[0][:], SG[0][:], AF.Ln, bias=EPS5)),
                 waits=[("DV", r["var"])], inc="AC")
        r["bc2p"] = sch.emit("scalar",
                             (lambda e: e.activation(PB[6][:], SG[0][:], AF.Exp,
                                                     scale=CMH)),
                             inc="AC")
        state["last_rowp"] = r["bc2p"]
        r["bc2q"] = sch.emit("vector",
                             (lambda e: e.scalar_tensor_tensor(PB[7][:], SG[1][:], -1.0,
                                                               PB[6][:], OP.mult,
                                                               OP.mult)),
                             waits=[("AC", r["bc2p"])], inc="DV")
        tv[j]["bc2p"] = r["bc2p"]
        tv[j]["bc2q"] = r["bc2q"]
        return r

    def normalize_bf16(res, v_bp, v_bq, pb_p, pb_q, bfs1_war_pe, q_dv=0, p_ac=0):
        last = 0
        chunks = []
        pp = pb_p[:, 0:TT] if pb_p in (SG[0], SG[1]) else pb_p[:]
        qq = pb_q[:, 0:TT] if pb_q in (SG[0], SG[1]) else pb_q[:]
        for c in range(KC):
            sch.emit("vector",
                     (lambda e, c=c: e.tensor_mul(LTMP[:, 0:TT],
                                                  res[:, c * TT:(c + 1) * TT], pp)),
                     waits=[("PE", v_bp), ("PE", v_bq), ("PE", bfs1_war_pe),
                            ("DV", q_dv), ("AC", p_ac)], inc="DV")
            last = sch.emit("vector",
                            (lambda e, c=c: e.tensor_add(BFS[:, c * TT:(c + 1) * TT],
                                                         LTMP[:, 0:TT], qq)),
                            inc="DV")
            chunks.append(last)
        return last, chunks

    def mid_make(i):
        g = {}
        def g_la_ixc():
            if DEBUG_TAPS and i == 1:
                tap("d_l8c", L8[:], ac_wait=tv[i]["sig_r"])
                tap("d_r1pre", SC_[:], ac_wait=tv[i]["sig_r"])
            for c in range(KC):
                sch.emit("vector",
                         (lambda e, c=c: e.tensor_scalar(
                             SA[:, c * TT:(c + 1) * TT], SC_[:, c * TT:(c + 1) * TT],
                             L8[:, c:c + 1], LOG1EM7, OP.mult, OP.max)),
                         waits=[("AC", tv[i]["sig_r"]), ("DV", v_l8)], inc="DV")
            for c in range(KC):
                tv[i]["ixc"] = sch.emit(
                    "vector",
                    (lambda e, c=c: e.tensor_mul(
                        SD[:, c * TT:(c + 1) * TT], SD[:, c * TT:(c + 1) * TT],
                        SB[:, c * TT:(c + 1) * TT])),
                    waits=[("AC", tv[i]["sig_i"])], inc="DV")
        def a_a2():
            for c in range(KC):
                tv[i]["a2"] = sch.emit(
                    "scalar",
                    (lambda e, c=c: e.activation(
                        SB[:, c * TT:(c + 1) * TT], SC_[:, c * TT:(c + 1) * TT],
                        AF.Exp, scale=L16[:, c:c + 1])),
                    waits=[("DV", tv[i]["ixc"])], inc="AC")
        def g_v():
            sch.emit("vector",
                     (lambda e: e.tensor_scalar(SC_[:], SB[:], 1.0, -1.0,
                                                OP.subtract, OP.mult)),
                     waits=[("AC", tv[i]["a2"])], inc="DV")
            tv[i]["vmax"] = sch.emit(
                "vector", (lambda e: e.tensor_scalar_max(SC_[:], SC_[:], 1e-6)), inc="DV")
        def a_bamp():
            tv[i]["bamp"] = sch.emit(
                "scalar", (lambda e: e.activation(SB[:], SC_[:], AF.Sqrt)),
                waits=[("DV", tv[i]["vmax"])], inc="AC")
        def g_bt():
            tv[i]["bt"] = sch.emit(
                "vector", (lambda e: e.tensor_mul(SD[:], SD[:], SB[:])),
                waits=[("AC", tv[i]["bamp"])], inc="DV")
        def g_ss(c0, c1):
            def f():
                for c in range(c0, c1):
                    sch.emit("vector",
                             (lambda e, c=c: e.tensor_tensor_scan(
                                 SB[:, c * TT:(c + 1) * TT], ONES[:],
                                 SA[:, c * TT:(c + 1) * TT], SCAR[:, c:c + 1],
                                 OP.mult, OP.add)),
                             waits=[("DV", state["scar_v"]), ("DV", tv[i]["bt"])],
                             inc="DV")
            return f
        def g_sred():
            for c in range(KC):
                sch.emit("vector",
                         (lambda e, c=c: e.tensor_reduce(
                             RED[:, c:c + 1], SA[:, c * TT:(c + 1) * TT],
                             mybir.AxisListType.X, OP.add)),
                         inc="DV")
            sch.emit("vector", (lambda e: e.engine_nop()), inc="DV")
            sch.emit("vector", (lambda e: e.engine_nop()), inc="DV")
            state["scar_v"] = sch.emit(
                "vector", (lambda e: e.tensor_add(SCAR[:], SCAR[:], RED[:])), inc="DV")
        def g_logA():
            tv[i]["logA"] = sch.emit(
                "vector", (lambda e: e.tensor_scalar_max(SB[:], SB[:], -80.0)), inc="DV")
            if DEBUG_TAPS and i == 1:
                tap("d_logA", SB[:], dv_wait=tv[i]["logA"])
        def a_en():
            tv[i]["en"] = sch.emit(
                "scalar", (lambda e: e.activation(SA[:], SB[:], AF.Exp, scale=CM1)),
                waits=[("DV", tv[i]["logA"])], inc="AC")
        def g_term():
            tv[i]["term"] = sch.emit(
                "vector", (lambda e: e.tensor_mul(SD[:], SD[:], SA[:])),
                waits=[("AC", tv[i]["en"])], inc="DV")
            if DEBUG_TAPS and i == 1:
                tap("d_term", SD[:], dv_wait=tv[i]["term"])
        def a_aexp():
            tv[i]["aexp"] = sch.emit(
                "scalar", (lambda e: e.activation(SC_[:], SB[:], AF.Exp)),
                waits=[("DV", tv[i]["term"])], inc="AC")
        def g_cs(c0, c1):
            def f():
                for c in range(c0, c1):
                    sch.emit("vector",
                             (lambda e, c=c: e.tensor_tensor_scan(
                                 SA[:, c * TT:(c + 1) * TT], ONES[:],
                                 SD[:, c * TT:(c + 1) * TT], CCAR[:, c:c + 1],
                                 OP.mult, OP.add)),
                             waits=[("DV", state["ccar_v"]), ("DV", tv[i]["term"])],
                             inc="DV")
            return f
        def g_cred():
            for c in range(KC):
                sch.emit("vector",
                         (lambda e, c=c: e.tensor_reduce(
                             RED[:, c:c + 1], SD[:, c * TT:(c + 1) * TT],
                             mybir.AxisListType.X, OP.add)),
                         inc="DV")
            sch.emit("vector", (lambda e: e.engine_nop()), inc="DV")
            sch.emit("vector", (lambda e: e.engine_nop()), inc="DV")
            state["ccar_v"] = sch.emit(
                "vector", (lambda e: e.tensor_add(CCAR[:], CCAR[:], RED[:])), inc="DV")
        def g_hatt():
            if DEBUG_TAPS and i == 1:
                tap("d_aexp", SC_[:], ac_wait=tv[i]["aexp"])
                tap("d_cs", SA[:], dv_wait=state["ccar_v"])
            tv[i]["hatt"] = sch.emit(
                "vector", (lambda e: e.tensor_mul(HATT[:], SC_[:], SA[:])),
                waits=[("AC", tv[i]["aexp"]),
                       ("PE", tv[i - 1].get("outw", 0) if i > 0 else 0)],
                inc="DV")
        g["dve"] = {2: [g_la_ixc], 4: [g_v], 6: [g_bt], 8: [g_ss(0, 4)],
                    10: [g_ss(4, 8)], 12: [g_sred, g_logA], 14: [g_term],
                    16: [g_cs(0, 4)], 18: [g_cs(4, 8)], 20: [g_cred], 21: [g_hatt]}
        g["act"] = {2: [a_a2], 4: [a_bamp], 13: [a_en], 17: [a_aexp]}
        g["all"] = [g_la_ixc, a_a2, g_v, a_bamp, g_bt, g_ss(0, 8), g_sred, g_logA,
                    a_en, g_term, a_aexp, g_cs(0, 8), g_cred, g_hatt]
        return g

    def ffn(res, lnb_ready_dv, pg_pair, pu_pair, pd_pair,
            gate_war_ac=0, up_war_ac=0, up_war_dv=0, mid=None, lnb_chunks=None):
        silu_done = {}
        h_done = {}
        gslot = uslot = None
        gu = uu = None
        for hc in range(HC):
            if mid is not None and INTERLEAVE:
                for fn_ in mid["dve"].get(hc, []):
                    fn_()
                for fn_ in mid["act"].get(hc, []):
                    fn_()
            if hc % 4 == 0:
                gslot, gu = unit_load(wg_s[:, hc * 1024:(hc + 4) * 1024], UNIT)
                uslot, uu = unit_load(wu_s[:, hc * 1024:(hc + 4) * 1024], UNIT)
            pg = pg_pair[hc % 2]
            pu = pu_pair[hc % 2]
            base = (hc % 4) * 1024
            if hc < 2 and lnb_chunks is not None:
                gv = uv = 0
                for kc in range(KC):
                    gw = [("DV", lnb_chunks[kc])]
                    uw = []
                    if kc == 0:
                        gw += [("WL", ring_state["loaded"][gu]),
                               ("AC", silu_done.get(hc - 2, gate_war_ac))]
                        uw += [("WL", ring_state["loaded"][uu]),
                               ("DV", h_done.get(hc - 2, up_war_dv)),
                               ("AC", up_war_ac)]
                    gv = sch.emit(
                        "tensor",
                        (lambda e, pg=pg, gslot=gslot, base=base, kc=kc:
                         e.matmul(pg[:], gslot[:, base + kc * 128: base + kc * 128 + 128],
                                  BFS[:, kc * TT:(kc + 1) * TT],
                                  start=(kc == 0), stop=(kc == KC - 1))),
                        waits=gw, inc="PE")
                    uv = sch.emit(
                        "tensor",
                        (lambda e, pu=pu, uslot=uslot, base=base, kc=kc:
                         e.matmul(pu[:], uslot[:, base + kc * 128: base + kc * 128 + 128],
                                  BFS[:, kc * TT:(kc + 1) * TT],
                                  start=(kc == 0), stop=(kc == KC - 1))),
                        waits=uw, inc="PE")
                unit_used(gu)
                unit_used(uu)
                sg = SG[hc % 2]
                sv = sch.emit("scalar",
                              (lambda e, sg=sg, pg=pg: e.activation(sg[:], pg[:], AF.Silu,
                                                                    scale=C127)),
                              waits=[("PE", gv), ("DV", h_done.get(hc - 2, 0))], inc="AC")
                silu_done[hc] = sv
                state["last_silu"] = sv
                h_done[hc] = sch.emit(
                    "vector",
                    (lambda e, sg=sg, pu=pu, hc=hc:
                     e.tensor_mul(HH[:, hc * TT:(hc + 1) * TT], sg[:], pu[:])),
                    waits=[("AC", sv), ("PE", uv)], inc="DV")
                continue
            def mm_gate(e, pg=pg, gslot=gslot, base=base):
                last = None
                for kc in range(KC):
                    last = e.matmul(pg[:], gslot[:, base + kc * 128: base + kc * 128 + 128],
                                    BFS[:, kc * TT:(kc + 1) * TT],
                                    start=(kc == 0), stop=(kc == KC - 1))
                return last
            gv = sch.emit("tensor", mm_gate,
                          waits=[("WL", ring_state["loaded"][gu]), ("DV", lnb_ready_dv),
                                 ("AC", silu_done.get(hc - 2, gate_war_ac))], inc="PE")
            unit_used(gu)
            def mm_up(e, pu=pu, uslot=uslot, base=base):
                last = None
                for kc in range(KC):
                    last = e.matmul(pu[:], uslot[:, base + kc * 128: base + kc * 128 + 128],
                                    BFS[:, kc * TT:(kc + 1) * TT],
                                    start=(kc == 0), stop=(kc == KC - 1))
                return last
            uv = sch.emit("tensor", mm_up,
                          waits=[("WL", ring_state["loaded"][uu]),
                                 ("DV", h_done.get(hc - 2, up_war_dv)),
                                 ("AC", up_war_ac if hc < 2 else 0)], inc="PE")
            unit_used(uu)
            sg = SG[hc % 2]
            sv = sch.emit("scalar",
                          (lambda e, sg=sg, pg=pg: e.activation(sg[:], pg[:], AF.Silu,
                                                                scale=C127)),
                          waits=[("PE", gv), ("DV", h_done.get(hc - 2, 0))], inc="AC")
            silu_done[hc] = sv
            state["last_silu"] = sv
            h_done[hc] = sch.emit(
                "vector",
                (lambda e, sg=sg, pu=pu, hc=hc:
                 e.tensor_mul(HH[:, hc * TT:(hc + 1) * TT], sg[:], pu[:])),
                waits=[("AC", sv), ("PE", uv)], inc="DV")
        gateup_pe = sch.val("PE")
        silu_last = silu_done[HC - 1]
        if mid is not None and not INTERLEAVE:
            for fn_ in mid["all"]:
                fn_()
        stt_done = {}
        for md in range(KC):
            dslot, du = unit_load(wd_s[:, md * HC * 128:(md + 1) * HC * 128], HC * 128)
            pd = pd_pair[md % 2]
            def mm_down(e, pd=pd, dslot=dslot):
                last = None
                for hc in range(HC):
                    last = e.matmul(pd[:], dslot[:, hc * 128: hc * 128 + 128],
                                    HH[:, hc * TT:(hc + 1) * TT],
                                    start=(hc == 0), stop=(hc == HC - 1))
                return last
            dv_ = sch.emit("tensor", mm_down,
                           waits=[("WL", ring_state["loaded"][du]), ("DV", h_done[HC - 1]),
                                  ("DV", stt_done.get(md - 2, 0))], inc="PE")
            unit_used(du)
            stt_done[md] = sch.emit(
                "vector",
                (lambda e, pd=pd, md=md: e.scalar_tensor_tensor(
                    res[:, md * TT:(md + 1) * TT], pd[:], DOWN_SCALE,
                    res[:, md * TT:(md + 1) * TT], OP.mult, OP.add)),
                waits=[("PE", dv_)], inc="DV")
        return gateup_pe, silu_last, sch.val("DV")

    def proj8(wsrc, pp_pair, dst, bias, rhs_dv, war_ac, war_dv, sig_store):
        units = []
        last_pe = 0
        for mc in range(KC):
            if mc % 4 == 0:
                units.append(unit_load(wsrc[:, mc * 1024:(mc + 4) * 1024], UNIT))
            slot, u = units[-1]
            pr = pp_pair[mc % 2]
            base = (mc % 4) * 1024
            def mm(e, pr=pr, slot=slot, base=base):
                last = None
                for kc in range(KC):
                    last = e.matmul(pr[:], slot[:, base + kc * 128: base + kc * 128 + 128],
                                    HATT[:, kc * TT:(kc + 1) * TT],
                                    start=(kc == 0), stop=(kc == KC - 1))
                return last
            rv = sch.emit("tensor", mm,
                          waits=[("WL", ring_state["loaded"][u]), ("DV", rhs_dv),
                                 ("AC", sig_store.get(mc - 2, war_ac)),
                                 ("DV", war_dv if mc < 2 else 0)], inc="PE")
            last_pe = rv
            unit_used(u)
            sig_store[mc] = sch.emit(
                "scalar",
                (lambda e, pr=pr, mc=mc: e.activation(
                    dst[:, mc * TT:(mc + 1) * TT], pr[:], AF.Sigmoid,
                    bias=bias[:, mc:mc + 1])),
                waits=[("PE", rv)], inc="AC")
        return last_pe

    def make_conv_mid(j, res_j, conv_war_dv):
        def mk(c):
            def f():
                sch.emit("vector",
                         (lambda e: e.tensor_mul(SD[:, 0:TT],
                                                 res_j[:, c * TT:(c + 1) * TT], PB[6][:])),
                         waits=[("AC", tv[j]["bc2p"]), ("DV", tv[j]["bc2q"]),
                                ("DV", conv_war_dv)], inc="DV")
                sch.emit("vector",
                         (lambda e: e.tensor_add(LTMP[:, 3:3 + TT], SD[:, 0:TT],
                                                 PB[7][:])), inc="DV")
                sch.emit("vector",
                         (lambda e: e.tensor_copy(LTMP[:, 0:3],
                                                  TAIL[:, c * 4:c * 4 + 3])), inc="DV")
                sch.emit("vector",
                         (lambda e: e.tensor_copy(TAIL[:, c * 4:c * 4 + 3],
                                                  LTMP[:, TT:TT + 3])), inc="DV")
                sch.emit("vector",
                         (lambda e: e.tensor_scalar(SB[:, c * TT:(c + 1) * TT],
                                                    LTMP[:, 0:TT], CW[:, c:c + 1],
                                                    CB[:, c:c + 1], OP.mult, OP.add)),
                         inc="DV")
                for k in range(1, 4):
                    sch.emit("vector",
                             (lambda e, k=k: e.scalar_tensor_tensor(
                                 SB[:, c * TT:(c + 1) * TT], LTMP[:, k:k + TT],
                                 CW[:, k * 8 + c:k * 8 + c + 1],
                                 SB[:, c * TT:(c + 1) * TT], OP.mult, OP.add)),
                             inc="DV")
                tv[j]["conv_dv"] = sch.val("DV")
            return f
        return {"dve": {3 + 2 * c: [mk(c)] for c in range(KC)}, "act": {}}

    def attn_proj(j):
        tv[j]["xcb"] = sch.emit("vector", (lambda e: e.tensor_copy(HATT[:], SB[:])),
                                waits=[("PE", tv[j]["hatt_war"])], inc="DV")
        sig_r = {}
        sig_i = {}
        proj8(wr_s, (PB[0], PB[1]), SC_, RB, tv[j]["xcb"],
              tv[j].get("ri_war_ac", 0), tv[j].get("ri_war_dv", 0), sig_r)
        tv[j]["ri_pe"] = proj8(wi_s, (PB[2], PB[3]), SD, IB, tv[j]["xcb"],
                               tv[j].get("ri_war_ac", 0), tv[j].get("ri_war_dv2", 0), sig_i)
        tv[j]["sig_r"] = sig_r[KC - 1]
        tv[j]["sig_i"] = sig_i[KC - 1]

    def attn_tail(j, res_j, conv_war_dv):
        for c in range(KC):
            sch.emit("vector",
                     (lambda e, c=c: e.tensor_mul(SD[:, 0:TT],
                                                  res_j[:, c * TT:(c + 1) * TT], PB[6][:])),
                     waits=[("AC", tv[j]["bc2p"]), ("DV", tv[j]["bc2q"]),
                            ("DV", conv_war_dv)], inc="DV")
            sch.emit("vector",
                     (lambda e: e.tensor_add(LTMP[:, 3:3 + TT], SD[:, 0:TT], PB[7][:])),
                     inc="DV")
            sch.emit("vector",
                     (lambda e, c=c: e.tensor_copy(LTMP[:, 0:3], TAIL[:, c * 4:c * 4 + 3])),
                     inc="DV")
            sch.emit("vector",
                     (lambda e, c=c: e.tensor_copy(TAIL[:, c * 4:c * 4 + 3],
                                                   LTMP[:, TT:TT + 3])),
                     inc="DV")
            sch.emit("vector",
                     (lambda e, c=c: e.tensor_scalar(SB[:, c * TT:(c + 1) * TT],
                                                     LTMP[:, 0:TT], CW[:, c:c + 1],
                                                     CB[:, c:c + 1], OP.mult, OP.add)),
                     inc="DV")
            for k in range(1, 4):
                sch.emit("vector",
                         (lambda e, c=c, k=k: e.scalar_tensor_tensor(
                             SB[:, c * TT:(c + 1) * TT], LTMP[:, k:k + TT],
                             CW[:, k * 8 + c:k * 8 + c + 1],
                             SB[:, c * TT:(c + 1) * TT], OP.mult, OP.add)),
                         inc="DV")
        tv[j]["conv_dv"] = sch.val("DV")
        if DEBUG_TAPS and j == 1:
            tap("d_xc1", SB[:], dv_wait=tv[j]["conv_dv"])
        tv[j]["xcb"] = sch.emit("vector", (lambda e: e.tensor_copy(HATT[:], SB[:])),
                                waits=[("PE", tv[j]["hatt_war"])], inc="DV")
        if DEBUG_TAPS and j == 1:
            tap("d_xcb1", BFS[:, 0:T], dv_wait=tv[j]["xcb"])
        sig_r = {}
        sig_i = {}
        proj8(wr_s, (PB[0], PB[1]), SC_, RB, tv[j]["xcb"],
              tv[j].get("ri_war_ac", 0), tv[j].get("ri_war_dv", 0), sig_r)
        tv[j]["ri_pe"] = proj8(wi_s, (PB[2], PB[3]), SD, IB, tv[j]["xcb"],
                               tv[j].get("ri_war_ac", 0), tv[j].get("ri_war_dv2", 0), sig_i)
        tv[j]["sig_r"] = sig_r[KC - 1]
        tv[j]["sig_i"] = sig_i[KC - 1]

    if DEBUG_TAPS:
        tap("d_l8a", L8[:], dv_wait=v_l8)
    rr = ln_stats_bcast(0, RES[0], PB[6], PB[7], 0, 0, eps=EPS5)
    lnb, lnbc = normalize_bf16(RES[0], 0, 0, SG[0], SG[1], 0,
                               q_dv=rr["q"], p_ac=rr["p"])
    gateup_pe, silu_last, stt_dv = ffn(RES[0], lnb, (PB[0], PB[1]), (PB[2], PB[3]),
                                       (PB[4], PB[5]), lnb_chunks=lnbc)
    tv[0]["gateup_pe"] = gateup_pe
    tv[0]["silu_last"] = silu_last
    tv[0]["stt_dv"] = stt_dv
    rr2 = ln2_stats_psum(0, RES[0], gateup_pe, stt_dv)
    tv[0]["ln2_sv"] = rr2["sv"]
    tv[0]["bc2p"], tv[0]["bc2q"] = rr2["bc2p"], rr2["bc2q"]
    tv[0]["ri_war_ac"] = silu_last
    tv[0]["ri_war_dv"] = lnb
    tv[0]["ri_war_dv2"] = stt_dv
    tv[0]["hatt_war"] = 0
    attn_tail(0, RES[0], 0)
    if DEBUG_TAPS:
        tap("d_l8b", L8[:], ac_wait=tv[0]["sig_i"])

    for i in range(NT):
        j = i + 1
        mid = mid_make(i)
        rr2 = None
        if j < NT:
            res_j = RES[j % 2]
            bfs1w = tv[i - 1]["gateup2_pe"] if i > 0 else tv[i]["ln2_sv"]
            rr = ln_stats_bcast(j, res_j, PB[6], PB[7],
                                bfs1w, tv[i]["ln2_sv"], eps=EPS5,
                                extra_pe_waits=[("DV", tv[i].get("conv_dv", 0)),
                                                ("AC", tv[i]["sig_i"])])
            lnb, lnbc = normalize_bf16(res_j, 0, 0, SG[0], SG[1], bfs1w,
                                 q_dv=rr["q"], p_ac=rr["p"])
            tv[j]["lnb"] = lnb
            gateup_pe, silu_last, stt_dv = ffn(
                res_j, lnb, (PB[0], PB[1]), (PB[2], PB[3]), (PB[4], PB[5]),
                gate_war_ac=state["last_silu"], up_war_ac=tv[i]["sig_i"],
                up_war_dv=tv[i].get("stt2_dv", tv[i].get("stt_dv", 0)), mid=mid,
                lnb_chunks=lnbc)
            tv[j]["gateup_pe"] = gateup_pe
            tv[j]["silu_last"] = silu_last
            tv[j]["stt_dv"] = stt_dv
            if DEBUG_TAPS and j == 1:
                tap("d_x1", RES[1][:], dv_wait=stt_dv)
            rr2 = ln2_stats_psum(j, res_j, gateup_pe, stt_dv)
            tv[j]["ln2_sv"] = rr2["sv"]
        else:
            for fn_ in mid["all"]:
                fn_()

        o_copy = {}
        units = []
        pb45_war = tv[j]["stt_dv"] if j < NT else tv[i - 1]["stt2_dv"]
        for md in range(KC):
            if md % 4 == 0:
                units.append(unit_load(wo_s[:, md * 1024:(md + 4) * 1024], UNIT))
            slot, u = units[-1]
            po = PB[4 + md % 2]
            base = (md % 4) * 1024
            def mm_o(e, po=po, slot=slot, base=base):
                last = None
                for kc in range(KC):
                    last = e.matmul(po[:], slot[:, base + kc * 128: base + kc * 128 + 128],
                                    HATT[:, kc * TT:(kc + 1) * TT],
                                    start=(kc == 0), stop=(kc == KC - 1))
                return last
            ov = sch.emit("tensor", mm_o,
                          waits=[("WL", ring_state["loaded"][u]), ("DV", tv[i]["hatt"]),
                                 ("DV", pb45_war), ("DV", o_copy.get(md - 2, 0))],
                          inc="PE")
            unit_used(u)
            o_copy[md] = sch.emit(
                "vector",
                (lambda e, po=po, md=md: e.tensor_copy(SB[:, md * TT:(md + 1) * TT], po[:])),
                waits=[("PE", ov)], inc="DV")
        tv[i]["outw"] = sch.val("PE")
        if DEBUG_TAPS and i == 1:
            tap("d_hatt", HATT[:], dv_wait=tv[i]["hatt"])
            tap("d_o", SB[:], dv_wait=o_copy[KC - 1])
        osq_last = 0
        bfs2_war = tv[j]["ln2_sv"] if j < NT else tv[i]["ln2_sv"]
        for md in range(KC):
            osq_last = sch.emit(
                "scalar",
                (lambda e, md=md: e.activation(
                    BFS[:, T + md * TT:T + (md + 1) * TT],
                    SB[:, md * TT:(md + 1) * TT], AF.Square)),
                waits=[("DV", o_copy[md]), ("PE", bfs2_war)], inc="AC")
        def mm_rms(e):
            last = None
            for md in range(KC):
                last = e.matmul(PB[4][:], ONESB[:, 0:P],
                                BFS[:, T + md * TT:T + (md + 1) * TT],
                                start=(md == 0), stop=(md == KC - 1))
            return last
        v_rs = sch.emit("tensor", mm_rms,
                        waits=[("AC", osq_last), ("DV", o_copy[KC - 1])], inc="PE")
        v_ms = sch.emit("vector",
                        (lambda e: e.tensor_scalar_mul(SG[0][:], PB[4][:], 1.0 / D)),
                        waits=[("PE", v_rs), ("AC", state["last_silu"]),
                               ("AC", state["last_rowp"])], inc="DV")
        sch.emit("scalar",
                 (lambda e: e.activation(SG[0][:], SG[0][:], AF.Ln, bias=EPS6)),
                 waits=[("DV", v_ms)], inc="AC")
        v_rp = sch.emit("scalar",
                        (lambda e: e.activation(SG[0][:], SG[0][:], AF.Exp, scale=CMH)),
                        inc="AC")
        state["last_rowp"] = v_rp
        res_i = RES[i % 2]
        apply_last = 0
        for md in range(KC):
            sch.emit("vector",
                     (lambda e, md=md: e.scalar_tensor_tensor(
                         SB[:, md * TT:(md + 1) * TT], SB[:, md * TT:(md + 1) * TT],
                         NW[:, md:md + 1], SG[0][:, 0:TT], OP.mult, OP.mult)),
                     waits=[("AC", v_rp)], inc="DV")
            apply_last = sch.emit(
                "vector",
                (lambda e, md=md: e.tensor_add(res_i[:, md * TT:(md + 1) * TT],
                                               res_i[:, md * TT:(md + 1) * TT],
                                               SB[:, md * TT:(md + 1) * TT])),
                inc="DV")
        if DEBUG_TAPS and i == 1:
            tap("d_msrow", SG[0][0:1, :], dv_wait=v_ms)
            tap("d_rstd", ROWP[0:1, 0:TT], ac_wait=v_rp)
            tap("d_x2", res_i[:], dv_wait=apply_last)
        bfs1_war = tv[j]["gateup_pe"] if j < NT else tv[i]["ri_pe"]
        rr3 = ln_stats_bcast(i, res_i, PB[2], PB[3], bfs1_war, v_rs,
                             res_dv=apply_last, eps=EPS5)
        lnb3, lnb3c = normalize_bf16(res_i, 0, 0, SG[0], SG[1], bfs1_war,
                              q_dv=rr3["q"], p_ac=rr3["p"])
        g2_war_ac = tv[j]["silu_last"] if j < NT else tv[i]["sig_i"]
        conv_mid = make_conv_mid(j, RES[j % 2], apply_last) if j < NT else None
        gateup2_pe, silu2_last, stt2_dv = ffn(
            res_i, lnb3, (PB[0], PB[1]), (PB[2], PB[3]), (PB[4], PB[5]),
            gate_war_ac=g2_war_ac, up_war_ac=0, up_war_dv=rr3["var"], mid=conv_mid,
            lnb_chunks=lnb3c)
        tv[i]["gateup2_pe"] = gateup2_pe
        tv[i]["stt2_dv"] = stt2_dv
        if j < NT:
            tv[j]["ri_war_ac"] = silu2_last
            tv[j]["ri_war_dv"] = lnb3
            tv[j]["ri_war_dv2"] = stt2_dv
        sch.emit("gpsimd",
                 (lambda e, i=i, res_i=res_i: e.dma_start(
                     out_h[:, :, i * TT:(i + 1) * TT],
                     res_i[:].rearrange("p (c t) -> p c t", c=KC))),
                 waits=[("DV", stt2_dv)], inc="OUT", incn=16)
        if i + 2 < NT:
            x_ready[i + 2] = sch.emit(
                "gpsimd",
                (lambda e, i=i: e.dma_start(
                    RES[i % 2][:].rearrange("p (c t) -> p c t", c=KC),
                    xh[:, :, (i + 2) * TT:(i + 3) * TT])),
                waits=[("OUT", 16 * (i + 1))], inc="X", incn=16)
        if j < NT:
            tv[j]["hatt_war"] = tv[i]["outw"]
            attn_proj(j)

    if False:
        d_x1 = nc.dram_tensor("d_x1", [P, T], F32, kind="ExternalOutput")
        d_hatt = nc.dram_tensor("d_hatt", [P, T], BF16, kind="ExternalOutput")
        d_r1 = nc.dram_tensor("d_r1", [P, T], F32, kind="ExternalOutput")
        d_i1 = nc.dram_tensor("d_i1", [P, T], F32, kind="ExternalOutput")
        d_xcb1 = nc.dram_tensor("d_xcb1", [P, T], BF16, kind="ExternalOutput")
        sch.emit("gpsimd", (lambda e: e.dma_start(d_x1[:], RES[1][:])),
                 waits=[("DV", tv[1]["stt_dv"])], inc="OUT", incn=0 if False else 16)
        sch.emit("gpsimd", (lambda e: e.dma_start(d_hatt[:], HATT[:])),
                 waits=[("DV", tv[0]["hatt"])], inc="OUT", incn=16)
        sch.emit("gpsimd", (lambda e: e.dma_start(d_r1[:], SC_[:])),
                 waits=[("AC", tv[1]["sig_r"])], inc="OUT", incn=16)
        sch.emit("gpsimd", (lambda e: e.dma_start(d_i1[:], SD[:])),
                 waits=[("AC", tv[1]["sig_i"])], inc="OUT", incn=16)
        sch.emit("gpsimd", (lambda e: e.dma_start(d_xcb1[:], BFS[:, 0:T])),
                 waits=[("DV", tv[1]["xcb"])], inc="OUT", incn=16)
        sch.ops["gpsimd"].append(([("OUT", 16 * NT + 80)], None, None))
    else:
        sch.ops["gpsimd"].append(([("OUT", sch.val("OUT"))], None, None))

    with nc.Block() as blk:
        def runner(lst):
            def body(e):
                for ws, fn, iv in lst:
                    for s, v in ws:
                        e.wait_ge(SEMS[s], v)
                    if fn is None:
                        continue
                    inst = fn(e)
                    if iv is not None:
                        inst.then_inc(SEMS[iv[0]], iv[1])
            return body
        blk.sync(runner(sch.ops["sync"]))
        blk.gpsimd(runner(sch.ops["gpsimd"]))
        blk.tensor(runner(sch.ops["tensor"]))
        blk.vector(runner(sch.ops["vector"]))
        blk.scalar(runner(sch.ops["scalar"]))
    return nc


def _host_layouts(inputs):
    wg = np.asarray(inputs["L0_ffn_w_gate"])
    wu = np.asarray(inputs["L0_ffn_w_up"])
    wd = np.asarray(inputs["L0_ffn_w_down"])
    wr = np.asarray(inputs["L0_attn_Wr_w"], dtype=np.float32)
    wi = np.asarray(inputs["L0_attn_Wi_w"], dtype=np.float32)
    wo = np.asarray(inputs["L0_attn_out_w"], dtype=np.float32)

    def hcmaj(w):
        return np.ascontiguousarray(
            w.reshape(HC, 128, KC, 128).transpose(3, 0, 2, 1).reshape(128, -1))

    def mdmaj(w):
        return np.ascontiguousarray(
            w.reshape(KC, 128, HC, 128).transpose(3, 0, 2, 1).reshape(128, -1))

    def mcmaj(w):
        return np.ascontiguousarray(
            w.reshape(KC, 128, KC, 128).transpose(3, 0, 2, 1).reshape(128, -1))

    def vec(v):
        return np.ascontiguousarray(np.asarray(v, dtype=np.float32).reshape(KC, 128).T)

    cwf = np.asarray(inputs["L0_attn_conv_weight"], dtype=np.float32)[:, 0, :]
    cw = np.ascontiguousarray(
        cwf.reshape(KC, 128, 4).transpose(1, 2, 0).reshape(128, 32))

    common = {
        "wg": hcmaj(wg), "wu": hcmaj(wu), "wd": mdmaj(wd),
        "wr": mcmaj(wr), "wi": mcmaj(wi), "wo": mcmaj(wo),
        "rb": vec(inputs["L0_attn_Wr_b"]), "ib": vec(inputs["L0_attn_Wi_b"]),
        "la": vec(inputs["L0_attn_log_a"]), "nw": vec(inputs["L0_attn_norm_w"]),
        "cw": cw, "cb": vec(inputs["L0_attn_conv_bias"]),
    }
    x = np.asarray(inputs["L0_attn_x"], dtype=np.float32)
    maps = []
    for b in range(8):
        xb = np.ascontiguousarray(
            x[b].T.reshape(KC, 128, T).transpose(1, 0, 2))
        m = dict(common)
        m["x"] = xb
        maps.append(m)
    return maps


def kernel(**inputs):
    in_maps = _host_layouts(inputs)
    nc = build_nc()
    res = run_bass_kernel_spmd(nc, in_maps, core_ids=list(range(8)))
    outs = []
    for b in range(8):
        o = np.asarray(res.results[b]["out"])
        outs.append(o.transpose(1, 0, 2).reshape(D, T).T)
    return np.stack(outs).astype(np.float32)


# version 11
# speedup vs baseline: 1.2125x; 1.0182x over previous
import numpy as np
import concourse.bass as bass
import concourse.mybir as mybir
from concourse.bass_utils import run_bass_kernel_spmd

F32 = mybir.dt.float32
BF16 = mybir.dt.bfloat16
I32 = mybir.dt.int32
AF = mybir.ActivationFunctionType
OP = mybir.AluOpType

D = 1024
KC = 8
H = 3072
HC = 24
T = 4096
TT = 512
NT = T // TT
P = 128
RING = 6
UNIT = 4096

S127 = 1.0 / 127.0
DOWN_SCALE = 0.5 * S127 * S127
LOG1EM7 = float(np.float32(np.log(np.float32(1e-7))))
DEBUG_TAPS = False
INTERLEAVE = True


class Sched:
    def __init__(self):
        self.ops = {"sync": [], "gpsimd": [], "tensor": [], "vector": [], "scalar": []}
        self.cnt = {}
        self.marks = {}

    def val(self, sem):
        return self.cnt.get(sem, 0)

    def emit(self, eng, fn, waits=(), inc=None, incn=1):
        ws = []
        for s, v in waits:
            if not v or v <= 0:
                continue
            key = (eng, s)
            if self.marks.get(key, 0) >= v:
                continue
            self.marks[key] = v
            ws.append((s, v))
        iv = None
        if inc is not None:
            self.cnt[inc] = self.cnt.get(inc, 0) + incn
            iv = (inc, incn)
        self.ops[eng].append((ws, fn, iv))
        return self.cnt.get(inc, 0) if inc else None


def build_nc():
    nc = bass.Bass("TRN2", target_bir_lowering=False)

    xh = nc.dram_tensor("x", [P, KC, T], F32, kind="ExternalInput")
    wg_i = nc.dram_tensor("wg", [P, HC * 8 * 128], I32, kind="ExternalInput")
    wu_i = nc.dram_tensor("wu", [P, HC * 8 * 128], I32, kind="ExternalInput")
    wd_i = nc.dram_tensor("wd", [P, 8 * HC * 128], I32, kind="ExternalInput")
    wr_i = nc.dram_tensor("wr", [P, KC * 1024], F32, kind="ExternalInput")
    wi_i = nc.dram_tensor("wi", [P, KC * 1024], F32, kind="ExternalInput")
    wo_i = nc.dram_tensor("wo", [P, KC * 1024], F32, kind="ExternalInput")
    rb_i = nc.dram_tensor("rb", [P, KC], F32, kind="ExternalInput")
    ib_i = nc.dram_tensor("ib", [P, KC], F32, kind="ExternalInput")
    la_i = nc.dram_tensor("la", [P, KC], F32, kind="ExternalInput")
    nw_i = nc.dram_tensor("nw", [P, KC], F32, kind="ExternalInput")
    cw_i = nc.dram_tensor("cw", [P, 32], F32, kind="ExternalInput")
    cb_i = nc.dram_tensor("cb", [P, KC], F32, kind="ExternalInput")
    out_h = nc.dram_tensor("out", [P, KC, T], F32, kind="ExternalOutput")

    wg_s = nc.dram_tensor("wg_s", [P, HC * 8 * 128], BF16, kind="Internal")
    wu_s = nc.dram_tensor("wu_s", [P, HC * 8 * 128], BF16, kind="Internal")
    wd_s = nc.dram_tensor("wd_s", [P, 8 * HC * 128], BF16, kind="Internal")
    wr_s = nc.dram_tensor("wr_s", [P, KC * 1024], BF16, kind="Internal")
    wi_s = nc.dram_tensor("wi_s", [P, KC * 1024], BF16, kind="Internal")
    wo_s = nc.dram_tensor("wo_s", [P, KC * 1024], BF16, kind="Internal")

    import contextlib
    ctx = contextlib.ExitStack()
    nc._kernel_ctx = ctx
    sb = lambda n, s, d=F32: ctx.enter_context(nc.sbuf_tensor(n, s, d))

    RES = [sb("RES0", [P, T]), sb("RES1", [P, T])]
    SA = sb("SA", [P, T])
    SB = sb("SB", [P, T])
    SC_ = sb("SC", [P, T])
    SD = sb("SD", [P, T])
    BFS = sb("BFS", [P, 2 * T], BF16)
    HATT = sb("HATT", [P, T], BF16)
    HH = sb("HH", [P, HC * TT], BF16)
    LTMP = sb("LTMP", [P, TT + 8])
    ROWP = LTMP
    TAIL = sb("TAIL", [P, KC * 4])
    WRG = [sb(f"WRG{j}", [P, UNIT], BF16) for j in range(RING)]
    SG = [sb("SG0", [P, TT]), sb("SG1", [P, TT])]
    ONES = sb("ONESF", [P, TT])
    ONESR = ONES
    ONESB = sb("ONESB", [P, P], BF16)
    CONSTS = sb("CONSTS", [P, 8])
    CW = sb("CWt", [P, 32])
    CB = sb("CBt", [P, KC])
    RB = sb("RBt", [P, KC])
    IB = sb("IBt", [P, KC])
    LAT = sb("LAt", [P, KC])
    NW = sb("NWt", [P, KC])
    L8 = sb("L8", [P, KC])
    L16 = sb("L16", [P, KC])
    SCAR = sb("SCAR", [P, KC])
    CCAR = sb("CCAR", [P, KC])
    RED = sb("REDt", [P, KC])
    PB = [ctx.enter_context(nc.psum_tensor(f"PB{j}", [P, TT], F32)) for j in range(8)]

    cvals = [S127, -1.0, -0.5, 1e-5, 1e-6]
    for j_, v in enumerate(cvals):
        nc.const_aps.register(F32, float(np.float32(v)), CONSTS[:, j_:j_ + 1])
    C127, CM1, CMH, EPS5, EPS6 = [float(np.float32(v)) for v in cvals]

    SEMS = {}
    for n in ["PE", "DV", "AC", "WL", "X", "OUT", "CV"]:
        SEMS[n] = ctx.enter_context(nc.semaphore("sem_" + n))

    sch = Sched()
    ring_state = {"u": 0, "done": {}, "loaded": {}}

    def unit_load(src_ap, ncols):
        u = ring_state["u"]
        ring_state["u"] += 1
        slot = WRG[u % RING]
        waits = [("CV", src_cast_done[src_ap.tensor.name])]
        prev = u - RING
        if prev >= 0:
            waits.append(("PE", ring_state["done"].get(prev, 0)))
        def fn(e, slot=slot, src=src_ap, ncols=ncols):
            return e.dma_start(slot[:, 0:ncols], src)
        v = sch.emit("sync", fn, waits=waits, inc="WL", incn=16)
        ring_state["loaded"][u] = v
        return slot, u

    def unit_used(u):
        ring_state["done"][u] = sch.val("PE")

    src_cast_done = {}
    def cast_dma(dst, src):
        def fn(e, dst=dst, src=src):
            return e.dma_start(dst[:], src[:])
        v = sch.emit("gpsimd", fn, inc="CV", incn=16)
        src_cast_done[dst.name] = v

    cast_dma(wg_s, wg_i)
    cast_dma(wu_s, wu_i)
    cast_dma(wd_s, wd_i)
    cast_dma(wr_s, wr_i)
    cast_dma(wi_s, wi_i)
    cast_dma(wo_s, wo_i)

    for dst, src in [(CW, cw_i), (CB, cb_i), (RB, rb_i), (IB, ib_i),
                     (LAT, la_i), (NW, nw_i)]:
        sch.emit("sync", (lambda e, d=dst, s=src: e.dma_start(d[:], s[:])),
                 inc="X", incn=16)
    x_ready = {}
    for i in range(2):
        x_ready[i] = sch.emit(
            "gpsimd",
            (lambda e, i=i: e.dma_start(RES[i % 2][:].rearrange("p (c t) -> p c t", c=KC),
                                        xh[:, :, i * TT:(i + 1) * TT])),
            inc="X", incn=16)

    def ms(ap, v):
        sch.emit("gpsimd", (lambda e, ap=ap, v=v: e.memset(ap, v)))
    ms(ONES[:], 1.0)
    ms(ONESB[:], 1.0)
    ms(TAIL[:], 0.0)
    ms(SCAR[:], 0.0)
    ms(CCAR[:], 0.0)
    for j_, v in enumerate(cvals):
        ms(CONSTS[:, j_:j_ + 1], float(np.float32(v)))
    setup_done = sch.emit("gpsimd", (lambda e: e.memset(CONSTS[:, 7:8], 0.0)),
                          inc="X", incn=1)

    sch.emit("scalar", (lambda e: e.activation(L8[:], LAT[:], AF.Sigmoid)),
             waits=[("X", setup_done)], inc="AC")
    v_ln = sch.emit("scalar", (lambda e: e.activation(L8[:], L8[:], AF.Ln)), inc="AC")
    sch.emit("vector",
             (lambda e: e.tensor_scalar(L16[:], L8[:], 16.0, -1e30, OP.mult, OP.max)),
             waits=[("AC", v_ln)], inc="DV")
    v_l8 = sch.emit("vector",
                    (lambda e: e.tensor_scalar(L8[:], L8[:], 8.0, -1e30, OP.mult, OP.max)),
                    inc="DV")

    tv = [dict() for _ in range(NT + 1)]
    state = {"scar_v": 0, "ccar_v": 0, "last_silu": 0, "last_rowp": 0}

    dbg = {}
    if DEBUG_TAPS:
        dbg["d_scar"] = nc.dram_tensor("d_scar", [P, KC], F32, kind="ExternalOutput")
        for nm in ["d_l8a", "d_l8b", "d_l8c", "d_l8d"]:
            dbg[nm] = nc.dram_tensor(nm, [P, KC], F32, kind="ExternalOutput")
        dbg["d_r1pre"] = nc.dram_tensor("d_r1pre", [P, T], F32, kind="ExternalOutput")
        dbg["d_xc1"] = nc.dram_tensor("d_xc1", [P, T], F32, kind="ExternalOutput")
        dbg["d_xcb1"] = nc.dram_tensor("d_xcb1", [P, T], BF16, kind="ExternalOutput")
        dbg["d_rstd2"] = nc.dram_tensor("d_rstd2", [1, TT], F32, kind="ExternalOutput")
        dbg["d_q2"] = nc.dram_tensor("d_q2", [1, TT], F32, kind="ExternalOutput")
        for nm, dt_ in [("d_x1", F32), ("d_o", F32), ("d_x2", F32), ("d_hatt", BF16),
                        ("d_logA", F32), ("d_aexp", F32), ("d_cs", F32), ("d_term", F32),
                        ("d_la", F32), ("d_s", F32),
                        ("d_msrow", F32), ("d_rstd", F32)]:
            dbg[nm] = nc.dram_tensor(nm, [P, T] if nm not in ("d_msrow", "d_rstd")
                                     else [1, TT], dt_, kind="ExternalOutput")

    def tap(name, src_ap, dv_wait=None, ac_wait=None):
        if not DEBUG_TAPS:
            return
        w = []
        if dv_wait: w.append(("DV", dv_wait))
        if ac_wait: w.append(("AC", ac_wait))
        v = sch.emit("gpsimd", (lambda e, s=src_ap, d=dbg[name]: e.dma_start(d[:], s)),
                     waits=w, inc="OUT", incn=16)
        sch.emit("vector", (lambda e: e.engine_nop()), waits=[("OUT", v)], inc="DV")
        state["dbg_out"] = v

    def ln_stats(i, res, pb_m, pb_v, bfs1_war_pe, bfs2_war_pe, res_dv=0,
                 extra_pe_waits=()):
        r = {}
        r["xb"] = sch.emit("vector", (lambda e: e.tensor_copy(BFS[:, 0:T], res[:])),
                           waits=[("PE", bfs1_war_pe), ("X", x_ready.get(i, 0)),
                                  ("X", setup_done), ("DV", res_dv)], inc="DV")
        r["xsq"] = sch.emit("scalar",
                            (lambda e: e.activation(BFS[:, T:2 * T], res[:], AF.Square)),
                            waits=[("PE", bfs2_war_pe), ("X", x_ready.get(i, 0)),
                                   ("X", setup_done), ("DV", res_dv)], inc="AC")

        def mm_stats(e, pb, off):
            last = None
            for kc in range(KC):
                last = e.matmul(pb[0:1, :], ONESB[:, 0:1],
                                BFS[:, off + kc * TT: off + (kc + 1) * TT],
                                start=(kc == 0), stop=(kc == KC - 1))
            return last
        r["sm"] = sch.emit("tensor", (lambda e: mm_stats(e, pb_m, 0)),
                           waits=[("DV", r["xb"]), ("X", setup_done)] + list(extra_pe_waits),
                           inc="PE")
        r["sv"] = sch.emit("tensor", (lambda e: mm_stats(e, pb_v, T)),
                           waits=[("AC", r["xsq"])], inc="PE")
        sch.emit("vector",
                 (lambda e: e.tensor_scalar_mul(SG[0][0:1, :], pb_m[0:1, :], 1.0 / D)),
                 waits=[("PE", r["sm"]), ("AC", state["last_silu"]),
                        ("AC", state["last_rowp"])], inc="DV")
        sch.emit("vector",
                 (lambda e: e.tensor_mul(ROWP[0:1, 0:TT], SG[0][0:1, :], SG[0][0:1, :])),
                 inc="DV")
        r["var"] = sch.emit("vector",
                            (lambda e: e.scalar_tensor_tensor(
                                SG[1][0:1, :], pb_v[0:1, :], 1.0 / D,
                                ROWP[0:1, 0:TT], OP.mult, OP.subtract)),
                            waits=[("PE", r["sv"])], inc="DV")
        sch.emit("scalar",
                 (lambda e: e.activation(SG[1][0:1, :], SG[1][0:1, :], AF.Ln, bias=EPS5)),
                 waits=[("DV", r["var"])], inc="AC")
        r["p"] = sch.emit("scalar",
                          (lambda e: e.activation(ROWP[0:1, 0:TT], SG[1][0:1, :],
                                                  AF.Exp, scale=CMH)),
                          inc="AC")
        state["last_rowp"] = r["p"]
        r["q"] = sch.emit("vector",
                          (lambda e: e.scalar_tensor_tensor(SG[0][0:1, :], SG[0][0:1, :],
                                                            -1.0, ROWP[0:1, 0:TT],
                                                            OP.mult, OP.mult)),
                          waits=[("AC", r["p"])], inc="DV")
        r["rows_dv"] = sch.val("DV")
        return r

    def bcast_rows(rr, pb_p, pb_q, war_ac=0, war_dv=0):
        v_bp = sch.emit("tensor",
                        (lambda e: e.matmul(pb_p[:], ONESR[0:1, 0:P], ROWP[0:1, 0:TT],
                                            start=True, stop=True)),
                        waits=[("AC", rr["p"]), ("AC", war_ac), ("DV", war_dv)], inc="PE")
        v_bq = sch.emit("tensor",
                        (lambda e: e.matmul(pb_q[:], ONESR[0:1, 0:P], SG[0][0:1, :],
                                            start=True, stop=True)),
                        waits=[("DV", rr["q"])], inc="PE")
        return v_bp, v_bq

    def ln_stats_bcast(i, res, pb_m, pb_v, bfs1_war_pe, bfs2_war_pe, res_dv=0,
                       eps=None, extra_pe_waits=()):
        r = {}
        r["xb"] = sch.emit("vector", (lambda e: e.tensor_copy(BFS[:, 0:T], res[:])),
                           waits=[("PE", bfs1_war_pe), ("X", x_ready.get(i, 0)),
                                  ("X", setup_done), ("DV", res_dv)], inc="DV")
        r["xsq"] = sch.emit("scalar",
                            (lambda e: e.activation(BFS[:, T:2 * T], res[:], AF.Square)),
                            waits=[("PE", bfs2_war_pe), ("X", x_ready.get(i, 0)),
                                   ("X", setup_done), ("DV", res_dv)], inc="AC")
        def mm_stats(e, pb, off):
            last = None
            for kc in range(KC):
                last = e.matmul(pb[:], ONESB[:, 0:P],
                                BFS[:, off + kc * TT: off + (kc + 1) * TT],
                                start=(kc == 0), stop=(kc == KC - 1))
            return last
        r["sm"] = sch.emit("tensor", (lambda e: mm_stats(e, pb_m, 0)),
                           waits=[("DV", r["xb"]), ("X", setup_done)] + list(extra_pe_waits),
                           inc="PE")
        r["sv"] = sch.emit("tensor", (lambda e: mm_stats(e, pb_v, T)),
                           waits=[("AC", r["xsq"])], inc="PE")
        sch.emit("vector",
                 (lambda e: e.tensor_scalar_mul(SG[1][:], pb_m[:], 1.0 / D)),
                 waits=[("PE", r["sm"]), ("AC", state["last_silu"]),
                        ("AC", state["last_rowp"])], inc="DV")
        sch.emit("vector", (lambda e: e.tensor_mul(SG[0][:], SG[1][:], SG[1][:])),
                 inc="DV")
        r["var"] = sch.emit("vector",
                            (lambda e: e.scalar_tensor_tensor(
                                SG[0][:], pb_v[:], 1.0 / D, SG[0][:],
                                OP.mult, OP.subtract)),
                            waits=[("PE", r["sv"])], inc="DV")
        sch.emit("scalar",
                 (lambda e: e.activation(SG[0][:], SG[0][:], AF.Ln, bias=eps)),
                 waits=[("DV", r["var"])], inc="AC")
        r["p"] = sch.emit("scalar",
                          (lambda e: e.activation(SG[0][:], SG[0][:], AF.Exp, scale=CMH)),
                          inc="AC")
        state["last_rowp"] = r["p"]
        r["q"] = sch.emit("vector",
                          (lambda e: e.scalar_tensor_tensor(SG[1][:], SG[1][:], -1.0,
                                                            SG[0][:], OP.mult, OP.mult)),
                          waits=[("AC", r["p"])], inc="DV")
        return r

    def ln2_stats_psum(j, res, bfs_war_pe, res_dv):
        r = {}
        r["xb"] = sch.emit("vector", (lambda e: e.tensor_copy(BFS[:, 0:T], res[:])),
                           waits=[("PE", bfs_war_pe), ("DV", res_dv)], inc="DV")
        r["xsq"] = sch.emit("scalar",
                            (lambda e: e.activation(BFS[:, T:2 * T], res[:], AF.Square)),
                            waits=[("PE", bfs_war_pe), ("DV", res_dv)], inc="AC")
        def mm_stats(e, pb, off):
            last = None
            for kc in range(KC):
                last = e.matmul(pb[:], ONESB[:, 0:P],
                                BFS[:, off + kc * TT: off + (kc + 1) * TT],
                                start=(kc == 0), stop=(kc == KC - 1))
            return last
        r["sm"] = sch.emit("tensor", (lambda e: mm_stats(e, PB[6], 0)),
                           waits=[("DV", r["xb"])], inc="PE")
        r["sv"] = sch.emit("tensor", (lambda e: mm_stats(e, PB[7], T)),
                           waits=[("AC", r["xsq"])], inc="PE")
        sch.emit("vector",
                 (lambda e: e.tensor_scalar_mul(SG[1][:], PB[6][:], 1.0 / D)),
                 waits=[("PE", r["sm"]), ("AC", state["last_silu"]),
                        ("AC", state["last_rowp"])], inc="DV")
        sch.emit("vector", (lambda e: e.tensor_mul(SG[0][:], SG[1][:], SG[1][:])),
                 inc="DV")
        r["var"] = sch.emit("vector",
                            (lambda e: e.scalar_tensor_tensor(
                                SG[0][:], PB[7][:], 1.0 / D, SG[0][:],
                                OP.mult, OP.subtract)),
                            waits=[("PE", r["sv"])], inc="DV")
        sch.emit("scalar",
                 (lambda e: e.activation(SG[0][:], SG[0][:], AF.Ln, bias=EPS5)),
                 waits=[("DV", r["var"])], inc="AC")
        r["bc2p"] = sch.emit("scalar",
                             (lambda e: e.activation(PB[6][:], SG[0][:], AF.Exp,
                                                     scale=CMH)),
                             inc="AC")
        state["last_rowp"] = r["bc2p"]
        r["bc2q"] = sch.emit("vector",
                             (lambda e: e.scalar_tensor_tensor(PB[7][:], SG[1][:], -1.0,
                                                               PB[6][:], OP.mult,
                                                               OP.mult)),
                             waits=[("AC", r["bc2p"])], inc="DV")
        tv[j]["bc2p"] = r["bc2p"]
        tv[j]["bc2q"] = r["bc2q"]
        return r

    def normalize_bf16(res, v_bp, v_bq, pb_p, pb_q, bfs1_war_pe, q_dv=0, p_ac=0):
        last = 0
        chunks = []
        pp = pb_p[:, 0:TT] if pb_p in (SG[0], SG[1]) else pb_p[:]
        qq = pb_q[:, 0:TT] if pb_q in (SG[0], SG[1]) else pb_q[:]
        for c in range(KC):
            sch.emit("vector",
                     (lambda e, c=c: e.tensor_mul(LTMP[:, 0:TT],
                                                  res[:, c * TT:(c + 1) * TT], pp)),
                     waits=[("PE", v_bp), ("PE", v_bq), ("PE", bfs1_war_pe),
                            ("DV", q_dv), ("AC", p_ac)], inc="DV")
            last = sch.emit("vector",
                            (lambda e, c=c: e.tensor_add(BFS[:, c * TT:(c + 1) * TT],
                                                         LTMP[:, 0:TT], qq)),
                            inc="DV")
            chunks.append(last)
        return last, chunks

    def mid_make(i):
        g = {}
        def g_la_ixc():
            if DEBUG_TAPS and i == 1:
                tap("d_l8c", L8[:], ac_wait=tv[i]["sig_r"])
                tap("d_r1pre", SC_[:], ac_wait=tv[i]["sig_r"])
            for c in range(KC):
                sch.emit("vector",
                         (lambda e, c=c: e.tensor_scalar(
                             SA[:, c * TT:(c + 1) * TT], SC_[:, c * TT:(c + 1) * TT],
                             L8[:, c:c + 1], LOG1EM7, OP.mult, OP.max)),
                         waits=[("AC", tv[i]["sig_r"]), ("DV", v_l8)], inc="DV")
            for c in range(KC):
                tv[i]["ixc"] = sch.emit(
                    "vector",
                    (lambda e, c=c: e.tensor_mul(
                        SD[:, c * TT:(c + 1) * TT], SD[:, c * TT:(c + 1) * TT],
                        SB[:, c * TT:(c + 1) * TT])),
                    waits=[("AC", tv[i]["sig_i"])], inc="DV")
        def a_a2():
            for c in range(KC):
                tv[i]["a2"] = sch.emit(
                    "scalar",
                    (lambda e, c=c: e.activation(
                        SB[:, c * TT:(c + 1) * TT], SC_[:, c * TT:(c + 1) * TT],
                        AF.Exp, scale=L16[:, c:c + 1])),
                    waits=[("DV", tv[i]["ixc"])], inc="AC")
        def g_v():
            sch.emit("vector",
                     (lambda e: e.tensor_scalar(SC_[:], SB[:], 1.0, -1.0,
                                                OP.subtract, OP.mult)),
                     waits=[("AC", tv[i]["a2"])], inc="DV")
            tv[i]["vmax"] = sch.emit(
                "vector", (lambda e: e.tensor_scalar_max(SC_[:], SC_[:], 1e-6)), inc="DV")
        def a_bamp():
            tv[i]["bamp"] = sch.emit(
                "scalar", (lambda e: e.activation(SB[:], SC_[:], AF.Sqrt)),
                waits=[("DV", tv[i]["vmax"])], inc="AC")
        def g_bt():
            tv[i]["bt"] = sch.emit(
                "vector", (lambda e: e.tensor_mul(SD[:], SD[:], SB[:])),
                waits=[("AC", tv[i]["bamp"])], inc="DV")
        def g_ss(c0, c1):
            def f():
                for c in range(c0, c1):
                    sch.emit("vector",
                             (lambda e, c=c: e.tensor_tensor_scan(
                                 SB[:, c * TT:(c + 1) * TT], ONES[:],
                                 SA[:, c * TT:(c + 1) * TT], SCAR[:, c:c + 1],
                                 OP.mult, OP.add)),
                             waits=[("DV", state["scar_v"]), ("DV", tv[i]["bt"])],
                             inc="DV")
            return f
        def g_sred():
            for c in range(KC):
                sch.emit("vector",
                         (lambda e, c=c: e.tensor_reduce(
                             RED[:, c:c + 1], SA[:, c * TT:(c + 1) * TT],
                             mybir.AxisListType.X, OP.add)),
                         inc="DV")
            sch.emit("vector", (lambda e: e.engine_nop()), inc="DV")
            sch.emit("vector", (lambda e: e.engine_nop()), inc="DV")
            state["scar_v"] = sch.emit(
                "vector", (lambda e: e.tensor_add(SCAR[:], SCAR[:], RED[:])), inc="DV")
        def g_logA():
            tv[i]["logA"] = sch.emit(
                "vector", (lambda e: e.tensor_scalar_max(SB[:], SB[:], -80.0)), inc="DV")
            if DEBUG_TAPS and i == 1:
                tap("d_logA", SB[:], dv_wait=tv[i]["logA"])
        def a_en():
            tv[i]["en"] = sch.emit(
                "scalar", (lambda e: e.activation(SA[:], SB[:], AF.Exp, scale=CM1)),
                waits=[("DV", tv[i]["logA"])], inc="AC")
        def g_term():
            tv[i]["term"] = sch.emit(
                "vector", (lambda e: e.tensor_mul(SD[:], SD[:], SA[:])),
                waits=[("AC", tv[i]["en"])], inc="DV")
            if DEBUG_TAPS and i == 1:
                tap("d_term", SD[:], dv_wait=tv[i]["term"])
        def a_aexp():
            tv[i]["aexp"] = sch.emit(
                "scalar", (lambda e: e.activation(SC_[:], SB[:], AF.Exp)),
                waits=[("DV", tv[i]["term"])], inc="AC")
        def g_cs(c0, c1):
            def f():
                for c in range(c0, c1):
                    sch.emit("vector",
                             (lambda e, c=c: e.tensor_tensor_scan(
                                 SA[:, c * TT:(c + 1) * TT], ONES[:],
                                 SD[:, c * TT:(c + 1) * TT], CCAR[:, c:c + 1],
                                 OP.mult, OP.add)),
                             waits=[("DV", state["ccar_v"]), ("DV", tv[i]["term"])],
                             inc="DV")
            return f
        def g_cred():
            for c in range(KC):
                sch.emit("vector",
                         (lambda e, c=c: e.tensor_reduce(
                             RED[:, c:c + 1], SD[:, c * TT:(c + 1) * TT],
                             mybir.AxisListType.X, OP.add)),
                         inc="DV")
            sch.emit("vector", (lambda e: e.engine_nop()), inc="DV")
            sch.emit("vector", (lambda e: e.engine_nop()), inc="DV")
            state["ccar_v"] = sch.emit(
                "vector", (lambda e: e.tensor_add(CCAR[:], CCAR[:], RED[:])), inc="DV")
        def g_hatt():
            if DEBUG_TAPS and i == 1:
                tap("d_aexp", SC_[:], ac_wait=tv[i]["aexp"])
                tap("d_cs", SA[:], dv_wait=state["ccar_v"])
            tv[i]["hatt"] = sch.emit(
                "vector", (lambda e: e.tensor_mul(HATT[:], SC_[:], SA[:])),
                waits=[("AC", tv[i]["aexp"]),
                       ("PE", tv[i - 1].get("outw", 0) if i > 0 else 0)],
                inc="DV")
        g["dve"] = {2: [g_la_ixc], 4: [g_v], 6: [g_bt], 8: [g_ss(0, 4)],
                    10: [g_ss(4, 8)], 12: [g_sred, g_logA], 14: [g_term],
                    16: [g_cs(0, 4)], 18: [g_cs(4, 8)], 20: [g_cred], 21: [g_hatt]}
        g["act"] = {2: [a_a2], 4: [a_bamp], 13: [a_en], 17: [a_aexp]}
        g["all"] = [g_la_ixc, a_a2, g_v, a_bamp, g_bt, g_ss(0, 8), g_sred, g_logA,
                    a_en, g_term, a_aexp, g_cs(0, 8), g_cred, g_hatt]
        return g

    def ffn(res, lnb_ready_dv, pg_pair, pu_pair, pd_pair,
            gate_war_ac=0, up_war_ac=0, up_war_dv=0, mid=None, lnb_chunks=None):
        silu_done = {}
        h_done = {}
        gslot = uslot = None
        gu = uu = None
        for hc in range(HC):
            if mid is not None and INTERLEAVE:
                for fn_ in mid["dve"].get(hc, []):
                    fn_()
                for fn_ in mid["act"].get(hc, []):
                    fn_()
            if hc % 4 == 0:
                gslot, gu = unit_load(wg_s[:, hc * 1024:(hc + 4) * 1024], UNIT)
                uslot, uu = unit_load(wu_s[:, hc * 1024:(hc + 4) * 1024], UNIT)
            pg = pg_pair[hc % 2]
            pu = pu_pair[hc % 2]
            base = (hc % 4) * 1024
            if hc < 2 and lnb_chunks is not None:
                gv = uv = 0
                for kc in range(KC):
                    gw = [("DV", lnb_chunks[kc])]
                    uw = []
                    if kc == 0:
                        gw += [("WL", ring_state["loaded"][gu]),
                               ("AC", silu_done.get(hc - 2, gate_war_ac))]
                        uw += [("WL", ring_state["loaded"][uu]),
                               ("DV", h_done.get(hc - 2, up_war_dv)),
                               ("AC", up_war_ac)]
                    gv = sch.emit(
                        "tensor",
                        (lambda e, pg=pg, gslot=gslot, base=base, kc=kc:
                         e.matmul(pg[:], gslot[:, base + kc * 128: base + kc * 128 + 128],
                                  BFS[:, kc * TT:(kc + 1) * TT],
                                  start=(kc == 0), stop=(kc == KC - 1))),
                        waits=gw, inc="PE")
                    uv = sch.emit(
                        "tensor",
                        (lambda e, pu=pu, uslot=uslot, base=base, kc=kc:
                         e.matmul(pu[:], uslot[:, base + kc * 128: base + kc * 128 + 128],
                                  BFS[:, kc * TT:(kc + 1) * TT],
                                  start=(kc == 0), stop=(kc == KC - 1))),
                        waits=uw, inc="PE")
                unit_used(gu)
                unit_used(uu)
                sg = SG[hc % 2]
                sv = sch.emit("scalar",
                              (lambda e, sg=sg, pg=pg: e.activation(sg[:], pg[:], AF.Silu,
                                                                    scale=C127)),
                              waits=[("PE", gv), ("DV", h_done.get(hc - 2, 0))], inc="AC")
                silu_done[hc] = sv
                state["last_silu"] = sv
                h_done[hc] = sch.emit(
                    "vector",
                    (lambda e, sg=sg, pu=pu, hc=hc:
                     e.tensor_mul(HH[:, hc * TT:(hc + 1) * TT], sg[:], pu[:])),
                    waits=[("AC", sv), ("PE", uv)], inc="DV")
                continue
            def mm_gate(e, pg=pg, gslot=gslot, base=base):
                last = None
                for kc in range(KC):
                    last = e.matmul(pg[:], gslot[:, base + kc * 128: base + kc * 128 + 128],
                                    BFS[:, kc * TT:(kc + 1) * TT],
                                    start=(kc == 0), stop=(kc == KC - 1))
                return last
            gv = sch.emit("tensor", mm_gate,
                          waits=[("WL", ring_state["loaded"][gu]), ("DV", lnb_ready_dv),
                                 ("AC", silu_done.get(hc - 2, gate_war_ac))], inc="PE")
            unit_used(gu)
            def mm_up(e, pu=pu, uslot=uslot, base=base):
                last = None
                for kc in range(KC):
                    last = e.matmul(pu[:], uslot[:, base + kc * 128: base + kc * 128 + 128],
                                    BFS[:, kc * TT:(kc + 1) * TT],
                                    start=(kc == 0), stop=(kc == KC - 1))
                return last
            uv = sch.emit("tensor", mm_up,
                          waits=[("WL", ring_state["loaded"][uu]),
                                 ("DV", h_done.get(hc - 2, up_war_dv)),
                                 ("AC", up_war_ac if hc < 2 else 0)], inc="PE")
            unit_used(uu)
            sg = SG[hc % 2]
            sv = sch.emit("scalar",
                          (lambda e, sg=sg, pg=pg: e.activation(sg[:], pg[:], AF.Silu,
                                                                scale=C127)),
                          waits=[("PE", gv), ("DV", h_done.get(hc - 2, 0))], inc="AC")
            silu_done[hc] = sv
            state["last_silu"] = sv
            h_done[hc] = sch.emit(
                "vector",
                (lambda e, sg=sg, pu=pu, hc=hc:
                 e.tensor_mul(HH[:, hc * TT:(hc + 1) * TT], sg[:], pu[:])),
                waits=[("AC", sv), ("PE", uv)], inc="DV")
        gateup_pe = sch.val("PE")
        silu_last = silu_done[HC - 1]
        if mid is not None and not INTERLEAVE:
            for fn_ in mid["all"]:
                fn_()
        stt_done = {}
        for md in range(KC):
            dslot, du = unit_load(wd_s[:, md * HC * 128:(md + 1) * HC * 128], HC * 128)
            pd = pd_pair[md % 2]
            def mm_down(e, pd=pd, dslot=dslot):
                last = None
                for hc in range(HC):
                    last = e.matmul(pd[:], dslot[:, hc * 128: hc * 128 + 128],
                                    HH[:, hc * TT:(hc + 1) * TT],
                                    start=(hc == 0), stop=(hc == HC - 1))
                return last
            dv_ = sch.emit("tensor", mm_down,
                           waits=[("WL", ring_state["loaded"][du]), ("DV", h_done[HC - 1]),
                                  ("DV", stt_done.get(md - 2, 0))], inc="PE")
            unit_used(du)
            stt_done[md] = sch.emit(
                "vector",
                (lambda e, pd=pd, md=md: e.scalar_tensor_tensor(
                    res[:, md * TT:(md + 1) * TT], pd[:], DOWN_SCALE,
                    res[:, md * TT:(md + 1) * TT], OP.mult, OP.add)),
                waits=[("PE", dv_)], inc="DV")
        return gateup_pe, silu_last, sch.val("DV"), stt_done

    def proj8(wsrc, pp_pair, dst, bias, rhs_dv, war_ac, war_dv, sig_store):
        units = []
        last_pe = 0
        for mc in range(KC):
            if mc % 4 == 0:
                units.append(unit_load(wsrc[:, mc * 1024:(mc + 4) * 1024], UNIT))
            slot, u = units[-1]
            pr = pp_pair[mc % 2]
            base = (mc % 4) * 1024
            def mm(e, pr=pr, slot=slot, base=base):
                last = None
                for kc in range(KC):
                    last = e.matmul(pr[:], slot[:, base + kc * 128: base + kc * 128 + 128],
                                    HATT[:, kc * TT:(kc + 1) * TT],
                                    start=(kc == 0), stop=(kc == KC - 1))
                return last
            rv = sch.emit("tensor", mm,
                          waits=[("WL", ring_state["loaded"][u]), ("DV", rhs_dv),
                                 ("AC", sig_store.get(mc - 2, war_ac)),
                                 ("DV", war_dv if mc < 2 else 0)], inc="PE")
            last_pe = rv
            unit_used(u)
            sig_store[mc] = sch.emit(
                "scalar",
                (lambda e, pr=pr, mc=mc: e.activation(
                    dst[:, mc * TT:(mc + 1) * TT], pr[:], AF.Sigmoid,
                    bias=bias[:, mc:mc + 1])),
                waits=[("PE", rv)], inc="AC")
        return last_pe

    def make_conv_mid(j, res_j, conv_war_dv):
        def mk(c):
            def f():
                sch.emit("vector",
                         (lambda e: e.tensor_mul(SD[:, 0:TT],
                                                 res_j[:, c * TT:(c + 1) * TT], PB[6][:])),
                         waits=[("AC", tv[j]["bc2p"]), ("DV", tv[j]["bc2q"]),
                                ("DV", conv_war_dv)], inc="DV")
                sch.emit("vector",
                         (lambda e: e.tensor_add(LTMP[:, 3:3 + TT], SD[:, 0:TT],
                                                 PB[7][:])), inc="DV")
                sch.emit("vector",
                         (lambda e: e.tensor_copy(LTMP[:, 0:3],
                                                  TAIL[:, c * 4:c * 4 + 3])), inc="DV")
                sch.emit("vector",
                         (lambda e: e.tensor_copy(TAIL[:, c * 4:c * 4 + 3],
                                                  LTMP[:, TT:TT + 3])), inc="DV")
                sch.emit("vector",
                         (lambda e: e.tensor_scalar(SB[:, c * TT:(c + 1) * TT],
                                                    LTMP[:, 0:TT], CW[:, c:c + 1],
                                                    CB[:, c:c + 1], OP.mult, OP.add)),
                         inc="DV")
                for k in range(1, 4):
                    sch.emit("vector",
                             (lambda e, k=k: e.scalar_tensor_tensor(
                                 SB[:, c * TT:(c + 1) * TT], LTMP[:, k:k + TT],
                                 CW[:, k * 8 + c:k * 8 + c + 1],
                                 SB[:, c * TT:(c + 1) * TT], OP.mult, OP.add)),
                             inc="DV")
                tv[j]["conv_dv"] = sch.val("DV")
            return f
        return {"dve": {3 + 2 * c: [mk(c)] for c in range(KC)}, "act": {}}

    def attn_proj(j):
        tv[j]["xcb"] = sch.emit("vector", (lambda e: e.tensor_copy(HATT[:], SB[:])),
                                waits=[("PE", tv[j]["hatt_war"])], inc="DV")
        sig_r = {}
        sig_i = {}
        proj8(wr_s, (PB[0], PB[1]), SC_, RB, tv[j]["xcb"],
              tv[j].get("ri_war_ac", 0), tv[j].get("ri_war_dv", 0), sig_r)
        tv[j]["ri_pe"] = proj8(wi_s, (PB[2], PB[3]), SD, IB, tv[j]["xcb"],
                               tv[j].get("ri_war_ac", 0), tv[j].get("ri_war_dv2", 0), sig_i)
        tv[j]["sig_r"] = sig_r[KC - 1]
        tv[j]["sig_i"] = sig_i[KC - 1]

    def attn_tail(j, res_j, conv_war_dv):
        for c in range(KC):
            sch.emit("vector",
                     (lambda e, c=c: e.tensor_mul(SD[:, 0:TT],
                                                  res_j[:, c * TT:(c + 1) * TT], PB[6][:])),
                     waits=[("AC", tv[j]["bc2p"]), ("DV", tv[j]["bc2q"]),
                            ("DV", conv_war_dv)], inc="DV")
            sch.emit("vector",
                     (lambda e: e.tensor_add(LTMP[:, 3:3 + TT], SD[:, 0:TT], PB[7][:])),
                     inc="DV")
            sch.emit("vector",
                     (lambda e, c=c: e.tensor_copy(LTMP[:, 0:3], TAIL[:, c * 4:c * 4 + 3])),
                     inc="DV")
            sch.emit("vector",
                     (lambda e, c=c: e.tensor_copy(TAIL[:, c * 4:c * 4 + 3],
                                                   LTMP[:, TT:TT + 3])),
                     inc="DV")
            sch.emit("vector",
                     (lambda e, c=c: e.tensor_scalar(SB[:, c * TT:(c + 1) * TT],
                                                     LTMP[:, 0:TT], CW[:, c:c + 1],
                                                     CB[:, c:c + 1], OP.mult, OP.add)),
                     inc="DV")
            for k in range(1, 4):
                sch.emit("vector",
                         (lambda e, c=c, k=k: e.scalar_tensor_tensor(
                             SB[:, c * TT:(c + 1) * TT], LTMP[:, k:k + TT],
                             CW[:, k * 8 + c:k * 8 + c + 1],
                             SB[:, c * TT:(c + 1) * TT], OP.mult, OP.add)),
                         inc="DV")
        tv[j]["conv_dv"] = sch.val("DV")
        if DEBUG_TAPS and j == 1:
            tap("d_xc1", SB[:], dv_wait=tv[j]["conv_dv"])
        tv[j]["xcb"] = sch.emit("vector", (lambda e: e.tensor_copy(HATT[:], SB[:])),
                                waits=[("PE", tv[j]["hatt_war"])], inc="DV")
        if DEBUG_TAPS and j == 1:
            tap("d_xcb1", BFS[:, 0:T], dv_wait=tv[j]["xcb"])
        sig_r = {}
        sig_i = {}
        proj8(wr_s, (PB[0], PB[1]), SC_, RB, tv[j]["xcb"],
              tv[j].get("ri_war_ac", 0), tv[j].get("ri_war_dv", 0), sig_r)
        tv[j]["ri_pe"] = proj8(wi_s, (PB[2], PB[3]), SD, IB, tv[j]["xcb"],
                               tv[j].get("ri_war_ac", 0), tv[j].get("ri_war_dv2", 0), sig_i)
        tv[j]["sig_r"] = sig_r[KC - 1]
        tv[j]["sig_i"] = sig_i[KC - 1]

    if DEBUG_TAPS:
        tap("d_l8a", L8[:], dv_wait=v_l8)
    rr = ln_stats_bcast(0, RES[0], PB[6], PB[7], 0, 0, eps=EPS5)
    lnb, lnbc = normalize_bf16(RES[0], 0, 0, SG[0], SG[1], 0,
                               q_dv=rr["q"], p_ac=rr["p"])
    gateup_pe, silu_last, stt_dv, _ = ffn(RES[0], lnb, (PB[0], PB[1]),
                                          (PB[2], PB[3]), (PB[4], PB[5]),
                                          lnb_chunks=lnbc)
    tv[0]["gateup_pe"] = gateup_pe
    tv[0]["silu_last"] = silu_last
    tv[0]["stt_dv"] = stt_dv
    rr2 = ln2_stats_psum(0, RES[0], gateup_pe, stt_dv)
    tv[0]["ln2_sv"] = rr2["sv"]
    tv[0]["bc2p"], tv[0]["bc2q"] = rr2["bc2p"], rr2["bc2q"]
    tv[0]["ri_war_ac"] = silu_last
    tv[0]["ri_war_dv"] = lnb
    tv[0]["ri_war_dv2"] = stt_dv
    tv[0]["hatt_war"] = 0
    attn_tail(0, RES[0], 0)
    if DEBUG_TAPS:
        tap("d_l8b", L8[:], ac_wait=tv[0]["sig_i"])

    hoisted = {}
    for i in range(NT):
        j = i + 1
        mid = mid_make(i)
        rr2 = None
        if j < NT:
            res_j = RES[j % 2]
            bfs1w = tv[i - 1]["gateup2_pe"] if i > 0 else tv[i]["ln2_sv"]
            if j in hoisted:
                lnb, lnbc = hoisted[j]
            else:
                rr = ln_stats_bcast(j, res_j, PB[6], PB[7],
                                    bfs1w, tv[i]["ln2_sv"], eps=EPS5,
                                    extra_pe_waits=[("DV", tv[i].get("conv_dv", 0)),
                                                    ("AC", tv[i]["sig_i"])])
                lnb, lnbc = normalize_bf16(res_j, 0, 0, SG[0], SG[1], bfs1w,
                                     q_dv=rr["q"], p_ac=rr["p"])
            tv[j]["lnb"] = lnb
            gateup_pe, silu_last, stt_dv, _ = ffn(
                res_j, lnb, (PB[0], PB[1]), (PB[2], PB[3]), (PB[4], PB[5]),
                gate_war_ac=tv[i]["sig_i"], up_war_ac=tv[i]["sig_i"],
                up_war_dv=tv[i].get("stt2_dv", tv[i].get("stt_dv", 0)), mid=mid,
                lnb_chunks=lnbc)
            tv[j]["gateup_pe"] = gateup_pe
            tv[j]["silu_last"] = silu_last
            tv[j]["stt_dv"] = stt_dv
            if DEBUG_TAPS and j == 1:
                tap("d_x1", RES[1][:], dv_wait=stt_dv)
            rr2 = ln2_stats_psum(j, res_j, gateup_pe, stt_dv)
            tv[j]["ln2_sv"] = rr2["sv"]
        else:
            for fn_ in mid["all"]:
                fn_()

        o_copy = {}
        units = []
        pb45_war = tv[j]["stt_dv"] if j < NT else tv[i - 1]["stt2_dv"]
        for md in range(KC):
            if md % 4 == 0:
                units.append(unit_load(wo_s[:, md * 1024:(md + 4) * 1024], UNIT))
            slot, u = units[-1]
            po = PB[4 + md % 2]
            base = (md % 4) * 1024
            def mm_o(e, po=po, slot=slot, base=base):
                last = None
                for kc in range(KC):
                    last = e.matmul(po[:], slot[:, base + kc * 128: base + kc * 128 + 128],
                                    HATT[:, kc * TT:(kc + 1) * TT],
                                    start=(kc == 0), stop=(kc == KC - 1))
                return last
            ov = sch.emit("tensor", mm_o,
                          waits=[("WL", ring_state["loaded"][u]), ("DV", tv[i]["hatt"]),
                                 ("DV", pb45_war), ("DV", o_copy.get(md - 2, 0))],
                          inc="PE")
            unit_used(u)
            o_copy[md] = sch.emit(
                "vector",
                (lambda e, po=po, md=md: e.tensor_copy(SB[:, md * TT:(md + 1) * TT], po[:])),
                waits=[("PE", ov)], inc="DV")
        tv[i]["outw"] = sch.val("PE")
        if DEBUG_TAPS and i == 1:
            tap("d_hatt", HATT[:], dv_wait=tv[i]["hatt"])
            tap("d_o", SB[:], dv_wait=o_copy[KC - 1])
        osq_last = 0
        bfs2_war = tv[j]["ln2_sv"] if j < NT else tv[i]["ln2_sv"]
        for md in range(KC):
            osq_last = sch.emit(
                "scalar",
                (lambda e, md=md: e.activation(
                    BFS[:, T + md * TT:T + (md + 1) * TT],
                    SB[:, md * TT:(md + 1) * TT], AF.Square)),
                waits=[("DV", o_copy[md]), ("PE", bfs2_war)], inc="AC")
        def mm_rms(e):
            last = None
            for md in range(KC):
                last = e.matmul(PB[4][:], ONESB[:, 0:P],
                                BFS[:, T + md * TT:T + (md + 1) * TT],
                                start=(md == 0), stop=(md == KC - 1))
            return last
        v_rs = sch.emit("tensor", mm_rms,
                        waits=[("AC", osq_last), ("DV", o_copy[KC - 1])], inc="PE")
        v_ms = sch.emit("vector",
                        (lambda e: e.tensor_scalar_mul(SG[0][:], PB[4][:], 1.0 / D)),
                        waits=[("PE", v_rs), ("AC", state["last_silu"]),
                               ("AC", state["last_rowp"])], inc="DV")
        sch.emit("scalar",
                 (lambda e: e.activation(SG[0][:], SG[0][:], AF.Ln, bias=EPS6)),
                 waits=[("DV", v_ms)], inc="AC")
        v_rp = sch.emit("scalar",
                        (lambda e: e.activation(SG[0][:], SG[0][:], AF.Exp, scale=CMH)),
                        inc="AC")
        state["last_rowp"] = v_rp
        res_i = RES[i % 2]
        apply_last = 0
        for md in range(KC):
            sch.emit("vector",
                     (lambda e, md=md: e.scalar_tensor_tensor(
                         SB[:, md * TT:(md + 1) * TT], SB[:, md * TT:(md + 1) * TT],
                         NW[:, md:md + 1], SG[0][:, 0:TT], OP.mult, OP.mult)),
                     waits=[("AC", v_rp)], inc="DV")
            apply_last = sch.emit(
                "vector",
                (lambda e, md=md: e.tensor_add(res_i[:, md * TT:(md + 1) * TT],
                                               res_i[:, md * TT:(md + 1) * TT],
                                               SB[:, md * TT:(md + 1) * TT])),
                inc="DV")
        if DEBUG_TAPS and i == 1:
            tap("d_msrow", SG[0][0:1, :], dv_wait=v_ms)
            tap("d_rstd", ROWP[0:1, 0:TT], ac_wait=v_rp)
            tap("d_x2", res_i[:], dv_wait=apply_last)
        bfs1_war = tv[j]["gateup_pe"] if j < NT else tv[i]["ri_pe"]
        rr3 = ln_stats_bcast(i, res_i, PB[2], PB[3], bfs1_war, v_rs,
                             res_dv=apply_last, eps=EPS5)
        lnb3, lnb3c = normalize_bf16(res_i, 0, 0, SG[0], SG[1], bfs1_war,
                              q_dv=rr3["q"], p_ac=rr3["p"])
        g2_war_ac = tv[j]["silu_last"] if j < NT else tv[i]["sig_i"]
        conv_mid = make_conv_mid(j, RES[j % 2], apply_last) if j < NT else None
        gateup2_pe, silu2_last, stt2_dv, stt2_done = ffn(
            res_i, lnb3, (PB[0], PB[1]), (PB[2], PB[3]), (PB[4], PB[5]),
            gate_war_ac=g2_war_ac, up_war_ac=0, up_war_dv=rr3["var"], mid=conv_mid,
            lnb_chunks=lnb3c)
        tv[i]["gateup2_pe"] = gateup2_pe
        tv[i]["stt2_dv"] = stt2_dv
        if j < NT:
            tv[j]["ri_war_ac"] = silu2_last
            tv[j]["ri_war_dv"] = lnb3
            tv[j]["ri_war_dv2"] = stt2_dv
        store_done = 0
        for md in range(KC):
            store_done = sch.emit(
                "gpsimd",
                (lambda e, i=i, res_i=res_i, md=md: e.dma_start(
                    out_h[:, md, i * TT:(i + 1) * TT],
                    res_i[:, md * TT:(md + 1) * TT])),
                waits=[("DV", stt2_done[md])], inc="OUT", incn=16)
        if i + 2 < NT:
            x_ready[i + 2] = sch.emit(
                "gpsimd",
                (lambda e, i=i: e.dma_start(
                    RES[i % 2][:].rearrange("p (c t) -> p c t", c=KC),
                    xh[:, :, (i + 2) * TT:(i + 3) * TT])),
                waits=[("OUT", store_done)], inc="X", incn=16)
        if j < NT:
            tv[j]["hatt_war"] = tv[i]["outw"]
            tv[j]["xcb"] = sch.emit("vector",
                                    (lambda e: e.tensor_copy(HATT[:], SB[:])),
                                    waits=[("PE", tv[j]["hatt_war"])], inc="DV")
            sig_r = {}
            sig_i = {}
            proj8(wr_s, (PB[0], PB[1]), SC_, RB, tv[j]["xcb"],
                  tv[j]["ri_war_ac"], tv[j]["ri_war_dv"], sig_r)
            if j + 1 < NT:
                jn = j + 1
                rrh = ln_stats_bcast(jn, RES[jn % 2], PB[6], PB[7],
                                     gateup2_pe, rr3["sv"], eps=EPS5,
                                     extra_pe_waits=[("DV", tv[j]["conv_dv"]),
                                                     ("AC", tv[j]["bc2p"])])
                hoisted[jn] = normalize_bf16(RES[jn % 2], 0, 0, SG[0], SG[1],
                                             gateup2_pe, q_dv=rrh["q"],
                                             p_ac=rrh["p"])
            tv[j]["ri_pe"] = proj8(wi_s, (PB[2], PB[3]), SD, IB, tv[j]["xcb"],
                                   tv[j]["ri_war_ac"], tv[j]["ri_war_dv2"], sig_i)
            tv[j]["sig_r"] = sig_r[KC - 1]
            tv[j]["sig_i"] = sig_i[KC - 1]

    if False:
        d_x1 = nc.dram_tensor("d_x1", [P, T], F32, kind="ExternalOutput")
        d_hatt = nc.dram_tensor("d_hatt", [P, T], BF16, kind="ExternalOutput")
        d_r1 = nc.dram_tensor("d_r1", [P, T], F32, kind="ExternalOutput")
        d_i1 = nc.dram_tensor("d_i1", [P, T], F32, kind="ExternalOutput")
        d_xcb1 = nc.dram_tensor("d_xcb1", [P, T], BF16, kind="ExternalOutput")
        sch.emit("gpsimd", (lambda e: e.dma_start(d_x1[:], RES[1][:])),
                 waits=[("DV", tv[1]["stt_dv"])], inc="OUT", incn=0 if False else 16)
        sch.emit("gpsimd", (lambda e: e.dma_start(d_hatt[:], HATT[:])),
                 waits=[("DV", tv[0]["hatt"])], inc="OUT", incn=16)
        sch.emit("gpsimd", (lambda e: e.dma_start(d_r1[:], SC_[:])),
                 waits=[("AC", tv[1]["sig_r"])], inc="OUT", incn=16)
        sch.emit("gpsimd", (lambda e: e.dma_start(d_i1[:], SD[:])),
                 waits=[("AC", tv[1]["sig_i"])], inc="OUT", incn=16)
        sch.emit("gpsimd", (lambda e: e.dma_start(d_xcb1[:], BFS[:, 0:T])),
                 waits=[("DV", tv[1]["xcb"])], inc="OUT", incn=16)
        sch.ops["gpsimd"].append(([("OUT", 16 * NT + 80)], None, None))
    else:
        sch.ops["gpsimd"].append(([("OUT", sch.val("OUT"))], None, None))

    with nc.Block() as blk:
        def runner(lst):
            def body(e):
                for ws, fn, iv in lst:
                    for s, v in ws:
                        e.wait_ge(SEMS[s], v)
                    if fn is None:
                        continue
                    inst = fn(e)
                    if iv is not None:
                        inst.then_inc(SEMS[iv[0]], iv[1])
            return body
        blk.sync(runner(sch.ops["sync"]))
        blk.gpsimd(runner(sch.ops["gpsimd"]))
        blk.tensor(runner(sch.ops["tensor"]))
        blk.vector(runner(sch.ops["vector"]))
        blk.scalar(runner(sch.ops["scalar"]))
    return nc


def _host_layouts(inputs):
    wg = np.asarray(inputs["L0_ffn_w_gate"])
    wu = np.asarray(inputs["L0_ffn_w_up"])
    wd = np.asarray(inputs["L0_ffn_w_down"])
    wr = np.asarray(inputs["L0_attn_Wr_w"], dtype=np.float32)
    wi = np.asarray(inputs["L0_attn_Wi_w"], dtype=np.float32)
    wo = np.asarray(inputs["L0_attn_out_w"], dtype=np.float32)

    def hcmaj(w):
        return np.ascontiguousarray(
            w.reshape(HC, 128, KC, 128).transpose(3, 0, 2, 1).reshape(128, -1))

    def mdmaj(w):
        return np.ascontiguousarray(
            w.reshape(KC, 128, HC, 128).transpose(3, 0, 2, 1).reshape(128, -1))

    def mcmaj(w):
        return np.ascontiguousarray(
            w.reshape(KC, 128, KC, 128).transpose(3, 0, 2, 1).reshape(128, -1))

    def vec(v):
        return np.ascontiguousarray(np.asarray(v, dtype=np.float32).reshape(KC, 128).T)

    cwf = np.asarray(inputs["L0_attn_conv_weight"], dtype=np.float32)[:, 0, :]
    cw = np.ascontiguousarray(
        cwf.reshape(KC, 128, 4).transpose(1, 2, 0).reshape(128, 32))

    common = {
        "wg": hcmaj(wg), "wu": hcmaj(wu), "wd": mdmaj(wd),
        "wr": mcmaj(wr), "wi": mcmaj(wi), "wo": mcmaj(wo),
        "rb": vec(inputs["L0_attn_Wr_b"]), "ib": vec(inputs["L0_attn_Wi_b"]),
        "la": vec(inputs["L0_attn_log_a"]), "nw": vec(inputs["L0_attn_norm_w"]),
        "cw": cw, "cb": vec(inputs["L0_attn_conv_bias"]),
    }
    x = np.asarray(inputs["L0_attn_x"], dtype=np.float32)
    maps = []
    for b in range(8):
        xb = np.ascontiguousarray(
            x[b].T.reshape(KC, 128, T).transpose(1, 0, 2))
        m = dict(common)
        m["x"] = xb
        maps.append(m)
    return maps


def kernel(**inputs):
    in_maps = _host_layouts(inputs)
    nc = build_nc()
    res = run_bass_kernel_spmd(nc, in_maps, core_ids=list(range(8)))
    outs = []
    for b in range(8):
        o = np.asarray(res.results[b]["out"])
        outs.append(o.transpose(1, 0, 2).reshape(D, T).T)
    return np.stack(outs).astype(np.float32)


# version 12
# speedup vs baseline: 1.2267x; 1.0117x over previous
import numpy as np
import concourse.bass as bass
import concourse.mybir as mybir
from concourse.bass_utils import run_bass_kernel_spmd

F32 = mybir.dt.float32
BF16 = mybir.dt.bfloat16
I32 = mybir.dt.int32
AF = mybir.ActivationFunctionType
OP = mybir.AluOpType

D = 1024
KC = 8
H = 3072
HC = 24
T = 4096
TT = 512
NT = T // TT
P = 128
RING = 6
UNIT = 4096

S127 = 1.0 / 127.0
DOWN_SCALE = 0.5 * S127 * S127
LOG1EM7 = float(np.float32(np.log(np.float32(1e-7))))
DEBUG_TAPS = False
INTERLEAVE = True


class Sched:
    def __init__(self):
        self.ops = {"sync": [], "gpsimd": [], "tensor": [], "vector": [], "scalar": []}
        self.cnt = {}
        self.marks = {}

    def val(self, sem):
        return self.cnt.get(sem, 0)

    def emit(self, eng, fn, waits=(), inc=None, incn=1):
        ws = []
        for s, v in waits:
            if not v or v <= 0:
                continue
            key = (eng, s)
            if self.marks.get(key, 0) >= v:
                continue
            self.marks[key] = v
            ws.append((s, v))
        iv = None
        if inc is not None:
            self.cnt[inc] = self.cnt.get(inc, 0) + incn
            iv = (inc, incn)
        self.ops[eng].append((ws, fn, iv))
        return self.cnt.get(inc, 0) if inc else None


def build_nc():
    nc = bass.Bass("TRN2", target_bir_lowering=False)

    xh = nc.dram_tensor("x", [P, KC, T], F32, kind="ExternalInput")
    wg_i = nc.dram_tensor("wg", [P, HC * 8 * 128], I32, kind="ExternalInput")
    wu_i = nc.dram_tensor("wu", [P, HC * 8 * 128], I32, kind="ExternalInput")
    wd_i = nc.dram_tensor("wd", [P, 8 * HC * 128], I32, kind="ExternalInput")
    wr_i = nc.dram_tensor("wr", [P, KC * 1024], F32, kind="ExternalInput")
    wi_i = nc.dram_tensor("wi", [P, KC * 1024], F32, kind="ExternalInput")
    wo_i = nc.dram_tensor("wo", [P, KC * 1024], F32, kind="ExternalInput")
    rb_i = nc.dram_tensor("rb", [P, KC], F32, kind="ExternalInput")
    ib_i = nc.dram_tensor("ib", [P, KC], F32, kind="ExternalInput")
    la_i = nc.dram_tensor("la", [P, KC], F32, kind="ExternalInput")
    nw_i = nc.dram_tensor("nw", [P, KC], F32, kind="ExternalInput")
    cw_i = nc.dram_tensor("cw", [P, 32], F32, kind="ExternalInput")
    cb_i = nc.dram_tensor("cb", [P, KC], F32, kind="ExternalInput")
    out_h = nc.dram_tensor("out", [P, KC, T], F32, kind="ExternalOutput")

    wg_s = nc.dram_tensor("wg_s", [P, HC * 8 * 128], BF16, kind="Internal")
    wu_s = nc.dram_tensor("wu_s", [P, HC * 8 * 128], BF16, kind="Internal")
    wd_s = nc.dram_tensor("wd_s", [P, 8 * HC * 128], BF16, kind="Internal")
    wr_s = nc.dram_tensor("wr_s", [P, KC * 1024], BF16, kind="Internal")
    wi_s = nc.dram_tensor("wi_s", [P, KC * 1024], BF16, kind="Internal")
    wo_s = nc.dram_tensor("wo_s", [P, KC * 1024], BF16, kind="Internal")

    import contextlib
    ctx = contextlib.ExitStack()
    nc._kernel_ctx = ctx
    sb = lambda n, s, d=F32: ctx.enter_context(nc.sbuf_tensor(n, s, d))

    RES = [sb("RES0", [P, T]), sb("RES1", [P, T])]
    SA = sb("SA", [P, T])
    SB = sb("SB", [P, T])
    SC_ = sb("SC", [P, T])
    SD = sb("SD", [P, T])
    BFS = sb("BFS", [P, 2 * T], BF16)
    HATT = sb("HATT", [P, T], BF16)
    HH = sb("HH", [P, HC * TT], BF16)
    LTMP = sb("LTMP", [P, TT + 8])
    ROWP = LTMP
    TAIL = sb("TAIL", [P, KC * 4])
    WRG = [sb(f"WRG{j}", [P, UNIT], BF16) for j in range(RING)]
    SG = [sb("SG0", [P, TT]), sb("SG1", [P, TT])]
    ONES = sb("ONESF", [P, TT])
    ONESR = ONES
    ONESB = sb("ONESB", [P, P], BF16)
    CONSTS = sb("CONSTS", [P, 8])
    CW = sb("CWt", [P, 32])
    CB = sb("CBt", [P, KC])
    RB = sb("RBt", [P, KC])
    IB = sb("IBt", [P, KC])
    LAT = sb("LAt", [P, KC])
    NW = sb("NWt", [P, KC])
    L8 = sb("L8", [P, KC])
    L16 = sb("L16", [P, KC])
    SCAR = sb("SCAR", [P, KC])
    CCAR = sb("CCAR", [P, KC])
    RED = sb("REDt", [P, KC])
    PB = [ctx.enter_context(nc.psum_tensor(f"PB{j}", [P, TT], F32)) for j in range(8)]

    cvals = [S127, -1.0, -0.5, 1e-5, 1e-6]
    for j_, v in enumerate(cvals):
        nc.const_aps.register(F32, float(np.float32(v)), CONSTS[:, j_:j_ + 1])
    C127, CM1, CMH, EPS5, EPS6 = [float(np.float32(v)) for v in cvals]

    SEMS = {}
    for n in ["PE", "DV", "AC", "WL", "X", "OUT", "CV"]:
        SEMS[n] = ctx.enter_context(nc.semaphore("sem_" + n))

    sch = Sched()
    ring_state = {"u": 0, "done": {}, "loaded": {}}

    def unit_load(src_ap, ncols):
        u = ring_state["u"]
        ring_state["u"] += 1
        slot = WRG[u % RING]
        waits = [("CV", src_cast_done[src_ap.tensor.name])]
        prev = u - RING
        if prev >= 0:
            waits.append(("PE", ring_state["done"].get(prev, 0)))
        def fn(e, slot=slot, src=src_ap, ncols=ncols):
            return e.dma_start(slot[:, 0:ncols], src)
        v = sch.emit("sync", fn, waits=waits, inc="WL", incn=16)
        ring_state["loaded"][u] = v
        return slot, u

    def unit_used(u):
        ring_state["done"][u] = sch.val("PE")

    src_cast_done = {}
    def cast_dma(dst, src):
        def fn(e, dst=dst, src=src):
            return e.dma_start(dst[:], src[:])
        v = sch.emit("gpsimd", fn, inc="CV", incn=16)
        src_cast_done[dst.name] = v

    cast_dma(wg_s, wg_i)
    cast_dma(wu_s, wu_i)
    cast_dma(wd_s, wd_i)
    cast_dma(wr_s, wr_i)
    cast_dma(wi_s, wi_i)
    cast_dma(wo_s, wo_i)

    for dst, src in [(CW, cw_i), (CB, cb_i), (RB, rb_i), (IB, ib_i),
                     (LAT, la_i), (NW, nw_i)]:
        sch.emit("sync", (lambda e, d=dst, s=src: e.dma_start(d[:], s[:])),
                 inc="X", incn=16)
    x_ready = {}
    for i in range(2):
        x_ready[i] = sch.emit(
            "gpsimd",
            (lambda e, i=i: e.dma_start(RES[i % 2][:].rearrange("p (c t) -> p c t", c=KC),
                                        xh[:, :, i * TT:(i + 1) * TT])),
            inc="X", incn=16)

    def ms(ap, v):
        sch.emit("gpsimd", (lambda e, ap=ap, v=v: e.memset(ap, v)))
    ms(ONES[:], 1.0)
    ms(ONESB[:], 1.0)
    ms(TAIL[:], 0.0)
    ms(SCAR[:], 0.0)
    ms(CCAR[:], 0.0)
    for j_, v in enumerate(cvals):
        ms(CONSTS[:, j_:j_ + 1], float(np.float32(v)))
    setup_done = sch.emit("gpsimd", (lambda e: e.memset(CONSTS[:, 7:8], 0.0)),
                          inc="X", incn=1)

    sch.emit("scalar", (lambda e: e.activation(L8[:], LAT[:], AF.Sigmoid)),
             waits=[("X", setup_done)], inc="AC")
    v_ln = sch.emit("scalar", (lambda e: e.activation(L8[:], L8[:], AF.Ln)), inc="AC")
    sch.emit("vector",
             (lambda e: e.tensor_scalar(L16[:], L8[:], 16.0, -1e30, OP.mult, OP.max)),
             waits=[("AC", v_ln)], inc="DV")
    v_l8 = sch.emit("vector",
                    (lambda e: e.tensor_scalar(L8[:], L8[:], 8.0, -1e30, OP.mult, OP.max)),
                    inc="DV")

    tv = [dict() for _ in range(NT + 1)]
    state = {"scar_v": 0, "ccar_v": 0, "last_silu": 0, "last_rowp": 0}

    dbg = {}
    if DEBUG_TAPS:
        dbg["d_scar"] = nc.dram_tensor("d_scar", [P, KC], F32, kind="ExternalOutput")
        for nm in ["d_l8a", "d_l8b", "d_l8c", "d_l8d"]:
            dbg[nm] = nc.dram_tensor(nm, [P, KC], F32, kind="ExternalOutput")
        dbg["d_r1pre"] = nc.dram_tensor("d_r1pre", [P, T], F32, kind="ExternalOutput")
        dbg["d_xc1"] = nc.dram_tensor("d_xc1", [P, T], F32, kind="ExternalOutput")
        dbg["d_xcb1"] = nc.dram_tensor("d_xcb1", [P, T], BF16, kind="ExternalOutput")
        dbg["d_rstd2"] = nc.dram_tensor("d_rstd2", [1, TT], F32, kind="ExternalOutput")
        dbg["d_q2"] = nc.dram_tensor("d_q2", [1, TT], F32, kind="ExternalOutput")
        for nm, dt_ in [("d_x1", F32), ("d_o", F32), ("d_x2", F32), ("d_hatt", BF16),
                        ("d_logA", F32), ("d_aexp", F32), ("d_cs", F32), ("d_term", F32),
                        ("d_la", F32), ("d_s", F32),
                        ("d_msrow", F32), ("d_rstd", F32)]:
            dbg[nm] = nc.dram_tensor(nm, [P, T] if nm not in ("d_msrow", "d_rstd")
                                     else [1, TT], dt_, kind="ExternalOutput")

    def tap(name, src_ap, dv_wait=None, ac_wait=None):
        if not DEBUG_TAPS:
            return
        w = []
        if dv_wait: w.append(("DV", dv_wait))
        if ac_wait: w.append(("AC", ac_wait))
        v = sch.emit("gpsimd", (lambda e, s=src_ap, d=dbg[name]: e.dma_start(d[:], s)),
                     waits=w, inc="OUT", incn=16)
        sch.emit("vector", (lambda e: e.engine_nop()), waits=[("OUT", v)], inc="DV")
        state["dbg_out"] = v

    def ln_stats(i, res, pb_m, pb_v, bfs1_war_pe, bfs2_war_pe, res_dv=0,
                 extra_pe_waits=()):
        r = {}
        r["xb"] = sch.emit("vector", (lambda e: e.tensor_copy(BFS[:, 0:T], res[:])),
                           waits=[("PE", bfs1_war_pe), ("X", x_ready.get(i, 0)),
                                  ("X", setup_done), ("DV", res_dv)], inc="DV")
        r["xsq"] = sch.emit("scalar",
                            (lambda e: e.activation(BFS[:, T:2 * T], res[:], AF.Square)),
                            waits=[("PE", bfs2_war_pe), ("X", x_ready.get(i, 0)),
                                   ("X", setup_done), ("DV", res_dv)], inc="AC")

        def mm_stats(e, pb, off):
            last = None
            for kc in range(KC):
                last = e.matmul(pb[0:1, :], ONESB[:, 0:1],
                                BFS[:, off + kc * TT: off + (kc + 1) * TT],
                                start=(kc == 0), stop=(kc == KC - 1))
            return last
        r["sm"] = sch.emit("tensor", (lambda e: mm_stats(e, pb_m, 0)),
                           waits=[("DV", r["xb"]), ("X", setup_done)] + list(extra_pe_waits),
                           inc="PE")
        r["sv"] = sch.emit("tensor", (lambda e: mm_stats(e, pb_v, T)),
                           waits=[("AC", r["xsq"])], inc="PE")
        sch.emit("vector",
                 (lambda e: e.tensor_scalar_mul(SG[0][0:1, :], pb_m[0:1, :], 1.0 / D)),
                 waits=[("PE", r["sm"]), ("AC", state["last_silu"]),
                        ("AC", state["last_rowp"])], inc="DV")
        sch.emit("vector",
                 (lambda e: e.tensor_mul(ROWP[0:1, 0:TT], SG[0][0:1, :], SG[0][0:1, :])),
                 inc="DV")
        r["var"] = sch.emit("vector",
                            (lambda e: e.scalar_tensor_tensor(
                                SG[1][0:1, :], pb_v[0:1, :], 1.0 / D,
                                ROWP[0:1, 0:TT], OP.mult, OP.subtract)),
                            waits=[("PE", r["sv"])], inc="DV")
        sch.emit("scalar",
                 (lambda e: e.activation(SG[1][0:1, :], SG[1][0:1, :], AF.Ln, bias=EPS5)),
                 waits=[("DV", r["var"])], inc="AC")
        r["p"] = sch.emit("scalar",
                          (lambda e: e.activation(ROWP[0:1, 0:TT], SG[1][0:1, :],
                                                  AF.Exp, scale=CMH)),
                          inc="AC")
        state["last_rowp"] = r["p"]
        r["q"] = sch.emit("vector",
                          (lambda e: e.scalar_tensor_tensor(SG[0][0:1, :], SG[0][0:1, :],
                                                            -1.0, ROWP[0:1, 0:TT],
                                                            OP.mult, OP.mult)),
                          waits=[("AC", r["p"])], inc="DV")
        r["rows_dv"] = sch.val("DV")
        return r

    def bcast_rows(rr, pb_p, pb_q, war_ac=0, war_dv=0):
        v_bp = sch.emit("tensor",
                        (lambda e: e.matmul(pb_p[:], ONESR[0:1, 0:P], ROWP[0:1, 0:TT],
                                            start=True, stop=True)),
                        waits=[("AC", rr["p"]), ("AC", war_ac), ("DV", war_dv)], inc="PE")
        v_bq = sch.emit("tensor",
                        (lambda e: e.matmul(pb_q[:], ONESR[0:1, 0:P], SG[0][0:1, :],
                                            start=True, stop=True)),
                        waits=[("DV", rr["q"])], inc="PE")
        return v_bp, v_bq

    def ln_stats_bcast(i, res, pb_m, pb_v, bfs1_war_pe, bfs2_war_pe, res_dv=0,
                       eps=None, extra_pe_waits=()):
        r = {}
        r["xb"] = sch.emit("vector", (lambda e: e.tensor_copy(BFS[:, 0:T], res[:])),
                           waits=[("PE", bfs1_war_pe), ("X", x_ready.get(i, 0)),
                                  ("X", setup_done), ("DV", res_dv)], inc="DV")
        r["xsq"] = sch.emit("scalar",
                            (lambda e: e.activation(BFS[:, T:2 * T], res[:], AF.Square)),
                            waits=[("PE", bfs2_war_pe), ("X", x_ready.get(i, 0)),
                                   ("X", setup_done), ("DV", res_dv)], inc="AC")
        def mm_stats(e, pb, off):
            last = None
            for kc in range(KC):
                last = e.matmul(pb[:], ONESB[:, 0:P],
                                BFS[:, off + kc * TT: off + (kc + 1) * TT],
                                start=(kc == 0), stop=(kc == KC - 1))
            return last
        r["sm"] = sch.emit("tensor", (lambda e: mm_stats(e, pb_m, 0)),
                           waits=[("DV", r["xb"]), ("X", setup_done)] + list(extra_pe_waits),
                           inc="PE")
        r["sv"] = sch.emit("tensor", (lambda e: mm_stats(e, pb_v, T)),
                           waits=[("AC", r["xsq"])], inc="PE")
        sch.emit("vector",
                 (lambda e: e.tensor_scalar_mul(SG[1][:], pb_m[:], 1.0 / D)),
                 waits=[("PE", r["sm"]), ("AC", state["last_silu"]),
                        ("AC", state["last_rowp"])], inc="DV")
        sch.emit("vector", (lambda e: e.tensor_mul(SG[0][:], SG[1][:], SG[1][:])),
                 inc="DV")
        r["var"] = sch.emit("vector",
                            (lambda e: e.scalar_tensor_tensor(
                                SG[0][:], pb_v[:], 1.0 / D, SG[0][:],
                                OP.mult, OP.subtract)),
                            waits=[("PE", r["sv"])], inc="DV")
        sch.emit("scalar",
                 (lambda e: e.activation(SG[0][:], SG[0][:], AF.Ln, bias=eps)),
                 waits=[("DV", r["var"])], inc="AC")
        r["p"] = sch.emit("scalar",
                          (lambda e: e.activation(SG[0][:], SG[0][:], AF.Exp, scale=CMH)),
                          inc="AC")
        state["last_rowp"] = r["p"]
        r["q"] = sch.emit("vector",
                          (lambda e: e.scalar_tensor_tensor(SG[1][:], SG[1][:], -1.0,
                                                            SG[0][:], OP.mult, OP.mult)),
                          waits=[("AC", r["p"])], inc="DV")
        return r

    def ln2_stats_psum(j, res, bfs_war_pe, res_dv):
        r = {}
        r["xb"] = sch.emit("vector", (lambda e: e.tensor_copy(BFS[:, 0:T], res[:])),
                           waits=[("PE", bfs_war_pe), ("DV", res_dv)], inc="DV")
        r["xsq"] = sch.emit("scalar",
                            (lambda e: e.activation(BFS[:, T:2 * T], res[:], AF.Square)),
                            waits=[("PE", bfs_war_pe), ("DV", res_dv)], inc="AC")
        def mm_stats(e, pb, off):
            last = None
            for kc in range(KC):
                last = e.matmul(pb[:], ONESB[:, 0:P],
                                BFS[:, off + kc * TT: off + (kc + 1) * TT],
                                start=(kc == 0), stop=(kc == KC - 1))
            return last
        r["sm"] = sch.emit("tensor", (lambda e: mm_stats(e, PB[6], 0)),
                           waits=[("DV", r["xb"])], inc="PE")
        r["sv"] = sch.emit("tensor", (lambda e: mm_stats(e, PB[7], T)),
                           waits=[("AC", r["xsq"])], inc="PE")
        sch.emit("vector",
                 (lambda e: e.tensor_scalar_mul(SG[1][:], PB[6][:], 1.0 / D)),
                 waits=[("PE", r["sm"]), ("AC", state["last_silu"]),
                        ("AC", state["last_rowp"])], inc="DV")
        sch.emit("vector", (lambda e: e.tensor_mul(SG[0][:], SG[1][:], SG[1][:])),
                 inc="DV")
        r["var"] = sch.emit("vector",
                            (lambda e: e.scalar_tensor_tensor(
                                SG[0][:], PB[7][:], 1.0 / D, SG[0][:],
                                OP.mult, OP.subtract)),
                            waits=[("PE", r["sv"])], inc="DV")
        sch.emit("scalar",
                 (lambda e: e.activation(SG[0][:], SG[0][:], AF.Ln, bias=EPS5)),
                 waits=[("DV", r["var"])], inc="AC")
        r["bc2p"] = sch.emit("scalar",
                             (lambda e: e.activation(PB[6][:], SG[0][:], AF.Exp,
                                                     scale=CMH)),
                             inc="AC")
        state["last_rowp"] = r["bc2p"]
        r["bc2q"] = sch.emit("vector",
                             (lambda e: e.scalar_tensor_tensor(PB[7][:], SG[1][:], -1.0,
                                                               PB[6][:], OP.mult,
                                                               OP.mult)),
                             waits=[("AC", r["bc2p"])], inc="DV")
        tv[j]["bc2p"] = r["bc2p"]
        tv[j]["bc2q"] = r["bc2q"]
        return r

    def normalize_bf16(res, v_bp, v_bq, pb_p, pb_q, bfs1_war_pe, q_dv=0, p_ac=0):
        last = 0
        chunks = []
        pp = pb_p[:, 0:TT] if pb_p in (SG[0], SG[1]) else pb_p[:]
        qq = pb_q[:, 0:TT] if pb_q in (SG[0], SG[1]) else pb_q[:]
        for c in range(KC):
            sch.emit("vector",
                     (lambda e, c=c: e.tensor_mul(LTMP[:, 0:TT],
                                                  res[:, c * TT:(c + 1) * TT], pp)),
                     waits=[("PE", v_bp), ("PE", v_bq), ("PE", bfs1_war_pe),
                            ("DV", q_dv), ("AC", p_ac)], inc="DV")
            last = sch.emit("vector",
                            (lambda e, c=c: e.tensor_add(BFS[:, c * TT:(c + 1) * TT],
                                                         LTMP[:, 0:TT], qq)),
                            inc="DV")
            chunks.append(last)
        return last, chunks

    def mid_make(i):
        g = {}
        def g_la_ixc():
            if DEBUG_TAPS and i == 1:
                tap("d_l8c", L8[:], ac_wait=tv[i]["sig_r"])
                tap("d_r1pre", SC_[:], ac_wait=tv[i]["sig_r"])
            for c in range(KC):
                sch.emit("vector",
                         (lambda e, c=c: e.tensor_scalar(
                             SA[:, c * TT:(c + 1) * TT], SC_[:, c * TT:(c + 1) * TT],
                             L8[:, c:c + 1], LOG1EM7, OP.mult, OP.max)),
                         waits=[("AC", tv[i]["sig_r"]), ("DV", v_l8)], inc="DV")
            for c in range(KC):
                tv[i]["ixc"] = sch.emit(
                    "vector",
                    (lambda e, c=c: e.tensor_mul(
                        SD[:, c * TT:(c + 1) * TT], SD[:, c * TT:(c + 1) * TT],
                        SB[:, c * TT:(c + 1) * TT])),
                    waits=[("AC", tv[i]["sig_i"])], inc="DV")
        def a_a2():
            for c in range(KC):
                tv[i]["a2"] = sch.emit(
                    "scalar",
                    (lambda e, c=c: e.activation(
                        SB[:, c * TT:(c + 1) * TT], SC_[:, c * TT:(c + 1) * TT],
                        AF.Exp, scale=L16[:, c:c + 1])),
                    waits=[("DV", tv[i]["ixc"])], inc="AC")
        def g_v():
            sch.emit("vector",
                     (lambda e: e.tensor_scalar(SC_[:], SB[:], 1.0, -1.0,
                                                OP.subtract, OP.mult)),
                     waits=[("AC", tv[i]["a2"])], inc="DV")
            tv[i]["vmax"] = sch.emit(
                "vector", (lambda e: e.tensor_scalar_max(SC_[:], SC_[:], 1e-6)), inc="DV")
        def a_bamp():
            tv[i]["bamp"] = sch.emit(
                "scalar", (lambda e: e.activation(SB[:], SC_[:], AF.Sqrt)),
                waits=[("DV", tv[i]["vmax"])], inc="AC")
        def g_bt():
            tv[i]["bt"] = sch.emit(
                "vector", (lambda e: e.tensor_mul(SD[:], SD[:], SB[:])),
                waits=[("AC", tv[i]["bamp"])], inc="DV")
        def g_ss(c0, c1):
            def f():
                for c in range(c0, c1):
                    sch.emit("vector",
                             (lambda e, c=c: e.tensor_tensor_scan(
                                 SB[:, c * TT:(c + 1) * TT], ONES[:],
                                 SA[:, c * TT:(c + 1) * TT], SCAR[:, c:c + 1],
                                 OP.mult, OP.add)),
                             waits=[("DV", state["scar_v"]), ("DV", tv[i]["bt"])],
                             inc="DV")
            return f
        def g_sred():
            for c in range(KC):
                sch.emit("vector",
                         (lambda e, c=c: e.tensor_reduce(
                             RED[:, c:c + 1], SA[:, c * TT:(c + 1) * TT],
                             mybir.AxisListType.X, OP.add)),
                         inc="DV")
            sch.emit("vector", (lambda e: e.engine_nop()), inc="DV")
            sch.emit("vector", (lambda e: e.engine_nop()), inc="DV")
            state["scar_v"] = sch.emit(
                "vector", (lambda e: e.tensor_add(SCAR[:], SCAR[:], RED[:])), inc="DV")
        def g_logA():
            tv[i]["logA"] = sch.emit(
                "vector", (lambda e: e.tensor_scalar_max(SB[:], SB[:], -80.0)), inc="DV")
            if DEBUG_TAPS and i == 1:
                tap("d_logA", SB[:], dv_wait=tv[i]["logA"])
        def a_en():
            tv[i]["en"] = sch.emit(
                "scalar", (lambda e: e.activation(SA[:], SB[:], AF.Exp, scale=CM1)),
                waits=[("DV", tv[i]["logA"])], inc="AC")
        def g_term():
            tv[i]["term"] = sch.emit(
                "vector", (lambda e: e.tensor_mul(SD[:], SD[:], SA[:])),
                waits=[("AC", tv[i]["en"])], inc="DV")
            if DEBUG_TAPS and i == 1:
                tap("d_term", SD[:], dv_wait=tv[i]["term"])
        def a_aexp():
            tv[i]["aexp"] = sch.emit(
                "scalar", (lambda e: e.activation(SC_[:], SB[:], AF.Exp)),
                waits=[("DV", tv[i]["term"])], inc="AC")
        def g_cs(c0, c1):
            def f():
                for c in range(c0, c1):
                    sch.emit("vector",
                             (lambda e, c=c: e.tensor_tensor_scan(
                                 SA[:, c * TT:(c + 1) * TT], ONES[:],
                                 SD[:, c * TT:(c + 1) * TT], CCAR[:, c:c + 1],
                                 OP.mult, OP.add)),
                             waits=[("DV", state["ccar_v"]), ("DV", tv[i]["term"])],
                             inc="DV")
            return f
        def g_cred():
            for c in range(KC):
                sch.emit("vector",
                         (lambda e, c=c: e.tensor_reduce(
                             RED[:, c:c + 1], SD[:, c * TT:(c + 1) * TT],
                             mybir.AxisListType.X, OP.add)),
                         inc="DV")
            sch.emit("vector", (lambda e: e.engine_nop()), inc="DV")
            sch.emit("vector", (lambda e: e.engine_nop()), inc="DV")
            state["ccar_v"] = sch.emit(
                "vector", (lambda e: e.tensor_add(CCAR[:], CCAR[:], RED[:])), inc="DV")
        def g_hatt():
            if DEBUG_TAPS and i == 1:
                tap("d_aexp", SC_[:], ac_wait=tv[i]["aexp"])
                tap("d_cs", SA[:], dv_wait=state["ccar_v"])
            tv[i]["hatt"] = sch.emit(
                "vector", (lambda e: e.tensor_mul(HATT[:], SC_[:], SA[:])),
                waits=[("AC", tv[i]["aexp"]),
                       ("PE", tv[i - 1].get("outw", 0) if i > 0 else 0)],
                inc="DV")
        g["dve"] = {2: [g_la_ixc], 6: [g_v], 8: [g_bt], 10: [g_ss(0, 4)],
                    12: [g_ss(4, 8)], 14: [g_sred, g_logA], 16: [g_term],
                    18: [g_cs(0, 4)], 20: [g_cs(4, 8)], 22: [g_cred], 23: [g_hatt]}
        g["act"] = {4: [a_a2], 7: [a_bamp], 15: [a_en], 21: [a_aexp]}
        g["all"] = [g_la_ixc, a_a2, g_v, a_bamp, g_bt, g_ss(0, 8), g_sred, g_logA,
                    a_en, g_term, a_aexp, g_cs(0, 8), g_cred, g_hatt]
        return g

    def ffn(res, lnb_ready_dv, pg_pair, pu_pair, pd_pair,
            gate_war_ac=0, up_war_ac=0, up_war_dv=0, mid=None, lnb_chunks=None):
        silu_done = {}
        h_done = {}
        gslot = uslot = None
        gu = uu = None
        for hc in range(HC):
            if mid is not None and INTERLEAVE:
                for fn_ in mid["dve"].get(hc, []):
                    fn_()
                for fn_ in mid["act"].get(hc, []):
                    fn_()
            if hc % 4 == 0:
                gslot, gu = unit_load(wg_s[:, hc * 1024:(hc + 4) * 1024], UNIT)
                uslot, uu = unit_load(wu_s[:, hc * 1024:(hc + 4) * 1024], UNIT)
            pg = pg_pair[hc % len(pg_pair)]
            pu = pu_pair[hc % 2]
            base = (hc % 4) * 1024
            if hc < 2 and lnb_chunks is not None:
                gv = uv = 0
                for kc in range(KC):
                    gw = [("DV", lnb_chunks[kc])]
                    uw = []
                    if kc == 0:
                        gw += [("WL", ring_state["loaded"][gu]),
                               ("AC", silu_done.get(hc - len(pg_pair), gate_war_ac))]
                        uw += [("WL", ring_state["loaded"][uu]),
                               ("DV", h_done.get(hc - 2, up_war_dv)),
                               ("AC", up_war_ac)]
                    gv = sch.emit(
                        "tensor",
                        (lambda e, pg=pg, gslot=gslot, base=base, kc=kc:
                         e.matmul(pg[:], gslot[:, base + kc * 128: base + kc * 128 + 128],
                                  BFS[:, kc * TT:(kc + 1) * TT],
                                  start=(kc == 0), stop=(kc == KC - 1))),
                        waits=gw, inc="PE")
                    uv = sch.emit(
                        "tensor",
                        (lambda e, pu=pu, uslot=uslot, base=base, kc=kc:
                         e.matmul(pu[:], uslot[:, base + kc * 128: base + kc * 128 + 128],
                                  BFS[:, kc * TT:(kc + 1) * TT],
                                  start=(kc == 0), stop=(kc == KC - 1))),
                        waits=uw, inc="PE")
                unit_used(gu)
                unit_used(uu)
                sg = SG[hc % 2]
                sv = sch.emit("scalar",
                              (lambda e, sg=sg, pg=pg: e.activation(sg[:], pg[:], AF.Silu,
                                                                    scale=C127)),
                              waits=[("PE", gv), ("DV", h_done.get(hc - 2, 0))], inc="AC")
                silu_done[hc] = sv
                state["last_silu"] = sv
                h_done[hc] = sch.emit(
                    "vector",
                    (lambda e, sg=sg, pu=pu, hc=hc:
                     e.tensor_mul(HH[:, hc * TT:(hc + 1) * TT], sg[:], pu[:])),
                    waits=[("AC", sv), ("PE", uv)], inc="DV")
                continue
            def mm_gate(e, pg=pg, gslot=gslot, base=base):
                last = None
                for kc in range(KC):
                    last = e.matmul(pg[:], gslot[:, base + kc * 128: base + kc * 128 + 128],
                                    BFS[:, kc * TT:(kc + 1) * TT],
                                    start=(kc == 0), stop=(kc == KC - 1))
                return last
            gv = sch.emit("tensor", mm_gate,
                          waits=[("WL", ring_state["loaded"][gu]), ("DV", lnb_ready_dv),
                                 ("AC", silu_done.get(hc - len(pg_pair), gate_war_ac))],
                          inc="PE")
            unit_used(gu)
            def mm_up(e, pu=pu, uslot=uslot, base=base):
                last = None
                for kc in range(KC):
                    last = e.matmul(pu[:], uslot[:, base + kc * 128: base + kc * 128 + 128],
                                    BFS[:, kc * TT:(kc + 1) * TT],
                                    start=(kc == 0), stop=(kc == KC - 1))
                return last
            uv = sch.emit("tensor", mm_up,
                          waits=[("WL", ring_state["loaded"][uu]),
                                 ("DV", h_done.get(hc - 2, up_war_dv)),
                                 ("AC", up_war_ac if hc < 2 else 0)], inc="PE")
            unit_used(uu)
            sg = SG[hc % 2]
            sv = sch.emit("scalar",
                          (lambda e, sg=sg, pg=pg: e.activation(sg[:], pg[:], AF.Silu,
                                                                scale=C127)),
                          waits=[("PE", gv), ("DV", h_done.get(hc - 2, 0))], inc="AC")
            silu_done[hc] = sv
            state["last_silu"] = sv
            h_done[hc] = sch.emit(
                "vector",
                (lambda e, sg=sg, pu=pu, hc=hc:
                 e.tensor_mul(HH[:, hc * TT:(hc + 1) * TT], sg[:], pu[:])),
                waits=[("AC", sv), ("PE", uv)], inc="DV")
        gateup_pe = sch.val("PE")
        silu_last = silu_done[HC - 1]
        if mid is not None and not INTERLEAVE:
            for fn_ in mid["all"]:
                fn_()
        stt_done = {}
        for md in range(KC):
            dslot, du = unit_load(wd_s[:, md * HC * 128:(md + 1) * HC * 128], HC * 128)
            pd = pd_pair[md % 2]
            def mm_down(e, pd=pd, dslot=dslot):
                last = None
                for hc in range(HC):
                    last = e.matmul(pd[:], dslot[:, hc * 128: hc * 128 + 128],
                                    HH[:, hc * TT:(hc + 1) * TT],
                                    start=(hc == 0), stop=(hc == HC - 1))
                return last
            dv_ = sch.emit("tensor", mm_down,
                           waits=[("WL", ring_state["loaded"][du]), ("DV", h_done[HC - 1]),
                                  ("DV", stt_done.get(md - 2, 0))], inc="PE")
            unit_used(du)
            stt_done[md] = sch.emit(
                "vector",
                (lambda e, pd=pd, md=md: e.scalar_tensor_tensor(
                    res[:, md * TT:(md + 1) * TT], pd[:], DOWN_SCALE,
                    res[:, md * TT:(md + 1) * TT], OP.mult, OP.add)),
                waits=[("PE", dv_)], inc="DV")
        return gateup_pe, silu_last, sch.val("DV"), stt_done

    def proj8(wsrc, pp_pair, dst, bias, rhs_dv, war_ac, war_dv, sig_store):
        units = []
        last_pe = 0
        for mc in range(KC):
            if mc % 4 == 0:
                units.append(unit_load(wsrc[:, mc * 1024:(mc + 4) * 1024], UNIT))
            slot, u = units[-1]
            pr = pp_pair[mc % 2]
            base = (mc % 4) * 1024
            def mm(e, pr=pr, slot=slot, base=base):
                last = None
                for kc in range(KC):
                    last = e.matmul(pr[:], slot[:, base + kc * 128: base + kc * 128 + 128],
                                    HATT[:, kc * TT:(kc + 1) * TT],
                                    start=(kc == 0), stop=(kc == KC - 1))
                return last
            rv = sch.emit("tensor", mm,
                          waits=[("WL", ring_state["loaded"][u]), ("DV", rhs_dv),
                                 ("AC", sig_store.get(mc - 2, war_ac)),
                                 ("DV", war_dv if mc < 2 else 0)], inc="PE")
            last_pe = rv
            unit_used(u)
            sig_store[mc] = sch.emit(
                "scalar",
                (lambda e, pr=pr, mc=mc: e.activation(
                    dst[:, mc * TT:(mc + 1) * TT], pr[:], AF.Sigmoid,
                    bias=bias[:, mc:mc + 1])),
                waits=[("PE", rv)], inc="AC")
        return last_pe

    def make_conv_mid(j, res_j, conv_war_dv):
        def mk(c):
            def f():
                sch.emit("vector",
                         (lambda e: e.tensor_mul(SD[:, 0:TT],
                                                 res_j[:, c * TT:(c + 1) * TT], PB[6][:])),
                         waits=[("AC", tv[j]["bc2p"]), ("DV", tv[j]["bc2q"]),
                                ("DV", conv_war_dv)], inc="DV")
                sch.emit("vector",
                         (lambda e: e.tensor_add(LTMP[:, 3:3 + TT], SD[:, 0:TT],
                                                 PB[7][:])), inc="DV")
                sch.emit("vector",
                         (lambda e: e.tensor_copy(LTMP[:, 0:3],
                                                  TAIL[:, c * 4:c * 4 + 3])), inc="DV")
                sch.emit("vector",
                         (lambda e: e.tensor_copy(TAIL[:, c * 4:c * 4 + 3],
                                                  LTMP[:, TT:TT + 3])), inc="DV")
                sch.emit("vector",
                         (lambda e: e.tensor_scalar(SB[:, c * TT:(c + 1) * TT],
                                                    LTMP[:, 0:TT], CW[:, c:c + 1],
                                                    CB[:, c:c + 1], OP.mult, OP.add)),
                         inc="DV")
                for k in range(1, 4):
                    sch.emit("vector",
                             (lambda e, k=k: e.scalar_tensor_tensor(
                                 SB[:, c * TT:(c + 1) * TT], LTMP[:, k:k + TT],
                                 CW[:, k * 8 + c:k * 8 + c + 1],
                                 SB[:, c * TT:(c + 1) * TT], OP.mult, OP.add)),
                             inc="DV")
                tv[j]["conv_dv"] = sch.val("DV")
            return f
        return {"dve": {3 + 2 * c: [mk(c)] for c in range(KC)}, "act": {}}

    def attn_proj(j):
        tv[j]["xcb"] = sch.emit("vector", (lambda e: e.tensor_copy(HATT[:], SB[:])),
                                waits=[("PE", tv[j]["hatt_war"])], inc="DV")
        sig_r = {}
        sig_i = {}
        proj8(wr_s, (PB[0], PB[1]), SC_, RB, tv[j]["xcb"],
              tv[j].get("ri_war_ac", 0), tv[j].get("ri_war_dv", 0), sig_r)
        tv[j]["ri_pe"] = proj8(wi_s, (PB[2], PB[3]), SD, IB, tv[j]["xcb"],
                               tv[j].get("ri_war_ac", 0), tv[j].get("ri_war_dv2", 0), sig_i)
        tv[j]["sig_r"] = sig_r[KC - 1]
        tv[j]["sig_i"] = sig_i[KC - 1]

    def attn_tail(j, res_j, conv_war_dv):
        for c in range(KC):
            sch.emit("vector",
                     (lambda e, c=c: e.tensor_mul(SD[:, 0:TT],
                                                  res_j[:, c * TT:(c + 1) * TT], PB[6][:])),
                     waits=[("AC", tv[j]["bc2p"]), ("DV", tv[j]["bc2q"]),
                            ("DV", conv_war_dv)], inc="DV")
            sch.emit("vector",
                     (lambda e: e.tensor_add(LTMP[:, 3:3 + TT], SD[:, 0:TT], PB[7][:])),
                     inc="DV")
            sch.emit("vector",
                     (lambda e, c=c: e.tensor_copy(LTMP[:, 0:3], TAIL[:, c * 4:c * 4 + 3])),
                     inc="DV")
            sch.emit("vector",
                     (lambda e, c=c: e.tensor_copy(TAIL[:, c * 4:c * 4 + 3],
                                                   LTMP[:, TT:TT + 3])),
                     inc="DV")
            sch.emit("vector",
                     (lambda e, c=c: e.tensor_scalar(SB[:, c * TT:(c + 1) * TT],
                                                     LTMP[:, 0:TT], CW[:, c:c + 1],
                                                     CB[:, c:c + 1], OP.mult, OP.add)),
                     inc="DV")
            for k in range(1, 4):
                sch.emit("vector",
                         (lambda e, c=c, k=k: e.scalar_tensor_tensor(
                             SB[:, c * TT:(c + 1) * TT], LTMP[:, k:k + TT],
                             CW[:, k * 8 + c:k * 8 + c + 1],
                             SB[:, c * TT:(c + 1) * TT], OP.mult, OP.add)),
                         inc="DV")
        tv[j]["conv_dv"] = sch.val("DV")
        if DEBUG_TAPS and j == 1:
            tap("d_xc1", SB[:], dv_wait=tv[j]["conv_dv"])
        tv[j]["xcb"] = sch.emit("vector", (lambda e: e.tensor_copy(HATT[:], SB[:])),
                                waits=[("PE", tv[j]["hatt_war"])], inc="DV")
        if DEBUG_TAPS and j == 1:
            tap("d_xcb1", BFS[:, 0:T], dv_wait=tv[j]["xcb"])
        sig_r = {}
        sig_i = {}
        proj8(wr_s, (PB[0], PB[1]), SC_, RB, tv[j]["xcb"],
              tv[j].get("ri_war_ac", 0), tv[j].get("ri_war_dv", 0), sig_r)
        tv[j]["ri_pe"] = proj8(wi_s, (PB[2], PB[3]), SD, IB, tv[j]["xcb"],
                               tv[j].get("ri_war_ac", 0), tv[j].get("ri_war_dv2", 0), sig_i)
        tv[j]["sig_r"] = sig_r[KC - 1]
        tv[j]["sig_i"] = sig_i[KC - 1]

    if DEBUG_TAPS:
        tap("d_l8a", L8[:], dv_wait=v_l8)
    rr = ln_stats_bcast(0, RES[0], PB[6], PB[7], 0, 0, eps=EPS5)
    lnb, lnbc = normalize_bf16(RES[0], 0, 0, SG[0], SG[1], 0,
                               q_dv=rr["q"], p_ac=rr["p"])
    gateup_pe, silu_last, stt_dv, _ = ffn(RES[0], lnb, (PB[0], PB[1], PB[6]),
                                          (PB[2], PB[3]), (PB[4], PB[5]),
                                          lnb_chunks=lnbc)
    tv[0]["gateup_pe"] = gateup_pe
    tv[0]["silu_last"] = silu_last
    tv[0]["stt_dv"] = stt_dv
    rr2 = ln2_stats_psum(0, RES[0], gateup_pe, stt_dv)
    tv[0]["ln2_sv"] = rr2["sv"]
    tv[0]["bc2p"], tv[0]["bc2q"] = rr2["bc2p"], rr2["bc2q"]
    tv[0]["ri_war_ac"] = silu_last
    tv[0]["ri_war_dv"] = lnb
    tv[0]["ri_war_dv2"] = stt_dv
    tv[0]["hatt_war"] = 0
    attn_tail(0, RES[0], 0)
    if DEBUG_TAPS:
        tap("d_l8b", L8[:], ac_wait=tv[0]["sig_i"])

    hoisted = {}
    for i in range(NT):
        j = i + 1
        mid = mid_make(i)
        rr2 = None
        if j < NT:
            res_j = RES[j % 2]
            bfs1w = tv[i - 1]["gateup2_pe"] if i > 0 else tv[i]["ln2_sv"]
            if j in hoisted:
                lnb, lnbc = hoisted[j]
            else:
                rr = ln_stats_bcast(j, res_j, PB[6], PB[7],
                                    bfs1w, tv[i]["ln2_sv"], eps=EPS5,
                                    extra_pe_waits=[("DV", tv[i].get("conv_dv", 0)),
                                                    ("AC", tv[i]["sig_i"])])
                lnb, lnbc = normalize_bf16(res_j, 0, 0, SG[0], SG[1], bfs1w,
                                     q_dv=rr["q"], p_ac=rr["p"])
            tv[j]["lnb"] = lnb
            gateup_pe, silu_last, stt_dv, _ = ffn(
                res_j, lnb, (PB[0], PB[1], PB[6]), (PB[2], PB[3]), (PB[4], PB[5]),
                gate_war_ac=tv[i]["sig_i"], up_war_ac=tv[i]["sig_i"],
                up_war_dv=tv[i].get("stt2_dv", tv[i].get("stt_dv", 0)), mid=mid,
                lnb_chunks=lnbc)
            tv[j]["gateup_pe"] = gateup_pe
            tv[j]["silu_last"] = silu_last
            tv[j]["stt_dv"] = stt_dv
            if DEBUG_TAPS and j == 1:
                tap("d_x1", RES[1][:], dv_wait=stt_dv)
            rr2 = ln2_stats_psum(j, res_j, gateup_pe, stt_dv)
            tv[j]["ln2_sv"] = rr2["sv"]
        else:
            for fn_ in mid["all"]:
                fn_()

        o_copy = {}
        units = []
        pb45_war = tv[j]["stt_dv"] if j < NT else tv[i - 1]["stt2_dv"]
        for md in range(KC):
            if md % 4 == 0:
                units.append(unit_load(wo_s[:, md * 1024:(md + 4) * 1024], UNIT))
            slot, u = units[-1]
            po = PB[4 + md % 2]
            base = (md % 4) * 1024
            def mm_o(e, po=po, slot=slot, base=base):
                last = None
                for kc in range(KC):
                    last = e.matmul(po[:], slot[:, base + kc * 128: base + kc * 128 + 128],
                                    HATT[:, kc * TT:(kc + 1) * TT],
                                    start=(kc == 0), stop=(kc == KC - 1))
                return last
            ov = sch.emit("tensor", mm_o,
                          waits=[("WL", ring_state["loaded"][u]), ("DV", tv[i]["hatt"]),
                                 ("DV", pb45_war), ("DV", o_copy.get(md - 2, 0))],
                          inc="PE")
            unit_used(u)
            o_copy[md] = sch.emit(
                "vector",
                (lambda e, po=po, md=md: e.tensor_copy(SB[:, md * TT:(md + 1) * TT], po[:])),
                waits=[("PE", ov)], inc="DV")
        tv[i]["outw"] = sch.val("PE")
        if DEBUG_TAPS and i == 1:
            tap("d_hatt", HATT[:], dv_wait=tv[i]["hatt"])
            tap("d_o", SB[:], dv_wait=o_copy[KC - 1])
        osq_last = 0
        bfs2_war = tv[j]["ln2_sv"] if j < NT else tv[i]["ln2_sv"]
        for md in range(KC):
            osq_last = sch.emit(
                "scalar",
                (lambda e, md=md: e.activation(
                    BFS[:, T + md * TT:T + (md + 1) * TT],
                    SB[:, md * TT:(md + 1) * TT], AF.Square)),
                waits=[("DV", o_copy[md]), ("PE", bfs2_war)], inc="AC")
        def mm_rms(e):
            last = None
            for md in range(KC):
                last = e.matmul(PB[4][:], ONESB[:, 0:P],
                                BFS[:, T + md * TT:T + (md + 1) * TT],
                                start=(md == 0), stop=(md == KC - 1))
            return last
        v_rs = sch.emit("tensor", mm_rms,
                        waits=[("AC", osq_last), ("DV", o_copy[KC - 1])], inc="PE")
        v_ms = sch.emit("vector",
                        (lambda e: e.tensor_scalar_mul(SG[0][:], PB[4][:], 1.0 / D)),
                        waits=[("PE", v_rs), ("AC", state["last_silu"]),
                               ("AC", state["last_rowp"])], inc="DV")
        sch.emit("scalar",
                 (lambda e: e.activation(SG[0][:], SG[0][:], AF.Ln, bias=EPS6)),
                 waits=[("DV", v_ms)], inc="AC")
        v_rp = sch.emit("scalar",
                        (lambda e: e.activation(SG[0][:], SG[0][:], AF.Exp, scale=CMH)),
                        inc="AC")
        state["last_rowp"] = v_rp
        res_i = RES[i % 2]
        apply_last = 0
        for md in range(KC):
            sch.emit("vector",
                     (lambda e, md=md: e.scalar_tensor_tensor(
                         SB[:, md * TT:(md + 1) * TT], SB[:, md * TT:(md + 1) * TT],
                         NW[:, md:md + 1], SG[0][:, 0:TT], OP.mult, OP.mult)),
                     waits=[("AC", v_rp)], inc="DV")
            apply_last = sch.emit(
                "vector",
                (lambda e, md=md: e.tensor_add(res_i[:, md * TT:(md + 1) * TT],
                                               res_i[:, md * TT:(md + 1) * TT],
                                               SB[:, md * TT:(md + 1) * TT])),
                inc="DV")
        if DEBUG_TAPS and i == 1:
            tap("d_msrow", SG[0][0:1, :], dv_wait=v_ms)
            tap("d_rstd", ROWP[0:1, 0:TT], ac_wait=v_rp)
            tap("d_x2", res_i[:], dv_wait=apply_last)
        bfs1_war = tv[j]["gateup_pe"] if j < NT else tv[i]["ri_pe"]
        rr3 = ln_stats_bcast(i, res_i, PB[2], PB[3], bfs1_war, v_rs,
                             res_dv=apply_last, eps=EPS5)
        lnb3, lnb3c = normalize_bf16(res_i, 0, 0, SG[0], SG[1], bfs1_war,
                              q_dv=rr3["q"], p_ac=rr3["p"])
        g2_war_ac = tv[j]["silu_last"] if j < NT else tv[i]["sig_i"]
        conv_mid = make_conv_mid(j, RES[j % 2], apply_last) if j < NT else None
        gateup2_pe, silu2_last, stt2_dv, stt2_done = ffn(
            res_i, lnb3, (PB[0], PB[1]), (PB[2], PB[3]), (PB[4], PB[5]),
            gate_war_ac=g2_war_ac, up_war_ac=0, up_war_dv=rr3["var"], mid=conv_mid,
            lnb_chunks=lnb3c)
        tv[i]["gateup2_pe"] = gateup2_pe
        tv[i]["stt2_dv"] = stt2_dv
        if j < NT:
            tv[j]["ri_war_ac"] = silu2_last
            tv[j]["ri_war_dv"] = lnb3
            tv[j]["ri_war_dv2"] = stt2_dv
        store_done = 0
        for md in range(KC):
            store_done = sch.emit(
                "gpsimd",
                (lambda e, i=i, res_i=res_i, md=md: e.dma_start(
                    out_h[:, md, i * TT:(i + 1) * TT],
                    res_i[:, md * TT:(md + 1) * TT])),
                waits=[("DV", stt2_done[md])], inc="OUT", incn=16)
        if i + 2 < NT:
            x_ready[i + 2] = sch.emit(
                "gpsimd",
                (lambda e, i=i: e.dma_start(
                    RES[i % 2][:].rearrange("p (c t) -> p c t", c=KC),
                    xh[:, :, (i + 2) * TT:(i + 3) * TT])),
                waits=[("OUT", store_done)], inc="X", incn=16)
        if j < NT:
            tv[j]["hatt_war"] = tv[i]["outw"]
            tv[j]["xcb"] = sch.emit("vector",
                                    (lambda e: e.tensor_copy(HATT[:], SB[:])),
                                    waits=[("PE", tv[j]["hatt_war"])], inc="DV")
            sig_r = {}
            sig_i = {}
            proj8(wr_s, (PB[0], PB[1]), SC_, RB, tv[j]["xcb"],
                  tv[j]["ri_war_ac"], tv[j]["ri_war_dv"], sig_r)
            if j + 1 < NT:
                jn = j + 1
                rrh = ln_stats_bcast(jn, RES[jn % 2], PB[6], PB[7],
                                     gateup2_pe, rr3["sv"], eps=EPS5,
                                     extra_pe_waits=[("DV", tv[j]["conv_dv"]),
                                                     ("AC", tv[j]["bc2p"])])
                hoisted[jn] = normalize_bf16(RES[jn % 2], 0, 0, SG[0], SG[1],
                                             gateup2_pe, q_dv=rrh["q"],
                                             p_ac=rrh["p"])
            tv[j]["ri_pe"] = proj8(wi_s, (PB[2], PB[3]), SD, IB, tv[j]["xcb"],
                                   tv[j]["ri_war_ac"], tv[j]["ri_war_dv2"], sig_i)
            tv[j]["sig_r"] = sig_r[KC - 1]
            tv[j]["sig_i"] = sig_i[KC - 1]

    if False:
        d_x1 = nc.dram_tensor("d_x1", [P, T], F32, kind="ExternalOutput")
        d_hatt = nc.dram_tensor("d_hatt", [P, T], BF16, kind="ExternalOutput")
        d_r1 = nc.dram_tensor("d_r1", [P, T], F32, kind="ExternalOutput")
        d_i1 = nc.dram_tensor("d_i1", [P, T], F32, kind="ExternalOutput")
        d_xcb1 = nc.dram_tensor("d_xcb1", [P, T], BF16, kind="ExternalOutput")
        sch.emit("gpsimd", (lambda e: e.dma_start(d_x1[:], RES[1][:])),
                 waits=[("DV", tv[1]["stt_dv"])], inc="OUT", incn=0 if False else 16)
        sch.emit("gpsimd", (lambda e: e.dma_start(d_hatt[:], HATT[:])),
                 waits=[("DV", tv[0]["hatt"])], inc="OUT", incn=16)
        sch.emit("gpsimd", (lambda e: e.dma_start(d_r1[:], SC_[:])),
                 waits=[("AC", tv[1]["sig_r"])], inc="OUT", incn=16)
        sch.emit("gpsimd", (lambda e: e.dma_start(d_i1[:], SD[:])),
                 waits=[("AC", tv[1]["sig_i"])], inc="OUT", incn=16)
        sch.emit("gpsimd", (lambda e: e.dma_start(d_xcb1[:], BFS[:, 0:T])),
                 waits=[("DV", tv[1]["xcb"])], inc="OUT", incn=16)
        sch.ops["gpsimd"].append(([("OUT", 16 * NT + 80)], None, None))
    else:
        sch.ops["gpsimd"].append(([("OUT", sch.val("OUT"))], None, None))

    with nc.Block() as blk:
        def runner(lst):
            def body(e):
                for ws, fn, iv in lst:
                    for s, v in ws:
                        e.wait_ge(SEMS[s], v)
                    if fn is None:
                        continue
                    inst = fn(e)
                    if iv is not None:
                        inst.then_inc(SEMS[iv[0]], iv[1])
            return body
        blk.sync(runner(sch.ops["sync"]))
        blk.gpsimd(runner(sch.ops["gpsimd"]))
        blk.tensor(runner(sch.ops["tensor"]))
        blk.vector(runner(sch.ops["vector"]))
        blk.scalar(runner(sch.ops["scalar"]))
    return nc


def _host_layouts(inputs):
    wg = np.asarray(inputs["L0_ffn_w_gate"])
    wu = np.asarray(inputs["L0_ffn_w_up"])
    wd = np.asarray(inputs["L0_ffn_w_down"])
    wr = np.asarray(inputs["L0_attn_Wr_w"], dtype=np.float32)
    wi = np.asarray(inputs["L0_attn_Wi_w"], dtype=np.float32)
    wo = np.asarray(inputs["L0_attn_out_w"], dtype=np.float32)

    def hcmaj(w):
        return np.ascontiguousarray(
            w.reshape(HC, 128, KC, 128).transpose(3, 0, 2, 1).reshape(128, -1))

    def mdmaj(w):
        return np.ascontiguousarray(
            w.reshape(KC, 128, HC, 128).transpose(3, 0, 2, 1).reshape(128, -1))

    def mcmaj(w):
        return np.ascontiguousarray(
            w.reshape(KC, 128, KC, 128).transpose(3, 0, 2, 1).reshape(128, -1))

    def vec(v):
        return np.ascontiguousarray(np.asarray(v, dtype=np.float32).reshape(KC, 128).T)

    cwf = np.asarray(inputs["L0_attn_conv_weight"], dtype=np.float32)[:, 0, :]
    cw = np.ascontiguousarray(
        cwf.reshape(KC, 128, 4).transpose(1, 2, 0).reshape(128, 32))

    common = {
        "wg": hcmaj(wg), "wu": hcmaj(wu), "wd": mdmaj(wd),
        "wr": mcmaj(wr), "wi": mcmaj(wi), "wo": mcmaj(wo),
        "rb": vec(inputs["L0_attn_Wr_b"]), "ib": vec(inputs["L0_attn_Wi_b"]),
        "la": vec(inputs["L0_attn_log_a"]), "nw": vec(inputs["L0_attn_norm_w"]),
        "cw": cw, "cb": vec(inputs["L0_attn_conv_bias"]),
    }
    x = np.asarray(inputs["L0_attn_x"], dtype=np.float32)
    maps = []
    for b in range(8):
        xb = np.ascontiguousarray(
            x[b].T.reshape(KC, 128, T).transpose(1, 0, 2))
        m = dict(common)
        m["x"] = xb
        maps.append(m)
    return maps


def kernel(**inputs):
    in_maps = _host_layouts(inputs)
    nc = build_nc()
    res = run_bass_kernel_spmd(nc, in_maps, core_ids=list(range(8)))
    outs = []
    for b in range(8):
        o = np.asarray(res.results[b]["out"])
        outs.append(o.transpose(1, 0, 2).reshape(D, T).T)
    return np.stack(outs).astype(np.float32)
